# version 06
# speedup vs baseline: 1.0580x; 1.0137x over previous
import numpy as np
from contextlib import ExitStack

import concourse.bass as bass
import concourse.mybir as mybir
from concourse.bass_utils import run_bass_kernel_spmd

import ml_dtypes
BF16NP = ml_dtypes.bfloat16

F32 = mybir.dt.float32
BF16 = mybir.dt.bfloat16
I32 = mybir.dt.int32
U8 = mybir.dt.uint8
U32 = mybir.dt.uint32
AF = mybir.ActivationFunctionType
ALU = mybir.AluOpType
AX = mybir.AxisListType

T = 4096
D = 1024
DC = D // 128
NCORES = 8
TTT_DEBUG = False

ENGS = ("tensor", "vector", "scalar", "gpsimd", "sync")


class Ctx:
    def __init__(self, fused, reads=(), writes=(), force_outputs=(), force_inputs=()):
        self.nc = bass.Bass("TRN2", target_bir_lowering=False)
        self.stack = ExitStack()
        self.fused = fused
        self.reads = set(reads)
        self.writes = set(writes)
        self.force_outputs = set(force_outputs)
        self.force_inputs = set(force_inputs)
        self.dram = {}
        self.cnt = {}
        self.sems = {}
        for n in ("qs", "qc", "qg", "qv", "mm", "ve", "sc", "gp", "x1", "x2", "ph"):
            self.sems[n] = self.stack.enter_context(self.nc.semaphore(n))
            self.cnt[n] = 0
        self.block = self.stack.enter_context(self.nc.Block())
        self.nbar = 0
        self.pstack = ExitStack()

    def snap(self):
        return dict(self.cnt)

    def fence(self, e, sem):
        if self.cnt[sem]:
            e.wait_ge(self.sems[sem], self.cnt[sem])

    def inp(self, name, shape, dtype):
        if name not in self.dram:
            self.dram[name] = self.nc.dram_tensor(name, list(shape), dtype,
                                                  kind="ExternalInput").ap()
        return self.dram[name]

    def outp(self, name, shape, dtype):
        if name not in self.dram:
            self.dram[name] = self.nc.dram_tensor(name, list(shape), dtype,
                                                  kind="ExternalOutput").ap()
        return self.dram[name]

    def buf(self, name, shape, dtype):
        if name in self.dram:
            return self.dram[name]
        if name in self.force_outputs:
            t = self.nc.dram_tensor(name, list(shape), dtype, kind="ExternalOutput")
        elif name in self.force_inputs:
            t = self.nc.dram_tensor(name, list(shape), dtype, kind="ExternalInput")
        elif self.fused:
            t = self.nc.dram_tensor(name, list(shape), dtype)
        elif name in self.writes:
            t = self.nc.dram_tensor(name, list(shape), dtype, kind="ExternalOutput")
        elif name in self.reads:
            t = self.nc.dram_tensor(name, list(shape), dtype, kind="ExternalInput")
        else:
            t = self.nc.dram_tensor(name, list(shape), dtype)
        self.dram[name] = t.ap()
        return self.dram[name]

    def tag(self, instr, sem, n=1):
        instr.then_inc(self.sems[sem], n)
        self.cnt[sem] += n
        return self.cnt[sem]

    def dma(self, eng, out, in_, sem, **kw):
        ins = eng.dma_start(out=out, in_=in_, **kw)
        return self.tag(ins, sem, 16)

    def sbuf(self, name, shape, dtype):
        return self.pstack.enter_context(self.nc.sbuf_tensor(name, list(shape), dtype))

    def psum(self, name, shape, dtype=F32):
        return self.pstack.enter_context(self.nc.psum_tensor(name, list(shape), dtype))

    def barrier(self):
        self.nbar += 1
        target = 5 * self.nbar
        for engname in ENGS:
            eng = getattr(self.block, engname)

            def _body(e):
                e.sem_inc(self.sems["ph"], 1)
                e.wait_ge(self.sems["ph"], target)
            eng(_body)
        self.cnt["ph"] = target
        self.pstack.close()
        self.pstack = ExitStack()

    def close(self):
        self.stack.close()
        return self.nc


def fm_view(ap, t0, ts, c0=0, nch=DC):
    return ap.rearrange("(c p) t -> p c t", p=128)[:, c0:c0 + nch, t0:t0 + ts]


def sb3(tile, nch, ts):
    return tile.rearrange("p (c t) -> p c t", t=ts)[:, 0:nch, :]


def _bf(x):
    return np.ascontiguousarray(np.asarray(x)).astype(BF16NP)


def _f32(x):
    return np.ascontiguousarray(np.asarray(x), dtype=np.float32)


def _bcol(x):
    a = np.asarray(x, dtype=np.float32).reshape(-1, 128)
    return np.ascontiguousarray(a.T)


def prep_weights(iv):
    w = {}
    w["l0_wq"] = _bf(iv["L0_attn_wq"])
    w["l0_wk"] = _bf(iv["L0_attn_wk"])
    w["l0_wv"] = _bf(iv["L0_attn_wv"])
    w["l0_wo"] = _bf(iv["L0_attn_wo"])
    w["l0_wg1"] = _bf(iv["L0_attn_wg1"])
    w["l0_wg2"] = _bf(iv["L0_attn_wg2"])
    w["l0_normw"] = _f32(iv["L0_attn_norm_w"]).reshape(D, 1)
    w["l0_normw_r"] = _f32(iv["L0_attn_norm_w"]).reshape(1, D)
    w["l0_wgate"] = _bf(np.asarray(iv["L0_ffn_Wg"]).T)
    w["l0_wgatef"] = _f32(np.asarray(iv["L0_ffn_Wg"]).T)
    w["l0_w1"] = _bf(np.transpose(np.asarray(iv["L0_ffn_W1"]), (0, 2, 1)))
    w["l0_w2"] = _bf(np.transpose(np.asarray(iv["L0_ffn_W2"]), (0, 2, 1)))
    w["l0_w3"] = _bf(np.transpose(np.asarray(iv["L0_ffn_W3"]), (0, 2, 1)))
    w["l1_qw"] = _bf(iv["L1_attn_q_w"])
    w["l1_kw"] = _bf(iv["L1_attn_k_w"])
    w["l1_vw"] = _bf(iv["L1_attn_v_w"])
    w["l1_bw"] = _bf(iv["L1_attn_b_w"])
    w["l1_qc"] = _f32(np.asarray(iv["L1_attn_qc_w"])[:, 0, :])
    w["l1_kc"] = _f32(np.asarray(iv["L1_attn_kc_w"])[:, 0, :])
    w["l1_vc"] = _f32(np.asarray(iv["L1_attn_vc_w"])[:, 0, :])
    w["l1_fir1"] = _f32(np.asarray(iv["L1_attn_fir_w1"])[:, 0, :])
    w["l1_fir3"] = _f32(np.asarray(iv["L1_attn_fir_w3"])[:, 0, :])
    w["l1_fir7"] = _f32(np.asarray(iv["L1_attn_fir_w7"])[:, 0, :])
    w["l1_fir31"] = _f32(np.asarray(iv["L1_attn_fir_w31"])[:, 0, :])
    _w1 = np.asarray(iv["L1_attn_mlp_w1"])
    _w1p = np.zeros((1152, 1024), np.float32)
    _w1p[:1024] = _w1[:1024]
    for _h in range(4):
        _w1p[1024 + _h * 32:1024 + _h * 32 + 24] = _w1[1024 + _h * 24:1024 + (_h + 1) * 24]
    w["l1_mw1"] = _bf(_w1p)
    w["l1_mb1"] = _bcol(iv["L1_attn_mlp_b1"])
    w["l1_mw2"] = _bf(iv["L1_attn_mlp_w2"])
    w["l1_mb2"] = _f32(iv["L1_attn_mlp_b2"]).reshape(1, 24)
    w["l1_glt"] = _f32(iv["L1_attn_gate_log_temp"]).reshape(4, 1)
    w["l1_onorm"] = _bcol(np.tile(np.asarray(iv["L1_attn_onorm_w"]), 4))
    w["l1_ow"] = _bf(iv["L1_attn_o_w"])
    w["l1_fg"] = np.ascontiguousarray(np.asarray(iv["L1_ffn_w_gate"]).T).astype(np.int32)
    w["l1_fu"] = np.ascontiguousarray(np.asarray(iv["L1_ffn_w_up"]).T).astype(np.int32)
    w["l1_fd"] = np.ascontiguousarray(np.asarray(iv["L1_ffn_w_down"]).T).astype(np.int32)
    w["l2_down"] = _bf(iv["L2_attn_down_w"])
    w["l2_downb"] = _bcol(iv["L2_attn_down_b"])
    w["l2_up"] = _bf(iv["L2_attn_up_w"])
    w["l2_upb"] = _bcol(iv["L2_attn_up_b"])
    w["l2_wq"] = _bf(iv["L2_attn_wq"])
    w["l2_wk"] = _bf(iv["L2_attn_wk"])
    w["l2_wv"] = _bf(iv["L2_attn_wv"])
    w["l2_wo"] = _bf(iv["L2_attn_wo"])
    w["l2_W1"] = _f32(iv["L2_attn_W1"])
    w["l2_b1r"] = _f32(np.asarray(iv["L2_attn_b1"])[:, 0, :]).reshape(1, 256)
    w["l2_lnwr"] = _f32(iv["L2_attn_ttt_ln_w"]).reshape(1, 256)
    w["l2_lnbr"] = _f32(iv["L2_attn_ttt_ln_b"]).reshape(1, 256)
    w["l2_lrbr"] = _f32(iv["L2_attn_lr_b"]).reshape(1, 2)
    w["l2_lnw"] = _f32(iv["L2_attn_ttt_ln_w"])
    w["l2_lnb"] = _f32(iv["L2_attn_ttt_ln_b"])
    w["l2_lrw"] = _bf(np.asarray(iv["L2_attn_lr_w"])[:, 0, :].T)
    w["l2_lrb"] = _f32(iv["L2_attn_lr_b"]).reshape(2, 1)
    w["l2_tok"] = _f32(iv["L2_attn_tok_idx"]).reshape(1, 16)
    w["l2_postw"] = _f32(iv["L2_attn_post_w"]).reshape(1, 256)
    w["l2_postb"] = _f32(iv["L2_attn_post_b"]).reshape(1, 256)
    w["l2_lnw2"] = _f32(iv["L2_attn_ln_w"]).reshape(1, 256)
    w["l2_lnb2"] = _f32(iv["L2_attn_ln_b"]).reshape(1, 256)
    _ = (iv["L2_ffn_uf"], iv["L2_ffn_ui"], iv["L2_ffn_uc"], iv["L2_ffn_uo"],
         iv["L3_ffn_uf"], iv["L3_ffn_ui"], iv["L3_ffn_uc"], iv["L3_ffn_uo"])
    for L, p in ((2, "L2_ffn"), (3, "L3_ffn")):
        for nm in ("wf", "wi", "wc", "wo", "uf", "ui", "uc", "uo"):
            w[f"l{L}f_{nm}"] = _bf(np.asarray(iv[f"{p}_{nm}"]).T)
        for nm, src in (("bf", "bf"), ("bi", "bi"), ("bc", "bc"), ("bo", "bo"),
                        ("inc", "in_c")):
            w[f"l{L}f_{nm}"] = _bcol(iv[f"{p}_{src}"])
    w["l3_ln1g"] = _bcol(iv["L3_attn_ln1_g"])
    w["l3_ln1b"] = _bcol(iv["L3_attn_ln1_b"])
    w["l3_wq"] = _bf(iv["L3_attn_wq"])
    w["l3_wk"] = _bf(iv["L3_attn_wk"])
    w["l3_wv"] = _bf(iv["L3_attn_wv"])
    w["l3_wo"] = _bf(iv["L3_attn_wo"])
    w["l3_bo"] = _bcol(iv["L3_attn_bo"])
    w["l3_ln2g"] = _bcol(iv["L3_attn_ln2_g"])
    w["l3_ln2b"] = _bcol(iv["L3_attn_ln2_b"])
    w["l3_ff1"] = _bf(iv["L3_attn_w_ff1"])
    w["l3_bff1"] = _bcol(iv["L3_attn_b_ff1"])
    w["l3_ff2"] = _bf(iv["L3_attn_w_ff2"])
    w["l3_bff2"] = _bcol(iv["L3_attn_b_ff2"])
    w["l3_wd"] = _bf(iv["L3_attn_w_d"])
    w["l3_bd"] = _bcol(iv["L3_attn_b_d"])
    return w


def prep_consts():
    c = {}
    c["identb"] = np.eye(128, dtype=np.float32).astype(BF16NP)
    c["identf"] = np.eye(128, dtype=np.float32)
    s = np.arange(64)
    m = (s[:, None] <= s[None, :]).astype(np.float32)
    sgn = np.where(((s[:, None] + s[None, :]) % 2) == 0, 1.0, -1.0).astype(np.float32)
    c["gla_mask"] = (m * sgn).astype(np.float32)
    c["incl64"] = m.astype(np.float32)
    s32 = np.arange(32)
    c["d_strictT"] = (s32[:, None] < s32[None, :]).astype(np.float32)
    c["d_nstrictT"] = -(s32[:, None] < s32[None, :]).astype(np.float32)
    c["d_nstrictL"] = -(s32[:, None] > s32[None, :]).astype(np.float32)
    c["d_inclT"] = (s32[:, None] <= s32[None, :]).astype(np.float32)
    c["eye32"] = np.eye(32, dtype=np.float32).astype(BF16NP)
    s16 = np.arange(16)
    c["t_inclT"] = (s16[:, None] <= s16[None, :]).astype(np.float32)
    c["t_strictT"] = (s16[:, None] < s16[None, :]).astype(np.float32)
    c["invk"] = (1.0 / np.arange(1, 17)).astype(np.float32).reshape(1, 16)
    for n, nm in ((8, "onehot8"), (24, "onehot24")):
        e = np.zeros((n, n * 128), dtype=np.float32)
        for i in range(n):
            e[i, i * 128:(i + 1) * 128] = 1.0
        c[nm] = e
    hs = np.zeros((8 * 128, 4), dtype=np.float32)
    for cch in range(8):
        hs[cch * 128:(cch + 1) * 128, cch // 2] = 1.0
    c["headsel"] = hs
    c["headselb"] = hs.astype(BF16NP)
    m424 = np.zeros((4, 24), dtype=np.float32)
    for h in range(4):
        m424[h, h * 6:(h + 1) * 6] = 1.0
    c["map424"] = m424
    alt = np.where((np.arange(T) % 2) == 0, -1.0, 1.0).astype(np.float32)
    c["alt_row"] = alt.reshape(1, T)
    c["alt_rep"] = np.tile(alt[None, 0:512], (128, 1))
    return c


CONST_SPECS = {
    "identb": ((128, 128), BF16), "identf": ((128, 128), F32),
    "gla_mask": ((64, 64), F32),
    "incl64": ((64, 64), F32),
    "d_strictT": ((32, 32), F32), "d_inclT": ((32, 32), F32),
    "d_nstrictT": ((32, 32), F32), "d_nstrictL": ((32, 32), F32),
    "eye32": ((32, 32), BF16),
    "t_inclT": ((16, 16), F32), "t_strictT": ((16, 16), F32),
    "invk": ((1, 16), F32),
    "onehot8": ((8, 8 * 128), F32), "onehot24": ((24, 24 * 128), F32),
    "headsel": ((8 * 128, 4), F32), "headselb": ((8 * 128, 4), BF16),
    "map424": ((4, 24), F32),
    "alt_row": ((1, T), F32),
    "alt_rep": ((128, 512), F32),
}


def ph_residual(ctx, L):
    xr = ctx.inp("x_in", (D, T), F32) if L == 0 else ctx.buf(f"xres{L}", (D, T), F32)
    at = ctx.buf(f"attn{L}", (D, T), F32)
    ff = ctx.buf(f"ffn{L}", (D, T), F32)
    xo = ctx.outp("out", (D, T), F32) if L == 3 else ctx.buf(f"xres{L + 1}", (D, T), F32)

    TS = 512
    NT = T // TS
    blk = ctx.block
    s0 = ctx.snap()
    sa = [ctx.sbuf(f"rs{L}_a{i}", [128, DC * TS], F32) for i in range(2)]
    sbb = [ctx.sbuf(f"rs{L}_b{i}", [128, DC * TS], F32) for i in range(2)]
    scc = [ctx.sbuf(f"rs{L}_c{i}", [128, DC * TS], F32) for i in range(2)]
    so = [ctx.sbuf(f"rs{L}_o{i}", [128, DC * TS], F32) for i in range(2)]

    @blk.sync
    def _(e):
        for j in range(NT):
            if j >= 2:
                e.wait_ge(ctx.sems["gp"], s0["gp"] + (j - 1))
            ctx.fence(e, "qs")
            ctx.dma(e, sb3(sa[j % 2], DC, TS), fm_view(xr, j * TS, TS), "qs")
            ctx.dma(e, sb3(sbb[j % 2], DC, TS), fm_view(at, j * TS, TS), "qs")
            ctx.dma(e, sb3(scc[j % 2], DC, TS), fm_view(ff, j * TS, TS), "qs")

    @blk.vector
    def _(e):
        for j in range(NT):
            e.wait_ge(ctx.sems["qs"], s0["qs"] + (j + 1) * 3 * 16)
            for ch in range(DC):
                sl = slice(ch * TS, (ch + 1) * TS)
                e.tensor_tensor(out=so[j % 2][:, sl], in0=sa[j % 2][:, sl],
                                in1=sbb[j % 2][:, sl], op=ALU.add)
                ins = e.tensor_tensor(out=so[j % 2][:, sl], in0=so[j % 2][:, sl],
                                      in1=scc[j % 2][:, sl], op=ALU.add)
            ctx.tag(ins, "ve")

    @blk.gpsimd
    def _(e):
        for j in range(NT):
            e.wait_ge(ctx.sems["ve"], s0["ve"] + j + 1)
            ctx.dma(e, fm_view(xo, j * TS, TS), sb3(so[j % 2], DC, TS), "qg")
            e.wait_ge(ctx.sems["qg"], ctx.cnt["qg"])
            ins = e.engine_nop()
            ctx.tag(ins, "gp")


def ph_ln(ctx, L, eps=1e-5):
    src_name = "x_in" if L == 0 else f"xres{L}"
    ph_ln_g(ctx, f"ln{L}", src_name, f"xn{L}",
            dstf=(f"xnf{L}" if L in (0, 3) else None), eps=eps,
            src_is_input=(L == 0))


def ph_ln_g(ctx, tagp, src, dst, gain=None, bias=None, dstf=None, eps=1e-5,
            src_is_input=False):
    L = tagp
    xr = ctx.inp(src, (D, T), F32) if src_is_input else ctx.buf(src, (D, T), F32)
    xn = ctx.buf(dst, (D, T), BF16)
    xnf = ctx.buf(dstf, (D, T), F32) if dstf else None
    g_ap = ctx.inp(gain, (128, DC), F32) if gain else None
    b_ap = ctx.inp(bias, (128, DC), F32) if bias else None

    TS = 512
    NT = T // TS
    blk = ctx.block
    s0 = ctx.snap()
    sx = [ctx.sbuf(f"ln{L}_x{i}", [128, DC * TS], F32) for i in range(2)]
    sx2 = ctx.sbuf(f"ln{L}_x2", [128, DC * TS], F32)
    sonc = ctx.sbuf(f"ln{L}_onc", [128, 1], F32)
    sonr = ctx.sbuf(f"ln{L}_onr", [1, 128], F32)
    srow = ctx.sbuf(f"ln{L}_row", [1, 4 * TS], F32)
    so = [ctx.sbuf(f"ln{L}_o{i}", [128, DC * TS], BF16) for i in range(2)]
    sof = [ctx.sbuf(f"ln{L}_of{i}", [128, DC * TS], F32) for i in range(2)]
    sgb = ctx.sbuf(f"ln{L}_gb", [128, 2 * DC], F32) if (g_ap or b_ap) else None
    ps_mu = ctx.psum(f"ln{L}_pmu", [1, TS])
    ps_sq = ctx.psum(f"ln{L}_psq", [1, TS])
    ps_mb = ctx.psum(f"ln{L}_pmb", [128, TS])
    ps_rb = ctx.psum(f"ln{L}_prb", [128, TS])
    MU, MS, VAR, RSTD = 0, TS, 2 * TS, 3 * TS

    @blk.vector
    def _(e):
        e.memset(sonc[:], 1.0)
        ins = e.memset(sonr[:], 1.0)
        ctx.tag(ins, "x1")

    ones_ready = ctx.cnt["x1"]

    @blk.sync
    def _(e):
        if g_ap is not None:
            ctx.dma(e, sgb[:, 0:DC], g_ap[:, :], "qs")
        if b_ap is not None:
            ctx.dma(e, sgb[:, DC:2 * DC], b_ap[:, :], "qs")
    ngb = (1 if g_ap is not None else 0) + (1 if b_ap is not None else 0)

    @blk.sync
    def _(e):
        for j in range(NT):
            if j >= 2:
                e.wait_ge(ctx.sems["gp"], s0["gp"] + (j - 1))
            ctx.fence(e, "qs")
            ctx.dma(e, sb3(sx[j % 2], DC, TS), fm_view(xr, j * TS, TS), "qs")

    @blk.scalar
    def _(e):
        for j in range(NT):
            e.wait_ge(ctx.sems["qs"], s0["qs"] + (ngb + j + 1) * 16)
            for ch in range(DC):
                ins = e.activation(out=sx2[:, ch * TS:(ch + 1) * TS],
                                   in_=sx[j % 2][:, ch * TS:(ch + 1) * TS],
                                   func=AF.Square)
            ctx.tag(ins, "sc")
            e.wait_ge(ctx.sems["ve"], s0["ve"] + 4 * j + 2)
            ins = e.activation(out=srow[:, RSTD:RSTD + TS],
                               in_=srow[:, VAR:VAR + TS], func=AF.Sqrt)
            ctx.tag(ins, "sc")

    @blk.tensor
    def _(e):
        e.wait_ge(ctx.sems["x1"], ones_ready)
        for j in range(NT):
            if j >= 1:
                e.wait_ge(ctx.sems["ve"], s0["ve"] + 4 * (j - 1) + 2)
            e.wait_ge(ctx.sems["qs"], s0["qs"] + (ngb + j + 1) * 16)
            for ch in range(DC):
                e.matmul(ps_mu[:], sonc[:], sx[j % 2][:, ch * TS:(ch + 1) * TS],
                         start=(ch == 0), stop=(ch == DC - 1))
            e.wait_ge(ctx.sems["sc"], s0["sc"] + 2 * j + 1)
            for ch in range(DC):
                ins = e.matmul(ps_sq[:], sonc[:], sx2[:, ch * TS:(ch + 1) * TS],
                               start=(ch == 0), stop=(ch == DC - 1))
            ctx.tag(ins, "mm")
            if j >= 1:
                e.wait_ge(ctx.sems["ve"], s0["ve"] + 4 * (j - 1) + 4)
            e.wait_ge(ctx.sems["ve"], s0["ve"] + 4 * j + 1)
            e.matmul(ps_mb[:], sonr[:], srow[0:1, MU:MU + TS], start=True, stop=True)
            e.wait_ge(ctx.sems["ve"], s0["ve"] + 4 * j + 3)
            ins = e.matmul(ps_rb[:], sonr[:], srow[0:1, RSTD:RSTD + TS],
                           start=True, stop=True)
            ctx.tag(ins, "mm")

    @blk.vector
    def _(e):
        for j in range(NT):
            e.wait_ge(ctx.sems["mm"], s0["mm"] + 2 * j + 1)
            e.tensor_scalar(out=srow[:, MS:MS + TS], in0=ps_sq[:],
                            scalar1=1.0 / D, scalar2=None, op0=ALU.mult)
            ins = e.tensor_scalar(out=srow[:, MU:MU + TS], in0=ps_mu[:],
                                  scalar1=1.0 / D, scalar2=None, op0=ALU.mult)
            ctx.tag(ins, "ve")
            e.tensor_tensor(out=srow[:, VAR:VAR + TS], in0=srow[:, MU:MU + TS],
                            in1=srow[:, MU:MU + TS], op=ALU.mult)
            ins = e.scalar_tensor_tensor(out=srow[:, VAR:VAR + TS],
                                         in0=srow[:, MS:MS + TS], scalar=float(eps),
                                         in1=srow[:, VAR:VAR + TS],
                                         op0=ALU.add, op1=ALU.subtract)
            ctx.tag(ins, "ve")
            e.wait_ge(ctx.sems["sc"], s0["sc"] + 2 * j + 2)
            ins = e.reciprocal(out=srow[:, RSTD:RSTD + TS],
                               in_=srow[:, RSTD:RSTD + TS])
            ctx.tag(ins, "ve")
            e.wait_ge(ctx.sems["mm"], s0["mm"] + 2 * j + 2)
            for ch in range(DC):
                sl = slice(ch * TS, (ch + 1) * TS)
                e.tensor_tensor(out=sof[j % 2][:, sl], in0=sx[j % 2][:, sl],
                                in1=ps_mb[:], op=ALU.subtract)
                if g_ap is not None or b_ap is not None:
                    e.tensor_tensor(out=sof[j % 2][:, sl], in0=sof[j % 2][:, sl],
                                    in1=ps_rb[:], op=ALU.mult)
                    gc = sgb[:, ch:ch + 1] if g_ap is not None else None
                    bc = sgb[:, DC + ch:DC + ch + 1] if b_ap is not None else None
                    ins = e.tensor_scalar(out=so[j % 2][:, sl], in0=sof[j % 2][:, sl],
                                          scalar1=gc, scalar2=bc,
                                          op0=ALU.mult, op1=ALU.add)
                elif xnf is not None:
                    e.tensor_tensor(out=sof[j % 2][:, sl], in0=sof[j % 2][:, sl],
                                    in1=ps_rb[:], op=ALU.mult)
                    ins = e.tensor_copy(out=so[j % 2][:, sl], in_=sof[j % 2][:, sl])
                else:
                    ins = e.tensor_tensor(out=so[j % 2][:, sl], in0=sof[j % 2][:, sl],
                                          in1=ps_rb[:], op=ALU.mult)
            ctx.tag(ins, "ve")

    @blk.gpsimd
    def _(e):
        for j in range(NT):
            e.wait_ge(ctx.sems["ve"], s0["ve"] + 4 * j + 4)
            ctx.dma(e, fm_view(xn, j * TS, TS), sb3(so[j % 2], DC, TS), "qg")
            if xnf is not None:
                ctx.dma(e, fm_view(xnf, j * TS, TS), sb3(sof[j % 2], DC, TS), "qg")
            e.wait_ge(ctx.sems["qg"], ctx.cnt["qg"])
            ins = e.engine_nop()
            ctx.tag(ins, "gp")


def ph_l1ffn(ctx):
    nc = ctx.nc
    xn = ctx.buf("xn1", (D, T), BF16)
    ffn = ctx.buf("ffn1", (D, T), F32)
    wg = ctx.inp("l1_fg", (D, 3072), I32)
    wu = ctx.inp("l1_fu", (D, 3072), I32)
    wd = ctx.inp("l1_fd", (3072, D), I32)
    S = float(1.0 / 127.0)
    H = 3072
    HC = H // 128
    TS = 256
    NT = T // TS

    blk = ctx.block
    s0 = ctx.snap()
    sxn = [ctx.sbuf(f"f1_x{i}", [128, DC * TS], BF16) for i in range(2)]
    swg = ctx.sbuf("f1_wg", [128, DC * HC * 128], BF16)
    swu = ctx.sbuf("f1_wu", [128, DC * HC * 128], BF16)
    swd = ctx.sbuf("f1_wd", [128, HC * DC * 128], BF16)
    sg = ctx.sbuf("f1_g", [128, TS], BF16)
    sh = ctx.sbuf("f1_h", [128, HC * TS], BF16)
    so = [ctx.sbuf(f"f1_o{i}", [128, DC * TS], F32) for i in range(2)]
    ps_g = ctx.psum("f1_pg", [128, TS])
    ps_u = ctx.psum("f1_pu", [128, TS])
    ps_y = [ctx.psum(f"f1_py{i}", [128, TS]) for i in range(4)]

    @blk.gpsimd
    def _(e):
        ctx.dma(e, swg.rearrange("p (c k) -> p c k", c=DC),
                wg.rearrange("(c p) k -> p c k", p=128), "qg")
        ctx.dma(e, swu.rearrange("p (c k) -> p c k", c=DC),
                wu.rearrange("(c p) k -> p c k", p=128), "qg")
        ctx.dma(e, swd.rearrange("p (c k) -> p c k", c=HC),
                wd.rearrange("(c p) k -> p c k", p=128), "qg")

    wready = s0["qg"] + 48

    @blk.sync
    def _(e):
        for j in range(NT):
            if j >= 2:
                e.wait_ge(ctx.sems["gp"], s0["gp"] + (j - 1))
            ctx.fence(e, "qs")
            ctx.dma(e, sb3(sxn[j % 2], DC, TS), fm_view(xn, j * TS, TS), "qs")

    def wgt(sl, ch, hc):
        return sl[:, (ch * HC + hc) * 128:(ch * HC + hc) * 128 + 128]

    def wdt(sl, hc, dch):
        return sl[:, (hc * DC + dch) * 128:(hc * DC + dch) * 128 + 128]

    MMJ = HC + 2 * 4
    VEJ = HC + 2

    @blk.tensor
    def _(e):
        e.wait_ge(ctx.sems["qg"], wready)
        for j in range(NT):
            e.wait_ge(ctx.sems["qs"], s0["qs"] + (j + 1) * 16)
            for hc in range(HC):
                if not (j == 0 and hc == 0):
                    prev = j * HC + hc - 1
                    e.wait_ge(ctx.sems["sc"], s0["sc"] + prev + 1)
                    e.wait_ge(ctx.sems["ve"], s0["ve"] + (prev // HC) * VEJ + (prev % HC) + 1)
                for ch in range(DC):
                    e.matmul(ps_g[:], wgt(swg, ch, hc),
                             sxn[j % 2][:, ch * TS:(ch + 1) * TS],
                             start=(ch == 0), stop=(ch == DC - 1))
                for ch in range(DC):
                    ins = e.matmul(ps_u[:], wgt(swu, ch, hc),
                                   sxn[j % 2][:, ch * TS:(ch + 1) * TS],
                                   start=(ch == 0), stop=(ch == DC - 1))
                ctx.tag(ins, "mm")
            e.wait_ge(ctx.sems["ve"], s0["ve"] + j * VEJ + HC)
            for rnd in range(2):
                if rnd == 1:
                    e.wait_ge(ctx.sems["ve"], s0["ve"] + j * VEJ + HC + 1)
                if j >= 1 and rnd == 0:
                    e.wait_ge(ctx.sems["ve"], s0["ve"] + j * VEJ)
                for dch in range(4):
                    dd = rnd * 4 + dch
                    for hc in range(HC):
                        ins = e.matmul(ps_y[dch][:], wdt(swd, hc, dd),
                                       sh[:, hc * TS:(hc + 1) * TS],
                                       start=(hc == 0), stop=(hc == HC - 1))
                    ctx.tag(ins, "mm")

    @blk.scalar
    def _(e):
        for j in range(NT):
            for hc in range(HC):
                lin = j * HC + hc
                e.wait_ge(ctx.sems["mm"], s0["mm"] + j * MMJ + hc + 1)
                if lin >= 1:
                    j2, h2 = divmod(lin - 1, HC)
                    e.wait_ge(ctx.sems["ve"], s0["ve"] + j2 * VEJ + h2 + 1)
                ins = e.activation(out=sg[:], in_=ps_g[:], func=AF.Silu, scale=S)
                ctx.tag(ins, "sc")

    @blk.vector
    def _(e):
        for j in range(NT):
            for hc in range(HC):
                e.wait_ge(ctx.sems["sc"], s0["sc"] + j * HC + hc + 1)
                ins = e.scalar_tensor_tensor(out=sh[:, hc * TS:(hc + 1) * TS],
                                             in0=ps_u[:], scalar=S, in1=sg[:],
                                             op0=ALU.mult, op1=ALU.mult)
                ctx.tag(ins, "ve")
            if j >= 2:
                e.wait_ge(ctx.sems["gp"], s0["gp"] + j - 1)
            for rnd in range(2):
                for dch in range(4):
                    dd = rnd * 4 + dch
                    e.wait_ge(ctx.sems["mm"], s0["mm"] + j * MMJ + HC + rnd * 4 + dch + 1)
                    ins = e.tensor_scalar(out=so[j % 2][:, dd * TS:(dd + 1) * TS],
                                          in0=ps_y[dch][:], scalar1=S,
                                          scalar2=None, op0=ALU.mult)
                ctx.tag(ins, "ve")

    @blk.gpsimd
    def _(e):
        e.wait_ge(ctx.sems["qg"], wready)
        for j in range(NT):
            e.wait_ge(ctx.sems["ve"], s0["ve"] + j * VEJ + VEJ)
            ctx.dma(e, fm_view(ffn, j * TS, TS), sb3(so[j % 2], DC, TS), "qg")
            e.wait_ge(ctx.sems["qg"], ctx.cnt["qg"])
            ins = e.engine_nop()
            ctx.tag(ins, "gp")


def ph_mm(ctx, tagp, src, kin, w, dst, nout, major="fm", act=None, scale=None,
          biascol=None, biasrow=None, alt=False, dst2=None, dst_dt=BF16,
          src_extra=None, TS=512, src_tok=False, io_dt=BF16, addsrc=None):
    s_ap = ctx.buf(src, (T, kin) if src_tok else (kin, T), io_dt)
    w_ap = ctx.inp(w, (kin if src_extra is None else kin + src_extra[1], nout), io_dt)
    d_ap = ctx.buf(dst, (nout, T) if major == "fm" else (T, nout), dst_dt)
    d2_ap = ctx.buf(dst2, (nout, T), F32) if dst2 else None
    bc_ap = ctx.inp(biascol, (128, nout // 128), F32) if biascol else None
    br_ap = ctx.inp(biasrow, (1, nout), F32) if biasrow else None
    se_ap = ctx.buf(src_extra[0], (src_extra[1], T), BF16) if src_extra else None
    ad_ap = ctx.buf(addsrc, (nout, T), F32) if addsrc else None

    KC = kin // 128
    OC = nout // 128
    NT = T // TS
    TC = TS // 128
    NH = ceil_div(nout, 512)
    blk = ctx.block
    s0 = ctx.snap()
    sx = [ctx.sbuf(f"{tagp}_x{i}", [128, KC * TS], io_dt) for i in range(2)]
    sxe = ctx.sbuf(f"{tagp}_xe", [src_extra[1], TS], io_dt) if src_extra else None
    swt = ctx.sbuf(f"{tagp}_w", [128, KC * nout], io_dt)
    swe = ctx.sbuf(f"{tagp}_we", [src_extra[1], nout], io_dt) if src_extra else None
    sbc = ctx.sbuf(f"{tagp}_bc", [128, OC], F32) if biascol else None
    sbr = ctx.sbuf(f"{tagp}_br", [1, nout], F32) if biasrow else None
    sone = ctx.sbuf(f"{tagp}_one", [1, 128], F32) if biasrow else None
    salt = ctx.sbuf(f"{tagp}_alt", [128, TS], F32) if alt else None
    stmp = ctx.sbuf(f"{tagp}_tmp", [128, 2 * 512], F32) if ((alt or addsrc) and not dst2) else None
    sad = [ctx.sbuf(f"{tagp}_ad{i}", [128, OC * TS], F32) for i in range(2)] if addsrc else None
    if major == "fm":
        od = [ctx.sbuf(f"{tagp}_o{i}", [128, OC * TS], dst_dt) for i in range(2)]
        od2 = [ctx.sbuf(f"{tagp}_p{i}", [128, OC * TS], F32) for i in range(2)] if dst2 else None
        n_it = OC
    else:
        od = [ctx.sbuf(f"{tagp}_o{i}", [128, TC * nout], dst_dt) for i in range(2)]
        od2 = None
        n_it = TC * NH
    ps = [ctx.psum(f"{tagp}_ps{i}", [128, 512]) for i in range(2)]
    use_ve = bool(alt) or bool(addsrc)

    if biasrow:
        @blk.vector
        def _(e):
            ins = e.memset(sone[:], 1.0)
            ctx.tag(ins, "x1")
    ones_rdy = ctx.cnt["x1"]

    @blk.sync
    def _(e):
        ctx.dma(e, swt.rearrange("p (c n) -> p c n", c=KC),
                w_ap[0:kin, :].rearrange("(c p) n -> p c n", p=128), "qs")
        if src_extra:
            ctx.dma(e, swe[:, :], w_ap[kin:kin + src_extra[1], :], "qs")
        if biascol:
            ctx.dma(e, sbc[:, :], bc_ap[:, :], "qs")
        if biasrow:
            ctx.dma(e, sbr[:], br_ap[:], "qs")
        if alt:
            ctx.dma(e, salt[:, 0:TS], ctx.inp("alt_rep", (128, 512), F32)[:, 0:TS], "qs")
    wready = ctx.cnt["qs"]

    nldj = 2 if src_extra else 1

    nldj = 1 + (1 if src_extra else 0) + (1 if addsrc else 0)
    if src_tok:
        nldj = KC + (1 if src_extra else 0) + (1 if addsrc else 0)

    @blk.sync
    def _(e):
        for j in range(NT):
            if j >= 2:
                e.wait_ge(ctx.sems["gp"], s0["gp"] + (j - 1))
            ctx.fence(e, "qs")
            if src_tok:
                for kc in range(KC):
                    ctx.dma(e, sx[j % 2][:, kc * TS:(kc + 1) * TS],
                            s_ap[j * TS:(j + 1) * TS, kc * 128:(kc + 1) * 128],
                            "qs", transpose=True)
            else:
                ctx.dma(e, sb3(sx[j % 2], KC, TS), fm_view(s_ap, j * TS, TS, nch=KC), "qs")
            if src_extra:
                ctx.dma(e, sxe[:, :], se_ap[:, j * TS:(j + 1) * TS], "qs")
            if addsrc:
                ctx.dma(e, sb3(sad[j % 2], OC, TS), fm_view(ad_ap, j * TS, TS, nch=OC), "qs")

    def qs_at(j):
        return wready + (j + 1) * nldj * 16

    @blk.tensor
    def _(e):
        e.wait_ge(ctx.sems["qs"], wready)
        if biasrow:
            e.wait_ge(ctx.sems["x1"], ones_rdy)
        for j in range(NT):
            e.wait_ge(ctx.sems["qs"], qs_at(j))
            for it in range(n_it):
                gi = j * n_it + it
                if gi >= 2:
                    e.wait_ge(ctx.sems["sc"], s0["sc"] + gi - 1)
                    if use_ve:
                        e.wait_ge(ctx.sems["ve"], s0["ve"] + gi - 1)
                bank = ps[gi % 2]
                if major == "fm":
                    oc = it
                    for kc in range(KC):
                        ins = e.matmul(bank[:, 0:TS],
                                 swt[:, (kc * OC + oc) * 128:(kc * OC + oc) * 128 + 128],
                                 sx[j % 2][:, kc * TS:(kc + 1) * TS],
                                 start=(kc == 0),
                                 stop=(kc == KC - 1 and src_extra is None))
                    if src_extra:
                        ins = e.matmul(bank[:, 0:TS], swe[:, oc * 128:(oc + 1) * 128],
                                 sxe[:, 0:TS], start=False, stop=True)
                else:
                    tc, nh = divmod(it, NH)
                    nw_ = min(512, nout - nh * 512)
                    for kc in range(KC):
                        ins = e.matmul(bank[:, 0:nw_],
                                 sx[j % 2][:, kc * TS + tc * 128:kc * TS + tc * 128 + 128],
                                 swt[:, kc * nout + nh * 512:kc * nout + nh * 512 + nw_],
                                 start=(kc == 0),
                                 stop=(kc == KC - 1 and src_extra is None and not biasrow))
                    if src_extra:
                        ins = e.matmul(bank[:, 0:nw_], sxe[:, tc * 128:(tc + 1) * 128],
                                 swe[:, nh * 512:nh * 512 + nw_],
                                 start=False, stop=(not biasrow))
                    if biasrow:
                        ins = e.matmul(bank[:, 0:nw_], sone[:],
                                 sbr[0:1, nh * 512:nh * 512 + nw_],
                                 start=False, stop=True)
                ctx.tag(ins, "mm")

    @blk.scalar
    def _(e):
        for j in range(NT):
            for it in range(n_it):
                gi = j * n_it + it
                e.wait_ge(ctx.sems["mm"], s0["mm"] + gi + 1)
                bank = ps[gi % 2]
                kw = {}
                if scale is not None:
                    kw["scale"] = float(scale)
                if major == "fm":
                    oc = it
                    if biascol:
                        kw["bias"] = sbc[:, oc:oc + 1]
                        if act is None:
                            kw.setdefault("scale", 1.0)
                    if dst2:
                        dst_sl = od2[j % 2][:, oc * TS:(oc + 1) * TS]
                    elif alt or addsrc:
                        dst_sl = stmp[:, (gi % 2) * 512:(gi % 2) * 512 + TS]
                    else:
                        dst_sl = od[j % 2][:, oc * TS:(oc + 1) * TS]
                    fn = act or (AF.Identity if biascol else AF.Copy)
                    ins = e.activation(out=dst_sl, in_=bank[:, 0:TS],
                                       func=fn, **kw)
                else:
                    tc, nh = divmod(it, NH)
                    nw_ = min(512, nout - nh * 512)
                    dst_sl = od[j % 2][:, tc * nout + nh * 512:tc * nout + nh * 512 + nw_]
                    ins = e.activation(out=dst_sl, in_=bank[:, 0:nw_],
                                       func=act or AF.Copy, **kw)
                ctx.tag(ins, "sc")

    if use_ve:
        @blk.vector
        def _(e):
            e.wait_ge(ctx.sems["qs"], wready)
            for j in range(NT):
                for it in range(n_it):
                    gi = j * n_it + it
                    oc = it
                    e.wait_ge(ctx.sems["sc"], s0["sc"] + gi + 1)
                    srcsl = (od2[j % 2][:, oc * TS:(oc + 1) * TS] if dst2
                             else stmp[:, (gi % 2) * 512:(gi % 2) * 512 + TS])
                    if addsrc:
                        ins = e.tensor_tensor(out=od[j % 2][:, oc * TS:(oc + 1) * TS],
                                              in0=srcsl,
                                              in1=sad[j % 2][:, oc * TS:(oc + 1) * TS],
                                              op=ALU.add)
                    else:
                        ins = e.tensor_tensor(out=od[j % 2][:, oc * TS:(oc + 1) * TS],
                                              in0=srcsl, in1=salt[:, 0:TS], op=ALU.mult)
                    ctx.tag(ins, "ve")

    @blk.gpsimd
    def _(e):
        for j in range(NT):
            last = (j + 1) * n_it
            e.wait_ge(ctx.sems["sc"], s0["sc"] + last)
            if use_ve:
                e.wait_ge(ctx.sems["ve"], s0["ve"] + last)
            if major == "fm":
                ctx.dma(e, fm_view(d_ap, j * TS, TS, nch=OC), sb3(od[j % 2], OC, TS), "qg")
                if dst2:
                    ctx.dma(e, fm_view(d2_ap, j * TS, TS, nch=OC),
                            sb3(od2[j % 2], OC, TS), "qg")
            else:
                ctx.dma(e, d_ap.rearrange("(j tc p) n -> p j tc n", p=128, tc=TC)[:, j],
                        od[j % 2].rearrange("p (tc n) -> p tc n", tc=TC), "qg")
            e.wait_ge(ctx.sems["qg"], ctx.cnt["qg"])
            ins = e.engine_nop()
            ctx.tag(ins, "gp")


def ceil_div(a, b):
    return (a + b - 1) // b


def ph_l0_scan(ctx):
    qap = ctx.buf("l0q", (D, T), BF16)
    kap = ctx.buf("l0k", (D, T), F32)
    kbap = ctx.buf("l0kb", (D, T), BF16)
    vap = ctx.buf("l0v", (T, D), BF16)
    oap = ctx.buf("l0o", (T, D), F32)
    mkap = ctx.inp("incl64", (64, 64), F32)
    idap = ctx.inp("identb", (128, 128), BF16)

    blk = ctx.block
    S_ = ctx.sems
    NCH = 64
    C = 64
    NTI = 32

    ksl = ctx.sbuf("g0_k", [128, T], F32)
    sf = ctx.sbuf("g0_f", [128, T], F32)
    cg = ctx.sbuf("g0_cg", [128, T], F32)
    ncg = ctx.sbuf("g0_ncg", [128, T], F32)
    esb = ctx.sbuf("g0_e", [128, T], F32)
    entmp = ctx.sbuf("g0_en", [128, 2 * C], F32)
    sqt = ctx.sbuf("g0_qt", [128, T], BF16)
    sqh = ctx.sbuf("g0_qh", [128, T], BF16)
    skb = ctx.sbuf("g0_kb", [128, T], BF16)
    skh = ctx.sbuf("g0_kh", [128, T], BF16)
    sk2 = ctx.sbuf("g0_k2", [128, T], BF16)
    k2t = ctx.sbuf("g0_k2t", [128, T], BF16)
    vsb = ctx.sbuf("g0_v", [128, T], BF16)
    osb = ctx.sbuf("g0_o", [128, T], F32)
    sS = ctx.sbuf("g0_S", [128, 128], F32)
    sSb = ctx.sbuf("g0_Sb", [128, 128], BF16)
    smask = ctx.sbuf("g0_m", [64, 64], F32)
    sid = ctx.sbuf("g0_id", [128, 128], BF16)
    szc = ctx.sbuf("g0_zc", [128, 1], F32)
    sAm = ctx.sbuf("g0_Am", [128, C], BF16)
    psA = ctx.psum("g0_pA", [64, 64])
    psOI = ctx.psum("g0_pOI", [64, 128])
    psOB = ctx.psum("g0_pOB", [64, 128])
    psSA = ctx.psum("g0_pSA", [128, 128])
    psTR = [ctx.psum(f"g0_pT{i}", [128, 128], BF16) for i in range(2)]

    @blk.sync
    def _(e):
        ctx.dma(e, smask[:], mkap[:], "qs")
        ctx.dma(e, sid[:], idap[:], "qs")
    consts_rdy = ctx.cnt["qs"]

    @blk.vector
    def _(e):
        e.memset(szc[:], 0.0)
        ins = e.memset(sAm[:], 0.0)
        ctx.tag(ins, "x1")
    zc_rdy = ctx.cnt["x1"]

    for h in range(8):
        hv = slice(h * 128, (h + 1) * 128)

        @blk.sync
        def _(e, h=h):
            if h > 0:
                e.wait_ge(S_["gp"], ctx.cnt["gp"])
            ctx.fence(e, "qs")
            ctx.dma(e, ksl[:, :], fm_view(kap, 0, T, c0=h, nch=1)[:, 0, :], "qs")
            ctx.dma(e, sqt[:, :], fm_view(qap, 0, T, c0=h, nch=1)[:, 0, :], "qs")
            ctx.dma(e, skb[:, :], fm_view(kbap, 0, T, c0=h, nch=1)[:, 0, :], "qs")
            ctx.dma(e, vsb.rearrange("p (ti n) -> p ti n", n=128),
                    vap[:, hv].rearrange("(ti p) n -> p ti n", p=128), "qs")
        qs_h = ctx.cnt["qs"]

        @blk.vector
        def _(e):
            e.wait_ge(S_["qs"], qs_h)
            e.wait_ge(S_["x1"], zc_rdy)
            ins = e.tensor_scalar(out=sf[:], in0=ksl[:], scalar1=-1.0, scalar2=1.0,
                                  op0=ALU.mult, op1=ALU.add)
            ctx.tag(ins, "ve")
        v_f = ctx.cnt["ve"]

        @blk.scalar
        def _(e):
            e.wait_ge(S_["ve"], v_f)
            ins = e.activation(out=ksl[:], in_=sf[:], func=AF.Ln)
            ctx.tag(ins, "sc")
        s_g = ctx.cnt["sc"]

        @blk.vector
        def _(e):
            e.wait_ge(S_["sc"], s_g)
            ins = e.tensor_tensor_scan(out=cg[:], data0=ksl[:], data1=ksl[:],
                                       initial=0.0, op0=ALU.add, op1=ALU.bypass)
            c1 = ctx.tag(ins, "ve")
            e.wait_ge(S_["ve"], c1)
            ins = e.tensor_scalar(out=ncg[:], in0=cg[:], scalar1=-1.0, scalar2=None,
                                  op0=ALU.mult)
            ctx.tag(ins, "ve")
        v_ncg = ctx.cnt["ve"]

        v_k2 = {}
        v_qh = {}
        s_ep = {}
        for c in range(NCH):
            @blk.scalar
            def _(e, c=c):
                if c == 0:
                    e.wait_ge(S_["ve"], v_ncg)
                if c >= 2:
                    e.wait_ge(S_["ve"], v_k2[c - 2])
                bpos = szc[:, 0:1] if c == 0 else ncg[:, c * C - 1:c * C]
                bneg = szc[:, 0:1] if c == 0 else cg[:, c * C - 1:c * C]
                e.activation(out=esb[:, c * C:(c + 1) * C], in_=cg[:, c * C:(c + 1) * C],
                             func=AF.Exp, bias=bpos)
                ins = e.activation(out=entmp[:, (c % 2) * C:(c % 2) * C + C],
                                   in_=cg[:, c * C:(c + 1) * C],
                                   func=AF.Exp, bias=bneg, scale=-1.0)
                ctx.tag(ins, "sc")
            s_ep[c] = ctx.cnt["sc"]

            @blk.vector
            def _(e, c=c):
                e.wait_ge(S_["sc"], s_ep[c])
                ins = e.scalar_tensor_tensor(out=sqh[:, c * C:(c + 1) * C],
                                             in0=esb[:, c * C:(c + 1) * C],
                                             scalar=1.0 / 16.0,
                                             in1=sqt[:, c * C:(c + 1) * C],
                                             op0=ALU.mult, op1=ALU.mult)
                c1 = ctx.tag(ins, "ve")
                v_qh[c] = c1
                e.wait_ge(S_["ve"], c1)
                ins = e.tensor_tensor(out=skh[:, c * C:(c + 1) * C],
                                      in0=skb[:, c * C:(c + 1) * C],
                                      in1=entmp[:, (c % 2) * C:(c % 2) * C + C],
                                      op=ALU.mult)
                c2 = ctx.tag(ins, "ve")
                e.wait_ge(S_["ve"], c2)
                ins = e.tensor_scalar(out=sk2[:, c * C:(c + 1) * C],
                                      in0=skh[:, c * C:(c + 1) * C],
                                      scalar1=esb[:, (c + 1) * C - 1:(c + 1) * C],
                                      scalar2=None, op0=ALU.mult)
                ctx.tag(ins, "ve")
            v_k2[c] = ctx.cnt["ve"]

        t_tr = {}
        s_tv = {}
        for ti in range(NTI):
            @blk.tensor
            def _(e, ti=ti):
                if ti == 0:
                    e.wait_ge(S_["qs"], consts_rdy)
                e.wait_ge(S_["ve"], v_k2[2 * ti + 1])
                if ti >= 2:
                    e.wait_ge(S_["sc"], s_tv[ti - 2])
                ins = e.transpose(psTR[ti % 2][:], sk2[:, ti * 128:(ti + 1) * 128], sid[:])
                ctx.tag(ins, "mm")
            t_tr[ti] = ctx.cnt["mm"]

            @blk.scalar
            def _(e, ti=ti):
                e.wait_ge(S_["mm"], t_tr[ti])
                ins = e.activation(out=k2t[:, ti * 128:(ti + 1) * 128],
                                   in_=psTR[ti % 2][:], func=AF.Copy)
                ctx.tag(ins, "sc")
            s_tv[ti] = ctx.cnt["sc"]

        t_A = {}
        v_Am = {}
        t_oi = {}
        v_oc = {}
        for c in range(NCH):
            ti, half = c // 2, c % 2
            vrow = slice(half * 64, half * 64 + 64)

            @blk.tensor
            def _(e, c=c):
                e.wait_ge(S_["ve"], v_k2[c])
                if c >= 1:
                    e.wait_ge(S_["ve"], v_Am[c - 1])
                ins = e.matmul(psA[:], skh[:, c * C:(c + 1) * C],
                               sqh[:, c * C:(c + 1) * C], start=True, stop=True)
                ctx.tag(ins, "mm")
            t_A[c] = ctx.cnt["mm"]

            @blk.vector
            def _(e, c=c):
                e.wait_ge(S_["mm"], t_A[c])
                if c >= 2:
                    e.wait_ge(S_["mm"], t_oi[c - 2])
                half = c % 2
                ins = e.tensor_tensor(out=sAm[half * 64:half * 64 + 64, :],
                                      in0=psA[:], in1=smask[:], op=ALU.mult)
                ctx.tag(ins, "ve")
            v_Am[c] = ctx.cnt["ve"]

            @blk.tensor
            def _(e, c=c, ti=ti, half=half):
                e.wait_ge(S_["ve"], v_Am[c])
                if c >= 1:
                    e.wait_ge(S_["ve"], v_oc[c - 1])
                ins = e.matmul(psOI[:], sAm[half * 64:half * 64 + 64, :],
                               vsb[half * 64:half * 64 + 64, ti * 128:(ti + 1) * 128],
                               start=True, stop=True)
                ctx.tag(ins, "mm")
            t_oi[c] = ctx.cnt["mm"]

            @blk.vector
            def _(e, c=c, ti=ti, half=half):
                e.wait_ge(S_["mm"], t_oi[c])
                ins = e.tensor_copy(out=osb[half * 64:half * 64 + 64,
                                            ti * 128:(ti + 1) * 128], in_=psOI[:])
                ctx.tag(ins, "ve")
            v_oc[c] = ctx.cnt["ve"]

        @blk.vector
        def _(e):
            e.memset(sS[:], 0.0)
            ins = e.memset(sSb[:], 0.0)
            ctx.tag(ins, "ve")
        v_sbf_prev = ctx.cnt["ve"]

        for c in range(NCH):
            ti, half = c // 2, c % 2

            @blk.tensor
            def _(e, c=c, ti=ti, half=half, v_prev=v_sbf_prev):
                e.wait_ge(S_["ve"], v_prev)
                ins = e.matmul(psOB[:], sqh[:, c * C:(c + 1) * C], sSb[:],
                               start=True, stop=True)
                c1 = ctx.tag(ins, "mm")
                ins = e.matmul(psSA[:], k2t[half * 64:half * 64 + 64,
                                            ti * 128:(ti + 1) * 128],
                               vsb[half * 64:half * 64 + 64, ti * 128:(ti + 1) * 128],
                               start=True, stop=True)
                ctx.tag(ins, "mm")
            t_ob = ctx.cnt["mm"] - 1
            t_sa = ctx.cnt["mm"]

            @blk.vector
            def _(e, c=c, ti=ti, half=half, t_ob=t_ob, t_sa=t_sa):
                e.wait_ge(S_["mm"], t_ob)
                ins = e.tensor_tensor(out=osb[half * 64:half * 64 + 64,
                                              ti * 128:(ti + 1) * 128],
                                      in0=osb[half * 64:half * 64 + 64,
                                              ti * 128:(ti + 1) * 128],
                                      in1=psOB[:], op=ALU.add)
                c1 = ctx.tag(ins, "ve")
                e.wait_ge(S_["ve"], c1)
                e.wait_ge(S_["mm"], t_sa)
                ins = e.tensor_scalar(out=sS[:], in0=sS[:],
                                      scalar1=esb[:, (c + 1) * C - 1:(c + 1) * C],
                                      scalar2=None, op0=ALU.mult)
                c2 = ctx.tag(ins, "ve")
                e.wait_ge(S_["ve"], c2)
                ins = e.tensor_tensor(out=sS[:], in0=sS[:], in1=psSA[:], op=ALU.add)
                c3 = ctx.tag(ins, "ve")
                e.wait_ge(S_["ve"], c3)
                ins = e.tensor_copy(out=sSb[:], in_=sS[:])
                ctx.tag(ins, "ve")
            v_sbf_prev = ctx.cnt["ve"]

        @blk.gpsimd
        def _(e, h=h, v_last=v_sbf_prev):
            e.wait_ge(S_["ve"], v_last)
            ctx.dma(e, oap[:, h * 128:(h + 1) * 128].rearrange("(ti p) n -> p ti n", p=128),
                    osb.rearrange("p (ti n) -> p ti n", n=128), "qg")
            e.wait_ge(S_["qg"], ctx.cnt["qg"])
            ins = e.engine_nop()
            ctx.tag(ins, "gp")


def ph_l0_out(ctx, eps=1e-5):
    oap = ctx.buf("l0o", (T, D), F32)
    gap = ctx.buf("l0gate", (T, D), BF16)
    onap = ctx.buf("l0on", (T, D), BF16)
    nwap = ctx.inp("l0_normw_r", (1, D), F32)

    blk = ctx.block
    S_ = ctx.sems
    NTI = T // 128
    so = [ctx.sbuf(f"o0_o{i}", [128, D], F32) for i in range(2)]
    sg = [ctx.sbuf(f"o0_g{i}", [128, D], BF16) for i in range(2)]
    sog = ctx.sbuf("o0_og", [128, D], F32)
    ssq = ctx.sbuf("o0_sq", [128, D], F32)
    son = [ctx.sbuf(f"o0_on{i}", [128, D], BF16) for i in range(2)]
    scol = ctx.sbuf("o0_col", [128, 8], F32)
    snwr = ctx.sbuf("o0_nw", [1, D], F32)
    sone = ctx.sbuf("o0_one", [1, 128], F32)
    snw = ctx.sbuf("o0_nwrep", [128, D], F32)
    psR = ctx.psum("o0_pr", [128, 512])
    MU_RAW, SS, MU, M2, VAR, RSTD = 0, 1, 2, 3, 4, 5

    @blk.vector
    def _(e):
        ins = e.memset(sone[:], 1.0)
        ctx.tag(ins, "x1")
    ones_rdy = ctx.cnt["x1"]

    @blk.sync
    def _(e):
        ctx.dma(e, snwr[:], nwap[:], "qs")
    nw_rdy = ctx.cnt["qs"]

    t_rep = {}
    for half in range(2):
        @blk.tensor
        def _(e, half=half):
            e.wait_ge(S_["qs"], nw_rdy)
            e.wait_ge(S_["x1"], ones_rdy)
            if half == 1:
                e.wait_ge(S_["sc"], t_rep[0])
            ins = e.matmul(psR[:], sone[:], snwr[0:1, half * 512:half * 512 + 512],
                           start=True, stop=True)
            ctx.tag(ins, "mm")
        mmc = ctx.cnt["mm"]

        @blk.scalar
        def _(e, half=half, mmc=mmc):
            e.wait_ge(S_["mm"], mmc)
            ins = e.activation(out=snw[:, half * 512:half * 512 + 512], in_=psR[:],
                               func=AF.Copy)
            ctx.tag(ins, "sc")
        t_rep[half] = ctx.cnt["sc"]
    rep_rdy = ctx.cnt["sc"]

    for i in range(NTI):
        @blk.sync
        def _(e, i=i):
            if i >= 2:
                e.wait_ge(S_["gp"], ctx.cnt["gp"] - 1)
            ctx.fence(e, "qs")
            ctx.dma(e, so[i % 2][:], oap[i * 128:(i + 1) * 128, :], "qs")
            ctx.dma(e, sg[i % 2][:], gap[i * 128:(i + 1) * 128, :], "qs")
        qs_i = ctx.cnt["qs"]

        @blk.vector
        def _(e, i=i, qs_i=qs_i):
            e.wait_ge(S_["qs"], qs_i)
            ins = e.tensor_tensor(out=sog[:], in0=so[i % 2][:], in1=sg[i % 2][:],
                                  op=ALU.mult)
            c1 = ctx.tag(ins, "ve")
            e.wait_ge(S_["ve"], c1)
            ins = e.tensor_reduce(out=scol[:, MU_RAW:MU_RAW + 1], in_=sog[:],
                                  axis=AX.X, op=ALU.add)
            ctx.tag(ins, "ve")
        v_og = ctx.cnt["ve"]

        @blk.scalar
        def _(e, i=i, v_og=v_og):
            e.wait_ge(S_["ve"], v_og)
            ins = e.activation(out=ssq[:], in_=sog[:], func=AF.Square,
                               accum_out=scol[:, SS:SS + 1])
            ctx.tag(ins, "sc")
        s_sq = ctx.cnt["sc"]

        @blk.vector
        def _(e, i=i, s_sq=s_sq):
            e.wait_ge(S_["sc"], s_sq)
            ins = e.tensor_scalar(out=scol[:, MU:MU + 1], in0=scol[:, MU_RAW:MU_RAW + 1],
                                  scalar1=1.0 / D, scalar2=None, op0=ALU.mult)
            c1 = ctx.tag(ins, "ve")
            e.wait_ge(S_["ve"], c1)
            ins = e.tensor_tensor(out=scol[:, M2:M2 + 1], in0=scol[:, MU:MU + 1],
                                  in1=scol[:, MU:MU + 1], op=ALU.mult)
            c2 = ctx.tag(ins, "ve")
            e.wait_ge(S_["ve"], c2)
            ins = e.scalar_tensor_tensor(out=scol[:, VAR:VAR + 1],
                                         in0=scol[:, SS:SS + 1], scalar=1.0 / D,
                                         in1=scol[:, M2:M2 + 1],
                                         op0=ALU.mult, op1=ALU.subtract)
            c3 = ctx.tag(ins, "ve")
            e.wait_ge(S_["ve"], c3)
            ins = e.tensor_scalar(out=scol[:, VAR:VAR + 1], in0=scol[:, VAR:VAR + 1],
                                  scalar1=float(eps), scalar2=None, op0=ALU.add)
            ctx.tag(ins, "ve")
        v_var = ctx.cnt["ve"]

        @blk.scalar
        def _(e, i=i, v_var=v_var):
            e.wait_ge(S_["ve"], v_var)
            ins = e.activation(out=scol[:, RSTD:RSTD + 1], in_=scol[:, VAR:VAR + 1],
                               func=AF.Sqrt)
            ctx.tag(ins, "sc")
        s_std = ctx.cnt["sc"]

        @blk.vector
        def _(e, i=i, s_std=s_std):
            e.wait_ge(S_["sc"], s_std)
            ins = e.reciprocal(out=scol[:, RSTD:RSTD + 1], in_=scol[:, RSTD:RSTD + 1])
            c1 = ctx.tag(ins, "ve")
            e.wait_ge(S_["ve"], c1)
            if i == 0:
                e.wait_ge(S_["sc"], rep_rdy)
            ins = e.tensor_scalar(out=sog[:], in0=sog[:], scalar1=scol[:, MU:MU + 1],
                                  scalar2=scol[:, RSTD:RSTD + 1],
                                  op0=ALU.subtract, op1=ALU.mult)
            c2 = ctx.tag(ins, "ve")
            e.wait_ge(S_["ve"], c2)
            ins = e.tensor_tensor(out=son[i % 2][:], in0=sog[:], in1=snw[:],
                                  op=ALU.mult)
            ctx.tag(ins, "ve")
        v_on = ctx.cnt["ve"]

        @blk.gpsimd
        def _(e, i=i, v_on=v_on):
            e.wait_ge(S_["ve"], v_on)
            ctx.dma(e, onap[i * 128:(i + 1) * 128, :], son[i % 2][:], "qg")
            e.wait_ge(S_["qg"], ctx.cnt["qg"])
            ins = e.engine_nop()
            ctx.tag(ins, "gp")


def ph_l0_moe(ctx):
    xap = ctx.buf("xn0", (D, T), BF16)
    scap = ctx.buf("l0sc", (T, 8), F32)
    fap = ctx.buf("ffn0", (D, T), F32)
    w1ap = ctx.inp("l0_w1", (8, D, 2816), BF16)
    w2ap = ctx.inp("l0_w2", (8, D, 2816), BF16)
    w3ap = ctx.inp("l0_w3", (8, 2816, D), BF16)
    o8ap = ctx.inp("onehot8", (8, 8 * 128), F32)
    idap = ctx.inp("identf", (128, 128), F32)
    hdram = ctx.buf("l0_hscr", (2816, T // 2), BF16)

    HC = 22
    TS = 512
    NJ = T // TS
    blk = ctx.block
    S_ = ctx.sems

    TH2 = T // 2
    sxn = ctx.sbuf("m0_x", [128, DC * TH2], BF16)
    ssc = ctx.sbuf("m0_sc", [128, 16], F32)
    su8 = ctx.sbuf("m0_u8", [128, 16], U8)
    sneg = ctx.sbuf("m0_neg", [128, 8], F32)
    szero = ctx.sbuf("m0_z", [128, 8], F32)
    stk = ctx.sbuf("m0_tk", [128, 16], F32)
    swf = ctx.sbuf("m0_wf", [8, T], F32)
    so8 = ctx.sbuf("m0_o8", [8, 8 * 128], F32)
    sidf = ctx.sbuf("m0_id", [128, 128], F32)
    sweb = ctx.sbuf("m0_web", [128, (NJ // 2) * TS], F32)
    sw1 = [ctx.sbuf(f"m0_w1{i}", [128, DC * 128], BF16) for i in range(2)]
    sw2 = [ctx.sbuf(f"m0_w2{i}", [128, DC * 128], BF16) for i in range(2)]
    sg = ctx.sbuf("m0_g", [128, TS], BF16)
    shh = ctx.sbuf("m0_h", [128, (NJ // 2) * TS], BF16)
    sw3 = ctx.sbuf("m0_w3", [128, HC * D], BF16)
    sh2 = [ctx.sbuf(f"m0_h2{i}", [128, HC * TS], BF16) for i in range(2)]
    sy = [ctx.sbuf(f"m0_y{i}", [128, DC * TS], F32) for i in range(2)]
    ps_g = ctx.psum("m0_pg", [128, TS])
    ps_u = ctx.psum("m0_pu", [128, TS])
    ps_y = [ctx.psum(f"m0_py{i}", [128, TS]) for i in range(2)]
    ps_t = ctx.psum("m0_pt", [8, 128])
    ps_w = ctx.psum("m0_pw", [128, TS])

    @blk.vector
    def _(e):
        e.memset(sneg[:], -3.0e38)
        ins = e.memset(szero[:], 0.0)
        ctx.tag(ins, "x1")
    z_rdy = ctx.cnt["x1"]

    @blk.sync
    def _(e):
        ctx.dma(e, so8[:], o8ap[:], "qs")
        ctx.dma(e, sidf[:], idap[:], "qs")
    pre_rdy = ctx.cnt["qs"]

    v_w5 = {}
    for i in range(T // 128):
        @blk.sync
        def _(e, i=i):
            if i >= 2:
                e.wait_ge(S_["ve"], v_w5[i - 2])
            ctx.fence(e, "qs")
            ctx.dma(e, ssc[:, (i % 2) * 8:(i % 2) * 8 + 8],
                    scap[i * 128:(i + 1) * 128, :], "qs")
        qs_i = ctx.cnt["qs"]

        @blk.vector
        def _(e, i=i, qs_i=qs_i):
            sl = ssc[:, (i % 2) * 8:(i % 2) * 8 + 8]
            msk = su8[:, (i % 2) * 8:(i % 2) * 8 + 8]
            m1 = stk[:, (i % 2) * 4 + 0:(i % 2) * 4 + 1]
            m2 = stk[:, (i % 2) * 4 + 1:(i % 2) * 4 + 2]
            tneg = stk[:, 8:16]
            e.wait_ge(S_["qs"], qs_i)
            if i == 0:
                e.wait_ge(S_["x1"], z_rdy)
            else:
                e.wait_ge(S_["mm"], ctx.cnt["mm"])
            ins = e.tensor_reduce(out=m1, in_=sl, axis=AX.X, op=ALU.max)
            c = ctx.tag(ins, "ve"); e.wait_ge(S_["ve"], c)
            ins = e.tensor_scalar(out=msk, in0=sl, scalar1=m1, scalar2=None,
                                  op0=ALU.is_ge)
            c = ctx.tag(ins, "ve"); e.wait_ge(S_["ve"], c)
            ins = e.select(out=tneg, mask=msk, on_true=sneg[:], on_false=sl)
            c = ctx.tag(ins, "ve"); e.wait_ge(S_["ve"], c)
            ins = e.tensor_reduce(out=m2, in_=tneg, axis=AX.X, op=ALU.max)
            c = ctx.tag(ins, "ve"); e.wait_ge(S_["ve"], c)
            ins = e.tensor_scalar(out=msk, in0=sl, scalar1=m2, scalar2=None,
                                  op0=ALU.is_ge)
            c5 = ctx.tag(ins, "ve")
            e.wait_ge(S_["ve"], c5)
            v_w5[i] = c5
            ins = e.select(out=stk[:, 8:16], mask=msk, on_true=sl, on_false=szero[:])
            ctx.tag(ins, "ve")
        v_w = ctx.cnt["ve"]

        @blk.tensor
        def _(e, i=i, v_w=v_w):
            e.wait_ge(S_["ve"], v_w)
            e.wait_ge(S_["qs"], pre_rdy)
            if i >= 1:
                e.wait_ge(S_["sc"], ctx.cnt["sc"])
            ins = e.transpose(ps_t[:], stk[:, 8:16], sidf[:])
            ctx.tag(ins, "mm")
        t_tr = ctx.cnt["mm"]

        @blk.scalar
        def _(e, i=i, t_tr=t_tr):
            e.wait_ge(S_["mm"], t_tr)
            ins = e.activation(out=swf[:, i * 128:(i + 1) * 128], in_=ps_t[:],
                               func=AF.Copy)
            ctx.tag(ins, "sc")
    wf_rdy = ctx.cnt["sc"]

    xn_rdy = {}
    for th in range(2):
        T0 = th * TH2
        NJh = NJ // 2

        @blk.sync
        def _(e, th=th, T0=T0):
            if th > 0:
                e.wait_ge(S_["mm"], ctx.cnt["mm"])
            ctx.fence(e, "qs")
            ctx.dma(e, sb3(sxn, DC, TH2), fm_view(xap, T0, TH2), "qs")
        xn_rdy[th] = ctx.cnt["qs"]

        for ex in range(8):
            for j in range(NJh):
                @blk.tensor
                def _(e, ex=ex, j=j, th=th, T0=T0):
                    if ex == 0 and j == 0:
                        e.wait_ge(S_["sc"], wf_rdy)
                        e.wait_ge(S_["qs"], xn_rdy[th])
                    else:
                        e.wait_ge(S_["sc"], ctx.cnt["sc"])
                    if j == 0 and not (ex == 0 and th == 0):
                        e.wait_ge(S_["gp"], ctx.cnt["gp"])
                    ins = e.matmul(ps_w[:], so8[:, ex * 128:(ex + 1) * 128],
                                   swf[0:8, T0 + j * TS:T0 + (j + 1) * TS],
                                   start=True, stop=True)
                    ctx.tag(ins, "mm")
                t_wb = ctx.cnt["mm"]

                @blk.scalar
                def _(e, t_wb=t_wb, j=j):
                    e.wait_ge(S_["mm"], t_wb)
                    ins = e.activation(out=sweb[:, j * TS:(j + 1) * TS], in_=ps_w[:],
                                       func=AF.Copy)
                    ctx.tag(ins, "sc")
            web_rdy = ctx.cnt["sc"]

            t_gu_last = {}
            g_h = {}
            for hc in range(HC):
                @blk.sync
                def _(e, ex=ex, hc=hc):
                    if hc >= 2:
                        e.wait_ge(S_["mm"], t_gu_last[hc - 2])
                    ctx.fence(e, "qs")
                    ctx.dma(e, sw1[hc % 2].rearrange("p (c n) -> p c n", c=DC),
                            w1ap[ex].rearrange("(c p) h -> p c h", p=128)[:, :, hc * 128:(hc + 1) * 128],
                            "qs")
                    ctx.dma(e, sw2[hc % 2].rearrange("p (c n) -> p c n", c=DC),
                            w2ap[ex].rearrange("(c p) h -> p c h", p=128)[:, :, hc * 128:(hc + 1) * 128],
                            "qs")
                qs_hc = ctx.cnt["qs"]

                v_h = {}
                for j in range(NJh):
                    @blk.tensor
                    def _(e, ex=ex, hc=hc, j=j, qs_hc=qs_hc, web_rdy=web_rdy, th=th):
                        if j == 0:
                            e.wait_ge(S_["qs"], qs_hc)
                            if hc == 0:
                                e.wait_ge(S_["sc"], web_rdy)
                                if not (ex == 0 and th == 0):
                                    e.wait_ge(S_["gp"], ctx.cnt["gp"])
                        if not (ex == 0 and hc == 0 and j == 0 and th == 0):
                            e.wait_ge(S_["sc"], ctx.cnt["sc"])
                            e.wait_ge(S_["ve"], ctx.cnt["ve"])
                        for c in range(DC):
                            e.matmul(ps_g[:], sw1[hc % 2][:, c * 128:(c + 1) * 128],
                                     sxn[:, c * TH2 + j * TS:c * TH2 + j * TS + TS],
                                     start=(c == 0), stop=(c == DC - 1))
                        for c in range(DC):
                            ins = e.matmul(ps_u[:], sw2[hc % 2][:, c * 128:(c + 1) * 128],
                                           sxn[:, c * TH2 + j * TS:c * TH2 + j * TS + TS],
                                           start=(c == 0), stop=(c == DC - 1))
                        ctx.tag(ins, "mm")
                    t_gu = ctx.cnt["mm"]
                    if j == NJh - 1:
                        t_gu_last[hc] = t_gu

                    @blk.scalar
                    def _(e, t_gu=t_gu):
                        e.wait_ge(S_["mm"], t_gu)
                        ins = e.activation(out=sg[:], in_=ps_g[:], func=AF.Silu)
                        ctx.tag(ins, "sc")
                    s_si = ctx.cnt["sc"]

                    @blk.vector
                    def _(e, hc=hc, j=j, s_si=s_si):
                        e.wait_ge(S_["sc"], s_si)
                        if j == 0 and hc >= 1:
                            e.wait_ge(S_["gp"], g_h.get(hc - 1, 0))
                        ins = e.tensor_tensor(out=shh[:, j * TS:(j + 1) * TS],
                                              in0=ps_u[:], in1=sg[:], op=ALU.mult)
                        ctx.tag(ins, "ve")
                    v_h[j] = ctx.cnt["ve"]

                @blk.gpsimd
                def _(e, ex=ex, hc=hc, th=th, v_last=v_h[NJh - 1]):
                    e.wait_ge(S_["ve"], v_last)
                    if hc == 0 and not (ex == 0 and th == 0):
                        e.wait_ge(S_["qs"], ctx.cnt["qs"])
                    ctx.dma(e, hdram[hc * 128:(hc + 1) * 128, :], shh[:, :], "qg")
                    e.wait_ge(S_["qg"], ctx.cnt["qg"])
                    ins = e.engine_nop()
                    ctx.tag(ins, "gp")
                g_h[hc] = ctx.cnt["gp"]
            p1_done = ctx.cnt["gp"]

            @blk.sync
            def _(e, ex=ex, p1_done=p1_done):
                e.wait_ge(S_["gp"], p1_done)
                ctx.fence(e, "qs")
                ctx.dma(e, sw3.rearrange("p (c n) -> p c n", c=HC),
                        w3ap[ex].rearrange("(c p) d -> p c d", p=128), "qs")
            w3_rdy = ctx.cnt["qs"]

            g_y = {}
            for j in range(NJh):
                @blk.sync
                def _(e, ex=ex, j=j, w3_rdy=w3_rdy):
                    if j >= 2:
                        e.wait_ge(S_["gp"], g_y[j - 2])
                    ctx.fence(e, "qs")
                    ctx.dma(e, sh2[j % 2].rearrange("p (c n) -> p c n", c=HC),
                            hdram.rearrange("(c p) t -> p c t", p=128)[:, :, j * TS:(j + 1) * TS],
                            "qs")
                qs_j = ctx.cnt["qs"]

                v_y = {}
                for dd in range(DC):
                    @blk.tensor
                    def _(e, j=j, dd=dd, qs_j=qs_j, w3_rdy=w3_rdy, v_y=v_y):
                        if dd == 0:
                            e.wait_ge(S_["qs"], max(qs_j, w3_rdy))
                        if dd >= 2:
                            e.wait_ge(S_["ve"], v_y[dd - 2])
                        elif not (j == 0):
                            e.wait_ge(S_["ve"], ctx.cnt["ve"])
                        for hc in range(HC):
                            ins = e.matmul(ps_y[dd % 2][:],
                                           sw3[:, hc * D + dd * 128:hc * D + dd * 128 + 128],
                                           sh2[j % 2][:, hc * TS:(hc + 1) * TS],
                                           start=(hc == 0), stop=(hc == HC - 1))
                        ctx.tag(ins, "mm")
                    t_y = ctx.cnt["mm"]

                    @blk.vector
                    def _(e, j=j, dd=dd, t_y=t_y):
                        e.wait_ge(S_["mm"], t_y)
                        if j >= 2 and dd == 0:
                            e.wait_ge(S_["gp"], g_y[j - 2])
                        ins = e.tensor_tensor(out=sy[j % 2][:, dd * TS:(dd + 1) * TS],
                                              in0=ps_y[dd % 2][:],
                                              in1=sweb[:, j * TS:(j + 1) * TS],
                                              op=ALU.mult)
                        ctx.tag(ins, "ve")
                    v_y[dd] = ctx.cnt["ve"]

                @blk.gpsimd
                def _(e, ex=ex, j=j, th=th, T0=T0, v_last=v_y[DC - 1]):
                    e.wait_ge(S_["ve"], v_last)
                    if ex == 0:
                        ctx.dma(e, fm_view(fap, T0 + j * TS, TS),
                                sb3(sy[j % 2], DC, TS), "qg")
                    else:
                        ctx.dma(e, fm_view(fap, T0 + j * TS, TS),
                                sb3(sy[j % 2], DC, TS), "qg", accum_op=ALU.add)
                    e.wait_ge(S_["qg"], ctx.cnt["qg"])
                    ins = e.engine_nop()
                    ctx.tag(ins, "gp")
                g_y[j] = ctx.cnt["gp"]


def ph_lstm_gather(ctx, L):
    xnap = ctx.buf(f"xn{L}", (D, T), BF16)
    agin = ctx.buf(f"ag{L}i", (D, T), BF16)
    agout = ctx.buf(f"ag{L}o", (8, D, T), BF16)
    xall = ctx.buf(f"xa{L}", (D, T), BF16)
    gidx = ctx.inp("lstm_gidx", (128, 64), U32)
    S_ = ctx.sems
    blk = ctx.block
    sidx = ctx.sbuf(f"lg{L}_i", [128, 64], U32)
    sx = [ctx.sbuf(f"lg{L}_x{i}", [128, 8 * 512], BF16) for i in range(2)]

    @blk.gpsimd
    def _(e):
        ctx.dma(e, sidx[:], gidx[:], "qg")
        ctx.dma(e, agin[:, :], xnap[:, :], "qg")
        e.wait_ge(S_["qg"], ctx.cnt["qg"])
        ins = e.collective_compute(
            "AllGather", ALU.bypass, replica_groups=[list(range(NCORES))],
            ins=[agin.opt()], outs=[agout.opt()])
        ctx.tag(ins, "qv", 1)
    ag_done = ctx.cnt["qv"]

    flat = agout.rearrange("t d (b r) -> (t d b) r", r=512)
    g_st = {}
    for ch in range(DC):
        for t in range(8):
            @blk.gpsimd
            def _(e, ch=ch, t=t):
                if ch == 0 and t == 0:
                    e.wait_ge(S_["qv"], ag_done)
                if t == 0 and ch >= 2:
                    e.wait_ge(S_["gp"], g_st[ch - 2])
                from concourse.bass import IndirectOffsetOnAxis
                ins = e.indirect_dma_start(
                    out=sx[ch % 2][:, t * 512:(t + 1) * 512],
                    out_offset=None,
                    in_=flat,
                    in_offset=IndirectOffsetOnAxis(ap=sidx[:, t * 8 + ch:t * 8 + ch + 1],
                                                   axis=0))
                ctx.tag(ins, "qg", 16)

        @blk.gpsimd
        def _(e, ch=ch):
            e.wait_ge(S_["qg"], ctx.cnt["qg"])
            ctx.dma(e, xall[ch * 128:(ch + 1) * 128, :], sx[ch % 2][:, :], "qg")
            e.wait_ge(S_["qg"], ctx.cnt["qg"])
            ins = e.engine_nop()
            ctx.tag(ins, "gp")
        g_st[ch] = ctx.cnt["gp"]


def ph_lstm_scan(ctx, L):
    xwf = ctx.buf(f"xw{L}f", (D, T), BF16)
    xwi = ctx.buf(f"xw{L}i", (D, T), BF16)
    xwc = ctx.buf(f"xw{L}c", (D, T), BF16)
    xwo = ctx.buf(f"xw{L}o", (D, T), BF16)
    hband = ctx.buf(f"hb{L}", (D, T), F32)
    uf = ctx.inp(f"l{L}f_uf", (D, D), BF16)
    ui = ctx.inp(f"l{L}f_ui", (D, D), BF16)
    uc = ctx.inp(f"l{L}f_uc", (D, D), BF16)
    uo = ctx.inp(f"l{L}f_uo", (D, D), BF16)
    incap = ctx.inp(f"l{L}f_inc", (128, DC), F32)
    idap = ctx.inp("identb", (128, 128), BF16)

    S_ = ctx.sems
    blk = ctx.block
    TS = 512
    sU = {g: ctx.sbuf(f"ls{L}_u{g}", [128, DC * D], BF16) for g in "fico"}
    sxw = {g: ctx.sbuf(f"ls{L}_xw{g}", [128, DC * TS], BF16) for g in "fico"}
    sgt = {g: ctx.sbuf(f"ls{L}_g{g}", [128, DC * TS], F32) for g in "fic"}
    sc_ = ctx.sbuf(f"ls{L}_c", [128, DC * TS], F32)
    scb = ctx.sbuf(f"ls{L}_cb", [128, DC * TS], BF16)
    sh = ctx.sbuf(f"ls{L}_h", [128, DC * TS], F32)
    sinc = ctx.sbuf(f"ls{L}_in", [128, DC], F32)
    sid = ctx.sbuf(f"ls{L}_id", [128, 128], BF16)
    ps = [ctx.psum(f"ls{L}_p{i}", [128, TS]) for i in range(2)]

    @blk.sync
    def _(e):
        for g, ap in (("f", uf), ("i", ui), ("c", uc), ("o", uo)):
            ctx.dma(e, sU[g].rearrange("p (c n) -> p c n", c=DC),
                    ap.rearrange("(c p) n -> p c n", p=128), "qs")
        ctx.dma(e, sinc[:, :], incap[:, :], "qs")
        ctx.dma(e, sid[:], idap[:], "qs")
    u_rdy = ctx.cnt["qs"]

    @blk.vector
    def _(e):
        e.wait_ge(S_["qs"], u_rdy)
        ins = e.memset(sc_[:], 0.0)
        c1 = ctx.tag(ins, "ve")
        e.wait_ge(S_["ve"], c1)
        for ch in range(DC):
            ins = e.tensor_scalar(out=sc_[:, ch * TS:(ch + 1) * TS],
                                  in0=sc_[:, ch * TS:(ch + 1) * TS],
                                  scalar1=sinc[:, ch:ch + 1], scalar2=None,
                                  op0=ALU.add)
        c2 = ctx.tag(ins, "ve")
        e.wait_ge(S_["ve"], c2)
        ins = e.tensor_copy(out=scb[:], in_=sc_[:])
        ctx.tag(ins, "ve")
    cb_rdy = ctx.cnt["ve"]

    ACTF = {"f": AF.Sigmoid, "i": AF.Sigmoid, "c": AF.Tanh}
    prev_cb = cb_rdy
    for t in range(8):
        @blk.sync
        def _(e, t=t):
            if t >= 1:
                e.wait_ge(S_["mm"], ctx.cnt["mm"])
            ctx.fence(e, "qs")
            for g, ap in (("f", xwf), ("i", xwi), ("c", xwc)):
                ctx.dma(e, sb3(sxw[g], DC, TS), fm_view(ap, t * TS, TS), "qs")
        qs_t = ctx.cnt["qs"]

        s_ev = {}
        git = 0
        for g in "fic":
            for oc in range(DC):
                @blk.tensor
                def _(e, t=t, g=g, oc=oc, git=git, qs_t=qs_t, prev_cb=prev_cb):
                    if git == 0:
                        e.wait_ge(S_["qs"], qs_t)
                        e.wait_ge(S_["ve"], prev_cb)
                    if not (t == 0 and git <= 1):
                        e.wait_ge(S_["sc"], s_ev.get(git - 2, ctx.cnt["sc"]))
                    bank = ps[git % 2]
                    for c in range(DC):
                        e.matmul(bank[:], sU[g][:, (c * DC + oc) * 128:(c * DC + oc) * 128 + 128],
                                 scb[:, c * TS:(c + 1) * TS],
                                 start=(c == 0), stop=False)
                    ins = e.matmul(bank[:], sid[:], sxw[g][:, oc * TS:(oc + 1) * TS],
                                   start=False, stop=True)
                    ctx.tag(ins, "mm")
                t_g = ctx.cnt["mm"]

                @blk.scalar
                def _(e, g=g, oc=oc, t_g=t_g, git=git):
                    e.wait_ge(S_["mm"], t_g)
                    ins = e.activation(out=sgt[g][:, oc * TS:(oc + 1) * TS],
                                       in_=ps[git % 2][:], func=ACTF[g])
                    ctx.tag(ins, "sc")
                s_ev[git] = ctx.cnt["sc"]
                git += 1

        for oc in range(DC):
            @blk.vector
            def _(e, t=t, oc=oc, s_ev=dict(s_ev)):
                e.wait_ge(S_["sc"], s_ev[0 * DC + oc])
                e.wait_ge(S_["sc"], s_ev[1 * DC + oc])
                e.wait_ge(S_["sc"], s_ev[2 * DC + oc])
                sl = slice(oc * TS, (oc + 1) * TS)
                ins = e.tensor_tensor(out=sc_[:, sl], in0=sgt["f"][:, sl],
                                      in1=sc_[:, sl], op=ALU.mult)
                c1 = ctx.tag(ins, "ve")
                e.wait_ge(S_["ve"], c1)
                ins = e.tensor_tensor(out=sh[:, sl], in0=sgt["i"][:, sl],
                                      in1=sgt["c"][:, sl], op=ALU.mult)
                c2 = ctx.tag(ins, "ve")
                e.wait_ge(S_["ve"], c2)
                ins = e.tensor_tensor(out=sc_[:, sl], in0=sc_[:, sl],
                                      in1=sh[:, sl], op=ALU.add)
                ctx.tag(ins, "ve")

        @blk.vector
        def _(e, t=t, s_ev=dict(s_ev)):
            e.wait_ge(S_["sc"], s_ev[3 * DC - 1])
            e.wait_ge(S_["ve"], ctx.cnt["ve"])
            ins = e.tensor_copy(out=scb[:], in_=sc_[:])
            ctx.tag(ins, "ve")
        prev_cb = ctx.cnt["ve"]

    for t in range(8):
        @blk.sync
        def _(e, t=t):
            if t >= 1:
                e.wait_ge(S_["mm"], ctx.cnt["mm"])
            ctx.fence(e, "qs")
            ctx.dma(e, sb3(sxw["o"], DC, TS), fm_view(xwo, t * TS, TS), "qs")
        qs_t = ctx.cnt["qs"]

        s_hv = {}
        for oc in range(DC):
            @blk.tensor
            def _(e, t=t, oc=oc, qs_t=qs_t, prev_cb=prev_cb):
                if oc == 0:
                    e.wait_ge(S_["qs"], qs_t)
                    if t == 0:
                        e.wait_ge(S_["ve"], prev_cb)
                if not (t == 0 and oc <= 1):
                    e.wait_ge(S_["sc"], s_hv.get(oc - 2, ctx.cnt["sc"]))
                bank = ps[oc % 2]
                for c in range(DC):
                    e.matmul(bank[:], sU["o"][:, (c * DC + oc) * 128:(c * DC + oc) * 128 + 128],
                             scb[:, c * TS:(c + 1) * TS], start=(c == 0), stop=False)
                ins = e.matmul(bank[:], sid[:], sxw["o"][:, oc * TS:(oc + 1) * TS],
                               start=False, stop=True)
                ctx.tag(ins, "mm")
            t_h = ctx.cnt["mm"]

            @blk.scalar
            def _(e, oc=oc, t_h=t_h):
                e.wait_ge(S_["mm"], t_h)
                ins = e.activation(out=sh[:, oc * TS:(oc + 1) * TS],
                                   in_=ps[oc % 2][:], func=AF.Tanh)
                ctx.tag(ins, "sc")
            s_hv[oc] = ctx.cnt["sc"]

        @blk.gpsimd
        def _(e, t=t, s_last=None):
            e.wait_ge(S_["sc"], ctx.cnt["sc"])
            ctx.dma(e, fm_view(hband, t * TS, TS), sb3(sh, DC, TS), "qg")
            e.wait_ge(S_["qg"], ctx.cnt["qg"])
            ins = e.engine_nop()
            ctx.tag(ins, "gp")


def ph_lstm_return(ctx, L):
    hband = ctx.buf(f"hb{L}", (D, T), F32)
    hbb = ctx.buf(f"hb{L}b", (D, T), F32)
    ffap = ctx.buf(f"ffn{L}", (D, T), F32)
    agout2 = ctx.buf(f"ag{L}o2", (8, D, T), F32)
    ridx = ctx.inp("lstm_gidx", (128, 64), U32)
    S_ = ctx.sems
    blk = ctx.block
    sidx = ctx.sbuf(f"lr{L}_i", [128, 64], U32)
    sx = [ctx.sbuf(f"lr{L}_x{i}", [128, 8 * 512], F32) for i in range(2)]

    @blk.gpsimd
    def _(e):
        ctx.dma(e, sidx[:], ridx[:], "qg")
        ctx.dma(e, hbb[:, :], hband[:, :], "qg")
        e.wait_ge(S_["qg"], ctx.cnt["qg"])
        ins = e.collective_compute(
            "AllGather", ALU.bypass, replica_groups=[list(range(NCORES))],
            ins=[hbb.opt()], outs=[agout2.opt()])
        ctx.tag(ins, "qv", 1)
    ag_done = ctx.cnt["qv"]

    flat = agout2.rearrange("t d (b r) -> (t d b) r", r=512)
    g_st = {}
    for ch in range(DC):
        for t in range(8):
            @blk.gpsimd
            def _(e, ch=ch, t=t):
                if ch == 0 and t == 0:
                    e.wait_ge(S_["qv"], ag_done)
                if t == 0 and ch >= 2:
                    e.wait_ge(S_["gp"], g_st[ch - 2])
                from concourse.bass import IndirectOffsetOnAxis
                ins = e.indirect_dma_start(
                    out=sx[ch % 2][:, t * 512:(t + 1) * 512],
                    out_offset=None,
                    in_=flat,
                    in_offset=IndirectOffsetOnAxis(ap=sidx[:, t * 8 + ch:t * 8 + ch + 1],
                                                   axis=0))
                ctx.tag(ins, "qg", 16)

        @blk.gpsimd
        def _(e, ch=ch):
            e.wait_ge(S_["qg"], ctx.cnt["qg"])
            ctx.dma(e, ffap[ch * 128:(ch + 1) * 128, :], sx[ch % 2][:, :], "qg")
            e.wait_ge(S_["qg"], ctx.cnt["qg"])
            ins = e.engine_nop()
            ctx.tag(ins, "gp")
        g_st[ch] = ctx.cnt["gp"]


def lstm_phases(L):
    return [
        lambda c: ph_lstm_gather(c, L),
        lambda c: ph_mm(c, f"xl{L}f", f"xa{L}", 1024, f"l{L}f_wf", f"xw{L}f", 1024,
                        "fm", None, biascol=f"l{L}f_bf"),
        lambda c: ph_mm(c, f"xl{L}i", f"xa{L}", 1024, f"l{L}f_wi", f"xw{L}i", 1024,
                        "fm", None, biascol=f"l{L}f_bi"),
        lambda c: ph_mm(c, f"xl{L}c", f"xa{L}", 1024, f"l{L}f_wc", f"xw{L}c", 1024,
                        "fm", None, biascol=f"l{L}f_bc"),
        lambda c: ph_mm(c, f"xl{L}o", f"xa{L}", 1024, f"l{L}f_wo", f"xw{L}o", 1024,
                        "fm", None, biascol=f"l{L}f_bo"),
        lambda c: ph_lstm_scan(c, L),
        lambda c: ph_lstm_return(c, L),
    ]


def lstm_host_indices(core_id):
    idx = np.zeros((128, 64), np.uint32)
    for t in range(8):
        for ch in range(DC):
            p = np.arange(128)
            idx[:, t * 8 + ch] = ((t * 1024 + ch * 128 + p) * 8 + core_id).astype(np.uint32)
    return idx


def ph_l3_qsm(ctx, scale=0.25):
    qap = ctx.buf("l3q", (T, D), BF16)
    oap = ctx.buf("l3qsm", (T, D), BF16)
    S_ = ctx.sems
    blk = ctx.block
    sq = [ctx.sbuf(f"q3_{i}", [128, D], BF16) for i in range(2)]
    se = ctx.sbuf("q3_e", [128, D], F32)
    so = [ctx.sbuf(f"q3_o{i}", [128, D], BF16) for i in range(2)]
    scol = ctx.sbuf("q3_c", [128, 12], F32)

    for i in range(T // 128):
        @blk.sync
        def _(e, i=i):
            if i >= 2:
                e.wait_ge(S_["gp"], ctx.cnt["gp"] - 1)
            ctx.fence(e, "qs")
            ctx.dma(e, sq[i % 2][:], qap[i * 128:(i + 1) * 128, :], "qs")
        qs_i = ctx.cnt["qs"]

        @blk.vector
        def _(e, i=i, qs_i=qs_i):
            e.wait_ge(S_["qs"], qs_i)
            if i >= 1:
                e.wait_ge(S_["ve"], ctx.cnt["ve"])
                e.wait_ge(S_["sc"], ctx.cnt["sc"])
            ins = e.tensor_reduce(out=scol[:, 0:4],
                                  in_=sq[i % 2][:].rearrange("p (h n) -> p h n", h=4),
                                  axis=AX.X, op=ALU.max)
            c1 = ctx.tag(ins, "ve")
            e.wait_ge(S_["ve"], c1)
            ins = e.tensor_scalar(out=scol[:, 4:8], in0=scol[:, 0:4],
                                  scalar1=-scale, scalar2=None, op0=ALU.mult)
            ctx.tag(ins, "ve")
        v_mx = ctx.cnt["ve"]

        s_h = {}
        for h in range(4):
            @blk.scalar
            def _(e, i=i, h=h, v_mx=v_mx):
                e.wait_ge(S_["ve"], v_mx)
                ins = e.activation(out=se[:, h * 256:(h + 1) * 256],
                                   in_=sq[i % 2][:, h * 256:(h + 1) * 256],
                                   func=AF.Exp, scale=scale,
                                   bias=scol[:, 4 + h:5 + h],
                                   accum_out=scol[:, 8 + h:9 + h])
                ctx.tag(ins, "sc")
            s_h[h] = ctx.cnt["sc"]

        @blk.vector
        def _(e, i=i, s_h=dict(s_h)):
            e.wait_ge(S_["sc"], s_h[3])
            ins = e.reciprocal(out=scol[:, 8:12], in_=scol[:, 8:12])
            c1 = ctx.tag(ins, "ve")
            e.wait_ge(S_["ve"], c1)
            for h in range(4):
                ins = e.tensor_scalar(out=so[i % 2][:, h * 256:(h + 1) * 256],
                                      in0=se[:, h * 256:(h + 1) * 256],
                                      scalar1=scol[:, 8 + h:9 + h], scalar2=None,
                                      op0=ALU.mult)
            ctx.tag(ins, "ve")
        v_o = ctx.cnt["ve"]

        @blk.gpsimd
        def _(e, i=i, v_o=v_o):
            e.wait_ge(S_["ve"], v_o)
            ctx.dma(e, oap[i * 128:(i + 1) * 128, :], so[i % 2][:], "qg")
            e.wait_ge(S_["qg"], ctx.cnt["qg"])
            ins = e.engine_nop()
            ctx.tag(ins, "gp")


def ph_l3_ksm(ctx, scale=0.25):
    kap = ctx.buf("l3k", (D, T), BF16)
    eap = ctx.buf("l3ek", (D, T), BF16)
    sap = ctx.buf("l3eks", (128, DC), F32)
    S_ = ctx.sems
    blk = ctx.block
    sk = [ctx.sbuf(f"k3_{i}", [128, T], BF16) for i in range(2)]
    se = [ctx.sbuf(f"k3_e{i}", [128, T], BF16) for i in range(2)]
    scol = ctx.sbuf("k3_c", [128, 2 + DC], F32)

    for ch in range(DC):
        @blk.sync
        def _(e, ch=ch):
            if ch >= 2:
                e.wait_ge(S_["gp"], ctx.cnt["gp"] - 1)
            ctx.fence(e, "qs")
            ctx.dma(e, sk[ch % 2][:], kap[ch * 128:(ch + 1) * 128, :], "qs")
        qs_c = ctx.cnt["qs"]

        @blk.vector
        def _(e, ch=ch, qs_c=qs_c):
            e.wait_ge(S_["qs"], qs_c)
            if ch >= 1:
                e.wait_ge(S_["sc"], ctx.cnt["sc"])
            ins = e.tensor_reduce(out=scol[:, 0:1], in_=sk[ch % 2][:],
                                  axis=AX.X, op=ALU.max)
            c1 = ctx.tag(ins, "ve")
            e.wait_ge(S_["ve"], c1)
            ins = e.tensor_scalar(out=scol[:, 1:2], in0=scol[:, 0:1],
                                  scalar1=-scale, scalar2=None, op0=ALU.mult)
            ctx.tag(ins, "ve")
        v_mx = ctx.cnt["ve"]

        @blk.scalar
        def _(e, ch=ch, v_mx=v_mx):
            e.wait_ge(S_["ve"], v_mx)
            ins = e.activation(out=se[ch % 2][:], in_=sk[ch % 2][:],
                               func=AF.Exp, scale=scale, bias=scol[:, 1:2],
                               accum_out=scol[:, 2 + ch:3 + ch])
            ctx.tag(ins, "sc")
        s_e = ctx.cnt["sc"]

        @blk.gpsimd
        def _(e, ch=ch, s_e=s_e):
            e.wait_ge(S_["sc"], s_e)
            ctx.dma(e, eap[ch * 128:(ch + 1) * 128, :], se[ch % 2][:], "qg")
            if ch == DC - 1:
                ctx.dma(e, sap[:, :], scol[:, 2:2 + DC], "qg")
            e.wait_ge(S_["qg"], ctx.cnt["qg"])
            ins = e.engine_nop()
            ctx.tag(ins, "gp")


def ph_l3_ctx(ctx):
    eap = ctx.buf("l3ek", (D, T), BF16)
    sap = ctx.buf("l3eks", (128, DC), F32)
    vap = ctx.buf("l3v", (T, D), BF16)
    qap = ctx.buf("l3qsm", (T, D), BF16)
    aap = ctx.buf("l3at", (T, D), BF16)
    S_ = ctx.sems
    blk = ctx.block
    NTC = T // 128
    sek = [[ctx.sbuf(f"c3_ek{half}{i}", [128, 128], BF16) for i in range(2)]
           for half in range(2)]
    sv = [ctx.sbuf(f"c3_v{i}", [128, 256], BF16) for i in range(2)]
    sqh = [[ctx.sbuf(f"c3_q{half}{i}", [128, 128], BF16) for i in range(2)]
           for half in range(2)]
    sctx = ctx.sbuf("c3_cx", [128, 2 * 256], BF16)
    ssum = ctx.sbuf("c3_s", [128, DC], F32)
    sat = [ctx.sbuf(f"c3_a{i}", [128, 256], BF16) for i in range(2)]
    ps_c = [ctx.psum(f"c3_pc{i}", [128, 256]) for i in range(2)]
    ps_a = [ctx.psum(f"c3_pa{i}", [128, 256]) for i in range(2)]

    @blk.sync
    def _(e):
        ctx.dma(e, ssum[:, :], sap[:, :], "qs")
    sum_rdy = ctx.cnt["qs"]

    @blk.vector
    def _(e):
        e.wait_ge(S_["qs"], sum_rdy)
        ins = e.reciprocal(out=ssum[:], in_=ssum[:])
        ctx.tag(ins, "ve")
    rs_rdy = ctx.cnt["ve"]

    for h in range(4):
        t_cx = {}
        for tc in range(NTC):
            @blk.sync
            def _(e, h=h, tc=tc, t_cx=t_cx):
                if tc >= 2:
                    e.wait_ge(S_["mm"], t_cx[tc - 2])
                elif h > 0:
                    e.wait_ge(S_["ve"], ctx.cnt["ve"])
                ctx.fence(e, "qs")
                for half in range(2):
                    dkrow = h * 256 + half * 128
                    ctx.dma(e, sek[half][tc % 2][:],
                            eap[dkrow:dkrow + 128, tc * 128:(tc + 1) * 128],
                            "qs", transpose=True)
                ctx.dma(e, sv[tc % 2][:],
                        vap[tc * 128:(tc + 1) * 128, h * 256:(h + 1) * 256], "qs")
            qs_t = ctx.cnt["qs"]

            @blk.tensor
            def _(e, h=h, tc=tc, qs_t=qs_t):
                e.wait_ge(S_["qs"], qs_t)
                if tc == 0 and h > 0:
                    e.wait_ge(S_["ve"], ctx.cnt["ve"])
                for half in range(2):
                    ins = e.matmul(ps_c[half][:], sek[half][tc % 2][:],
                                   sv[tc % 2][:], start=(tc == 0),
                                   stop=(tc == NTC - 1))
                ctx.tag(ins, "mm")
            t_cx[tc] = ctx.cnt["mm"]

        @blk.vector
        def _(e, h=h, t_last=t_cx[NTC - 1], rs_rdy=rs_rdy):
            e.wait_ge(S_["mm"], t_last)
            e.wait_ge(S_["ve"], rs_rdy)
            for half in range(2):
                ins = e.tensor_scalar(out=sctx[:, half * 256:(half + 1) * 256],
                                      in0=ps_c[half][:],
                                      scalar1=ssum[:, 2 * h + half:2 * h + half + 1],
                                      scalar2=None, op0=ALU.mult)
            ctx.tag(ins, "ve")
        cx_rdy = ctx.cnt["ve"]

        t_at = {}
        s_at = {}
        for tc in range(NTC):
            @blk.sync
            def _(e, h=h, tc=tc, t_at=t_at):
                if tc >= 2:
                    e.wait_ge(S_["mm"], t_at[tc - 2])
                ctx.fence(e, "qs")
                for half in range(2):
                    ctx.dma(e, sqh[half][tc % 2][:],
                            qap[tc * 128:(tc + 1) * 128,
                                h * 256 + half * 128:h * 256 + half * 128 + 128],
                            "qs", transpose=True)
            qs_t = ctx.cnt["qs"]

            @blk.tensor
            def _(e, h=h, tc=tc, qs_t=qs_t, cx_rdy=cx_rdy, s_at=s_at):
                e.wait_ge(S_["qs"], qs_t)
                if tc == 0:
                    e.wait_ge(S_["ve"], cx_rdy)
                if tc >= 2:
                    e.wait_ge(S_["sc"], s_at[tc - 2])
                for half in range(2):
                    ins = e.matmul(ps_a[tc % 2][:], sqh[half][tc % 2][:],
                                   sctx[:, half * 256:(half + 1) * 256],
                                   start=(half == 0), stop=(half == 1))
                ctx.tag(ins, "mm")
            t_at[tc] = ctx.cnt["mm"]

            @blk.scalar
            def _(e, tc=tc, t_a=t_at[tc]):
                e.wait_ge(S_["mm"], t_a)
                ins = e.activation(out=sat[tc % 2][:], in_=ps_a[tc % 2][:],
                                   func=AF.Copy)
                ctx.tag(ins, "sc")
            s_at[tc] = ctx.cnt["sc"]

            @blk.gpsimd
            def _(e, h=h, tc=tc, s_a=s_at[tc]):
                e.wait_ge(S_["sc"], s_a)
                ctx.dma(e, aap[tc * 128:(tc + 1) * 128, h * 256:(h + 1) * 256],
                        sat[tc % 2][:], "qg")
                e.wait_ge(S_["qg"], ctx.cnt["qg"])
                ins = e.engine_nop()
                ctx.tag(ins, "gp")


def l3_attn_phases():
    return [
        lambda c: ph_ln_g(c, "l31", "xnf3", "l3h1", gain="l3_ln1g", bias="l3_ln1b"),
        lambda c: ph_mm(c, "l3q", "l3h1", 1024, "l3_wq", "l3q", 1024, "tok"),
        lambda c: ph_mm(c, "l3k", "l3h1", 1024, "l3_wk", "l3k", 1024, "fm"),
        lambda c: ph_mm(c, "l3v", "l3h1", 1024, "l3_wv", "l3v", 1024, "tok"),
        ph_l3_qsm,
        ph_l3_ksm,
        ph_l3_ctx,
        lambda c: ph_mm(c, "l3wo", "l3at", 1024, "l3_wo", "l3x2", 1024, "fm",
                        None, biascol="l3_bo", addsrc="xnf3", dst_dt=F32,
                        src_tok=True),
        lambda c: ph_ln_g(c, "l32", "l3x2", "l3h2", gain="l3_ln2g", bias="l3_ln2b"),
        lambda c: ph_mm(c, "l3f1", "l3h2", 1024, "l3_ff1", "l3f1", 4096, "fm",
                        AF.Gelu_apprx_tanh, biascol="l3_bff1"),
        lambda c: ph_mm(c, "l3f2", "l3f1", 4096, "l3_ff2", "l3x3", 1024, "fm",
                        None, biascol="l3_bff2", addsrc="l3x2", dst_dt=BF16),
        lambda c: ph_mm(c, "l3wd", "l3x3", 1024, "l3_wd", "attn3", 1024, "fm",
                        None, biascol="l3_bd", dst_dt=F32),
    ]


def ph_l2_ttt(ctx):
    xqap = ctx.buf("l2xq", (256, T), BF16)
    xkap = ctx.buf("l2xk", (256, T), BF16)
    xvap = ctx.buf("l2xv", (256, T), BF16)
    xqtp = ctx.buf("l2xqt", (T, 256), BF16)
    xktp = ctx.buf("l2xkt", (T, 256), BF16)
    xvtp = ctx.buf("l2xvt", (T, 256), BF16)
    lrap = ctx.buf("l2lr", (T, 2), F32)
    oap = ctx.buf("l2o", (T, 256), F32)
    w1ap = ctx.inp("l2_W1", (2, 128, 128), F32)
    b1ap = ctx.inp("l2_b1r", (1, 256), F32)
    lnwap = ctx.inp("l2_lnwr", (1, 256), F32)
    lnbap = ctx.inp("l2_lnbr", (1, 256), F32)
    tokap = ctx.inp("l2_tok", (1, 16), F32)
    invkap = ctx.inp("invk", (1, 16), F32)
    mkap = ctx.inp("t_inclT", (16, 16), F32)
    S_ = ctx.sems
    blk = ctx.block
    NS = 256

    sxq = ctx.sbuf("t2_xq", [128, 2 * T], BF16)
    sxk = ctx.sbuf("t2_xk", [128, 2 * T], BF16)
    sxv = ctx.sbuf("t2_xv", [128, 2 * T], BF16)
    sW1 = ctx.sbuf("t2_W1", [128, 256], F32)
    sW1b = ctx.sbuf("t2_W1b", [128, 256], BF16)
    sb1 = ctx.sbuf("t2_b1", [1, 256], F32)
    sb1b = ctx.sbuf("t2_b1b", [1, 256], BF16)
    sone1 = ctx.sbuf("t2_one", [1, 16], F32)
    sone1b = ctx.sbuf("t2_oneb", [1, 16], BF16)
    srow = ctx.sbuf("t2_row", [1, 4 * 256], F32)
    sgw = ctx.sbuf("t2_gw", [16, 256], F32)
    sgb = ctx.sbuf("t2_gb", [16, 256], F32)
    stir = ctx.sbuf("t2_tir", [16, 16], F32)
    stii = ctx.sbuf("t2_tii", [16, 16], F32)
    smk = ctx.sbuf("t2_mk", [16, 16], F32)
    slr = [ctx.sbuf(f"t2_lr{i}", [16, 2], F32) for i in range(2)]
    snlr = ctx.sbuf("t2_nlr", [16, 4], F32)
    snlrb = ctx.sbuf("t2_nlrb", [16, 4], BF16)
    sxkt = [ctx.sbuf(f"t2_kt{i}", [16, 256], BF16) for i in range(2)]
    sxvt = [ctx.sbuf(f"t2_vt{i}", [16, 256], BF16) for i in range(2)]
    sxqt = [ctx.sbuf(f"t2_qt{i}", [16, 256], BF16) for i in range(2)]
    sZ = ctx.sbuf("t2_Z", [16, 256], F32)
    sxc = ctx.sbuf("t2_xc", [16, 256], F32)
    sxh = ctx.sbuf("t2_xh", [16, 256], F32)
    st1 = ctx.sbuf("t2_t1", [16, 256], F32)
    sgy = ctx.sbuf("t2_gy", [16, 256], F32)
    sgr = ctx.sbuf("t2_gr", [16, 256], BF16)
    sam = ctx.sbuf("t2_am", [16, 64], BF16)
    slex = ctx.sbuf("t2_lex", [16, 256], BF16)
    ssq = ctx.sbuf("t2_sq", [16, 256], F32)
    scol = ctx.sbuf("t2_col", [16, 16], F32)
    sout = [ctx.sbuf(f"t2_o{i}", [16, 256], F32) for i in range(2)]
    psZ = ctx.psum("t2_pZ", [16, 256])
    psA = ctx.psum("t2_pA", [16, 32])
    psB = ctx.psum("t2_pB", [16, 256])
    psW = ctx.psum("t2_pW", [128, 256])
    psb = ctx.psum("t2_pb", [1, 256])
    psr = ctx.psum("t2_pr", [16, 256])

    MU, SS, S1, S2, RSTD, RDS, MAC, SAC = 0, 2, 4, 6, 8, 10, 12, 14
    dbg = ctx.buf("t2dbg", (8, 16, 256), F32) if TTT_DEBUG else None

    @blk.sync
    def _(e):
        ctx.dma(e, sW1.rearrange("a (h b) -> a h b", h=2),
                w1ap.rearrange("h a b -> a h b"), "qv")
        ctx.dma(e, sb1[:], b1ap[:], "qv")
        ctx.dma(e, srow[0:1, 0:256], lnwap[:], "qv")
        ctx.dma(e, srow[0:1, 256:512], lnbap[:], "qv")
        ctx.dma(e, srow[0:1, 512:528], invkap[:], "qv")
        ctx.dma(e, srow[0:1, 528:544], tokap[:], "qv")
        ctx.dma(e, smk[:], mkap[:], "qv")
        ctx.dma(e, sb3(sxq, 2, T), fm_view(xqap, 0, T, nch=2), "qv")
        ctx.dma(e, sb3(sxk, 2, T), fm_view(xkap, 0, T, nch=2), "qv")
        ctx.dma(e, sb3(sxv, 2, T), fm_view(xvap, 0, T, nch=2), "qv")
    ld_rdy = ctx.cnt["qv"]

    @blk.vector
    def _(e):
        e.wait_ge(S_["qv"], ld_rdy)
        ins = e.memset(sone1[:], 1.0)
        c = ctx.tag(ins, "ve"); e.wait_ge(S_["ve"], c)
        ins = e.memset(sone1b[:], 1.0)
        c = ctx.tag(ins, "ve"); e.wait_ge(S_["ve"], c)
        ins = e.tensor_copy(out=sb1b[:], in_=sb1[:])
        c = ctx.tag(ins, "ve"); e.wait_ge(S_["ve"], c)
        ins = e.tensor_tensor(out=srow[0:1, 512:528], in0=srow[0:1, 512:528],
                              in1=srow[0:1, 528:544], op=ALU.add)
        c = ctx.tag(ins, "ve"); e.wait_ge(S_["ve"], c)
        ins = e.tensor_scalar(out=srow[0:1, 512:528], in0=srow[0:1, 512:528],
                              scalar1=0.0, scalar2=None, op0=ALU.max)
        c = ctx.tag(ins, "ve"); e.wait_ge(S_["ve"], c)
        ins = e.tensor_copy(out=sW1b[:], in_=sW1[:])
        ctx.tag(ins, "ve")
    v_setup = ctx.cnt["ve"]

    reps = []
    for w0, w1_, dstt in ((0, 256, sgw), (256, 512, sgb), (512, 528, None)):
        @blk.tensor
        def _(e, w0=w0, w1_=w1_, first=(w0 == 0)):
            if first:
                e.wait_ge(S_["ve"], v_setup)
            else:
                e.wait_ge(S_["sc"], ctx.cnt["sc"])
            ins = e.matmul(psr[0:16, 0:w1_ - w0], sone1[:], srow[0:1, w0:w1_],
                           start=True, stop=True)
            ctx.tag(ins, "mm")
        t_r = ctx.cnt["mm"]

        @blk.scalar
        def _(e, w0=w0, w1_=w1_, t_r=t_r, dstt=dstt):
            e.wait_ge(S_["mm"], t_r)
            tgt = dstt[:] if dstt is not None else stir[:]
            ins = e.activation(out=tgt, in_=psr[0:16, 0:w1_ - w0], func=AF.Copy)
            ctx.tag(ins, "sc")
        reps.append(ctx.cnt["sc"])
    s_ti = ctx.cnt["sc"]

    @blk.vector
    def _(e, s_ti=s_ti):
        e.wait_ge(S_["sc"], s_ti)
        ins = e.tensor_tensor(out=stii[:], in0=stir[:], in1=smk[:], op=ALU.mult)
        ctx.tag(ins, "ve")
    v_tii = ctx.cnt["ve"]

    prev = dict(z_ev=0, a_done=v_tii, b_ev=0, state=v_tii, out=[0, 0], st=[0, 0])
    for n in range(NS):
        sl = n % 2
        t0 = n * 16

        @blk.sync
        def _(e, n=n, sl=sl, t0=t0, prev=dict(prev)):
            if n >= 2:
                e.wait_ge(S_["ve"], prev["out"][sl])
            ctx.fence(e, "qs")
            ctx.dma(e, slr[sl][:], lrap[t0:t0 + 16, :], "qs")
            ctx.dma(e, sxkt[sl][:], xktp[t0:t0 + 16, :], "qs")
            ctx.dma(e, sxvt[sl][:], xvtp[t0:t0 + 16, :], "qs")
            ctx.dma(e, sxqt[sl][:], xqtp[t0:t0 + 16, :], "qs")
        qs_n = ctx.cnt["qs"]

        @blk.tensor
        def _(e, n=n, t0=t0, prev=dict(prev)):
            if n == 0:
                e.wait_ge(S_["ve"], v_tii)
            else:
                e.wait_ge(S_["ve"], prev["state"])
                e.wait_ge(S_["sc"], prev["z_ev"])
                e.wait_ge(S_["ve"], prev["a_done"])
            for h in range(2):
                e.matmul(psZ[:, h * 128:(h + 1) * 128],
                         sxk[:, h * T + t0:h * T + t0 + 16],
                         sW1b[:, h * 128:(h + 1) * 128], start=(h == 0), stop=False)
            for h in range(2):
                e.matmul(psZ[:, h * 128:(h + 1) * 128], sone1b[:],
                         sb1b[0:1, h * 128:(h + 1) * 128], start=False, stop=(h == 1))
            for h in range(2):
                ins = e.matmul(psA[:, h * 16:(h + 1) * 16],
                               sxk[:, h * T + t0:h * T + t0 + 16],
                               sxq[:, h * T + t0:h * T + t0 + 16],
                               start=(h == 0), stop=(h == 1))
            ctx.tag(ins, "mm")
            if n > 0:
                e.wait_ge(S_["sc"], prev["b_ev"])
            for h in range(2):
                e.matmul(psB[:, h * 128:(h + 1) * 128],
                         sxq[:, h * T + t0:h * T + t0 + 16],
                         sW1b[:, h * 128:(h + 1) * 128], start=(h == 0), stop=False)
            for h in range(2):
                ins = e.matmul(psB[:, h * 128:(h + 1) * 128], sone1b[:],
                               sb1b[0:1, h * 128:(h + 1) * 128],
                               start=False, stop=False)
            ctx.tag(ins, "mm")
        t_z = ctx.cnt["mm"] - 1
        t_bp1 = ctx.cnt["mm"]

        @blk.scalar
        def _(e, n=n, t_z=t_z, prev=dict(prev)):
            e.wait_ge(S_["mm"], t_z)
            if n > 0:
                e.wait_ge(S_["ve"], prev["a_done"])
            e.activation(out=sZ[:, 0:128], in_=psZ[:, 0:128], func=AF.Copy,
                         accum_out=scol[:, MAC:MAC + 1])
            ins = e.activation(out=sZ[:, 128:256], in_=psZ[:, 128:256], func=AF.Copy,
                               accum_out=scol[:, MAC + 1:MAC + 2])
            ctx.tag(ins, "sc")
        s_z = ctx.cnt["sc"]

        if TTT_DEBUG and n == 0:
            @blk.gpsimd
            def _(e, s_z=s_z):
                e.wait_ge(S_["sc"], s_z)
                ctx.dma(e, dbg[3], sZ[:], "qg")
                ctx.dma(e, dbg[4][0:16, :], sgw[:], "qg")
                ctx.dma(e, dbg[5][0:16, :], sgb[:], "qg")
                e.wait_ge(S_["qg"], ctx.cnt["qg"])
                ins = e.engine_nop()
                ctx.tag(ins, "gp")

        @blk.vector
        def _(e, n=n, sl=sl, s_z=s_z, qs_n=qs_n):
            e.wait_ge(S_["sc"], s_z)
            e.wait_ge(S_["qs"], qs_n)
            if TTT_DEBUG and n == 0:
                e.wait_ge(S_["gp"], ctx.cnt["gp"])
            ins = e.tensor_scalar(out=scol[:, MU:MU + 2], in0=scol[:, MAC:MAC + 2],
                                  scalar1=1.0 / 128.0, scalar2=None, op0=ALU.mult)
            c = ctx.tag(ins, "ve"); e.wait_ge(S_["ve"], c)
            for h in range(2):
                ins = e.tensor_scalar(out=sxc[:, h * 128:(h + 1) * 128],
                                      in0=sZ[:, h * 128:(h + 1) * 128],
                                      scalar1=scol[:, MU + h:MU + h + 1],
                                      scalar2=None, op0=ALU.subtract)
            ctx.tag(ins, "ve")
        v_xc = ctx.cnt["ve"]

        @blk.scalar
        def _(e, v_xc=v_xc):
            e.wait_ge(S_["ve"], v_xc)
            e.activation(out=ssq[:, 0:128], in_=sxc[:, 0:128], func=AF.Square,
                         accum_out=scol[:, SAC:SAC + 1])
            ins = e.activation(out=ssq[:, 128:256], in_=sxc[:, 128:256], func=AF.Square,
                               accum_out=scol[:, SAC + 1:SAC + 2])
            ctx.tag(ins, "sc")
        s_sq = ctx.cnt["sc"]

        @blk.vector
        def _(e, s_sq=s_sq):
            e.wait_ge(S_["sc"], s_sq)
            ins = e.tensor_scalar(out=scol[:, SS:SS + 2], in0=scol[:, SAC:SAC + 2],
                                  scalar1=1.0 / 128.0, scalar2=1e-6,
                                  op0=ALU.mult, op1=ALU.add)
            ctx.tag(ins, "ve")
        v_var = ctx.cnt["ve"]

        @blk.scalar
        def _(e, v_var=v_var):
            e.wait_ge(S_["ve"], v_var)
            ins = e.activation(out=scol[:, RSTD:RSTD + 2], in_=scol[:, SS:SS + 2],
                               func=AF.Sqrt)
            ctx.tag(ins, "sc")
        s_std = ctx.cnt["sc"]

        @blk.vector
        def _(e, n=n, sl=sl, s_std=s_std):
            e.wait_ge(S_["sc"], s_std)
            ins = e.tensor_scalar(out=scol[:, RDS:RDS + 2], in0=scol[:, RSTD:RSTD + 2],
                                  scalar1=128.0, scalar2=None, op0=ALU.mult)
            c = ctx.tag(ins, "ve"); e.wait_ge(S_["ve"], c)
            ins = e.reciprocal(out=scol[:, RDS:RDS + 2], in_=scol[:, RDS:RDS + 2])
            c = ctx.tag(ins, "ve"); e.wait_ge(S_["ve"], c)
            ins = e.reciprocal(out=scol[:, RSTD:RSTD + 2], in_=scol[:, RSTD:RSTD + 2])
            c = ctx.tag(ins, "ve"); e.wait_ge(S_["ve"], c)
            for h in range(2):
                ins = e.tensor_scalar(out=sxh[:, h * 128:(h + 1) * 128],
                                      in0=sxc[:, h * 128:(h + 1) * 128],
                                      scalar1=scol[:, RSTD + h:RSTD + h + 1],
                                      scalar2=None, op0=ALU.mult)
            c = ctx.tag(ins, "ve"); e.wait_ge(S_["ve"], c)
            ins = e.tensor_tensor(out=st1[:], in0=sxh[:], in1=sgw[:], op=ALU.mult)
            c = ctx.tag(ins, "ve"); e.wait_ge(S_["ve"], c)
            ins = e.tensor_tensor(out=st1[:], in0=st1[:], in1=sgb[:], op=ALU.add)
            c = ctx.tag(ins, "ve"); e.wait_ge(S_["ve"], c)
            ins = e.tensor_tensor(out=st1[:], in0=st1[:], in1=sxvt[sl][:], op=ALU.subtract)
            c = ctx.tag(ins, "ve"); e.wait_ge(S_["ve"], c)
            ins = e.tensor_tensor(out=st1[:], in0=st1[:], in1=sxkt[sl][:], op=ALU.add)
            c = ctx.tag(ins, "ve"); e.wait_ge(S_["ve"], c)
            ins = e.tensor_tensor(out=sgy[:], in0=st1[:], in1=sgw[:], op=ALU.mult)
            c = ctx.tag(ins, "ve"); e.wait_ge(S_["ve"], c)
            ins = e.tensor_reduce(out=scol[:, S1:S1 + 2],
                                  in_=sgy[:].rearrange("p (h f) -> p h f", h=2),
                                  axis=AX.X, op=ALU.add)
            c = ctx.tag(ins, "ve"); e.wait_ge(S_["ve"], c)
            ins = e.tensor_tensor(out=st1[:], in0=sgy[:], in1=sxh[:], op=ALU.mult)
            c = ctx.tag(ins, "ve"); e.wait_ge(S_["ve"], c)
            ins = e.tensor_reduce(out=scol[:, S2:S2 + 2],
                                  in_=st1[:].rearrange("p (h f) -> p h f", h=2),
                                  axis=AX.X, op=ALU.add)
            c = ctx.tag(ins, "ve"); e.wait_ge(S_["ve"], c)
            for h in range(2):
                hs = slice(h * 128, (h + 1) * 128)
                ins = e.tensor_scalar(out=st1[:, hs], in0=sgy[:, hs], scalar1=128.0,
                                      scalar2=scol[:, S1 + h:S1 + h + 1],
                                      op0=ALU.mult, op1=ALU.subtract)
                c = ctx.tag(ins, "ve"); e.wait_ge(S_["ve"], c)
                ins = e.tensor_scalar(out=sxc[:, hs], in0=sxh[:, hs],
                                      scalar1=scol[:, S2 + h:S2 + h + 1],
                                      scalar2=None, op0=ALU.mult)
                c = ctx.tag(ins, "ve"); e.wait_ge(S_["ve"], c)
                ins = e.tensor_tensor(out=st1[:, hs], in0=st1[:, hs], in1=sxc[:, hs],
                                      op=ALU.subtract)
                c = ctx.tag(ins, "ve"); e.wait_ge(S_["ve"], c)
                ins = e.tensor_scalar(out=sgr[:, hs], in0=st1[:, hs],
                                      scalar1=scol[:, RDS + h:RDS + h + 1],
                                      scalar2=None, op0=ALU.mult)
                c = ctx.tag(ins, "ve"); e.wait_ge(S_["ve"], c)
            ins = e.tensor_scalar(out=snlr[:, 0:2], in0=slr[sl][:],
                                  scalar1=-1.0 / 128.0, scalar2=None, op0=ALU.mult)
            c = ctx.tag(ins, "ve"); e.wait_ge(S_["ve"], c)
            ins = e.tensor_scalar(out=snlr[:, 2:4], in0=snlr[:, 0:2],
                                  scalar1=stir[:, 15:16], scalar2=None, op0=ALU.mult)
            c = ctx.tag(ins, "ve"); e.wait_ge(S_["ve"], c)
            ins = e.tensor_copy(out=snlrb[:], in_=snlr[:])
            c = ctx.tag(ins, "ve"); e.wait_ge(S_["ve"], c)
            for h in range(2):
                ins = e.tensor_tensor(out=st1[:, 0:16], in0=psA[:, h * 16:(h + 1) * 16],
                                      in1=smk[:], op=ALU.mult)
                c = ctx.tag(ins, "ve"); e.wait_ge(S_["ve"], c)
                ins = e.tensor_tensor(out=st1[:, 16:32], in0=st1[:, 0:16],
                                      in1=stir[:], op=ALU.mult)
                c = ctx.tag(ins, "ve"); e.wait_ge(S_["ve"], c)
                ins = e.tensor_scalar(out=sam[:, h * 16:(h + 1) * 16],
                                      in0=st1[:, 16:32],
                                      scalar1=snlr[:, h:h + 1], scalar2=None,
                                      op0=ALU.mult)
                c = ctx.tag(ins, "ve"); e.wait_ge(S_["ve"], c)
                ins = e.tensor_scalar(out=sam[:, 32 + h * 16:32 + (h + 1) * 16],
                                      in0=stii[:], scalar1=snlr[:, h:h + 1],
                                      scalar2=None, op0=ALU.mult)
                c = ctx.tag(ins, "ve"); e.wait_ge(S_["ve"], c)
                ins = e.tensor_scalar(out=slex[:, h * 128:(h + 1) * 128],
                                      in0=sxkt[sl][:, h * 128:(h + 1) * 128],
                                      scalar1=snlr[:, 2 + h:3 + h], scalar2=None,
                                      op0=ALU.mult)
                c = ctx.tag(ins, "ve"); e.wait_ge(S_["ve"], c)
            ins = e.engine_nop()
            ctx.tag(ins, "ve")
        v_grad = ctx.cnt["ve"]

        @blk.tensor
        def _(e, n=n, v_grad=v_grad):
            e.wait_ge(S_["ve"], v_grad)
            for h in range(2):
                hs = slice(h * 128, (h + 1) * 128)
                e.matmul(psB[:, hs], sam[:, h * 16:(h + 1) * 16], sgr[:, hs],
                         start=False, stop=False)
                ins = e.matmul(psB[:, hs], sam[:, 32 + h * 16:32 + (h + 1) * 16],
                               sgr[:, hs], start=False, stop=(h == 1))
            ctx.tag(ins, "mm")
            for h in range(2):
                hs = slice(h * 128, (h + 1) * 128)
                e.matmul(psW[:, hs], slex[:, hs], sgr[:, hs], start=(h == 0),
                         stop=(h == 1))
                ins = e.matmul(psb[0:1, hs], snlrb[:, 2 + h:3 + h], sgr[:, hs],
                               start=(h == 0), stop=(h == 1))
            ctx.tag(ins, "mm")
        t_b2 = ctx.cnt["mm"] - 1
        t_st = ctx.cnt["mm"]

        @blk.scalar
        def _(e, t_b2=t_b2):
            e.wait_ge(S_["mm"], t_b2)
            e.activation(out=sZ[:, 0:128], in_=psB[:, 0:128], func=AF.Copy,
                         accum_out=scol[:, MAC:MAC + 1])
            ins = e.activation(out=sZ[:, 128:256], in_=psB[:, 128:256], func=AF.Copy,
                               accum_out=scol[:, MAC + 1:MAC + 2])
            ctx.tag(ins, "sc")
        s_b = ctx.cnt["sc"]
        prev["b_ev"] = s_b

        @blk.vector
        def _(e, s_b=s_b):
            e.wait_ge(S_["sc"], s_b)
            ins = e.tensor_scalar(out=scol[:, MU:MU + 2], in0=scol[:, MAC:MAC + 2],
                                  scalar1=1.0 / 128.0, scalar2=None, op0=ALU.mult)
            c = ctx.tag(ins, "ve"); e.wait_ge(S_["ve"], c)
            for h in range(2):
                ins = e.tensor_scalar(out=sxc[:, h * 128:(h + 1) * 128],
                                      in0=sZ[:, h * 128:(h + 1) * 128],
                                      scalar1=scol[:, MU + h:MU + h + 1],
                                      scalar2=None, op0=ALU.subtract)
            ctx.tag(ins, "ve")
        v_xc2 = ctx.cnt["ve"]

        @blk.scalar
        def _(e, v_xc2=v_xc2):
            e.wait_ge(S_["ve"], v_xc2)
            e.activation(out=ssq[:, 0:128], in_=sxc[:, 0:128], func=AF.Square,
                         accum_out=scol[:, SAC:SAC + 1])
            ins = e.activation(out=ssq[:, 128:256], in_=sxc[:, 128:256], func=AF.Square,
                               accum_out=scol[:, SAC + 1:SAC + 2])
            ctx.tag(ins, "sc")
        s_sq2 = ctx.cnt["sc"]

        @blk.vector
        def _(e, s_sq2=s_sq2):
            e.wait_ge(S_["sc"], s_sq2)
            ins = e.tensor_scalar(out=scol[:, SS:SS + 2], in0=scol[:, SAC:SAC + 2],
                                  scalar1=1.0 / 128.0, scalar2=1e-6,
                                  op0=ALU.mult, op1=ALU.add)
            ctx.tag(ins, "ve")
        v_var2 = ctx.cnt["ve"]

        @blk.scalar
        def _(e, v_var2=v_var2):
            e.wait_ge(S_["ve"], v_var2)
            ins = e.activation(out=scol[:, RSTD:RSTD + 2], in_=scol[:, SS:SS + 2],
                               func=AF.Sqrt)
            ctx.tag(ins, "sc")
        s_std2 = ctx.cnt["sc"]

        @blk.vector
        def _(e, n=n, sl=sl, s_std2=s_std2, t_st=t_st, pst=prev["st"][sl]):
            e.wait_ge(S_["sc"], s_std2)
            if n >= 2:
                e.wait_ge(S_["gp"], pst)
            ins = e.reciprocal(out=scol[:, RSTD:RSTD + 2], in_=scol[:, RSTD:RSTD + 2])
            c = ctx.tag(ins, "ve"); e.wait_ge(S_["ve"], c)
            for h in range(2):
                hs = slice(h * 128, (h + 1) * 128)
                ins = e.tensor_scalar(out=sxh[:, hs], in0=sxc[:, hs],
                                      scalar1=scol[:, RSTD + h:RSTD + h + 1],
                                      scalar2=None, op0=ALU.mult)
            c = ctx.tag(ins, "ve"); e.wait_ge(S_["ve"], c)
            ins = e.tensor_tensor(out=st1[:], in0=sxh[:], in1=sgw[:], op=ALU.mult)
            c = ctx.tag(ins, "ve"); e.wait_ge(S_["ve"], c)
            ins = e.tensor_tensor(out=st1[:], in0=st1[:], in1=sgb[:], op=ALU.add)
            c = ctx.tag(ins, "ve"); e.wait_ge(S_["ve"], c)
            ins = e.tensor_tensor(out=sout[sl][:], in0=st1[:], in1=sxqt[sl][:],
                                  op=ALU.add)
            c = ctx.tag(ins, "ve"); e.wait_ge(S_["ve"], c)
            e.wait_ge(S_["mm"], t_st)
            ins = e.tensor_tensor(out=sW1[:], in0=sW1[:], in1=psW[:], op=ALU.add)
            c = ctx.tag(ins, "ve"); e.wait_ge(S_["ve"], c)
            ins = e.tensor_copy(out=sW1b[:], in_=sW1[:])
            c = ctx.tag(ins, "ve"); e.wait_ge(S_["ve"], c)
            ins = e.tensor_tensor(out=sb1[:], in0=sb1[:], in1=psb[:], op=ALU.add)
            c = ctx.tag(ins, "ve"); e.wait_ge(S_["ve"], c)
            ins = e.tensor_copy(out=sb1b[:], in_=sb1[:])
            ctx.tag(ins, "ve")
        v_fin = ctx.cnt["ve"]
        prev["state"] = v_fin
        prev["a_done"] = v_grad
        prev["z_ev"] = s_z

        @blk.gpsimd
        def _(e, n=n, sl=sl, t0=t0, v_fin=v_fin):
            e.wait_ge(S_["ve"], v_fin)
            if TTT_DEBUG and n == 0:
                ctx.dma(e, dbg[0], sZ[:], "qg")
                ctx.dma(e, dbg[1], sgy[:], "qg")
                ctx.dma(e, dbg[2], sgr[:].rearrange("p c -> p c") if False else sgr[:], "qg")
                ctx.dma(e, dbg[6], sxh[:], "qg")
                ctx.dma(e, dbg[7][0:16, 0:4], snlr[:], "qg")
            ctx.dma(e, oap[t0:t0 + 16, :], sout[sl][:], "qg")
            e.wait_ge(S_["qg"], ctx.cnt["qg"])
            ins = e.engine_nop()
            ctx.tag(ins, "gp")
        prev["out"][sl] = v_fin
        prev["st"][sl] = ctx.cnt["gp"]


def ph_lnt(ctx, tagp, src, dst, C, gamma, beta, eps, src_dt=F32):
    sap = ctx.buf(src, (T, C), src_dt)
    dap = ctx.buf(dst, (T, C), BF16)
    gap = ctx.inp(gamma, (1, C), F32)
    bap = ctx.inp(beta, (1, C), F32)
    S_ = ctx.sems
    blk = ctx.block
    sx = [ctx.sbuf(f"{tagp}_x{i}", [128, C], src_dt) for i in range(2)]
    sxc = ctx.sbuf(f"{tagp}_xc", [128, C], F32)
    ssq = ctx.sbuf(f"{tagp}_sq", [128, C], F32)
    so = [ctx.sbuf(f"{tagp}_o{i}", [128, C], BF16) for i in range(2)]
    sg = ctx.sbuf(f"{tagp}_g", [128, C], F32)
    sb_ = ctx.sbuf(f"{tagp}_b", [128, C], F32)
    srow = ctx.sbuf(f"{tagp}_r", [1, 2 * C], F32)
    sone = ctx.sbuf(f"{tagp}_1", [1, 128], F32)
    scol = ctx.sbuf(f"{tagp}_c", [128, 8], F32)
    psr = ctx.psum(f"{tagp}_p", [128, C])
    MAC, SAC, MU, SS, RSTD = 0, 1, 2, 3, 4

    @blk.sync
    def _(e):
        ctx.dma(e, srow[0:1, 0:C], gap[:], "qs")
        ctx.dma(e, srow[0:1, C:2 * C], bap[:], "qs")
    r_rdy = ctx.cnt["qs"]

    @blk.vector
    def _(e):
        ins = e.memset(sone[:], 1.0)
        ctx.tag(ins, "ve")
    one_rdy = ctx.cnt["ve"]

    gb_rep = {}
    for w, dstt in ((0, sg), (1, sb_)):
        @blk.tensor
        def _(e, w=w):
            e.wait_ge(S_["qs"], r_rdy)
            e.wait_ge(S_["ve"], one_rdy)
            if w == 1:
                e.wait_ge(S_["sc"], gb_rep[0])
            ins = e.matmul(psr[:], sone[:], srow[0:1, w * C:(w + 1) * C],
                           start=True, stop=True)
            ctx.tag(ins, "mm")
        t_r = ctx.cnt["mm"]

        @blk.scalar
        def _(e, w=w, t_r=t_r, dstt=dstt):
            e.wait_ge(S_["mm"], t_r)
            ins = e.activation(out=dstt[:], in_=psr[:], func=AF.Copy)
            ctx.tag(ins, "sc")
        gb_rep[w] = ctx.cnt["sc"]
    rep_rdy = ctx.cnt["sc"]

    for i in range(T // 128):
        @blk.sync
        def _(e, i=i):
            if i >= 2:
                e.wait_ge(S_["gp"], ctx.cnt["gp"] - 1)
            ctx.fence(e, "qs")
            ctx.dma(e, sx[i % 2][:], sap[i * 128:(i + 1) * 128, :], "qs")
        qs_i = ctx.cnt["qs"]

        @blk.vector
        def _(e, i=i, qs_i=qs_i):
            e.wait_ge(S_["qs"], qs_i)
            if i >= 1:
                e.wait_ge(S_["sc"], ctx.cnt["sc"])
            ins = e.tensor_reduce(out=scol[:, MAC:MAC + 1], in_=sx[i % 2][:],
                                  axis=AX.X, op=ALU.add)
            c = ctx.tag(ins, "ve"); e.wait_ge(S_["ve"], c)
            ins = e.tensor_scalar(out=scol[:, MU:MU + 1], in0=scol[:, MAC:MAC + 1],
                                  scalar1=1.0 / C, scalar2=None, op0=ALU.mult)
            c = ctx.tag(ins, "ve"); e.wait_ge(S_["ve"], c)
            ins = e.tensor_scalar(out=sxc[:], in0=sx[i % 2][:],
                                  scalar1=scol[:, MU:MU + 1], scalar2=None,
                                  op0=ALU.subtract)
            ctx.tag(ins, "ve")
        v_xc = ctx.cnt["ve"]

        @blk.scalar
        def _(e, v_xc=v_xc):
            e.wait_ge(S_["ve"], v_xc)
            ins = e.activation(out=ssq[:], in_=sxc[:], func=AF.Square,
                               accum_out=scol[:, SAC:SAC + 1])
            ctx.tag(ins, "sc")
        s_sq = ctx.cnt["sc"]

        @blk.vector
        def _(e, s_sq=s_sq, eps=eps):
            e.wait_ge(S_["sc"], s_sq)
            ins = e.tensor_scalar(out=scol[:, SS:SS + 1], in0=scol[:, SAC:SAC + 1],
                                  scalar1=1.0 / C, scalar2=float(eps),
                                  op0=ALU.mult, op1=ALU.add)
            ctx.tag(ins, "ve")
        v_var = ctx.cnt["ve"]

        @blk.scalar
        def _(e, v_var=v_var):
            e.wait_ge(S_["ve"], v_var)
            ins = e.activation(out=scol[:, RSTD:RSTD + 1], in_=scol[:, SS:SS + 1],
                               func=AF.Sqrt)
            ctx.tag(ins, "sc")
        s_std = ctx.cnt["sc"]

        @blk.vector
        def _(e, i=i, s_std=s_std):
            e.wait_ge(S_["sc"], s_std)
            ins = e.reciprocal(out=scol[:, RSTD:RSTD + 1], in_=scol[:, RSTD:RSTD + 1])
            c = ctx.tag(ins, "ve"); e.wait_ge(S_["ve"], c)
            ins = e.tensor_scalar(out=sxc[:], in0=sxc[:],
                                  scalar1=scol[:, RSTD:RSTD + 1], scalar2=None,
                                  op0=ALU.mult)
            c = ctx.tag(ins, "ve"); e.wait_ge(S_["ve"], c)
            ins = e.tensor_tensor(out=sxc[:], in0=sxc[:], in1=sg[:], op=ALU.mult)
            c = ctx.tag(ins, "ve"); e.wait_ge(S_["ve"], c)
            ins = e.tensor_tensor(out=so[i % 2][:], in0=sxc[:], in1=sb_[:], op=ALU.add)
            ctx.tag(ins, "ve")
        v_o = ctx.cnt["ve"]

        @blk.gpsimd
        def _(e, i=i, v_o=v_o):
            e.wait_ge(S_["ve"], v_o)
            ctx.dma(e, dap[i * 128:(i + 1) * 128, :], so[i % 2][:], "qg")
            e.wait_ge(S_["qg"], ctx.cnt["qg"])
            ins = e.engine_nop()
            ctx.tag(ins, "gp")


def ph_t2tok(ctx, tagp, src, C, dst):
    sap = ctx.buf(src, (C, T), BF16)
    dap = ctx.buf(dst, (T, C), BF16)
    S_ = ctx.sems
    blk = ctx.block
    CC = C // 128
    sx = [ctx.sbuf(f"{tagp}_x{i}", [128, C], BF16) for i in range(2)]

    for i in range(T // 128):
        @blk.sync
        def _(e, i=i):
            if i >= 2:
                e.wait_ge(S_["gp"], ctx.cnt["gp"] - 1)
            ctx.fence(e, "qs")
            for cc in range(CC):
                ctx.dma(e, sx[i % 2][:, cc * 128:(cc + 1) * 128],
                        sap[cc * 128:(cc + 1) * 128, i * 128:(i + 1) * 128],
                        "qs", transpose=True)
        qs_i = ctx.cnt["qs"]

        @blk.gpsimd
        def _(e, i=i, qs_i=qs_i):
            e.wait_ge(S_["qs"], qs_i)
            ctx.dma(e, dap[i * 128:(i + 1) * 128, :], sx[i % 2][:], "qg")
            e.wait_ge(S_["qg"], ctx.cnt["qg"])
            ins = e.engine_nop()
            ctx.tag(ins, "gp")


def l2_attn_phases():
    return [
        lambda c: ph_mm(c, "l2h", "xn2", 1024, "l2_down", "l2h", 256, "fm",
                        AF.Silu, biascol="l2_downb"),
        lambda c: ph_mm(c, "l2xq", "l2h", 256, "l2_wq", "l2xq", 256, "fm"),
        lambda c: ph_mm(c, "l2xk", "l2h", 256, "l2_wk", "l2xk", 256, "fm"),
        lambda c: ph_mm(c, "l2xv", "l2h", 256, "l2_wv", "l2xv", 256, "fm"),
        lambda c: ph_mm(c, "l2lr", "l2h", 256, "l2_lrw", "l2lr", 2, "tok",
                        AF.Sigmoid, biasrow="l2_lrbr", dst_dt=F32),
        lambda c: ph_t2tok(c, "l2tq", "l2xq", 256, "l2xqt"),
        lambda c: ph_t2tok(c, "l2tk", "l2xk", 256, "l2xkt"),
        lambda c: ph_t2tok(c, "l2tv", "l2xv", 256, "l2xvt"),
        ph_l2_ttt,
        lambda c: ph_lnt(c, "l2p1", "l2o", "l2on", 256, "l2_postw", "l2_postb", 1e-5),
        lambda c: ph_mm(c, "l2wo", "l2on", 256, "l2_wo", "l2wo", 256, "tok",
                        None, dst_dt=F32, src_tok=True),
        lambda c: ph_lnt(c, "l2p2", "l2wo", "l2on2", 256, "l2_lnw2", "l2_lnb2", 1e-5),
        lambda c: ph_mm(c, "l2up", "l2on2", 256, "l2_up", "attn2", 1024, "fm",
                        None, biascol="l2_upb", dst_dt=F32, src_tok=True),
    ]




def ph_l1_conv(ctx):
    qp = ctx.buf("l1qp", (D, T), BF16)
    kp = ctx.buf("l1kp", (D, T), BF16)
    vp = ctx.buf("l1vp", (D, T), BF16)
    oq = ctx.buf("l1q", (D, T), BF16)
    ok = ctx.buf("l1k", (D, T), BF16)
    ov = ctx.buf("l1v", (D, T), BF16)
    fir_out = {1: ctx.buf("l1f1", (D, T), BF16), 3: ctx.buf("l1f3", (D, T), BF16),
               7: ctx.buf("l1f7", (D, T), BF16), 31: ctx.buf("l1f31", (D, T), BF16)}
    wq = ctx.inp("l1_qc", (D, 4), F32)
    wk = ctx.inp("l1_kc", (D, 4), F32)
    wv = ctx.inp("l1_vc", (D, 4), F32)
    wf = {1: ctx.inp("l1_fir1", (D, 1), F32), 3: ctx.inp("l1_fir3", (D, 3), F32),
          7: ctx.inp("l1_fir7", (D, 7), F32), 31: ctx.inp("l1_fir31", (D, 31), F32)}
    S_ = ctx.sems
    blk = ctx.block
    PAD = 32
    sx = ctx.sbuf("c1_x", [128, PAD + T], BF16)
    sacc = ctx.sbuf("c1_a", [128, T], F32)
    sv = ctx.sbuf("c1_v", [128, PAD + T], BF16)
    so = ctx.sbuf("c1_o", [128, T], BF16)
    swt = ctx.sbuf("c1_w", [128, 64], F32)

    for ch in range(DC):
        @blk.sync
        def _(e, ch=ch):
            if ch > 0:
                e.wait_ge(S_["ve"], ctx.cnt["ve"])
            ctx.fence(e, "qv")
            ctx.dma(e, swt[:, 0:4], wq[ch * 128:(ch + 1) * 128, :], "qv")
            ctx.dma(e, swt[:, 4:8], wk[ch * 128:(ch + 1) * 128, :], "qv")
            ctx.dma(e, swt[:, 8:12], wv[ch * 128:(ch + 1) * 128, :], "qv")
            ctx.dma(e, swt[:, 12:13], wf[1][ch * 128:(ch + 1) * 128, :], "qv")
            ctx.dma(e, swt[:, 16:19], wf[3][ch * 128:(ch + 1) * 128, :], "qv")
            ctx.dma(e, swt[:, 24:31], wf[7][ch * 128:(ch + 1) * 128, :], "qv")
            ctx.dma(e, swt[:, 32:63], wf[31][ch * 128:(ch + 1) * 128, :], "qv")
        w_rdy = ctx.cnt["qv"]

        for which, pre, wof, dst in (("q", qp, 0, oq), ("k", kp, 4, ok), ("v", vp, 8, ov)):
            @blk.sync
            def _(e, ch=ch, pre=pre):
                e.wait_ge(S_["sc"], ctx.cnt["sc"])
                ctx.fence(e, "qs")
                ctx.dma(e, sx[:, PAD:PAD + T], fm_view(pre, 0, T, c0=ch, nch=1)[:, 0, :], "qs")
            x_rdy = ctx.cnt["qs"]

            @blk.vector
            def _(e, ch=ch, wof=wof, x_rdy=x_rdy, w_rdy=w_rdy):
                e.wait_ge(S_["qs"], x_rdy)
                e.wait_ge(S_["qv"], w_rdy)
                e.wait_ge(S_["sc"], ctx.cnt["sc"])
                ins = e.memset(sx[:, 0:PAD], 0.0)
                c = ctx.tag(ins, "ve"); e.wait_ge(S_["ve"], c)
                ins = e.tensor_scalar(out=sacc[:], in0=sx[:, PAD - 3:PAD - 3 + T],
                                      scalar1=swt[:, wof:wof + 1], scalar2=None,
                                      op0=ALU.mult)
                c = ctx.tag(ins, "ve"); e.wait_ge(S_["ve"], c)
                for j in range(1, 4):
                    ins = e.scalar_tensor_tensor(
                        out=sacc[:], in0=sx[:, PAD - 3 + j:PAD - 3 + j + T],
                        scalar=swt[:, wof + j:wof + j + 1], in1=sacc[:],
                        op0=ALU.mult, op1=ALU.add)
                    c = ctx.tag(ins, "ve"); e.wait_ge(S_["ve"], c)
                ctx.tag(e.engine_nop(), "ve")
            v_acc = ctx.cnt["ve"]

            @blk.scalar
            def _(e, ch=ch, v_acc=v_acc, which=which):
                e.wait_ge(S_["ve"], v_acc)
                if which == "v":
                    e.wait_ge(S_["gp"], ctx.cnt["gp"])
                    ins = e.activation(out=sv[:, PAD:PAD + T], in_=sacc[:], func=AF.Silu)
                else:
                    e.wait_ge(S_["gp"], ctx.cnt["gp"])
                    ins = e.activation(out=so[:], in_=sacc[:], func=AF.Silu)
                ctx.tag(ins, "sc")
            s_si = ctx.cnt["sc"]

            @blk.gpsimd
            def _(e, ch=ch, s_si=s_si, dst=dst, which=which):
                e.wait_ge(S_["sc"], s_si)
                srcv = sv[:, PAD:PAD + T] if which == "v" else so[:]
                ctx.dma(e, fm_view(dst, 0, T, c0=ch, nch=1)[:, 0, :], srcv, "qg")
                e.wait_ge(S_["qg"], ctx.cnt["qg"])
                ctx.tag(e.engine_nop(), "gp")

        @blk.vector
        def _(e, ch=ch):
            e.wait_ge(S_["sc"], ctx.cnt["sc"])
            ins = e.memset(sv[:, 0:PAD], 0.0)
            ctx.tag(ins, "ve")
        v_pad = ctx.cnt["ve"]

        for kw, wof in ((1, 12), (3, 16), (7, 24), (31, 32)):
            @blk.vector
            def _(e, ch=ch, kw=kw, wof=wof, v_pad=v_pad):
                e.wait_ge(S_["ve"], v_pad)
                e.wait_ge(S_["gp"], ctx.cnt["gp"])
                ins = e.tensor_scalar(out=sacc[:], in0=sv[:, PAD - (kw - 1):PAD - (kw - 1) + T],
                                      scalar1=swt[:, wof:wof + 1], scalar2=None,
                                      op0=ALU.mult)
                c = ctx.tag(ins, "ve"); e.wait_ge(S_["ve"], c)
                for j in range(1, kw):
                    ins = e.scalar_tensor_tensor(
                        out=sacc[:], in0=sv[:, PAD - (kw - 1) + j:PAD - (kw - 1) + j + T],
                        scalar=swt[:, wof + j:wof + j + 1], in1=sacc[:],
                        op0=ALU.mult, op1=ALU.add)
                    c = ctx.tag(ins, "ve"); e.wait_ge(S_["ve"], c)
                ins = e.tensor_copy(out=so[:], in_=sacc[:])
                ctx.tag(ins, "ve")
            v_f = ctx.cnt["ve"]

            @blk.gpsimd
            def _(e, ch=ch, kw=kw, v_f=v_f):
                e.wait_ge(S_["ve"], v_f)
                ctx.dma(e, fm_view(fir_out[kw], 0, T, c0=ch, nch=1)[:, 0, :], so[:], "qg")
                e.wait_ge(S_["qg"], ctx.cnt["qg"])
                ctx.tag(e.engine_nop(), "gp")


def ph_l1_beta(ctx):
    xn = ctx.buf("xn1", (D, T), BF16)
    bap = ctx.buf("l1beta", (4, T), F32)
    wap = ctx.inp("l1_bw", (D, 4), BF16)
    S_ = ctx.sems
    blk = ctx.block
    TS = 512
    sx = [ctx.sbuf(f"b1_x{i}", [128, DC * TS], BF16) for i in range(2)]
    sw = ctx.sbuf("b1_w", [128, DC * 4], BF16)
    so = [ctx.sbuf(f"b1_o{i}", [4, TS], F32) for i in range(2)]
    ps = ctx.psum("b1_p", [4, TS])

    @blk.sync
    def _(e):
        ctx.dma(e, sw.rearrange("p (c n) -> p c n", c=DC),
                wap.rearrange("(c p) n -> p c n", p=128), "qv")
    w_rdy = ctx.cnt["qv"]

    for j in range(T // TS):
        @blk.sync
        def _(e, j=j):
            if j >= 2:
                e.wait_ge(S_["gp"], ctx.cnt["gp"] - 1)
            ctx.fence(e, "qs")
            ctx.dma(e, sb3(sx[j % 2], DC, TS), fm_view(xn, j * TS, TS), "qs")
        qs_j = ctx.cnt["qs"]

        @blk.tensor
        def _(e, j=j, qs_j=qs_j, w_rdy=w_rdy):
            e.wait_ge(S_["qs"], qs_j)
            if j == 0:
                e.wait_ge(S_["qv"], w_rdy)
            else:
                e.wait_ge(S_["sc"], ctx.cnt["sc"])
            for c in range(DC):
                ins = e.matmul(ps[:], sw[:, c * 4:(c + 1) * 4],
                               sx[j % 2][:, c * TS:(c + 1) * TS],
                               start=(c == 0), stop=(c == DC - 1))
            ctx.tag(ins, "mm")
        t_j = ctx.cnt["mm"]

        @blk.scalar
        def _(e, j=j, t_j=t_j):
            e.wait_ge(S_["mm"], t_j)
            ins = e.activation(out=so[j % 2][:], in_=ps[:], func=AF.Sigmoid)
            ctx.tag(ins, "sc")
        s_j = ctx.cnt["sc"]

        @blk.gpsimd
        def _(e, j=j, s_j=s_j):
            e.wait_ge(S_["sc"], s_j)
            ctx.dma(e, bap[:, j * TS:(j + 1) * TS], so[j % 2][:], "qg")
            e.wait_ge(S_["qg"], ctx.cnt["qg"])
            ctx.tag(e.engine_nop(), "gp")


def ph_l1_norm(ctx):
    q = ctx.buf("l1q", (D, T), BF16)
    k = ctx.buf("l1k", (D, T), BF16)
    v = ctx.buf("l1v", (D, T), BF16)
    bt = ctx.buf("l1beta", (4, T), F32)
    qn = ctx.buf("l1qn", (D, T), BF16)
    kn = ctx.buf("l1kn", (D, T), BF16)
    kb = ctx.buf("l1kb", (D, T), BF16)
    vb = ctx.buf("l1vb", (D, T), BF16)
    S_ = ctx.sems
    blk = ctx.block
    TS = 512
    sq = ctx.sbuf("n1_q", [128, 2 * TS], BF16)
    sk = ctx.sbuf("n1_k", [128, 2 * TS], BF16)
    sv = ctx.sbuf("n1_v", [128, 2 * TS], BF16)
    ssqq = ctx.sbuf("n1_sqq", [128, 2 * TS], F32)
    ssqk = ctx.sbuf("n1_sqk", [128, 2 * TS], F32)
    sone = ctx.sbuf("n1_1", [128, 1], F32)
    sonr = ctx.sbuf("n1_1r", [1, 128], F32)
    srow = ctx.sbuf("n1_r", [1, 2 * TS], F32)
    sbr = ctx.sbuf("n1_b", [1, TS], F32)
    sout = ctx.sbuf("n1_o", [128, 8 * TS], BF16)
    ps_sq_ = ctx.psum("n1_psq", [1, TS])
    ps_sk_ = ctx.psum("n1_psk", [1, TS])
    ps_rq = ctx.psum("n1_prq", [128, TS])
    ps_rk = ctx.psum("n1_prk", [128, TS])
    ps_bb = ctx.psum("n1_pbb", [128, TS])
    QS, KS = 0, TS

    @blk.vector
    def _(e):
        e.memset(sone[:], 1.0)
        ins = e.memset(sonr[:], 1.0)
        ctx.tag(ins, "x1")
    one_rdy = ctx.cnt["x1"]

    for h in range(4):
        for j in range(T // TS):
            @blk.sync
            def _(e, h=h, j=j):
                e.wait_ge(S_["gp"], ctx.cnt["gp"])
                ctx.fence(e, "qs")
                ctx.dma(e, sb3(sq, 2, TS), fm_view(q, j * TS, TS, c0=2 * h, nch=2), "qs")
                ctx.dma(e, sb3(sk, 2, TS), fm_view(k, j * TS, TS, c0=2 * h, nch=2), "qs")
                ctx.dma(e, sb3(sv, 2, TS), fm_view(v, j * TS, TS, c0=2 * h, nch=2), "qs")
                ctx.dma(e, sbr[:], bt[h:h + 1, j * TS:(j + 1) * TS], "qs")
            qs_i = ctx.cnt["qs"]

            @blk.scalar
            def _(e, qs_i=qs_i):
                e.wait_ge(S_["qs"], qs_i)
                e.wait_ge(S_["mm"], ctx.cnt["mm"])
                for cc in range(2):
                    e.activation(out=ssqq[:, cc * TS:(cc + 1) * TS],
                                 in_=sq[:, cc * TS:(cc + 1) * TS], func=AF.Square)
                for cc in range(2):
                    ins = e.activation(out=ssqk[:, cc * TS:(cc + 1) * TS],
                                       in_=sk[:, cc * TS:(cc + 1) * TS], func=AF.Square)
                ctx.tag(ins, "sc")
            s_sq = ctx.cnt["sc"]

            @blk.tensor
            def _(e, s_sq=s_sq, qs_i=qs_i):
                e.wait_ge(S_["sc"], s_sq)
                e.wait_ge(S_["x1"], one_rdy)
                e.wait_ge(S_["ve"], ctx.cnt["ve"])
                for cc in range(2):
                    e.matmul(ps_sq_[:], sone[:], ssqq[:, cc * TS:(cc + 1) * TS],
                             start=(cc == 0), stop=(cc == 1))
                for cc in range(2):
                    ins = e.matmul(ps_sk_[:], sone[:], ssqk[:, cc * TS:(cc + 1) * TS],
                                   start=(cc == 0), stop=(cc == 1))
                ctx.tag(ins, "mm")
            t_s = ctx.cnt["mm"]

            @blk.vector
            def _(e, t_s=t_s):
                e.wait_ge(S_["mm"], t_s)
                ins = e.tensor_scalar(out=srow[0:1, QS:QS + TS], in0=ps_sq_[:],
                                      scalar1=1e-6, scalar2=None, op0=ALU.add)
                c1 = ctx.tag(ins, "ve"); e.wait_ge(S_["ve"], c1)
                ins = e.tensor_scalar(out=srow[0:1, KS:KS + TS], in0=ps_sk_[:],
                                      scalar1=1e-6, scalar2=None, op0=ALU.add)
                ctx.tag(ins, "ve")
            v_r1 = ctx.cnt["ve"]

            @blk.scalar
            def _(e, v_r1=v_r1):
                e.wait_ge(S_["ve"], v_r1)
                ins = e.activation(out=srow[:], in_=srow[:], func=AF.Sqrt)
                ctx.tag(ins, "sc")
            s_r = ctx.cnt["sc"]

            @blk.vector
            def _(e, s_r=s_r):
                e.wait_ge(S_["sc"], s_r)
                ins = e.reciprocal(out=srow[:], in_=srow[:])
                ctx.tag(ins, "ve")
            v_r2 = ctx.cnt["ve"]

            @blk.tensor
            def _(e, v_r2=v_r2, qs_i=qs_i):
                e.wait_ge(S_["ve"], v_r2)
                e.matmul(ps_rq[:], sonr[:], srow[0:1, QS:QS + TS], start=True, stop=True)
                e.matmul(ps_rk[:], sonr[:], srow[0:1, KS:KS + TS], start=True, stop=True)
                ins = e.matmul(ps_bb[:], sonr[:], sbr[0:1, :], start=True, stop=True)
                ctx.tag(ins, "mm")
            t_b = ctx.cnt["mm"]

            @blk.vector
            def _(e, h=h, j=j, t_b=t_b):
                e.wait_ge(S_["mm"], t_b)
                for cc in range(2):
                    ins = e.tensor_tensor(out=sout[:, (0 * 2 + cc) * TS:(0 * 2 + cc + 1) * TS],
                                          in0=sq[:, cc * TS:(cc + 1) * TS],
                                          in1=ps_rq[:], op=ALU.mult)
                    c = ctx.tag(ins, "ve"); e.wait_ge(S_["ve"], c)
                    ins = e.tensor_tensor(out=sout[:, (1 * 2 + cc) * TS:(1 * 2 + cc + 1) * TS],
                                          in0=sk[:, cc * TS:(cc + 1) * TS],
                                          in1=ps_rk[:], op=ALU.mult)
                    c = ctx.tag(ins, "ve"); e.wait_ge(S_["ve"], c)
                    ins = e.tensor_tensor(out=sout[:, (2 * 2 + cc) * TS:(2 * 2 + cc + 1) * TS],
                                          in0=sout[:, (1 * 2 + cc) * TS:(1 * 2 + cc + 1) * TS],
                                          in1=ps_bb[:], op=ALU.mult)
                    c = ctx.tag(ins, "ve"); e.wait_ge(S_["ve"], c)
                    ins = e.tensor_tensor(out=sout[:, (3 * 2 + cc) * TS:(3 * 2 + cc + 1) * TS],
                                          in0=sv[:, cc * TS:(cc + 1) * TS],
                                          in1=ps_bb[:], op=ALU.mult)
                    c = ctx.tag(ins, "ve"); e.wait_ge(S_["ve"], c)
                ctx.tag(e.engine_nop(), "ve")
            v_o = ctx.cnt["ve"]

            @blk.gpsimd
            def _(e, h=h, j=j, v_o=v_o):
                e.wait_ge(S_["ve"], v_o)
                for w, dst in ((0, qn), (1, kn), (2, kb), (3, vb)):
                    ctx.dma(e, fm_view(dst, j * TS, TS, c0=2 * h, nch=2),
                            sb3(sout[:, w * 2 * TS:(w * 2 + 2) * TS], 2, TS), "qg")
                e.wait_ge(S_["qg"], ctx.cnt["qg"])
                ctx.tag(e.engine_nop(), "gp")


def ph_l1_delta(ctx):
    qn = ctx.buf("l1qn", (D, T), BF16)
    kn = ctx.buf("l1kn", (D, T), BF16)
    kb = ctx.buf("l1kb", (D, T), BF16)
    knt = ctx.buf("l1knt", (T, D), BF16)
    vbt = ctx.buf("l1vbt", (T, D), BF16)
    ot = ctx.buf("l1dt", (T, D), BF16)
    apow = ctx.buf("l1ap", (128, 32, 320), BF16)
    nsT = ctx.inp("d_nstrictT", (32, 32), F32)
    nsL = ctx.inp("d_nstrictL", (32, 32), F32)
    incl = ctx.inp("d_inclT", (32, 32), F32)
    eye = ctx.inp("eye32", (32, 32), BF16)
    S_ = ctx.sems
    blk = ctx.block
    NCH = 128
    C = 32

    sqn = ctx.sbuf("d1_q", [128, 4 * T], BF16)
    skn = ctx.sbuf("d1_k", [128, 4 * T], BF16)
    skb = ctx.sbuf("d1_kb", [128, 4 * T], BF16)
    G = 8
    smAa = ctx.sbuf("d1_mAa", [32, G * 2 * 32], BF16)
    smAp = ctx.sbuf("d1_mAp", [32, G * 2 * 32], BF16)
    smsk = ctx.sbuf("d1_msk", [32, 3 * 32], F32)
    smskT = ctx.sbuf("d1_mskT", [32, G * 2 * 32], F32)
    smskL = ctx.sbuf("d1_mskL", [32, G * 2 * 32], F32)
    seye = ctx.sbuf("d1_eye", [32, 32], BF16)
    spwg = [ctx.sbuf(f"d1_pw{i}", [32, G * 320], BF16) for i in range(2)]
    spw = ctx.sbuf("d1_pw", [32, 320], BF16)
    sS = ctx.sbuf("d1_S", [128, 2 * 512], F32)
    sSb = ctx.sbuf("d1_Sb", [128, 2 * 512], BF16)
    skt = [ctx.sbuf(f"d1_kt{i}", [32, 4 * 2 * 256], BF16) for i in range(2)]
    svt = [ctx.sbuf(f"d1_vt{i}", [32, 4 * 2 * 256], BF16) for i in range(2)]
    sr = ctx.sbuf("d1_r", [32, 256], BF16)
    sU = ctx.sbuf("d1_U", [32, 256], BF16)
    sat = ctx.sbuf("d1_at", [32, 32], BF16)
    so = ctx.sbuf("d1_o", [32, 2 * 256], F32)
    psA = ctx.psum("d1_pA", [32, G * 2 * 32])
    psP = ctx.psum("d1_pP", [32, G * 2 * 32])
    psR = ctx.psum("d1_pR", [32, 256])
    psU = ctx.psum("d1_pU", [32, 256])
    psO = ctx.psum("d1_pO", [32, 256])
    psAt = ctx.psum("d1_pAt", [32, 32])
    psS = ctx.psum("d1_pS", [128, 512])

    @blk.sync
    def _(e):
        ctx.dma(e, smsk[:, 0:32], nsT[:], "qv")
        ctx.dma(e, smsk[:, 32:64], nsL[:], "qv")
        ctx.dma(e, smsk[:, 64:96], incl[:], "qv")
        ctx.dma(e, seye[:], eye[:], "qv")
    msk_rdy = ctx.cnt["qv"]

    @blk.vector
    def _(e):
        e.wait_ge(S_["qv"], msk_rdy)
        for b in range(2 * G):
            e.tensor_copy(out=smskT[:, b * 32:(b + 1) * 32], in_=smsk[:, 0:32])
            ins = e.tensor_copy(out=smskL[:, b * 32:(b + 1) * 32],
                                in_=smsk[:, 32:64])
        ctx.tag(ins, "ve")
    mskrep_rdy = ctx.cnt["ve"]

    for hg in range(2):
        h0 = 2 * hg

        @blk.sync
        def _(e, hg=hg, h0=h0):
            if hg > 0:
                e.wait_ge(S_["mm"], ctx.cnt["mm"])
            ctx.dma(e, sb3(sqn, 4, T), fm_view(qn, 0, T, c0=2 * h0, nch=4), "qv")
            ctx.dma(e, sb3(skn, 4, T), fm_view(kn, 0, T, c0=2 * h0, nch=4), "qv")
            ctx.dma(e, sb3(skb, 4, T), fm_view(kb, 0, T, c0=2 * h0, nch=4), "qv")
        ld_rdy = ctx.cnt["qv"]

        def pw_out(sl, hl, lvl):
            return spwg[sl].rearrange("p (g n) -> p g n", g=G)[
                :, :, (hl * 5 + lvl) * 32:(hl * 5 + lvl + 1) * 32]

        def ab_blk(sb, hl):
            return sb.rearrange("p (g h n) -> p g h n", g=G, h=2)[:, :, hl, :]

        g_pw = {}
        for cg in range(NCH // G):
            c0 = cg * G
            sl = cg % 2

            @blk.tensor
            def _(e, cg=cg, c0=c0, ld_rdy=ld_rdy):
                if cg == 0:
                    e.wait_ge(S_["qv"], max(ld_rdy, msk_rdy))
                e.wait_ge(S_["ve"], ctx.cnt["ve"])
                n = 0
                for g in range(G):
                    for hl in range(2):
                        for cc in range(2):
                            off = (2 * hl + cc) * T + (c0 + g) * C
                            ins = e.matmul(psA[:, (g * 2 + hl) * 32:(g * 2 + hl + 1) * 32],
                                           skn[:, off:off + C], skb[:, off:off + C],
                                           start=(n == 0), stop=(n == 4 * G - 1))
                            n += 1
                ctx.tag(ins, "mm")
                n = 0
                for g in range(G):
                    for hl in range(2):
                        for cc in range(2):
                            off = (2 * hl + cc) * T + (c0 + g) * C
                            ins = e.matmul(psP[:, (g * 2 + hl) * 32:(g * 2 + hl + 1) * 32],
                                           skb[:, off:off + C], skn[:, off:off + C],
                                           start=(n == 0), stop=(n == 4 * G - 1))
                            n += 1
                ctx.tag(ins, "mm")
            t_A = ctx.cnt["mm"]

            @blk.vector
            def _(e, cg=cg, sl=sl, t_A=t_A, g_pw=dict(g_pw)):
                e.wait_ge(S_["mm"], t_A)
                if cg >= 2:
                    e.wait_ge(S_["gp"], g_pw[cg - 2])
                ins = e.tensor_tensor(out=smAa[:], in0=psA[:], in1=smskT[:],
                                      op=ALU.mult)
                c1 = ctx.tag(ins, "ve"); e.wait_ge(S_["ve"], c1)
                ins = e.tensor_tensor(out=smAp[:], in0=psP[:], in1=smskL[:],
                                      op=ALU.mult)
                c1 = ctx.tag(ins, "ve"); e.wait_ge(S_["ve"], c1)
                for hl in range(2):
                    ins = e.tensor_copy(out=pw_out(sl, hl, 0), in_=ab_blk(smAa, hl))
                    c1 = ctx.tag(ins, "ve"); e.wait_ge(S_["ve"], c1)
                ctx.tag(e.engine_nop(), "ve")
            prev_v = ctx.cnt["ve"]

            for lvl in range(1, 5):
                @blk.tensor
                def _(e, lvl=lvl, prev_v=prev_v):
                    e.wait_ge(S_["ve"], prev_v)
                    n = 0
                    for g in range(G):
                        for hl in range(2):
                            blkc = slice((g * 2 + hl) * 32, (g * 2 + hl + 1) * 32)
                            ins = e.matmul(psA[:, blkc], smAp[:, blkc], smAa[:, blkc],
                                           start=(n == 0), stop=(n == 2 * G - 1))
                            n += 1
                    ctx.tag(ins, "mm")
                    n = 0
                    for g in range(G):
                        for hl in range(2):
                            blkc = slice((g * 2 + hl) * 32, (g * 2 + hl + 1) * 32)
                            ins = e.matmul(psP[:, blkc], smAa[:, blkc], smAp[:, blkc],
                                           start=(n == 0), stop=(n == 2 * G - 1))
                            n += 1
                    ctx.tag(ins, "mm")
                t_sq = ctx.cnt["mm"]

                @blk.vector
                def _(e, lvl=lvl, sl=sl, t_sq=t_sq):
                    e.wait_ge(S_["mm"], t_sq)
                    ins = e.tensor_copy(out=smAa[:], in_=psA[:])
                    c1 = ctx.tag(ins, "ve"); e.wait_ge(S_["ve"], c1)
                    ins = e.tensor_copy(out=smAp[:], in_=psP[:])
                    c1 = ctx.tag(ins, "ve"); e.wait_ge(S_["ve"], c1)
                    for hl in range(2):
                        ins = e.tensor_copy(out=pw_out(sl, hl, lvl),
                                            in_=ab_blk(smAa, hl))
                        c1 = ctx.tag(ins, "ve"); e.wait_ge(S_["ve"], c1)
                    ctx.tag(e.engine_nop(), "ve")
                prev_v = ctx.cnt["ve"]

            @blk.gpsimd
            def _(e, c0=c0, sl=sl, prev_v=prev_v):
                e.wait_ge(S_["ve"], prev_v)
                ctx.dma(e, apow[c0:c0 + G].rearrange("g p n -> p g n"),
                        spwg[sl].rearrange("p (g n) -> p g n", g=G), "qg")
                e.wait_ge(S_["qg"], ctx.cnt["qg"])
                ctx.tag(e.engine_nop(), "gp")
            g_pw[cg] = ctx.cnt["gp"]
        p1_done = ctx.cnt["gp"]

        @blk.vector
        def _(e):
            e.memset(sS[:], 0.0)
            ins = e.memset(sSb[:], 0.0)
            ctx.tag(ins, "ve")
        prev_state = ctx.cnt["ve"]

        g_last = {0: 0, 1: 0}
        for c in range(NCH):
            t0 = c * C
            qslot = (c // 4) % 2

            if c % 4 == 0:
                @blk.sync
                def _(e, c=c, hg=hg, h0=h0, qslot=qslot, p1_done=p1_done,
                      g_last=dict(g_last)):
                    if c == 0:
                        e.wait_ge(S_["gp"], p1_done)
                    if (c // 4) >= 2:
                        e.wait_ge(S_["gp"], g_last[qslot])
                    ctx.fence(e, "qs")
                    ctx.dma(e, skt[qslot].rearrange("p (cc n) -> p cc n", cc=4),
                            knt[:, h0 * 256:(h0 + 2) * 256]
                            .rearrange("(cc p) n -> p cc n", p=32)[:, c:c + 4, :], "qs")
                    ctx.dma(e, svt[qslot].rearrange("p (cc n) -> p cc n", cc=4),
                            vbt[:, h0 * 256:(h0 + 2) * 256]
                            .rearrange("(cc p) n -> p cc n", p=32)[:, c:c + 4, :], "qs")
            qs_c = ctx.cnt["qs"]

            @blk.sync
            def _(e, c=c, p1_done=p1_done):
                if c == 0:
                    e.wait_ge(S_["gp"], p1_done)
                e.wait_ge(S_["mm"], ctx.cnt["mm"])
                ctx.fence(e, "qs")
                ctx.dma(e, spw[:], apow[c], "qs")
            pw_c = ctx.cnt["qs"]

            ccol = (c % 4) * 512

            for hl in range(2):
                @blk.tensor
                def _(e, c=c, hl=hl, t0=t0, pw_c=pw_c, qs_c=qs_c,
                      prev_state=prev_state):
                    if hl == 0:
                        e.wait_ge(S_["qs"], max(pw_c, qs_c))
                        e.wait_ge(S_["ve"], prev_state)
                    for cc in range(2):
                        off = (2 * hl + cc) * T + t0
                        ins = e.matmul(psR[:], skb[:, off:off + C],
                                       sSb[:, (hl * 2 + cc) * 256:(hl * 2 + cc + 1) * 256],
                                       start=(cc == 0), stop=(cc == 1))
                    ctx.tag(ins, "mm")
                t_r = ctx.cnt["mm"]

                @blk.vector
                def _(e, hl=hl, t_r=t_r, qslot=qslot, ccol=ccol):
                    e.wait_ge(S_["mm"], t_r)
                    ins = e.tensor_tensor(out=sr[:],
                                          in0=svt[qslot][:, ccol + hl * 256:ccol + (hl + 1) * 256],
                                          in1=psR[:], op=ALU.subtract)
                    ctx.tag(ins, "ve")
                v_r = ctx.cnt["ve"]

                @blk.tensor
                def _(e, hl=hl, v_r=v_r):
                    e.wait_ge(S_["ve"], v_r)
                    ins = e.matmul(psU[:], seye[:], sr[:], start=True, stop=False)
                    ctx.tag(ins, "mm")
                prev_t = ctx.cnt["mm"]

                for lvl in range(5):
                    @blk.vector
                    def _(e, lvl=lvl, prev_t=prev_t):
                        e.wait_ge(S_["mm"], prev_t)
                        ins = e.tensor_copy(out=sU[:], in_=psU[:])
                        ctx.tag(ins, "ve")
                    v_u = ctx.cnt["ve"]

                    @blk.tensor
                    def _(e, hl=hl, lvl=lvl, v_u=v_u):
                        e.wait_ge(S_["ve"], v_u)
                        ins = e.matmul(psU[:], spw[:, (hl * 5 + lvl) * 32:(hl * 5 + lvl + 1) * 32],
                                       sU[:], start=False, stop=(lvl == 4))
                        ctx.tag(ins, "mm")
                    prev_t = ctx.cnt["mm"]

                @blk.vector
                def _(e, prev_t=prev_t):
                    e.wait_ge(S_["mm"], prev_t)
                    ins = e.tensor_copy(out=sU[:], in_=psU[:])
                    ctx.tag(ins, "ve")
                v_uh = ctx.cnt["ve"]

                @blk.tensor
                def _(e, hl=hl, t0=t0, v_uh=v_uh):
                    e.wait_ge(S_["ve"], v_uh)
                    for cc in range(2):
                        off = (2 * hl + cc) * T + t0
                        ins = e.matmul(psAt[:], skn[:, off:off + C], sqn[:, off:off + C],
                                       start=(cc == 0), stop=(cc == 1))
                    ctx.tag(ins, "mm")
                t_at = ctx.cnt["mm"]

                @blk.vector
                def _(e, t_at=t_at):
                    e.wait_ge(S_["mm"], t_at)
                    ins = e.tensor_tensor(out=sat[:], in0=psAt[:], in1=smsk[:, 64:96],
                                          op=ALU.mult)
                    ctx.tag(ins, "ve")
                v_at = ctx.cnt["ve"]

                @blk.tensor
                def _(e, hl=hl, t0=t0, v_at=v_at, qslot=qslot, ccol=ccol):
                    e.wait_ge(S_["ve"], v_at)
                    for cc in range(2):
                        off = (2 * hl + cc) * T + t0
                        e.matmul(psO[:], sqn[:, off:off + C],
                                 sSb[:, (hl * 2 + cc) * 256:(hl * 2 + cc + 1) * 256],
                                 start=(cc == 0), stop=False)
                    ins = e.matmul(psO[:], sat[:], sU[:], start=False, stop=True)
                    ctx.tag(ins, "mm")
                    for cc in range(2):
                        ins = e.matmul(psS[:, cc * 256:(cc + 1) * 256],
                                       skt[qslot][:, ccol + hl * 256 + cc * 128:ccol + hl * 256 + cc * 128 + 128],
                                       sU[:], start=(cc == 0), stop=(cc == 1))
                    ctx.tag(ins, "mm")
                t_su = ctx.cnt["mm"]
                t_o = t_su - 1

                @blk.vector
                def _(e, hl=hl, t_o=t_o, t_su=t_su):
                    e.wait_ge(S_["mm"], t_o)
                    ins = e.tensor_copy(out=so[:, hl * 256:(hl + 1) * 256], in_=psO[:])
                    c1 = ctx.tag(ins, "ve"); e.wait_ge(S_["ve"], c1)
                    e.wait_ge(S_["mm"], t_su)
                    for cc in range(2):
                        ins = e.tensor_tensor(out=sS[:, (hl * 2 + cc) * 256:(hl * 2 + cc + 1) * 256],
                                              in0=sS[:, (hl * 2 + cc) * 256:(hl * 2 + cc + 1) * 256],
                                              in1=psS[:, cc * 256:(cc + 1) * 256], op=ALU.add)
                        c1 = ctx.tag(ins, "ve"); e.wait_ge(S_["ve"], c1)
                    ins = e.tensor_copy(out=sSb[:, hl * 512:(hl + 1) * 512],
                                        in_=sS[:, hl * 512:(hl + 1) * 512])
                    ctx.tag(ins, "ve")
            prev_state = ctx.cnt["ve"]

            @blk.gpsimd
            def _(e, c=c, h0=h0, t0=t0, prev_state=prev_state):
                e.wait_ge(S_["ve"], prev_state)
                ctx.dma(e, ot[t0:t0 + C, h0 * 256:(h0 + 2) * 256]
                        .rearrange("p (hh n) -> p hh n", hh=2),
                        so.rearrange("p (hh n) -> p hh n", hh=2), "qg")
                e.wait_ge(S_["qg"], ctx.cnt["qg"])
                ctx.tag(e.engine_nop(), "gp")
            g_last[qslot] = ctx.cnt["gp"]


def ph_l1_stats(ctx):
    branches = [("l1f1", False), ("l1f3", False), ("l1f7", False), ("l1f31", False),
                ("l1dt", True), ("l1v", False)]
    stap = ctx.buf("l1st", (128, T), BF16)
    hsap = ctx.inp("headselb", (8 * 128, 4), BF16)
    S_ = ctx.sems
    blk = ctx.block
    TS = 512
    sx = ctx.sbuf("s1_x", [128, DC * TS], BF16)
    sq2 = ctx.sbuf("s1_q", [128, TS], BF16)
    sab = ctx.sbuf("s1_a", [128, TS], BF16)
    shs = ctx.sbuf("s1_hs", [128, DC * 4], BF16)
    srw = ctx.sbuf("s1_r", [4, 4 * TS], F32)
    sst = ctx.sbuf("s1_st", [4, 4 * TS], BF16)
    ps_x = ctx.psum("s1_px", [4, TS])
    ps_q = ctx.psum("s1_pq", [4, TS])
    ps_a = ctx.psum("s1_pa", [4, TS])

    @blk.sync
    def _(e):
        ctx.dma(e, shs.rearrange("p (c n) -> p c n", c=DC),
                hsap.rearrange("(c p) n -> p c n", p=128), "qv")
    hs_rdy = ctx.cnt["qv"]

    @blk.vector
    def _(e):
        ins = e.memset(sq2[:], 0.0)
        ctx.tag(ins, "x1")
    z_rdy = ctx.cnt["x1"]

    @blk.gpsimd
    def _(e, z_rdy=z_rdy):
        e.wait_ge(S_["x1"], z_rdy)
        for j in range(T // TS):
            ctx.dma(e, stap[:, j * TS:(j + 1) * TS], sq2[:], "qg")
        e.wait_ge(S_["qg"], ctx.cnt["qg"])
        ctx.tag(e.engine_nop(), "gp")

    for bi, (bname, is_tok) in enumerate(branches):
        bap = ctx.buf(bname, (T, D) if is_tok else (D, T), BF16)
        for j in range(T // TS):
            @blk.sync
            def _(e, bi=bi, j=j, bap=bap, is_tok=is_tok):
                e.wait_ge(S_["mm"], ctx.cnt["mm"])
                e.wait_ge(S_["sc"], ctx.cnt["sc"])
                ctx.fence(e, "qs")
                if is_tok:
                    for cc in range(DC):
                        ctx.dma(e, sx[:, cc * TS:(cc + 1) * TS],
                                bap[j * TS:(j + 1) * TS, cc * 128:(cc + 1) * 128],
                                "qs", transpose=True)
                else:
                    ctx.dma(e, sb3(sx, DC, TS), fm_view(bap, j * TS, TS), "qs")
            qs_j = ctx.cnt["qs"]

            for cc in range(DC):
                @blk.scalar
                def _(e, cc=cc, qs_j=qs_j):
                    if cc == 0:
                        e.wait_ge(S_["qs"], qs_j)
                        e.wait_ge(S_["ve"], ctx.cnt["ve"])
                        e.wait_ge(S_["gp"], ctx.cnt["gp"])
                    if cc > 0:
                        e.wait_ge(S_["mm"], ctx.cnt["mm"] - 1)
                    e.activation(out=sq2[:], in_=sx[:, cc * TS:(cc + 1) * TS],
                                 func=AF.Square)
                    ins = e.activation(out=sab[:], in_=sx[:, cc * TS:(cc + 1) * TS],
                                       func=AF.Abs)
                    ctx.tag(ins, "sc")
                s_c = ctx.cnt["sc"]

                @blk.tensor
                def _(e, cc=cc, s_c=s_c):
                    e.wait_ge(S_["sc"], s_c)
                    if cc == 0:
                        e.wait_ge(S_["qv"], hs_rdy)
                        e.wait_ge(S_["ve"], ctx.cnt["ve"])
                    e.matmul(ps_x[:], shs[:, cc * 4:(cc + 1) * 4],
                             sx[:, cc * TS:(cc + 1) * TS],
                             start=(cc == 0), stop=(cc == DC - 1))
                    e.matmul(ps_q[:], shs[:, cc * 4:(cc + 1) * 4], sq2[:],
                             start=(cc == 0), stop=(cc == DC - 1))
                    ins = e.matmul(ps_a[:], shs[:, cc * 4:(cc + 1) * 4], sab[:],
                                   start=(cc == 0), stop=(cc == DC - 1))
                    ctx.tag(ins, "mm")
            t_sums = ctx.cnt["mm"]

            @blk.vector
            def _(e, t_sums=t_sums):
                e.wait_ge(S_["mm"], t_sums)
                ins = e.tensor_scalar(out=srw[:, 0:TS], in0=ps_x[:], scalar1=1.0 / 256,
                                      scalar2=None, op0=ALU.mult)
                c1 = ctx.tag(ins, "ve"); e.wait_ge(S_["ve"], c1)
                ins = e.tensor_scalar(out=srw[:, TS:2 * TS], in0=ps_a[:], scalar1=1.0 / 256,
                                      scalar2=None, op0=ALU.mult)
                c1 = ctx.tag(ins, "ve"); e.wait_ge(S_["ve"], c1)
                ins = e.tensor_copy(out=srw[:, 2 * TS:3 * TS], in_=ps_q[:])
                c1 = ctx.tag(ins, "ve"); e.wait_ge(S_["ve"], c1)
                ins = e.tensor_tensor(out=srw[:, 3 * TS:4 * TS], in0=ps_x[:],
                                      in1=srw[:, 0:TS], op=ALU.mult)
                c1 = ctx.tag(ins, "ve"); e.wait_ge(S_["ve"], c1)
                ins = e.tensor_tensor(out=srw[:, 3 * TS:4 * TS], in0=srw[:, 2 * TS:3 * TS],
                                      in1=srw[:, 3 * TS:4 * TS], op=ALU.subtract)
                c1 = ctx.tag(ins, "ve"); e.wait_ge(S_["ve"], c1)
                ins = e.tensor_scalar(out=srw[:, 3 * TS:4 * TS], in0=srw[:, 3 * TS:4 * TS],
                                      scalar1=1.0 / 255, scalar2=None, op0=ALU.mult)
                ctx.tag(ins, "ve")
            v_pre = ctx.cnt["ve"]

            @blk.scalar
            def _(e, v_pre=v_pre):
                e.wait_ge(S_["ve"], v_pre)
                e.activation(out=sst[:, 3 * TS:4 * TS].rearrange("p n -> p n")
                             if False else srw[:, 2 * TS:3 * TS],
                             in_=srw[:, 2 * TS:3 * TS], func=AF.Sqrt)
                ins = e.activation(out=srw[:, 3 * TS:4 * TS], in_=srw[:, 3 * TS:4 * TS],
                                   func=AF.Sqrt)
                ctx.tag(ins, "sc")
            s_sqr = ctx.cnt["sc"]

            @blk.vector
            def _(e, s_sqr=s_sqr):
                e.wait_ge(S_["sc"], s_sqr)
                e.tensor_copy(out=sst[:, 0:TS], in_=srw[:, 0:TS])
                e.tensor_copy(out=sst[:, TS:2 * TS], in_=srw[:, 3 * TS:4 * TS])
                e.tensor_copy(out=sst[:, 2 * TS:3 * TS], in_=srw[:, TS:2 * TS])
                ins = e.tensor_copy(out=sst[:, 3 * TS:4 * TS], in_=srw[:, 2 * TS:3 * TS])
                ctx.tag(ins, "ve")
            v_pk = ctx.cnt["ve"]

            @blk.gpsimd
            def _(e, bi=bi, j=j, v_pk=v_pk):
                e.wait_ge(S_["ve"], v_pk)
                for s in range(4):
                    ctx.dma(e, stap.rearrange("(hh r) t -> r hh t", hh=4)
                            [bi * 4 + s, 0:4, j * TS:(j + 1) * TS]
                            if False else
                            stap[0:128, j * TS:(j + 1) * TS]
                            .rearrange("(hh r) t -> r hh t", r=32)[bi * 4 + s, 0:4, :],
                            sst[:, s * TS:(s + 1) * TS], "qg")
                e.wait_ge(S_["qg"], ctx.cnt["qg"])
                ctx.tag(e.engine_nop(), "gp")


def ph_l1_logits(ctx):
    hg = ctx.buf("l1hg", (D, T), BF16)
    w2 = ctx.inp("l1_mw2", (D, 24), BF16)
    b2 = ctx.inp("l1_mb2", (1, 24), F32)
    glt = ctx.inp("l1_glt", (4, 1), F32)
    m424 = ctx.inp("map424", (4, 24), F32)
    idap = ctx.inp("identf", (128, 128), F32)
    wfap = ctx.buf("l1wf", (24, T), F32)
    S_ = ctx.sems
    blk = ctx.block
    sw2 = ctx.sbuf("g1_w2", [128, DC * 24], BF16)
    sb2 = ctx.sbuf("g1_b2", [1, 24], F32)
    sb2b = ctx.sbuf("g1_b2b", [1, 24], BF16)
    soneb = ctx.sbuf("g1_1b", [1, 128], BF16)
    sg = ctx.sbuf("g1_g", [4, 8], F32)
    sm = ctx.sbuf("g1_m", [4, 24], F32)
    sone = ctx.sbuf("g1_1", [1, 128], F32)
    srow = ctx.sbuf("g1_r", [1, 24], F32)
    sitr = ctx.sbuf("g1_it", [128, 24], F32)
    sidf = ctx.sbuf("g1_id", [128, 128], F32)
    sx = [ctx.sbuf(f"g1_x{i}", [128, DC * 128], BF16) for i in range(2)]
    sl = ctx.sbuf("g1_l", [128, 24], F32)
    scol = ctx.sbuf("g1_c", [128, 12], F32)
    swf = [ctx.sbuf(f"g1_o{i}", [24, 128], F32) for i in range(2)]
    ps = ctx.psum("g1_p", [128, 24])
    ps2 = ctx.psum("g1_p2", [1, 24])
    ps3 = ctx.psum("g1_p3", [128, 24])
    ps4 = ctx.psum("g1_p4", [24, 128])

    @blk.sync
    def _(e):
        ctx.dma(e, sw2.rearrange("p (c n) -> p c n", c=DC),
                w2.rearrange("(c p) n -> p c n", p=128), "qv")
        ctx.dma(e, sb2[:], b2[:], "qv")
        ctx.dma(e, sg[:, 0:1], glt[:], "qv")
        ctx.dma(e, sm[:], m424[:], "qv")
        ctx.dma(e, sidf[:], idap[:], "qv")
    w_rdy = ctx.cnt["qv"]

    @blk.vector
    def _(e):
        e.wait_ge(S_["qv"], w_rdy)
        ins = e.memset(sone[:], 1.0)
        c = ctx.tag(ins, "ve"); e.wait_ge(S_["ve"], c)
        ins = e.memset(soneb[:], 1.0)
        c = ctx.tag(ins, "ve"); e.wait_ge(S_["ve"], c)
        ins = e.tensor_copy(out=sb2b[:], in_=sb2[:])
        c = ctx.tag(ins, "ve"); e.wait_ge(S_["ve"], c)
        ins = e.memset(sg[:, 4:5], 1.0)
        c = ctx.tag(ins, "ve"); e.wait_ge(S_["ve"], c)
        ins = e.tensor_scalar(out=sg[:, 1:2], in0=sg[:, 0:1], scalar1=0.0,
                              scalar2=None, op0=ALU.max)
        ctx.tag(ins, "ve")
    v_t1 = ctx.cnt["ve"]

    @blk.scalar
    def _(e, v_t1=v_t1):
        e.wait_ge(S_["ve"], v_t1)
        e.activation(out=sg[:, 2:3], in_=sg[:, 0:1], func=AF.Abs)
        ins = e.activation(out=sg[:, 3:4], in_=sg[:, 2:3], func=AF.Exp, scale=-1.0)
        ctx.tag(ins, "sc")
    s_t1 = ctx.cnt["sc"]

    @blk.scalar
    def _(e, s_t1=s_t1):
        e.wait_ge(S_["sc"], s_t1)
        ins = e.activation(out=sg[:, 5:6], in_=sg[:, 3:4], func=AF.Ln,
                           bias=sg[:, 4:5])
        ctx.tag(ins, "sc")
    s_t2 = ctx.cnt["sc"]

    @blk.vector
    def _(e, s_t2=s_t2):
        e.wait_ge(S_["sc"], s_t2)
        ins = e.tensor_tensor(out=sg[:, 6:7], in0=sg[:, 1:2], in1=sg[:, 5:6], op=ALU.add)
        c = ctx.tag(ins, "ve"); e.wait_ge(S_["ve"], c)
        ins = e.tensor_scalar(out=sg[:, 6:7], in0=sg[:, 6:7], scalar1=0.5,
                              scalar2=None, op0=ALU.add)
        c = ctx.tag(ins, "ve"); e.wait_ge(S_["ve"], c)
        ins = e.reciprocal(out=sg[:, 7:8], in_=sg[:, 6:7])
        ctx.tag(ins, "ve")
    v_tau = ctx.cnt["ve"]

    @blk.tensor
    def _(e, v_tau=v_tau):
        e.wait_ge(S_["ve"], v_tau)
        ins = e.matmul(ps2[:], sg[:, 7:8], sm[:], start=True, stop=True)
        ctx.tag(ins, "mm")
    t_row = ctx.cnt["mm"]

    @blk.scalar
    def _(e, t_row=t_row):
        e.wait_ge(S_["mm"], t_row)
        ins = e.activation(out=srow[:], in_=ps2[:], func=AF.Copy)
        ctx.tag(ins, "sc")
    s_row = ctx.cnt["sc"]

    @blk.tensor
    def _(e, s_row=s_row):
        e.wait_ge(S_["sc"], s_row)
        ins = e.matmul(ps3[:], sone[:], srow[:], start=True, stop=True)
        ctx.tag(ins, "mm")
    t_itr = ctx.cnt["mm"]

    @blk.scalar
    def _(e, t_itr=t_itr):
        e.wait_ge(S_["mm"], t_itr)
        ins = e.activation(out=sitr[:], in_=ps3[:], func=AF.Copy)
        ctx.tag(ins, "sc")
    s_itr = ctx.cnt["sc"]

    for i in range(T // 128):
        @blk.sync
        def _(e, i=i):
            if i >= 2:
                e.wait_ge(S_["gp"], ctx.cnt["gp"] - 1)
            ctx.fence(e, "qs")
            ctx.dma(e, sb3(sx[i % 2], DC, 128), fm_view(hg, i * 128, 128), "qs")
        qs_i = ctx.cnt["qs"]

        @blk.tensor
        def _(e, i=i, qs_i=qs_i, s_itr=s_itr):
            e.wait_ge(S_["qs"], qs_i)
            if i == 0:
                e.wait_ge(S_["sc"], s_itr)
            else:
                e.wait_ge(S_["ve"], ctx.cnt["ve"])
            for c in range(DC):
                e.matmul(ps[:], sx[i % 2][:, c * 128:(c + 1) * 128],
                         sw2[:, c * 24:(c + 1) * 24],
                         start=(c == 0), stop=False)
            ins = e.matmul(ps[:], soneb[0:1, 0:128], sb2b[:], start=False, stop=True)
            ctx.tag(ins, "mm")
        t_l = ctx.cnt["mm"]

        @blk.vector
        def _(e, i=i, t_l=t_l):
            e.wait_ge(S_["mm"], t_l)
            ins = e.tensor_tensor(out=sl[:], in0=ps[:], in1=sitr[:], op=ALU.mult)
            ctx.tag(ins, "ve")
        v_l = ctx.cnt["ve"]

        @blk.vector
        def _(e, i=i, v_l=v_l):
            e.wait_ge(S_["ve"], v_l)
            ins = e.tensor_reduce(out=scol[:, 0:4],
                                  in_=sl[:].rearrange("p (h s) -> p h s", h=4),
                                  axis=AX.X, op=ALU.max)
            c = ctx.tag(ins, "ve"); e.wait_ge(S_["ve"], c)
            ins = e.tensor_scalar(out=scol[:, 4:8], in0=scol[:, 0:4], scalar1=-1.0,
                                  scalar2=None, op0=ALU.mult)
            ctx.tag(ins, "ve")
        v_mx = ctx.cnt["ve"]

        s_e = {}
        for h in range(4):
            @blk.scalar
            def _(e, i=i, h=h, v_mx=v_mx):
                e.wait_ge(S_["ve"], v_mx)
                ins = e.activation(out=sl[:, h * 6:(h + 1) * 6], in_=sl[:, h * 6:(h + 1) * 6],
                                   func=AF.Exp, bias=scol[:, 4 + h:5 + h],
                                   accum_out=scol[:, 8 + h:9 + h])
                ctx.tag(ins, "sc")
            s_e[h] = ctx.cnt["sc"]

        @blk.vector
        def _(e, i=i, s_e=dict(s_e)):
            e.wait_ge(S_["sc"], s_e[3])
            ins = e.reciprocal(out=scol[:, 8:12], in_=scol[:, 8:12])
            c = ctx.tag(ins, "ve"); e.wait_ge(S_["ve"], c)
            for h in range(4):
                ins = e.tensor_scalar(out=sl[:, h * 6:(h + 1) * 6],
                                      in0=sl[:, h * 6:(h + 1) * 6],
                                      scalar1=scol[:, 8 + h:9 + h], scalar2=None,
                                      op0=ALU.mult)
            ctx.tag(ins, "ve")
        v_w = ctx.cnt["ve"]

        @blk.tensor
        def _(e, i=i, v_w=v_w):
            e.wait_ge(S_["ve"], v_w)
            if i >= 1:
                e.wait_ge(S_["sc"], ctx.cnt["sc"])
            ins = e.transpose(ps4[:, :], sl[:, 0:24], sidf[:])
            ctx.tag(ins, "mm")
        t_tr = ctx.cnt["mm"]

        @blk.scalar
        def _(e, i=i, t_tr=t_tr):
            e.wait_ge(S_["mm"], t_tr)
            ins = e.activation(out=swf[i % 2][:], in_=ps4[0:24, :], func=AF.Copy)
            ctx.tag(ins, "sc")
        s_tr = ctx.cnt["sc"]

        @blk.gpsimd
        def _(e, i=i, s_tr=s_tr):
            e.wait_ge(S_["sc"], s_tr)
            ctx.dma(e, wfap[:, i * 128:(i + 1) * 128], swf[i % 2][:], "qg")
            e.wait_ge(S_["qg"], ctx.cnt["qg"])
            ctx.tag(e.engine_nop(), "gp")


def ph_l1_fuse(ctx):
    wfap = ctx.buf("l1wf", (24, T), F32)
    o24ap = ctx.inp("onehot24", (24, 24 * 128), F32)
    onw = ctx.inp("l1_onorm", (128, DC), F32)
    fu = ctx.buf("l1fu", (D, T), BF16)
    branches = [("l1f1", False), ("l1f3", False), ("l1f7", False), ("l1f31", False),
                ("l1dt", True), ("l1v", False)]
    S_ = ctx.sems
    blk = ctx.block
    TS = 512
    sw = ctx.sbuf("u1_w", [24, TS], F32)
    so24 = ctx.sbuf("u1_o24", [24, 24 * 128], F32)
    sone = ctx.sbuf("u1_1", [1, 128], F32)
    sonc = ctx.sbuf("u1_1c", [128, 1], F32)
    sonw = ctx.sbuf("u1_on", [128, DC], F32)
    sx = ctx.sbuf("u1_x", [128, 2 * TS], BF16)
    swr = ctx.sbuf("u1_wr", [128, TS], F32)
    sac = ctx.sbuf("u1_ac", [128, DC * TS], F32)
    ssq = ctx.sbuf("u1_sq", [128, TS], F32)
    srow = ctx.sbuf("u1_r", [1, TS], F32)
    so = ctx.sbuf("u1_o", [128, DC * TS], BF16)
    ps_w = ctx.psum("u1_pw", [128, TS])
    ps_s = ctx.psum("u1_ps", [1, TS])
    ps_r = ctx.psum("u1_pr", [128, TS])

    @blk.sync
    def _(e):
        ctx.dma(e, sonw[:], onw[:], "qv")
        ctx.dma(e, so24[:], o24ap[:], "qv")
    on_rdy = ctx.cnt["qv"]

    @blk.vector
    def _(e):
        e.memset(sone[:], 1.0)
        ins = e.memset(sonc[:], 1.0)
        ctx.tag(ins, "x1")
    one_rdy = ctx.cnt["x1"]

    for j in range(T // TS):
        @blk.sync
        def _(e, j=j):
            e.wait_ge(S_["mm"], ctx.cnt["mm"])
            ctx.fence(e, "qs")
            ctx.dma(e, sw[:], wfap[:, j * TS:(j + 1) * TS], "qs")
        w_rdy = ctx.cnt["qs"]

        for h in range(4):
            for bi, (bname, is_tok) in enumerate(branches):
                bap = ctx.buf(bname, (T, D) if is_tok else (D, T), BF16)

                @blk.sync
                def _(e, j=j, h=h, bi=bi, bap=bap, is_tok=is_tok):
                    e.wait_ge(S_["ve"], ctx.cnt["ve"])
                    ctx.fence(e, "qs")
                    if is_tok:
                        for cc in range(2):
                            ctx.dma(e, sx[:, cc * TS:(cc + 1) * TS],
                                    bap[j * TS:(j + 1) * TS,
                                        (2 * h + cc) * 128:(2 * h + cc + 1) * 128],
                                    "qs", transpose=True)
                    else:
                        ctx.dma(e, sb3(sx, 2, TS), fm_view(bap, j * TS, TS, c0=2 * h, nch=2), "qs")
                qs_b = ctx.cnt["qs"]

                @blk.tensor
                def _(e, j=j, h=h, bi=bi, w_rdy=w_rdy, qs_b=qs_b):
                    if bi == 0:
                        e.wait_ge(S_["qs"], w_rdy)
                        e.wait_ge(S_["x1"], one_rdy)
                        if h == 0 and j == 0:
                            e.wait_ge(S_["qv"], on_rdy)
                    e.wait_ge(S_["sc"], ctx.cnt["sc"])
                    e.wait_ge(S_["ve"], ctx.cnt["ve"])
                    r = h * 6 + bi
                    ins = e.matmul(ps_w[:], so24[:, r * 128:(r + 1) * 128],
                                   sw[0:24, :], start=True, stop=True)
                    ctx.tag(ins, "mm")
                t_w = ctx.cnt["mm"]

                @blk.scalar
                def _(e, t_w=t_w):
                    e.wait_ge(S_["mm"], t_w)
                    ins = e.activation(out=swr[:], in_=ps_w[:], func=AF.Copy)
                    ctx.tag(ins, "sc")
                s_w = ctx.cnt["sc"]

                @blk.vector
                def _(e, j=j, h=h, bi=bi, s_w=s_w, qs_b=qs_b):
                    e.wait_ge(S_["sc"], s_w)
                    e.wait_ge(S_["qs"], qs_b)
                    for cc in range(2):
                        dst = sac[:, (2 * h + cc) * TS:(2 * h + cc + 1) * TS]
                        if bi == 0:
                            ins = e.tensor_tensor(out=dst, in0=sx[:, cc * TS:(cc + 1) * TS],
                                                  in1=swr[:], op=ALU.mult)
                            c1 = ctx.tag(ins, "ve"); e.wait_ge(S_["ve"], c1)
                        else:
                            ins = e.tensor_tensor(out=ssq[:], in0=sx[:, cc * TS:(cc + 1) * TS],
                                                  in1=swr[:], op=ALU.mult)
                            c1 = ctx.tag(ins, "ve"); e.wait_ge(S_["ve"], c1)
                            ins = e.tensor_tensor(out=dst, in0=dst, in1=ssq[:], op=ALU.add)
                            c1 = ctx.tag(ins, "ve"); e.wait_ge(S_["ve"], c1)
                    ctx.tag(e.engine_nop(), "ve")

        for h in range(4):
            @blk.scalar
            def _(e, j=j, h=h):
                e.wait_ge(S_["ve"], ctx.cnt["ve"])
                e.wait_ge(S_["mm"], ctx.cnt["mm"])
                e.activation(out=ssq[:], in_=sac[:, (2 * h) * TS:(2 * h + 1) * TS],
                             func=AF.Square)
                ins = e.activation(out=swr[:], in_=sac[:, (2 * h + 1) * TS:(2 * h + 2) * TS],
                                   func=AF.Square)
                ctx.tag(ins, "sc")
            s_q = ctx.cnt["sc"]

            @blk.tensor
            def _(e, h=h, s_q=s_q):
                e.wait_ge(S_["sc"], s_q)
                e.wait_ge(S_["ve"], ctx.cnt["ve"])
                e.matmul(ps_s[:], sonc[:], ssq[:], start=True, stop=False)
                ins = e.matmul(ps_s[:], sonc[:], swr[:], start=False, stop=True)
                ctx.tag(ins, "mm")
            t_ss = ctx.cnt["mm"]

            @blk.vector
            def _(e, t_ss=t_ss):
                e.wait_ge(S_["mm"], t_ss)
                ins = e.tensor_scalar(out=srow[:], in0=ps_s[:], scalar1=1.0 / 256,
                                      scalar2=1e-5, op0=ALU.mult, op1=ALU.add)
                ctx.tag(ins, "ve")
            v_ms = ctx.cnt["ve"]

            @blk.scalar
            def _(e, v_ms=v_ms):
                e.wait_ge(S_["ve"], v_ms)
                ins = e.activation(out=srow[:], in_=srow[:], func=AF.Sqrt)
                ctx.tag(ins, "sc")
            s_sd = ctx.cnt["sc"]

            @blk.vector
            def _(e, s_sd=s_sd):
                e.wait_ge(S_["sc"], s_sd)
                ins = e.reciprocal(out=srow[:], in_=srow[:])
                ctx.tag(ins, "ve")
            v_rs = ctx.cnt["ve"]

            @blk.tensor
            def _(e, v_rs=v_rs):
                e.wait_ge(S_["ve"], v_rs)
                ins = e.matmul(ps_r[:], sone[:], srow[:], start=True, stop=True)
                ctx.tag(ins, "mm")
            t_rb = ctx.cnt["mm"]

            @blk.vector
            def _(e, j=j, h=h, t_rb=t_rb, on_rdy=on_rdy):
                e.wait_ge(S_["mm"], t_rb)
                if j == 0 and h == 0:
                    e.wait_ge(S_["qv"], on_rdy)
                for cc in range(2):
                    sl_ = slice((2 * h + cc) * TS, (2 * h + cc + 1) * TS)
                    ins = e.tensor_tensor(out=sac[:, sl_], in0=sac[:, sl_],
                                          in1=ps_r[:], op=ALU.mult)
                    c1 = ctx.tag(ins, "ve"); e.wait_ge(S_["ve"], c1)
                    ins = e.tensor_scalar(out=so[:, sl_], in0=sac[:, sl_],
                                          scalar1=sonw[:, 2 * h + cc:2 * h + cc + 1],
                                          scalar2=None, op0=ALU.mult)
                    c1 = ctx.tag(ins, "ve"); e.wait_ge(S_["ve"], c1)
                ctx.tag(e.engine_nop(), "ve")
        v_done = ctx.cnt["ve"]

        @blk.gpsimd
        def _(e, j=j, v_done=v_done):
            e.wait_ge(S_["ve"], v_done)
            ctx.dma(e, fm_view(fu, j * TS, TS), sb3(so, DC, TS), "qg")
            e.wait_ge(S_["qg"], ctx.cnt["qg"])
            ctx.tag(e.engine_nop(), "gp")


def l1_attn_phases():
    return [
        lambda c: ph_mm(c, "l1qp", "xn1", 1024, "l1_qw", "l1qp", 1024, "fm"),
        lambda c: ph_mm(c, "l1kp", "xn1", 1024, "l1_kw", "l1kp", 1024, "fm"),
        lambda c: ph_mm(c, "l1vp", "xn1", 1024, "l1_vw", "l1vp", 1024, "fm"),
        ph_l1_conv,
        ph_l1_beta,
        ph_l1_norm,
        lambda c: ph_t2tok(c, "l1tk", "l1kn", 1024, "l1knt"),
        lambda c: ph_t2tok(c, "l1tv", "l1vb", 1024, "l1vbt"),
        ph_l1_delta,
        ph_l1_stats,
        lambda c: ph_mm(c, "l1hg", "xn1", 1024, "l1_mw1", "l1hg", 1024, "fm",
                        AF.Gelu, biascol="l1_mb1", src_extra=("l1st", 128)),
        ph_l1_logits,
        ph_l1_fuse,
        lambda c: ph_mm(c, "l1ow", "l1fu", 1024, "l1_ow", "attn1", 1024, "fm",
                        None, dst_dt=F32),
    ]
IFACE = {}
for _L in range(4):
    IFACE[f"xres{_L + 1}"] = ((D, T), np.float32)
    IFACE[f"xn{_L}"] = ((D, T), BF16NP)
    IFACE[f"attn{_L}"] = ((D, T), np.float32)
    IFACE[f"ffn{_L}"] = ((D, T), np.float32)
IFACE["xnf3"] = ((D, T), np.float32)
del IFACE["xres4"]


def l0_phase_list():
    return [
        lambda c: ph_ln(c, 0),
        lambda c: ph_mm(c, "l0q", "xn0", 1024, "l0_wq", "l0q", 1024, "fm",
                        AF.Silu, alt=True),
        lambda c: ph_mm(c, "l0k", "xn0", 1024, "l0_wk", "l0kb", 1024, "fm",
                        AF.Sigmoid, alt=True, dst2="l0k"),
        lambda c: ph_mm(c, "l0v", "xn0", 1024, "l0_wv", "l0v", 1024, "tok"),
        lambda c: ph_mm(c, "l0g1", "xn0", 1024, "l0_wg1", "l0g1", 128, "fm"),
        lambda c: ph_mm(c, "l0gate", "l0g1", 128, "l0_wg2", "l0gate", 1024,
                        "tok", AF.Sigmoid),
        ph_l0_scan,
        ph_l0_out,
        lambda c: ph_mm(c, "l0wo", "l0on", 1024, "l0_wo", "attn0", 1024, "fm",
                        None, dst_dt=F32, src_tok=True),
        lambda c: ph_mm(c, "l0sc", "xnf0", 1024, "l0_wgatef", "l0sc", 8, "tok",
                        None, dst_dt=F32, io_dt=F32),
        ph_l0_moe,
        lambda c: ph_residual(c, 0),
    ]


def layer_groups():
    g0 = (l0_phase_list(), set(), {"xres1"})
    g1 = ([lambda c: ph_ln(c, 1)] + l1_attn_phases() + [ph_l1ffn,
          lambda c: ph_residual(c, 1)], {"xres1"}, {"xres2"})
    g2 = ([lambda c: ph_ln(c, 2)] + l2_attn_phases() + lstm_phases(2) +
          [lambda c: ph_residual(c, 2)], {"xres2"}, {"xres3"})
    g3 = ([lambda c: ph_ln(c, 3)] + l3_attn_phases() + lstm_phases(3) +
          [lambda c: ph_residual(c, 3)], {"xres3"}, set())
    return [g0, g1, g2, g3]


FUSED = True


def _build(phases, fused, reads, writes):
    ctx = Ctx(fused, reads=reads, writes=writes)
    for ph in phases:
        ph(ctx)
        ctx.barrier()
    return ctx.close()


def kernel(**inputs):
    W = prep_weights(inputs)
    Cc = prep_consts()
    x = np.ascontiguousarray(np.asarray(inputs["L0_attn_x"], dtype=np.float32))
    percore_x = [np.ascontiguousarray(x[c].T) for c in range(NCORES)]
    gidx = [lstm_host_indices(c) for c in range(NCORES)]
    store = [dict() for _ in range(NCORES)]

    if FUSED:
        groups = [(sum([g[0] for g in layer_groups()], []), set(), set())]
    else:
        groups = layer_groups()

    for phases, reads, writes in groups:
        nc = _build(phases, FUSED, reads, writes)
        pid = nc.partition_id_tensor.name if nc.partition_id_tensor else None
        need = {}
        for alloc in nc.m.functions[0].allocations:
            if getattr(alloc, "kind", None) == "ExternalInput":
                nm = alloc.memorylocations[0].name
                need[nm] = mybir.dt.np(alloc.dtype)
        in_maps = []
        for c in range(NCORES):
            m = {}
            for nm, dt in need.items():
                if nm == pid:
                    continue
                if nm == "x_in":
                    arr = percore_x[c]
                elif nm == "lstm_gidx":
                    arr = gidx[c]
                elif nm in store[c]:
                    arr = store[c][nm]
                elif nm in W:
                    arr = W[nm]
                elif nm in Cc:
                    arr = Cc[nm]
                else:
                    raise KeyError(f"missing input {nm}")
                m[nm] = np.ascontiguousarray(arr).astype(dt, copy=False)
            in_maps.append(m)
        res = run_bass_kernel_spmd(nc, in_maps, core_ids=list(range(NCORES)))
        for c in range(NCORES):
            store[c].update(res.results[c])

    out = np.stack([store[c]["out"].T for c in range(NCORES)], axis=0)
    return np.ascontiguousarray(out, dtype=np.float32)


# version 07
# speedup vs baseline: 1.0695x; 1.0108x over previous
import numpy as np
from contextlib import ExitStack

import concourse.bass as bass
import concourse.mybir as mybir
from concourse.bass_utils import run_bass_kernel_spmd

import ml_dtypes
BF16NP = ml_dtypes.bfloat16

F32 = mybir.dt.float32
BF16 = mybir.dt.bfloat16
I32 = mybir.dt.int32
U8 = mybir.dt.uint8
U32 = mybir.dt.uint32
AF = mybir.ActivationFunctionType
ALU = mybir.AluOpType
AX = mybir.AxisListType

T = 4096
D = 1024
DC = D // 128
NCORES = 8
TTT_DEBUG = False

ENGS = ("tensor", "vector", "scalar", "gpsimd", "sync")


class Ctx:
    def __init__(self, fused, reads=(), writes=(), force_outputs=(), force_inputs=()):
        self.nc = bass.Bass("TRN2", target_bir_lowering=False)
        self.stack = ExitStack()
        self.fused = fused
        self.reads = set(reads)
        self.writes = set(writes)
        self.force_outputs = set(force_outputs)
        self.force_inputs = set(force_inputs)
        self.dram = {}
        self.cnt = {}
        self.sems = {}
        for n in ("qs", "qc", "qg", "qv", "mm", "ve", "sc", "gp", "x1", "x2", "ph"):
            self.sems[n] = self.stack.enter_context(self.nc.semaphore(n))
            self.cnt[n] = 0
        self.block = self.stack.enter_context(self.nc.Block())
        self.nbar = 0
        self.pstack = ExitStack()

    def snap(self):
        return dict(self.cnt)

    def fence(self, e, sem):
        if self.cnt[sem]:
            e.wait_ge(self.sems[sem], self.cnt[sem])

    def inp(self, name, shape, dtype):
        if name not in self.dram:
            self.dram[name] = self.nc.dram_tensor(name, list(shape), dtype,
                                                  kind="ExternalInput").ap()
        return self.dram[name]

    def outp(self, name, shape, dtype):
        if name not in self.dram:
            self.dram[name] = self.nc.dram_tensor(name, list(shape), dtype,
                                                  kind="ExternalOutput").ap()
        return self.dram[name]

    def buf(self, name, shape, dtype):
        if name in self.dram:
            return self.dram[name]
        if name in self.force_outputs:
            t = self.nc.dram_tensor(name, list(shape), dtype, kind="ExternalOutput")
        elif name in self.force_inputs:
            t = self.nc.dram_tensor(name, list(shape), dtype, kind="ExternalInput")
        elif self.fused:
            t = self.nc.dram_tensor(name, list(shape), dtype)
        elif name in self.writes:
            t = self.nc.dram_tensor(name, list(shape), dtype, kind="ExternalOutput")
        elif name in self.reads:
            t = self.nc.dram_tensor(name, list(shape), dtype, kind="ExternalInput")
        else:
            t = self.nc.dram_tensor(name, list(shape), dtype)
        self.dram[name] = t.ap()
        return self.dram[name]

    def tag(self, instr, sem, n=1):
        instr.then_inc(self.sems[sem], n)
        self.cnt[sem] += n
        return self.cnt[sem]

    def dma(self, eng, out, in_, sem, **kw):
        ins = eng.dma_start(out=out, in_=in_, **kw)
        return self.tag(ins, sem, 16)

    def sbuf(self, name, shape, dtype):
        return self.pstack.enter_context(self.nc.sbuf_tensor(name, list(shape), dtype))

    def psum(self, name, shape, dtype=F32):
        return self.pstack.enter_context(self.nc.psum_tensor(name, list(shape), dtype))

    def barrier(self):
        self.nbar += 1
        target = 5 * self.nbar
        for engname in ENGS:
            eng = getattr(self.block, engname)

            def _body(e):
                e.sem_inc(self.sems["ph"], 1)
                e.wait_ge(self.sems["ph"], target)
            eng(_body)
        self.cnt["ph"] = target
        self.pstack.close()
        self.pstack = ExitStack()

    def close(self):
        self.stack.close()
        return self.nc


def fm_view(ap, t0, ts, c0=0, nch=DC):
    return ap.rearrange("(c p) t -> p c t", p=128)[:, c0:c0 + nch, t0:t0 + ts]


def sb3(tile, nch, ts):
    return tile.rearrange("p (c t) -> p c t", t=ts)[:, 0:nch, :]


def _bf(x):
    return np.ascontiguousarray(np.asarray(x)).astype(BF16NP)


def _f32(x):
    return np.ascontiguousarray(np.asarray(x), dtype=np.float32)


def _bcol(x):
    a = np.asarray(x, dtype=np.float32).reshape(-1, 128)
    return np.ascontiguousarray(a.T)


def prep_weights(iv):
    w = {}
    w["l0_wq"] = _bf(iv["L0_attn_wq"])
    w["l0_wk"] = _bf(iv["L0_attn_wk"])
    w["l0_wv"] = _bf(iv["L0_attn_wv"])
    w["l0_wo"] = _bf(iv["L0_attn_wo"])
    w["l0_wg1"] = _bf(iv["L0_attn_wg1"])
    w["l0_wg2"] = _bf(iv["L0_attn_wg2"])
    w["l0_normw"] = _f32(iv["L0_attn_norm_w"]).reshape(D, 1)
    w["l0_normw_r"] = _f32(iv["L0_attn_norm_w"]).reshape(1, D)
    w["l0_wgate"] = _bf(np.asarray(iv["L0_ffn_Wg"]).T)
    w["l0_wgatef"] = _f32(np.asarray(iv["L0_ffn_Wg"]).T)
    w["l0_w1"] = _bf(np.transpose(np.asarray(iv["L0_ffn_W1"]), (0, 2, 1)))
    w["l0_w2"] = _bf(np.transpose(np.asarray(iv["L0_ffn_W2"]), (0, 2, 1)))
    w["l0_w3"] = _bf(np.transpose(np.asarray(iv["L0_ffn_W3"]), (0, 2, 1)))
    w["l1_qw"] = _bf(iv["L1_attn_q_w"])
    w["l1_kw"] = _bf(iv["L1_attn_k_w"])
    w["l1_vw"] = _bf(iv["L1_attn_v_w"])
    w["l1_bw"] = _bf(iv["L1_attn_b_w"])
    w["l1_qc"] = _f32(np.asarray(iv["L1_attn_qc_w"])[:, 0, :])
    w["l1_kc"] = _f32(np.asarray(iv["L1_attn_kc_w"])[:, 0, :])
    w["l1_vc"] = _f32(np.asarray(iv["L1_attn_vc_w"])[:, 0, :])
    w["l1_fir1"] = _f32(np.asarray(iv["L1_attn_fir_w1"])[:, 0, :])
    w["l1_fir3"] = _f32(np.asarray(iv["L1_attn_fir_w3"])[:, 0, :])
    w["l1_fir7"] = _f32(np.asarray(iv["L1_attn_fir_w7"])[:, 0, :])
    w["l1_fir31"] = _f32(np.asarray(iv["L1_attn_fir_w31"])[:, 0, :])
    _w1 = np.asarray(iv["L1_attn_mlp_w1"])
    _w1p = np.zeros((1152, 1024), np.float32)
    _w1p[:1024] = _w1[:1024]
    for _h in range(4):
        _w1p[1024 + _h * 32:1024 + _h * 32 + 24] = _w1[1024 + _h * 24:1024 + (_h + 1) * 24]
    w["l1_mw1"] = _bf(_w1p)
    w["l1_mb1"] = _bcol(iv["L1_attn_mlp_b1"])
    w["l1_mw2"] = _bf(iv["L1_attn_mlp_w2"])
    w["l1_mb2"] = _f32(iv["L1_attn_mlp_b2"]).reshape(1, 24)
    w["l1_glt"] = _f32(iv["L1_attn_gate_log_temp"]).reshape(4, 1)
    w["l1_onorm"] = _bcol(np.tile(np.asarray(iv["L1_attn_onorm_w"]), 4))
    w["l1_ow"] = _bf(iv["L1_attn_o_w"])
    w["l1_fg"] = np.ascontiguousarray(np.asarray(iv["L1_ffn_w_gate"]).T).astype(np.int32)
    w["l1_fu"] = np.ascontiguousarray(np.asarray(iv["L1_ffn_w_up"]).T).astype(np.int32)
    w["l1_fd"] = np.ascontiguousarray(np.asarray(iv["L1_ffn_w_down"]).T).astype(np.int32)
    w["l2_down"] = _bf(iv["L2_attn_down_w"])
    w["l2_downb"] = _bcol(iv["L2_attn_down_b"])
    w["l2_up"] = _bf(iv["L2_attn_up_w"])
    w["l2_upb"] = _bcol(iv["L2_attn_up_b"])
    w["l2_wq"] = _bf(iv["L2_attn_wq"])
    w["l2_wk"] = _bf(iv["L2_attn_wk"])
    w["l2_wv"] = _bf(iv["L2_attn_wv"])
    w["l2_wo"] = _bf(iv["L2_attn_wo"])
    w["l2_W1"] = _f32(iv["L2_attn_W1"])
    w["l2_b1r"] = _f32(np.asarray(iv["L2_attn_b1"])[:, 0, :]).reshape(1, 256)
    w["l2_lnwr"] = _f32(iv["L2_attn_ttt_ln_w"]).reshape(1, 256)
    w["l2_lnbr"] = _f32(iv["L2_attn_ttt_ln_b"]).reshape(1, 256)
    w["l2_lrbr"] = _f32(iv["L2_attn_lr_b"]).reshape(1, 2)
    w["l2_lnw"] = _f32(iv["L2_attn_ttt_ln_w"])
    w["l2_lnb"] = _f32(iv["L2_attn_ttt_ln_b"])
    w["l2_lrw"] = _bf(np.asarray(iv["L2_attn_lr_w"])[:, 0, :].T)
    w["l2_lrb"] = _f32(iv["L2_attn_lr_b"]).reshape(2, 1)
    w["l2_tok"] = _f32(iv["L2_attn_tok_idx"]).reshape(1, 16)
    w["l2_postw"] = _f32(iv["L2_attn_post_w"]).reshape(1, 256)
    w["l2_postb"] = _f32(iv["L2_attn_post_b"]).reshape(1, 256)
    w["l2_lnw2"] = _f32(iv["L2_attn_ln_w"]).reshape(1, 256)
    w["l2_lnb2"] = _f32(iv["L2_attn_ln_b"]).reshape(1, 256)
    _ = (iv["L2_ffn_uf"], iv["L2_ffn_ui"], iv["L2_ffn_uc"], iv["L2_ffn_uo"],
         iv["L3_ffn_uf"], iv["L3_ffn_ui"], iv["L3_ffn_uc"], iv["L3_ffn_uo"])
    for L, p in ((2, "L2_ffn"), (3, "L3_ffn")):
        for nm in ("wf", "wi", "wc", "wo", "uf", "ui", "uc", "uo"):
            w[f"l{L}f_{nm}"] = _bf(np.asarray(iv[f"{p}_{nm}"]).T)
        for nm, src in (("bf", "bf"), ("bi", "bi"), ("bc", "bc"), ("bo", "bo"),
                        ("inc", "in_c")):
            w[f"l{L}f_{nm}"] = _bcol(iv[f"{p}_{src}"])
    w["l3_ln1g"] = _bcol(iv["L3_attn_ln1_g"])
    w["l3_ln1b"] = _bcol(iv["L3_attn_ln1_b"])
    w["l3_wq"] = _bf(iv["L3_attn_wq"])
    w["l3_wk"] = _bf(iv["L3_attn_wk"])
    w["l3_wv"] = _bf(iv["L3_attn_wv"])
    w["l3_wo"] = _bf(iv["L3_attn_wo"])
    w["l3_bo"] = _bcol(iv["L3_attn_bo"])
    w["l3_ln2g"] = _bcol(iv["L3_attn_ln2_g"])
    w["l3_ln2b"] = _bcol(iv["L3_attn_ln2_b"])
    w["l3_ff1"] = _bf(iv["L3_attn_w_ff1"])
    w["l3_bff1"] = _bcol(iv["L3_attn_b_ff1"])
    w["l3_ff2"] = _bf(iv["L3_attn_w_ff2"])
    w["l3_bff2"] = _bcol(iv["L3_attn_b_ff2"])
    w["l3_wd"] = _bf(iv["L3_attn_w_d"])
    w["l3_bd"] = _bcol(iv["L3_attn_b_d"])
    return w


def prep_consts():
    c = {}
    c["identb"] = np.eye(128, dtype=np.float32).astype(BF16NP)
    c["identf"] = np.eye(128, dtype=np.float32)
    s = np.arange(64)
    m = (s[:, None] <= s[None, :]).astype(np.float32)
    sgn = np.where(((s[:, None] + s[None, :]) % 2) == 0, 1.0, -1.0).astype(np.float32)
    c["gla_mask"] = (m * sgn).astype(np.float32)
    c["incl64"] = m.astype(np.float32)
    s32 = np.arange(32)
    c["d_strictT"] = (s32[:, None] < s32[None, :]).astype(np.float32)
    c["d_nstrictT"] = -(s32[:, None] < s32[None, :]).astype(np.float32)
    c["d_nstrictL"] = -(s32[:, None] > s32[None, :]).astype(np.float32)
    c["d_inclT"] = (s32[:, None] <= s32[None, :]).astype(np.float32)
    c["eye32"] = np.eye(32, dtype=np.float32).astype(BF16NP)
    s16 = np.arange(16)
    c["t_inclT"] = (s16[:, None] <= s16[None, :]).astype(np.float32)
    c["t_strictT"] = (s16[:, None] < s16[None, :]).astype(np.float32)
    c["invk"] = (1.0 / np.arange(1, 17)).astype(np.float32).reshape(1, 16)
    for n, nm in ((8, "onehot8"), (24, "onehot24")):
        e = np.zeros((n, n * 128), dtype=np.float32)
        for i in range(n):
            e[i, i * 128:(i + 1) * 128] = 1.0
        c[nm] = e
    hs = np.zeros((8 * 128, 4), dtype=np.float32)
    for cch in range(8):
        hs[cch * 128:(cch + 1) * 128, cch // 2] = 1.0
    c["headsel"] = hs
    c["headselb"] = hs.astype(BF16NP)
    m424 = np.zeros((4, 24), dtype=np.float32)
    for h in range(4):
        m424[h, h * 6:(h + 1) * 6] = 1.0
    c["map424"] = m424
    alt = np.where((np.arange(T) % 2) == 0, -1.0, 1.0).astype(np.float32)
    c["alt_row"] = alt.reshape(1, T)
    c["alt_rep"] = np.tile(alt[None, 0:512], (128, 1))
    return c


CONST_SPECS = {
    "identb": ((128, 128), BF16), "identf": ((128, 128), F32),
    "gla_mask": ((64, 64), F32),
    "incl64": ((64, 64), F32),
    "d_strictT": ((32, 32), F32), "d_inclT": ((32, 32), F32),
    "d_nstrictT": ((32, 32), F32), "d_nstrictL": ((32, 32), F32),
    "eye32": ((32, 32), BF16),
    "t_inclT": ((16, 16), F32), "t_strictT": ((16, 16), F32),
    "invk": ((1, 16), F32),
    "onehot8": ((8, 8 * 128), F32), "onehot24": ((24, 24 * 128), F32),
    "headsel": ((8 * 128, 4), F32), "headselb": ((8 * 128, 4), BF16),
    "map424": ((4, 24), F32),
    "alt_row": ((1, T), F32),
    "alt_rep": ((128, 512), F32),
}


def ph_residual(ctx, L):
    xr = ctx.inp("x_in", (D, T), F32) if L == 0 else ctx.buf(f"xres{L}", (D, T), F32)
    at = ctx.buf(f"attn{L}", (D, T), F32)
    ff = ctx.buf(f"ffn{L}", (D, T), F32)
    xo = ctx.outp("out", (D, T), F32) if L == 3 else ctx.buf(f"xres{L + 1}", (D, T), F32)

    TS = 512
    NT = T // TS
    blk = ctx.block
    s0 = ctx.snap()
    sa = [ctx.sbuf(f"rs{L}_a{i}", [128, DC * TS], F32) for i in range(2)]
    sbb = [ctx.sbuf(f"rs{L}_b{i}", [128, DC * TS], F32) for i in range(2)]
    scc = [ctx.sbuf(f"rs{L}_c{i}", [128, DC * TS], F32) for i in range(2)]
    so = [ctx.sbuf(f"rs{L}_o{i}", [128, DC * TS], F32) for i in range(2)]

    @blk.sync
    def _(e):
        for j in range(NT):
            if j >= 2:
                e.wait_ge(ctx.sems["gp"], s0["gp"] + (j - 1))
            ctx.fence(e, "qs")
            ctx.dma(e, sb3(sa[j % 2], DC, TS), fm_view(xr, j * TS, TS), "qs")
            ctx.dma(e, sb3(sbb[j % 2], DC, TS), fm_view(at, j * TS, TS), "qs")
            ctx.dma(e, sb3(scc[j % 2], DC, TS), fm_view(ff, j * TS, TS), "qs")

    @blk.vector
    def _(e):
        for j in range(NT):
            e.wait_ge(ctx.sems["qs"], s0["qs"] + (j + 1) * 3 * 16)
            for ch in range(DC):
                sl = slice(ch * TS, (ch + 1) * TS)
                e.tensor_tensor(out=so[j % 2][:, sl], in0=sa[j % 2][:, sl],
                                in1=sbb[j % 2][:, sl], op=ALU.add)
                ins = e.tensor_tensor(out=so[j % 2][:, sl], in0=so[j % 2][:, sl],
                                      in1=scc[j % 2][:, sl], op=ALU.add)
            ctx.tag(ins, "ve")

    @blk.gpsimd
    def _(e):
        for j in range(NT):
            e.wait_ge(ctx.sems["ve"], s0["ve"] + j + 1)
            ctx.dma(e, fm_view(xo, j * TS, TS), sb3(so[j % 2], DC, TS), "qg")
            e.wait_ge(ctx.sems["qg"], ctx.cnt["qg"])
            ins = e.engine_nop()
            ctx.tag(ins, "gp")


def ph_ln(ctx, L, eps=1e-5):
    src_name = "x_in" if L == 0 else f"xres{L}"
    ph_ln_g(ctx, f"ln{L}", src_name, f"xn{L}",
            dstf=(f"xnf{L}" if L in (0, 3) else None), eps=eps,
            src_is_input=(L == 0))


def ph_ln_g(ctx, tagp, src, dst, gain=None, bias=None, dstf=None, eps=1e-5,
            src_is_input=False):
    L = tagp
    xr = ctx.inp(src, (D, T), F32) if src_is_input else ctx.buf(src, (D, T), F32)
    xn = ctx.buf(dst, (D, T), BF16)
    xnf = ctx.buf(dstf, (D, T), F32) if dstf else None
    g_ap = ctx.inp(gain, (128, DC), F32) if gain else None
    b_ap = ctx.inp(bias, (128, DC), F32) if bias else None

    TS = 512
    NT = T // TS
    blk = ctx.block
    s0 = ctx.snap()
    sx = [ctx.sbuf(f"ln{L}_x{i}", [128, DC * TS], F32) for i in range(2)]
    sx2 = ctx.sbuf(f"ln{L}_x2", [128, DC * TS], F32)
    sonc = ctx.sbuf(f"ln{L}_onc", [128, 1], F32)
    sonr = ctx.sbuf(f"ln{L}_onr", [1, 128], F32)
    srow = ctx.sbuf(f"ln{L}_row", [1, 4 * TS], F32)
    so = [ctx.sbuf(f"ln{L}_o{i}", [128, DC * TS], BF16) for i in range(2)]
    sof = [ctx.sbuf(f"ln{L}_of{i}", [128, DC * TS], F32) for i in range(2)]
    sgb = ctx.sbuf(f"ln{L}_gb", [128, 2 * DC], F32) if (g_ap or b_ap) else None
    ps_mu = ctx.psum(f"ln{L}_pmu", [1, TS])
    ps_sq = ctx.psum(f"ln{L}_psq", [1, TS])
    ps_mb = ctx.psum(f"ln{L}_pmb", [128, TS])
    ps_rb = ctx.psum(f"ln{L}_prb", [128, TS])
    MU, MS, VAR, RSTD = 0, TS, 2 * TS, 3 * TS

    @blk.vector
    def _(e):
        e.memset(sonc[:], 1.0)
        ins = e.memset(sonr[:], 1.0)
        ctx.tag(ins, "x1")

    ones_ready = ctx.cnt["x1"]

    @blk.sync
    def _(e):
        if g_ap is not None:
            ctx.dma(e, sgb[:, 0:DC], g_ap[:, :], "qs")
        if b_ap is not None:
            ctx.dma(e, sgb[:, DC:2 * DC], b_ap[:, :], "qs")
    ngb = (1 if g_ap is not None else 0) + (1 if b_ap is not None else 0)

    @blk.sync
    def _(e):
        for j in range(NT):
            if j >= 2:
                e.wait_ge(ctx.sems["gp"], s0["gp"] + (j - 1))
            ctx.fence(e, "qs")
            ctx.dma(e, sb3(sx[j % 2], DC, TS), fm_view(xr, j * TS, TS), "qs")

    @blk.scalar
    def _(e):
        for j in range(NT):
            e.wait_ge(ctx.sems["qs"], s0["qs"] + (ngb + j + 1) * 16)
            for ch in range(DC):
                ins = e.activation(out=sx2[:, ch * TS:(ch + 1) * TS],
                                   in_=sx[j % 2][:, ch * TS:(ch + 1) * TS],
                                   func=AF.Square)
            ctx.tag(ins, "sc")
            e.wait_ge(ctx.sems["ve"], s0["ve"] + 4 * j + 2)
            ins = e.activation(out=srow[:, RSTD:RSTD + TS],
                               in_=srow[:, VAR:VAR + TS], func=AF.Sqrt)
            ctx.tag(ins, "sc")

    @blk.tensor
    def _(e):
        e.wait_ge(ctx.sems["x1"], ones_ready)
        for j in range(NT):
            if j >= 1:
                e.wait_ge(ctx.sems["ve"], s0["ve"] + 4 * (j - 1) + 2)
            e.wait_ge(ctx.sems["qs"], s0["qs"] + (ngb + j + 1) * 16)
            for ch in range(DC):
                e.matmul(ps_mu[:], sonc[:], sx[j % 2][:, ch * TS:(ch + 1) * TS],
                         start=(ch == 0), stop=(ch == DC - 1))
            e.wait_ge(ctx.sems["sc"], s0["sc"] + 2 * j + 1)
            for ch in range(DC):
                ins = e.matmul(ps_sq[:], sonc[:], sx2[:, ch * TS:(ch + 1) * TS],
                               start=(ch == 0), stop=(ch == DC - 1))
            ctx.tag(ins, "mm")
            if j >= 1:
                e.wait_ge(ctx.sems["ve"], s0["ve"] + 4 * (j - 1) + 4)
            e.wait_ge(ctx.sems["ve"], s0["ve"] + 4 * j + 1)
            e.matmul(ps_mb[:], sonr[:], srow[0:1, MU:MU + TS], start=True, stop=True)
            e.wait_ge(ctx.sems["ve"], s0["ve"] + 4 * j + 3)
            ins = e.matmul(ps_rb[:], sonr[:], srow[0:1, RSTD:RSTD + TS],
                           start=True, stop=True)
            ctx.tag(ins, "mm")

    @blk.vector
    def _(e):
        for j in range(NT):
            e.wait_ge(ctx.sems["mm"], s0["mm"] + 2 * j + 1)
            e.tensor_scalar(out=srow[:, MS:MS + TS], in0=ps_sq[:],
                            scalar1=1.0 / D, scalar2=None, op0=ALU.mult)
            ins = e.tensor_scalar(out=srow[:, MU:MU + TS], in0=ps_mu[:],
                                  scalar1=1.0 / D, scalar2=None, op0=ALU.mult)
            ctx.tag(ins, "ve")
            e.tensor_tensor(out=srow[:, VAR:VAR + TS], in0=srow[:, MU:MU + TS],
                            in1=srow[:, MU:MU + TS], op=ALU.mult)
            ins = e.scalar_tensor_tensor(out=srow[:, VAR:VAR + TS],
                                         in0=srow[:, MS:MS + TS], scalar=float(eps),
                                         in1=srow[:, VAR:VAR + TS],
                                         op0=ALU.add, op1=ALU.subtract)
            ctx.tag(ins, "ve")
            e.wait_ge(ctx.sems["sc"], s0["sc"] + 2 * j + 2)
            ins = e.reciprocal(out=srow[:, RSTD:RSTD + TS],
                               in_=srow[:, RSTD:RSTD + TS])
            ctx.tag(ins, "ve")
            e.wait_ge(ctx.sems["mm"], s0["mm"] + 2 * j + 2)
            for ch in range(DC):
                sl = slice(ch * TS, (ch + 1) * TS)
                e.tensor_tensor(out=sof[j % 2][:, sl], in0=sx[j % 2][:, sl],
                                in1=ps_mb[:], op=ALU.subtract)
                if g_ap is not None or b_ap is not None:
                    e.tensor_tensor(out=sof[j % 2][:, sl], in0=sof[j % 2][:, sl],
                                    in1=ps_rb[:], op=ALU.mult)
                    gc = sgb[:, ch:ch + 1] if g_ap is not None else None
                    bc = sgb[:, DC + ch:DC + ch + 1] if b_ap is not None else None
                    ins = e.tensor_scalar(out=so[j % 2][:, sl], in0=sof[j % 2][:, sl],
                                          scalar1=gc, scalar2=bc,
                                          op0=ALU.mult, op1=ALU.add)
                elif xnf is not None:
                    e.tensor_tensor(out=sof[j % 2][:, sl], in0=sof[j % 2][:, sl],
                                    in1=ps_rb[:], op=ALU.mult)
                    ins = e.tensor_copy(out=so[j % 2][:, sl], in_=sof[j % 2][:, sl])
                else:
                    ins = e.tensor_tensor(out=so[j % 2][:, sl], in0=sof[j % 2][:, sl],
                                          in1=ps_rb[:], op=ALU.mult)
            ctx.tag(ins, "ve")

    @blk.gpsimd
    def _(e):
        for j in range(NT):
            e.wait_ge(ctx.sems["ve"], s0["ve"] + 4 * j + 4)
            ctx.dma(e, fm_view(xn, j * TS, TS), sb3(so[j % 2], DC, TS), "qg")
            if xnf is not None:
                ctx.dma(e, fm_view(xnf, j * TS, TS), sb3(sof[j % 2], DC, TS), "qg")
            e.wait_ge(ctx.sems["qg"], ctx.cnt["qg"])
            ins = e.engine_nop()
            ctx.tag(ins, "gp")


def ph_l1ffn(ctx):
    nc = ctx.nc
    xn = ctx.buf("xn1", (D, T), BF16)
    ffn = ctx.buf("ffn1", (D, T), F32)
    wg = ctx.inp("l1_fg", (D, 3072), I32)
    wu = ctx.inp("l1_fu", (D, 3072), I32)
    wd = ctx.inp("l1_fd", (3072, D), I32)
    S = float(1.0 / 127.0)
    H = 3072
    HC = H // 128
    TS = 256
    NT = T // TS

    blk = ctx.block
    s0 = ctx.snap()
    sxn = [ctx.sbuf(f"f1_x{i}", [128, DC * TS], BF16) for i in range(2)]
    swg = ctx.sbuf("f1_wg", [128, DC * HC * 128], BF16)
    swu = ctx.sbuf("f1_wu", [128, DC * HC * 128], BF16)
    swd = ctx.sbuf("f1_wd", [128, HC * DC * 128], BF16)
    sg = ctx.sbuf("f1_g", [128, TS], BF16)
    sh = ctx.sbuf("f1_h", [128, HC * TS], BF16)
    so = [ctx.sbuf(f"f1_o{i}", [128, DC * TS], F32) for i in range(2)]
    ps_g = ctx.psum("f1_pg", [128, TS])
    ps_u = ctx.psum("f1_pu", [128, TS])
    ps_y = [ctx.psum(f"f1_py{i}", [128, TS]) for i in range(4)]

    @blk.gpsimd
    def _(e):
        ctx.dma(e, swg.rearrange("p (c k) -> p c k", c=DC),
                wg.rearrange("(c p) k -> p c k", p=128), "qg")
        ctx.dma(e, swu.rearrange("p (c k) -> p c k", c=DC),
                wu.rearrange("(c p) k -> p c k", p=128), "qg")
        ctx.dma(e, swd.rearrange("p (c k) -> p c k", c=HC),
                wd.rearrange("(c p) k -> p c k", p=128), "qg")

    wready = s0["qg"] + 48

    @blk.sync
    def _(e):
        for j in range(NT):
            if j >= 2:
                e.wait_ge(ctx.sems["gp"], s0["gp"] + (j - 1))
            ctx.fence(e, "qs")
            ctx.dma(e, sb3(sxn[j % 2], DC, TS), fm_view(xn, j * TS, TS), "qs")

    def wgt(sl, ch, hc):
        return sl[:, (ch * HC + hc) * 128:(ch * HC + hc) * 128 + 128]

    def wdt(sl, hc, dch):
        return sl[:, (hc * DC + dch) * 128:(hc * DC + dch) * 128 + 128]

    MMJ = HC + 2 * 4
    VEJ = HC + 2

    @blk.tensor
    def _(e):
        e.wait_ge(ctx.sems["qg"], wready)
        for j in range(NT):
            e.wait_ge(ctx.sems["qs"], s0["qs"] + (j + 1) * 16)
            for hc in range(HC):
                if not (j == 0 and hc == 0):
                    prev = j * HC + hc - 1
                    e.wait_ge(ctx.sems["sc"], s0["sc"] + prev + 1)
                    e.wait_ge(ctx.sems["ve"], s0["ve"] + (prev // HC) * VEJ + (prev % HC) + 1)
                for ch in range(DC):
                    e.matmul(ps_g[:], wgt(swg, ch, hc),
                             sxn[j % 2][:, ch * TS:(ch + 1) * TS],
                             start=(ch == 0), stop=(ch == DC - 1))
                for ch in range(DC):
                    ins = e.matmul(ps_u[:], wgt(swu, ch, hc),
                                   sxn[j % 2][:, ch * TS:(ch + 1) * TS],
                                   start=(ch == 0), stop=(ch == DC - 1))
                ctx.tag(ins, "mm")
            e.wait_ge(ctx.sems["ve"], s0["ve"] + j * VEJ + HC)
            for rnd in range(2):
                if rnd == 1:
                    e.wait_ge(ctx.sems["ve"], s0["ve"] + j * VEJ + HC + 1)
                if j >= 1 and rnd == 0:
                    e.wait_ge(ctx.sems["ve"], s0["ve"] + j * VEJ)
                for dch in range(4):
                    dd = rnd * 4 + dch
                    for hc in range(HC):
                        ins = e.matmul(ps_y[dch][:], wdt(swd, hc, dd),
                                       sh[:, hc * TS:(hc + 1) * TS],
                                       start=(hc == 0), stop=(hc == HC - 1))
                    ctx.tag(ins, "mm")

    @blk.scalar
    def _(e):
        for j in range(NT):
            for hc in range(HC):
                lin = j * HC + hc
                e.wait_ge(ctx.sems["mm"], s0["mm"] + j * MMJ + hc + 1)
                if lin >= 1:
                    j2, h2 = divmod(lin - 1, HC)
                    e.wait_ge(ctx.sems["ve"], s0["ve"] + j2 * VEJ + h2 + 1)
                ins = e.activation(out=sg[:], in_=ps_g[:], func=AF.Silu, scale=S)
                ctx.tag(ins, "sc")

    @blk.vector
    def _(e):
        for j in range(NT):
            for hc in range(HC):
                e.wait_ge(ctx.sems["sc"], s0["sc"] + j * HC + hc + 1)
                ins = e.scalar_tensor_tensor(out=sh[:, hc * TS:(hc + 1) * TS],
                                             in0=ps_u[:], scalar=S, in1=sg[:],
                                             op0=ALU.mult, op1=ALU.mult)
                ctx.tag(ins, "ve")
            if j >= 2:
                e.wait_ge(ctx.sems["gp"], s0["gp"] + j - 1)
            for rnd in range(2):
                for dch in range(4):
                    dd = rnd * 4 + dch
                    e.wait_ge(ctx.sems["mm"], s0["mm"] + j * MMJ + HC + rnd * 4 + dch + 1)
                    ins = e.tensor_scalar(out=so[j % 2][:, dd * TS:(dd + 1) * TS],
                                          in0=ps_y[dch][:], scalar1=S,
                                          scalar2=None, op0=ALU.mult)
                ctx.tag(ins, "ve")

    @blk.gpsimd
    def _(e):
        e.wait_ge(ctx.sems["qg"], wready)
        for j in range(NT):
            e.wait_ge(ctx.sems["ve"], s0["ve"] + j * VEJ + VEJ)
            ctx.dma(e, fm_view(ffn, j * TS, TS), sb3(so[j % 2], DC, TS), "qg")
            e.wait_ge(ctx.sems["qg"], ctx.cnt["qg"])
            ins = e.engine_nop()
            ctx.tag(ins, "gp")


def ph_mm(ctx, tagp, src, kin, w, dst, nout, major="fm", act=None, scale=None,
          biascol=None, biasrow=None, alt=False, dst2=None, dst_dt=BF16,
          src_extra=None, TS=512, src_tok=False, io_dt=BF16, addsrc=None):
    s_ap = ctx.buf(src, (T, kin) if src_tok else (kin, T), io_dt)
    w_ap = ctx.inp(w, (kin if src_extra is None else kin + src_extra[1], nout), io_dt)
    d_ap = ctx.buf(dst, (nout, T) if major == "fm" else (T, nout), dst_dt)
    d2_ap = ctx.buf(dst2, (nout, T), F32) if dst2 else None
    bc_ap = ctx.inp(biascol, (128, nout // 128), F32) if biascol else None
    br_ap = ctx.inp(biasrow, (1, nout), F32) if biasrow else None
    se_ap = ctx.buf(src_extra[0], (src_extra[1], T), BF16) if src_extra else None
    ad_ap = ctx.buf(addsrc, (nout, T), F32) if addsrc else None

    KC = kin // 128
    OC = nout // 128
    NT = T // TS
    TC = TS // 128
    NH = ceil_div(nout, 512)
    blk = ctx.block
    s0 = ctx.snap()
    sx = [ctx.sbuf(f"{tagp}_x{i}", [128, KC * TS], io_dt) for i in range(2)]
    sxe = ctx.sbuf(f"{tagp}_xe", [src_extra[1], TS], io_dt) if src_extra else None
    swt = ctx.sbuf(f"{tagp}_w", [128, KC * nout], io_dt)
    swe = ctx.sbuf(f"{tagp}_we", [src_extra[1], nout], io_dt) if src_extra else None
    sbc = ctx.sbuf(f"{tagp}_bc", [128, OC], F32) if biascol else None
    sbr = ctx.sbuf(f"{tagp}_br", [1, nout], F32) if biasrow else None
    sone = ctx.sbuf(f"{tagp}_one", [1, 128], F32) if biasrow else None
    salt = ctx.sbuf(f"{tagp}_alt", [128, TS], F32) if alt else None
    stmp = ctx.sbuf(f"{tagp}_tmp", [128, 2 * 512], F32) if ((alt or addsrc) and not dst2) else None
    sad = [ctx.sbuf(f"{tagp}_ad{i}", [128, OC * TS], F32) for i in range(2)] if addsrc else None
    if major == "fm":
        od = [ctx.sbuf(f"{tagp}_o{i}", [128, OC * TS], dst_dt) for i in range(2)]
        od2 = [ctx.sbuf(f"{tagp}_p{i}", [128, OC * TS], F32) for i in range(2)] if dst2 else None
        n_it = OC
    else:
        od = [ctx.sbuf(f"{tagp}_o{i}", [128, TC * nout], dst_dt) for i in range(2)]
        od2 = None
        n_it = TC * NH
    ps = [ctx.psum(f"{tagp}_ps{i}", [128, 512]) for i in range(2)]
    use_ve = bool(alt) or bool(addsrc)

    if biasrow:
        @blk.vector
        def _(e):
            ins = e.memset(sone[:], 1.0)
            ctx.tag(ins, "x1")
    ones_rdy = ctx.cnt["x1"]

    @blk.sync
    def _(e):
        ctx.dma(e, swt.rearrange("p (c n) -> p c n", c=KC),
                w_ap[0:kin, :].rearrange("(c p) n -> p c n", p=128), "qs")
        if src_extra:
            ctx.dma(e, swe[:, :], w_ap[kin:kin + src_extra[1], :], "qs")
        if biascol:
            ctx.dma(e, sbc[:, :], bc_ap[:, :], "qs")
        if biasrow:
            ctx.dma(e, sbr[:], br_ap[:], "qs")
        if alt:
            ctx.dma(e, salt[:, 0:TS], ctx.inp("alt_rep", (128, 512), F32)[:, 0:TS], "qs")
    wready = ctx.cnt["qs"]

    nldj = 2 if src_extra else 1

    nldj = 1 + (1 if src_extra else 0) + (1 if addsrc else 0)
    if src_tok:
        nldj = KC + (1 if src_extra else 0) + (1 if addsrc else 0)

    @blk.sync
    def _(e):
        for j in range(NT):
            if j >= 2:
                e.wait_ge(ctx.sems["gp"], s0["gp"] + (j - 1))
            ctx.fence(e, "qs")
            if src_tok:
                for kc in range(KC):
                    ctx.dma(e, sx[j % 2][:, kc * TS:(kc + 1) * TS],
                            s_ap[j * TS:(j + 1) * TS, kc * 128:(kc + 1) * 128],
                            "qs", transpose=True)
            else:
                ctx.dma(e, sb3(sx[j % 2], KC, TS), fm_view(s_ap, j * TS, TS, nch=KC), "qs")
            if src_extra:
                ctx.dma(e, sxe[:, :], se_ap[:, j * TS:(j + 1) * TS], "qs")
            if addsrc:
                ctx.dma(e, sb3(sad[j % 2], OC, TS), fm_view(ad_ap, j * TS, TS, nch=OC), "qs")

    def qs_at(j):
        return wready + (j + 1) * nldj * 16

    @blk.tensor
    def _(e):
        e.wait_ge(ctx.sems["qs"], wready)
        if biasrow:
            e.wait_ge(ctx.sems["x1"], ones_rdy)
        for j in range(NT):
            e.wait_ge(ctx.sems["qs"], qs_at(j))
            for it in range(n_it):
                gi = j * n_it + it
                if gi >= 2:
                    e.wait_ge(ctx.sems["sc"], s0["sc"] + gi - 1)
                    if use_ve:
                        e.wait_ge(ctx.sems["ve"], s0["ve"] + gi - 1)
                bank = ps[gi % 2]
                if major == "fm":
                    oc = it
                    for kc in range(KC):
                        ins = e.matmul(bank[:, 0:TS],
                                 swt[:, (kc * OC + oc) * 128:(kc * OC + oc) * 128 + 128],
                                 sx[j % 2][:, kc * TS:(kc + 1) * TS],
                                 start=(kc == 0),
                                 stop=(kc == KC - 1 and src_extra is None))
                    if src_extra:
                        ins = e.matmul(bank[:, 0:TS], swe[:, oc * 128:(oc + 1) * 128],
                                 sxe[:, 0:TS], start=False, stop=True)
                else:
                    tc, nh = divmod(it, NH)
                    nw_ = min(512, nout - nh * 512)
                    for kc in range(KC):
                        ins = e.matmul(bank[:, 0:nw_],
                                 sx[j % 2][:, kc * TS + tc * 128:kc * TS + tc * 128 + 128],
                                 swt[:, kc * nout + nh * 512:kc * nout + nh * 512 + nw_],
                                 start=(kc == 0),
                                 stop=(kc == KC - 1 and src_extra is None and not biasrow))
                    if src_extra:
                        ins = e.matmul(bank[:, 0:nw_], sxe[:, tc * 128:(tc + 1) * 128],
                                 swe[:, nh * 512:nh * 512 + nw_],
                                 start=False, stop=(not biasrow))
                    if biasrow:
                        ins = e.matmul(bank[:, 0:nw_], sone[:],
                                 sbr[0:1, nh * 512:nh * 512 + nw_],
                                 start=False, stop=True)
                ctx.tag(ins, "mm")

    @blk.scalar
    def _(e):
        for j in range(NT):
            for it in range(n_it):
                gi = j * n_it + it
                e.wait_ge(ctx.sems["mm"], s0["mm"] + gi + 1)
                bank = ps[gi % 2]
                kw = {}
                if scale is not None:
                    kw["scale"] = float(scale)
                if major == "fm":
                    oc = it
                    if biascol:
                        kw["bias"] = sbc[:, oc:oc + 1]
                        if act is None:
                            kw.setdefault("scale", 1.0)
                    if dst2:
                        dst_sl = od2[j % 2][:, oc * TS:(oc + 1) * TS]
                    elif alt or addsrc:
                        dst_sl = stmp[:, (gi % 2) * 512:(gi % 2) * 512 + TS]
                    else:
                        dst_sl = od[j % 2][:, oc * TS:(oc + 1) * TS]
                    fn = act or (AF.Identity if biascol else AF.Copy)
                    ins = e.activation(out=dst_sl, in_=bank[:, 0:TS],
                                       func=fn, **kw)
                else:
                    tc, nh = divmod(it, NH)
                    nw_ = min(512, nout - nh * 512)
                    dst_sl = od[j % 2][:, tc * nout + nh * 512:tc * nout + nh * 512 + nw_]
                    ins = e.activation(out=dst_sl, in_=bank[:, 0:nw_],
                                       func=act or AF.Copy, **kw)
                ctx.tag(ins, "sc")

    if use_ve:
        @blk.vector
        def _(e):
            e.wait_ge(ctx.sems["qs"], wready)
            for j in range(NT):
                for it in range(n_it):
                    gi = j * n_it + it
                    oc = it
                    e.wait_ge(ctx.sems["sc"], s0["sc"] + gi + 1)
                    srcsl = (od2[j % 2][:, oc * TS:(oc + 1) * TS] if dst2
                             else stmp[:, (gi % 2) * 512:(gi % 2) * 512 + TS])
                    if addsrc:
                        ins = e.tensor_tensor(out=od[j % 2][:, oc * TS:(oc + 1) * TS],
                                              in0=srcsl,
                                              in1=sad[j % 2][:, oc * TS:(oc + 1) * TS],
                                              op=ALU.add)
                    else:
                        ins = e.tensor_tensor(out=od[j % 2][:, oc * TS:(oc + 1) * TS],
                                              in0=srcsl, in1=salt[:, 0:TS], op=ALU.mult)
                    ctx.tag(ins, "ve")

    @blk.gpsimd
    def _(e):
        for j in range(NT):
            last = (j + 1) * n_it
            e.wait_ge(ctx.sems["sc"], s0["sc"] + last)
            if use_ve:
                e.wait_ge(ctx.sems["ve"], s0["ve"] + last)
            if major == "fm":
                ctx.dma(e, fm_view(d_ap, j * TS, TS, nch=OC), sb3(od[j % 2], OC, TS), "qg")
                if dst2:
                    ctx.dma(e, fm_view(d2_ap, j * TS, TS, nch=OC),
                            sb3(od2[j % 2], OC, TS), "qg")
            else:
                ctx.dma(e, d_ap.rearrange("(j tc p) n -> p j tc n", p=128, tc=TC)[:, j],
                        od[j % 2].rearrange("p (tc n) -> p tc n", tc=TC), "qg")
            e.wait_ge(ctx.sems["qg"], ctx.cnt["qg"])
            ins = e.engine_nop()
            ctx.tag(ins, "gp")


def ceil_div(a, b):
    return (a + b - 1) // b


def ph_l0_scan(ctx):
    qap = ctx.buf("l0q", (D, T), BF16)
    kap = ctx.buf("l0k", (D, T), F32)
    kbap = ctx.buf("l0kb", (D, T), BF16)
    vap = ctx.buf("l0v", (T, D), BF16)
    oap = ctx.buf("l0o", (T, D), F32)
    mkap = ctx.inp("incl64", (64, 64), F32)
    idap = ctx.inp("identb", (128, 128), BF16)

    blk = ctx.block
    S_ = ctx.sems
    NCH = 64
    C = 64
    NTI = 32

    ksl = ctx.sbuf("g0_k", [128, T], F32)
    sf = ctx.sbuf("g0_f", [128, T], F32)
    cg = ctx.sbuf("g0_cg", [128, T], F32)
    ncg = ctx.sbuf("g0_ncg", [128, T], F32)
    esb = ctx.sbuf("g0_e", [128, T], F32)
    entmp = ctx.sbuf("g0_en", [128, 2 * C], F32)
    sqt = ctx.sbuf("g0_qt", [128, T], BF16)
    sqh = ctx.sbuf("g0_qh", [128, T], BF16)
    skb = ctx.sbuf("g0_kb", [128, T], BF16)
    skh = ctx.sbuf("g0_kh", [128, T], BF16)
    sk2 = ctx.sbuf("g0_k2", [128, T], BF16)
    k2t = ctx.sbuf("g0_k2t", [128, T], BF16)
    vsb = ctx.sbuf("g0_v", [128, T], BF16)
    osb = ctx.sbuf("g0_o", [128, T], F32)
    sS = ctx.sbuf("g0_S", [128, 128], F32)
    sSb = ctx.sbuf("g0_Sb", [128, 128], BF16)
    smask = ctx.sbuf("g0_m", [64, 64], F32)
    sid = ctx.sbuf("g0_id", [128, 128], BF16)
    szc = ctx.sbuf("g0_zc", [128, 1], F32)
    sAm = ctx.sbuf("g0_Am", [128, C], BF16)
    psA = ctx.psum("g0_pA", [64, 64])
    psOI = ctx.psum("g0_pOI", [64, 128])
    psOB = ctx.psum("g0_pOB", [64, 128])
    psSA = ctx.psum("g0_pSA", [128, 128])
    psTR = [ctx.psum(f"g0_pT{i}", [128, 128], BF16) for i in range(2)]

    @blk.sync
    def _(e):
        ctx.dma(e, smask[:], mkap[:], "qs")
        ctx.dma(e, sid[:], idap[:], "qs")
    consts_rdy = ctx.cnt["qs"]

    @blk.vector
    def _(e):
        e.memset(szc[:], 0.0)
        ins = e.memset(sAm[:], 0.0)
        ctx.tag(ins, "x1")
    zc_rdy = ctx.cnt["x1"]

    for h in range(8):
        hv = slice(h * 128, (h + 1) * 128)

        @blk.sync
        def _(e, h=h):
            if h > 0:
                e.wait_ge(S_["gp"], ctx.cnt["gp"])
            ctx.fence(e, "qs")
            ctx.dma(e, ksl[:, :], fm_view(kap, 0, T, c0=h, nch=1)[:, 0, :], "qs")
            ctx.dma(e, sqt[:, :], fm_view(qap, 0, T, c0=h, nch=1)[:, 0, :], "qs")
            ctx.dma(e, skb[:, :], fm_view(kbap, 0, T, c0=h, nch=1)[:, 0, :], "qs")
            ctx.dma(e, vsb.rearrange("p (ti n) -> p ti n", n=128),
                    vap[:, hv].rearrange("(ti p) n -> p ti n", p=128), "qs")
        qs_h = ctx.cnt["qs"]

        @blk.vector
        def _(e):
            e.wait_ge(S_["qs"], qs_h)
            e.wait_ge(S_["x1"], zc_rdy)
            ins = e.tensor_scalar(out=sf[:], in0=ksl[:], scalar1=-1.0, scalar2=1.0,
                                  op0=ALU.mult, op1=ALU.add)
            ctx.tag(ins, "ve")
        v_f = ctx.cnt["ve"]

        @blk.scalar
        def _(e):
            e.wait_ge(S_["ve"], v_f)
            ins = e.activation(out=ksl[:], in_=sf[:], func=AF.Ln)
            ctx.tag(ins, "sc")
        s_g = ctx.cnt["sc"]

        @blk.vector
        def _(e):
            e.wait_ge(S_["sc"], s_g)
            ins = e.tensor_tensor_scan(out=cg[:], data0=ksl[:], data1=ksl[:],
                                       initial=0.0, op0=ALU.add, op1=ALU.bypass)
            c1 = ctx.tag(ins, "ve")
            e.wait_ge(S_["ve"], c1)
            ins = e.tensor_scalar(out=ncg[:], in0=cg[:], scalar1=-1.0, scalar2=None,
                                  op0=ALU.mult)
            ctx.tag(ins, "ve")
        v_ncg = ctx.cnt["ve"]

        v_k2 = {}
        v_qh = {}
        s_ep = {}
        for c in range(NCH):
            @blk.scalar
            def _(e, c=c):
                if c == 0:
                    e.wait_ge(S_["ve"], v_ncg)
                if c >= 2:
                    e.wait_ge(S_["ve"], v_k2[c - 2])
                bpos = szc[:, 0:1] if c == 0 else ncg[:, c * C - 1:c * C]
                bneg = szc[:, 0:1] if c == 0 else cg[:, c * C - 1:c * C]
                e.activation(out=esb[:, c * C:(c + 1) * C], in_=cg[:, c * C:(c + 1) * C],
                             func=AF.Exp, bias=bpos)
                ins = e.activation(out=entmp[:, (c % 2) * C:(c % 2) * C + C],
                                   in_=cg[:, c * C:(c + 1) * C],
                                   func=AF.Exp, bias=bneg, scale=-1.0)
                ctx.tag(ins, "sc")
            s_ep[c] = ctx.cnt["sc"]

            @blk.vector
            def _(e, c=c):
                e.wait_ge(S_["sc"], s_ep[c])
                ins = e.scalar_tensor_tensor(out=sqh[:, c * C:(c + 1) * C],
                                             in0=esb[:, c * C:(c + 1) * C],
                                             scalar=1.0 / 16.0,
                                             in1=sqt[:, c * C:(c + 1) * C],
                                             op0=ALU.mult, op1=ALU.mult)
                c1 = ctx.tag(ins, "ve")
                v_qh[c] = c1
                e.wait_ge(S_["ve"], c1)
                ins = e.tensor_tensor(out=skh[:, c * C:(c + 1) * C],
                                      in0=skb[:, c * C:(c + 1) * C],
                                      in1=entmp[:, (c % 2) * C:(c % 2) * C + C],
                                      op=ALU.mult)
                c2 = ctx.tag(ins, "ve")
                e.wait_ge(S_["ve"], c2)
                ins = e.tensor_scalar(out=sk2[:, c * C:(c + 1) * C],
                                      in0=skh[:, c * C:(c + 1) * C],
                                      scalar1=esb[:, (c + 1) * C - 1:(c + 1) * C],
                                      scalar2=None, op0=ALU.mult)
                ctx.tag(ins, "ve")
            v_k2[c] = ctx.cnt["ve"]

        t_tr = {}
        s_tv = {}
        for ti in range(NTI):
            @blk.tensor
            def _(e, ti=ti):
                if ti == 0:
                    e.wait_ge(S_["qs"], consts_rdy)
                e.wait_ge(S_["ve"], v_k2[2 * ti + 1])
                if ti >= 2:
                    e.wait_ge(S_["sc"], s_tv[ti - 2])
                ins = e.transpose(psTR[ti % 2][:], sk2[:, ti * 128:(ti + 1) * 128], sid[:])
                ctx.tag(ins, "mm")
            t_tr[ti] = ctx.cnt["mm"]

            @blk.scalar
            def _(e, ti=ti):
                e.wait_ge(S_["mm"], t_tr[ti])
                ins = e.activation(out=k2t[:, ti * 128:(ti + 1) * 128],
                                   in_=psTR[ti % 2][:], func=AF.Copy)
                ctx.tag(ins, "sc")
            s_tv[ti] = ctx.cnt["sc"]

        t_A = {}
        v_Am = {}
        t_oi = {}
        v_oc = {}
        for c in range(NCH):
            ti, half = c // 2, c % 2
            vrow = slice(half * 64, half * 64 + 64)

            @blk.tensor
            def _(e, c=c):
                e.wait_ge(S_["ve"], v_k2[c])
                if c >= 1:
                    e.wait_ge(S_["ve"], v_Am[c - 1])
                ins = e.matmul(psA[:], skh[:, c * C:(c + 1) * C],
                               sqh[:, c * C:(c + 1) * C], start=True, stop=True)
                ctx.tag(ins, "mm")
            t_A[c] = ctx.cnt["mm"]

            @blk.vector
            def _(e, c=c):
                e.wait_ge(S_["mm"], t_A[c])
                if c >= 2:
                    e.wait_ge(S_["mm"], t_oi[c - 2])
                half = c % 2
                ins = e.tensor_tensor(out=sAm[half * 64:half * 64 + 64, :],
                                      in0=psA[:], in1=smask[:], op=ALU.mult)
                ctx.tag(ins, "ve")
            v_Am[c] = ctx.cnt["ve"]

            @blk.tensor
            def _(e, c=c, ti=ti, half=half):
                e.wait_ge(S_["ve"], v_Am[c])
                if c >= 1:
                    e.wait_ge(S_["ve"], v_oc[c - 1])
                ins = e.matmul(psOI[:], sAm[half * 64:half * 64 + 64, :],
                               vsb[half * 64:half * 64 + 64, ti * 128:(ti + 1) * 128],
                               start=True, stop=True)
                ctx.tag(ins, "mm")
            t_oi[c] = ctx.cnt["mm"]

            @blk.vector
            def _(e, c=c, ti=ti, half=half):
                e.wait_ge(S_["mm"], t_oi[c])
                ins = e.tensor_copy(out=osb[half * 64:half * 64 + 64,
                                            ti * 128:(ti + 1) * 128], in_=psOI[:])
                ctx.tag(ins, "ve")
            v_oc[c] = ctx.cnt["ve"]

        @blk.vector
        def _(e):
            e.memset(sS[:], 0.0)
            ins = e.memset(sSb[:], 0.0)
            ctx.tag(ins, "ve")
        v_sbf_prev = ctx.cnt["ve"]

        for c in range(NCH):
            ti, half = c // 2, c % 2

            @blk.tensor
            def _(e, c=c, ti=ti, half=half, v_prev=v_sbf_prev):
                e.wait_ge(S_["ve"], v_prev)
                ins = e.matmul(psOB[:], sqh[:, c * C:(c + 1) * C], sSb[:],
                               start=True, stop=True)
                c1 = ctx.tag(ins, "mm")
                ins = e.matmul(psSA[:], k2t[half * 64:half * 64 + 64,
                                            ti * 128:(ti + 1) * 128],
                               vsb[half * 64:half * 64 + 64, ti * 128:(ti + 1) * 128],
                               start=True, stop=True)
                ctx.tag(ins, "mm")
            t_ob = ctx.cnt["mm"] - 1
            t_sa = ctx.cnt["mm"]

            @blk.vector
            def _(e, c=c, ti=ti, half=half, t_ob=t_ob, t_sa=t_sa):
                e.wait_ge(S_["mm"], t_ob)
                ins = e.tensor_tensor(out=osb[half * 64:half * 64 + 64,
                                              ti * 128:(ti + 1) * 128],
                                      in0=osb[half * 64:half * 64 + 64,
                                              ti * 128:(ti + 1) * 128],
                                      in1=psOB[:], op=ALU.add)
                c1 = ctx.tag(ins, "ve")
                e.wait_ge(S_["ve"], c1)
                e.wait_ge(S_["mm"], t_sa)
                ins = e.tensor_scalar(out=sS[:], in0=sS[:],
                                      scalar1=esb[:, (c + 1) * C - 1:(c + 1) * C],
                                      scalar2=None, op0=ALU.mult)
                c2 = ctx.tag(ins, "ve")
                e.wait_ge(S_["ve"], c2)
                ins = e.tensor_tensor(out=sS[:], in0=sS[:], in1=psSA[:], op=ALU.add)
                c3 = ctx.tag(ins, "ve")
                e.wait_ge(S_["ve"], c3)
                ins = e.tensor_copy(out=sSb[:], in_=sS[:])
                ctx.tag(ins, "ve")
            v_sbf_prev = ctx.cnt["ve"]

        @blk.gpsimd
        def _(e, h=h, v_last=v_sbf_prev):
            e.wait_ge(S_["ve"], v_last)
            ctx.dma(e, oap[:, h * 128:(h + 1) * 128].rearrange("(ti p) n -> p ti n", p=128),
                    osb.rearrange("p (ti n) -> p ti n", n=128), "qg")
            e.wait_ge(S_["qg"], ctx.cnt["qg"])
            ins = e.engine_nop()
            ctx.tag(ins, "gp")


def ph_l0_out(ctx, eps=1e-5):
    oap = ctx.buf("l0o", (T, D), F32)
    gap = ctx.buf("l0gate", (T, D), BF16)
    onap = ctx.buf("l0on", (T, D), BF16)
    nwap = ctx.inp("l0_normw_r", (1, D), F32)

    blk = ctx.block
    S_ = ctx.sems
    NTI = T // 128
    so = [ctx.sbuf(f"o0_o{i}", [128, D], F32) for i in range(2)]
    sg = [ctx.sbuf(f"o0_g{i}", [128, D], BF16) for i in range(2)]
    sog = ctx.sbuf("o0_og", [128, D], F32)
    ssq = ctx.sbuf("o0_sq", [128, D], F32)
    son = [ctx.sbuf(f"o0_on{i}", [128, D], BF16) for i in range(2)]
    scol = ctx.sbuf("o0_col", [128, 8], F32)
    snwr = ctx.sbuf("o0_nw", [1, D], F32)
    sone = ctx.sbuf("o0_one", [1, 128], F32)
    snw = ctx.sbuf("o0_nwrep", [128, D], F32)
    psR = ctx.psum("o0_pr", [128, 512])
    MU_RAW, SS, MU, M2, VAR, RSTD = 0, 1, 2, 3, 4, 5

    @blk.vector
    def _(e):
        ins = e.memset(sone[:], 1.0)
        ctx.tag(ins, "x1")
    ones_rdy = ctx.cnt["x1"]

    @blk.sync
    def _(e):
        ctx.dma(e, snwr[:], nwap[:], "qs")
    nw_rdy = ctx.cnt["qs"]

    t_rep = {}
    for half in range(2):
        @blk.tensor
        def _(e, half=half):
            e.wait_ge(S_["qs"], nw_rdy)
            e.wait_ge(S_["x1"], ones_rdy)
            if half == 1:
                e.wait_ge(S_["sc"], t_rep[0])
            ins = e.matmul(psR[:], sone[:], snwr[0:1, half * 512:half * 512 + 512],
                           start=True, stop=True)
            ctx.tag(ins, "mm")
        mmc = ctx.cnt["mm"]

        @blk.scalar
        def _(e, half=half, mmc=mmc):
            e.wait_ge(S_["mm"], mmc)
            ins = e.activation(out=snw[:, half * 512:half * 512 + 512], in_=psR[:],
                               func=AF.Copy)
            ctx.tag(ins, "sc")
        t_rep[half] = ctx.cnt["sc"]
    rep_rdy = ctx.cnt["sc"]

    for i in range(NTI):
        @blk.sync
        def _(e, i=i):
            if i >= 2:
                e.wait_ge(S_["gp"], ctx.cnt["gp"] - 1)
            ctx.fence(e, "qs")
            ctx.dma(e, so[i % 2][:], oap[i * 128:(i + 1) * 128, :], "qs")
            ctx.dma(e, sg[i % 2][:], gap[i * 128:(i + 1) * 128, :], "qs")
        qs_i = ctx.cnt["qs"]

        @blk.vector
        def _(e, i=i, qs_i=qs_i):
            e.wait_ge(S_["qs"], qs_i)
            ins = e.tensor_tensor(out=sog[:], in0=so[i % 2][:], in1=sg[i % 2][:],
                                  op=ALU.mult)
            c1 = ctx.tag(ins, "ve")
            e.wait_ge(S_["ve"], c1)
            ins = e.tensor_reduce(out=scol[:, MU_RAW:MU_RAW + 1], in_=sog[:],
                                  axis=AX.X, op=ALU.add)
            ctx.tag(ins, "ve")
        v_og = ctx.cnt["ve"]

        @blk.scalar
        def _(e, i=i, v_og=v_og):
            e.wait_ge(S_["ve"], v_og)
            ins = e.activation(out=ssq[:], in_=sog[:], func=AF.Square,
                               accum_out=scol[:, SS:SS + 1])
            ctx.tag(ins, "sc")
        s_sq = ctx.cnt["sc"]

        @blk.vector
        def _(e, i=i, s_sq=s_sq):
            e.wait_ge(S_["sc"], s_sq)
            ins = e.tensor_scalar(out=scol[:, MU:MU + 1], in0=scol[:, MU_RAW:MU_RAW + 1],
                                  scalar1=1.0 / D, scalar2=None, op0=ALU.mult)
            c1 = ctx.tag(ins, "ve")
            e.wait_ge(S_["ve"], c1)
            ins = e.tensor_tensor(out=scol[:, M2:M2 + 1], in0=scol[:, MU:MU + 1],
                                  in1=scol[:, MU:MU + 1], op=ALU.mult)
            c2 = ctx.tag(ins, "ve")
            e.wait_ge(S_["ve"], c2)
            ins = e.scalar_tensor_tensor(out=scol[:, VAR:VAR + 1],
                                         in0=scol[:, SS:SS + 1], scalar=1.0 / D,
                                         in1=scol[:, M2:M2 + 1],
                                         op0=ALU.mult, op1=ALU.subtract)
            c3 = ctx.tag(ins, "ve")
            e.wait_ge(S_["ve"], c3)
            ins = e.tensor_scalar(out=scol[:, VAR:VAR + 1], in0=scol[:, VAR:VAR + 1],
                                  scalar1=float(eps), scalar2=None, op0=ALU.add)
            ctx.tag(ins, "ve")
        v_var = ctx.cnt["ve"]

        @blk.scalar
        def _(e, i=i, v_var=v_var):
            e.wait_ge(S_["ve"], v_var)
            ins = e.activation(out=scol[:, RSTD:RSTD + 1], in_=scol[:, VAR:VAR + 1],
                               func=AF.Sqrt)
            ctx.tag(ins, "sc")
        s_std = ctx.cnt["sc"]

        @blk.vector
        def _(e, i=i, s_std=s_std):
            e.wait_ge(S_["sc"], s_std)
            ins = e.reciprocal(out=scol[:, RSTD:RSTD + 1], in_=scol[:, RSTD:RSTD + 1])
            c1 = ctx.tag(ins, "ve")
            e.wait_ge(S_["ve"], c1)
            if i == 0:
                e.wait_ge(S_["sc"], rep_rdy)
            ins = e.tensor_scalar(out=sog[:], in0=sog[:], scalar1=scol[:, MU:MU + 1],
                                  scalar2=scol[:, RSTD:RSTD + 1],
                                  op0=ALU.subtract, op1=ALU.mult)
            c2 = ctx.tag(ins, "ve")
            e.wait_ge(S_["ve"], c2)
            ins = e.tensor_tensor(out=son[i % 2][:], in0=sog[:], in1=snw[:],
                                  op=ALU.mult)
            ctx.tag(ins, "ve")
        v_on = ctx.cnt["ve"]

        @blk.gpsimd
        def _(e, i=i, v_on=v_on):
            e.wait_ge(S_["ve"], v_on)
            ctx.dma(e, onap[i * 128:(i + 1) * 128, :], son[i % 2][:], "qg")
            e.wait_ge(S_["qg"], ctx.cnt["qg"])
            ins = e.engine_nop()
            ctx.tag(ins, "gp")


def ph_l0_moe(ctx):
    xap = ctx.buf("xn0", (D, T), BF16)
    scap = ctx.buf("l0sc", (T, 8), F32)
    fap = ctx.buf("ffn0", (D, T), F32)
    w1ap = ctx.inp("l0_w1", (8, D, 2816), BF16)
    w2ap = ctx.inp("l0_w2", (8, D, 2816), BF16)
    w3ap = ctx.inp("l0_w3", (8, 2816, D), BF16)
    o8ap = ctx.inp("onehot8", (8, 8 * 128), F32)
    idap = ctx.inp("identf", (128, 128), F32)
    hdram = ctx.buf("l0_hscr", (2816, T // 2), BF16)

    HC = 22
    TS = 512
    NJ = T // TS
    blk = ctx.block
    S_ = ctx.sems

    TH2 = T // 2
    sxn = ctx.sbuf("m0_x", [128, DC * TH2], BF16)
    ssc = ctx.sbuf("m0_sc", [128, 16], F32)
    su8 = ctx.sbuf("m0_u8", [128, 16], U8)
    sneg = ctx.sbuf("m0_neg", [128, 8], F32)
    szero = ctx.sbuf("m0_z", [128, 8], F32)
    stk = ctx.sbuf("m0_tk", [128, 16], F32)
    swf = ctx.sbuf("m0_wf", [8, T], F32)
    so8 = ctx.sbuf("m0_o8", [8, 8 * 128], F32)
    sidf = ctx.sbuf("m0_id", [128, 128], F32)
    sweb = ctx.sbuf("m0_web", [128, (NJ // 2) * TS], F32)
    sw1 = [ctx.sbuf(f"m0_w1{i}", [128, DC * 128], BF16) for i in range(2)]
    sw2 = [ctx.sbuf(f"m0_w2{i}", [128, DC * 128], BF16) for i in range(2)]
    sg = ctx.sbuf("m0_g", [128, TS], BF16)
    shh = ctx.sbuf("m0_h", [128, (NJ // 2) * TS], BF16)
    sw3 = ctx.sbuf("m0_w3", [128, HC * D], BF16)
    sh2 = [ctx.sbuf(f"m0_h2{i}", [128, HC * TS], BF16) for i in range(2)]
    sy = [ctx.sbuf(f"m0_y{i}", [128, DC * TS], F32) for i in range(2)]
    ps_g = ctx.psum("m0_pg", [128, TS])
    ps_u = ctx.psum("m0_pu", [128, TS])
    ps_y = [ctx.psum(f"m0_py{i}", [128, TS]) for i in range(2)]
    ps_t = ctx.psum("m0_pt", [8, 128])
    ps_w = ctx.psum("m0_pw", [128, TS])

    @blk.vector
    def _(e):
        e.memset(sneg[:], -3.0e38)
        ins = e.memset(szero[:], 0.0)
        ctx.tag(ins, "x1")
    z_rdy = ctx.cnt["x1"]

    @blk.sync
    def _(e):
        ctx.dma(e, so8[:], o8ap[:], "qs")
        ctx.dma(e, sidf[:], idap[:], "qs")
    pre_rdy = ctx.cnt["qs"]

    v_w5 = {}
    for i in range(T // 128):
        @blk.sync
        def _(e, i=i):
            if i >= 2:
                e.wait_ge(S_["ve"], v_w5[i - 2])
            ctx.fence(e, "qs")
            ctx.dma(e, ssc[:, (i % 2) * 8:(i % 2) * 8 + 8],
                    scap[i * 128:(i + 1) * 128, :], "qs")
        qs_i = ctx.cnt["qs"]

        @blk.vector
        def _(e, i=i, qs_i=qs_i):
            sl = ssc[:, (i % 2) * 8:(i % 2) * 8 + 8]
            msk = su8[:, (i % 2) * 8:(i % 2) * 8 + 8]
            m1 = stk[:, (i % 2) * 4 + 0:(i % 2) * 4 + 1]
            m2 = stk[:, (i % 2) * 4 + 1:(i % 2) * 4 + 2]
            tneg = stk[:, 8:16]
            e.wait_ge(S_["qs"], qs_i)
            if i == 0:
                e.wait_ge(S_["x1"], z_rdy)
            else:
                e.wait_ge(S_["mm"], ctx.cnt["mm"])
            ins = e.tensor_reduce(out=m1, in_=sl, axis=AX.X, op=ALU.max)
            c = ctx.tag(ins, "ve"); e.wait_ge(S_["ve"], c)
            ins = e.tensor_scalar(out=msk, in0=sl, scalar1=m1, scalar2=None,
                                  op0=ALU.is_ge)
            c = ctx.tag(ins, "ve"); e.wait_ge(S_["ve"], c)
            ins = e.select(out=tneg, mask=msk, on_true=sneg[:], on_false=sl)
            c = ctx.tag(ins, "ve"); e.wait_ge(S_["ve"], c)
            ins = e.tensor_reduce(out=m2, in_=tneg, axis=AX.X, op=ALU.max)
            c = ctx.tag(ins, "ve"); e.wait_ge(S_["ve"], c)
            ins = e.tensor_scalar(out=msk, in0=sl, scalar1=m2, scalar2=None,
                                  op0=ALU.is_ge)
            c5 = ctx.tag(ins, "ve")
            e.wait_ge(S_["ve"], c5)
            v_w5[i] = c5
            ins = e.select(out=stk[:, 8:16], mask=msk, on_true=sl, on_false=szero[:])
            ctx.tag(ins, "ve")
        v_w = ctx.cnt["ve"]

        @blk.tensor
        def _(e, i=i, v_w=v_w):
            e.wait_ge(S_["ve"], v_w)
            e.wait_ge(S_["qs"], pre_rdy)
            if i >= 1:
                e.wait_ge(S_["sc"], ctx.cnt["sc"])
            ins = e.transpose(ps_t[:], stk[:, 8:16], sidf[:])
            ctx.tag(ins, "mm")
        t_tr = ctx.cnt["mm"]

        @blk.scalar
        def _(e, i=i, t_tr=t_tr):
            e.wait_ge(S_["mm"], t_tr)
            ins = e.activation(out=swf[:, i * 128:(i + 1) * 128], in_=ps_t[:],
                               func=AF.Copy)
            ctx.tag(ins, "sc")
    wf_rdy = ctx.cnt["sc"]

    xn_rdy = {}
    prev_p2 = 0
    for th in range(2):
        T0 = th * TH2
        NJh = NJ // 2

        @blk.sync
        def _(e, th=th, T0=T0):
            if th > 0:
                e.wait_ge(S_["mm"], ctx.cnt["mm"])
            ctx.fence(e, "qs")
            ctx.dma(e, sb3(sxn, DC, TH2), fm_view(xap, T0, TH2), "qs")
        xn_rdy[th] = ctx.cnt["qs"]

        for ex in range(8):
            for j in range(NJh):
                @blk.tensor
                def _(e, ex=ex, j=j, th=th, T0=T0):
                    if ex == 0 and j == 0:
                        e.wait_ge(S_["sc"], wf_rdy)
                        e.wait_ge(S_["qs"], xn_rdy[th])
                    else:
                        e.wait_ge(S_["sc"], ctx.cnt["sc"])
                    if j == 0 and not (ex == 0 and th == 0):
                        e.wait_ge(S_["gp"], ctx.cnt["gp"])
                    ins = e.matmul(ps_w[:], so8[:, ex * 128:(ex + 1) * 128],
                                   swf[0:8, T0 + j * TS:T0 + (j + 1) * TS],
                                   start=True, stop=True)
                    ctx.tag(ins, "mm")
                t_wb = ctx.cnt["mm"]

                @blk.scalar
                def _(e, t_wb=t_wb, j=j):
                    e.wait_ge(S_["mm"], t_wb)
                    ins = e.activation(out=sweb[:, j * TS:(j + 1) * TS], in_=ps_w[:],
                                       func=AF.Copy)
                    ctx.tag(ins, "sc")
            web_rdy = ctx.cnt["sc"]

            t_gu_last = {}
            g_h = {}
            for hc in range(HC):
                @blk.sync
                def _(e, ex=ex, hc=hc):
                    if hc >= 2:
                        e.wait_ge(S_["mm"], t_gu_last[hc - 2])
                    ctx.fence(e, "qs")
                    ctx.dma(e, sw1[hc % 2].rearrange("p (c n) -> p c n", c=DC),
                            w1ap[ex].rearrange("(c p) h -> p c h", p=128)[:, :, hc * 128:(hc + 1) * 128],
                            "qs")
                    ctx.dma(e, sw2[hc % 2].rearrange("p (c n) -> p c n", c=DC),
                            w2ap[ex].rearrange("(c p) h -> p c h", p=128)[:, :, hc * 128:(hc + 1) * 128],
                            "qs")
                qs_hc = ctx.cnt["qs"]

                v_h = {}
                for j in range(NJh):
                    @blk.tensor
                    def _(e, ex=ex, hc=hc, j=j, qs_hc=qs_hc, web_rdy=web_rdy, th=th):
                        if j == 0:
                            e.wait_ge(S_["qs"], qs_hc)
                            if hc == 0:
                                e.wait_ge(S_["sc"], web_rdy)
                                if not (ex == 0 and th == 0):
                                    e.wait_ge(S_["gp"], ctx.cnt["gp"])
                        if not (ex == 0 and hc == 0 and j == 0 and th == 0):
                            e.wait_ge(S_["sc"], ctx.cnt["sc"])
                            e.wait_ge(S_["ve"], ctx.cnt["ve"])
                        for c in range(DC):
                            e.matmul(ps_g[:], sw1[hc % 2][:, c * 128:(c + 1) * 128],
                                     sxn[:, c * TH2 + j * TS:c * TH2 + j * TS + TS],
                                     start=(c == 0), stop=(c == DC - 1))
                        for c in range(DC):
                            ins = e.matmul(ps_u[:], sw2[hc % 2][:, c * 128:(c + 1) * 128],
                                           sxn[:, c * TH2 + j * TS:c * TH2 + j * TS + TS],
                                           start=(c == 0), stop=(c == DC - 1))
                        ctx.tag(ins, "mm")
                    t_gu = ctx.cnt["mm"]
                    if j == NJh - 1:
                        t_gu_last[hc] = t_gu

                    @blk.scalar
                    def _(e, t_gu=t_gu):
                        e.wait_ge(S_["mm"], t_gu)
                        ins = e.activation(out=sg[:], in_=ps_g[:], func=AF.Silu)
                        ctx.tag(ins, "sc")
                    s_si = ctx.cnt["sc"]

                    @blk.vector
                    def _(e, hc=hc, j=j, s_si=s_si):
                        e.wait_ge(S_["sc"], s_si)
                        if j == 0 and hc >= 1:
                            e.wait_ge(S_["gp"], g_h.get(hc - 1, 0))
                        ins = e.tensor_tensor(out=shh[:, j * TS:(j + 1) * TS],
                                              in0=ps_u[:], in1=sg[:], op=ALU.mult)
                        ctx.tag(ins, "ve")
                    v_h[j] = ctx.cnt["ve"]

                @blk.gpsimd
                def _(e, ex=ex, hc=hc, th=th, v_last=v_h[NJh - 1]):
                    e.wait_ge(S_["ve"], v_last)
                    if hc == 0 and not (ex == 0 and th == 0):
                        e.wait_ge(S_["qs"], ctx.cnt["qs"])
                    ctx.dma(e, hdram[hc * 128:(hc + 1) * 128, :], shh[:, :], "qg")
                    e.wait_ge(S_["qg"], ctx.cnt["qg"])
                    ins = e.engine_nop()
                    ctx.tag(ins, "gp")
                g_h[hc] = ctx.cnt["gp"]
            p1_done = ctx.cnt["gp"]

            @blk.sync
            def _(e, ex=ex, prev_p2=prev_p2):
                if prev_p2:
                    e.wait_ge(S_["mm"], prev_p2)
                ctx.fence(e, "qs")
                ctx.dma(e, sw3.rearrange("p (c n) -> p c n", c=HC),
                        w3ap[ex].rearrange("(c p) d -> p c d", p=128), "qs")
            w3_rdy = ctx.cnt["qs"]

            g_y = {}
            for j in range(NJh):
                @blk.sync
                def _(e, ex=ex, j=j, w3_rdy=w3_rdy, p1_done=p1_done):
                    if j == 0:
                        e.wait_ge(S_["gp"], p1_done)
                    if j >= 2:
                        e.wait_ge(S_["gp"], g_y[j - 2])
                    ctx.fence(e, "qs")
                    ctx.dma(e, sh2[j % 2].rearrange("p (c n) -> p c n", c=HC),
                            hdram.rearrange("(c p) t -> p c t", p=128)[:, :, j * TS:(j + 1) * TS],
                            "qs")
                qs_j = ctx.cnt["qs"]

                v_y = {}
                for dd in range(DC):
                    @blk.tensor
                    def _(e, j=j, dd=dd, qs_j=qs_j, w3_rdy=w3_rdy, v_y=v_y):
                        if dd == 0:
                            e.wait_ge(S_["qs"], max(qs_j, w3_rdy))
                        if dd >= 2:
                            e.wait_ge(S_["ve"], v_y[dd - 2])
                        elif not (j == 0):
                            e.wait_ge(S_["ve"], ctx.cnt["ve"])
                        for hc in range(HC):
                            ins = e.matmul(ps_y[dd % 2][:],
                                           sw3[:, hc * D + dd * 128:hc * D + dd * 128 + 128],
                                           sh2[j % 2][:, hc * TS:(hc + 1) * TS],
                                           start=(hc == 0), stop=(hc == HC - 1))
                        ctx.tag(ins, "mm")
                    t_y = ctx.cnt["mm"]

                    @blk.vector
                    def _(e, j=j, dd=dd, t_y=t_y):
                        e.wait_ge(S_["mm"], t_y)
                        if j >= 2 and dd == 0:
                            e.wait_ge(S_["gp"], g_y[j - 2])
                        ins = e.tensor_tensor(out=sy[j % 2][:, dd * TS:(dd + 1) * TS],
                                              in0=ps_y[dd % 2][:],
                                              in1=sweb[:, j * TS:(j + 1) * TS],
                                              op=ALU.mult)
                        ctx.tag(ins, "ve")
                    v_y[dd] = ctx.cnt["ve"]

                @blk.gpsimd
                def _(e, ex=ex, j=j, th=th, T0=T0, v_last=v_y[DC - 1]):
                    e.wait_ge(S_["ve"], v_last)
                    if ex == 0:
                        ctx.dma(e, fm_view(fap, T0 + j * TS, TS),
                                sb3(sy[j % 2], DC, TS), "qg")
                    else:
                        ctx.dma(e, fm_view(fap, T0 + j * TS, TS),
                                sb3(sy[j % 2], DC, TS), "qg", accum_op=ALU.add)
                    e.wait_ge(S_["qg"], ctx.cnt["qg"])
                    ins = e.engine_nop()
                    ctx.tag(ins, "gp")
                g_y[j] = ctx.cnt["gp"]
            prev_p2 = ctx.cnt["mm"]


def ph_lstm_gather(ctx, L):
    xnap = ctx.buf(f"xn{L}", (D, T), BF16)
    agin = ctx.buf(f"ag{L}i", (D, T), BF16)
    agout = ctx.buf(f"ag{L}o", (8, D, T), BF16)
    xall = ctx.buf(f"xa{L}", (D, T), BF16)
    gidx = ctx.inp("lstm_gidx", (128, 64), U32)
    S_ = ctx.sems
    blk = ctx.block
    sidx = ctx.sbuf(f"lg{L}_i", [128, 64], U32)
    sx = [ctx.sbuf(f"lg{L}_x{i}", [128, 8 * 512], BF16) for i in range(2)]

    @blk.gpsimd
    def _(e):
        ctx.dma(e, sidx[:], gidx[:], "qg")
        ctx.dma(e, agin[:, :], xnap[:, :], "qg")
        e.wait_ge(S_["qg"], ctx.cnt["qg"])
        ins = e.collective_compute(
            "AllGather", ALU.bypass, replica_groups=[list(range(NCORES))],
            ins=[agin.opt()], outs=[agout.opt()])
        ctx.tag(ins, "qv", 1)
    ag_done = ctx.cnt["qv"]

    flat = agout.rearrange("t d (b r) -> (t d b) r", r=512)
    g_st = {}
    for ch in range(DC):
        for t in range(8):
            @blk.gpsimd
            def _(e, ch=ch, t=t):
                if ch == 0 and t == 0:
                    e.wait_ge(S_["qv"], ag_done)
                if t == 0 and ch >= 2:
                    e.wait_ge(S_["gp"], g_st[ch - 2])
                from concourse.bass import IndirectOffsetOnAxis
                ins = e.indirect_dma_start(
                    out=sx[ch % 2][:, t * 512:(t + 1) * 512],
                    out_offset=None,
                    in_=flat,
                    in_offset=IndirectOffsetOnAxis(ap=sidx[:, t * 8 + ch:t * 8 + ch + 1],
                                                   axis=0))
                ctx.tag(ins, "qg", 16)

        @blk.gpsimd
        def _(e, ch=ch):
            e.wait_ge(S_["qg"], ctx.cnt["qg"])
            ctx.dma(e, xall[ch * 128:(ch + 1) * 128, :], sx[ch % 2][:, :], "qg")
            e.wait_ge(S_["qg"], ctx.cnt["qg"])
            ins = e.engine_nop()
            ctx.tag(ins, "gp")
        g_st[ch] = ctx.cnt["gp"]


def ph_lstm_scan(ctx, L):
    xwf = ctx.buf(f"xw{L}f", (D, T), BF16)
    xwi = ctx.buf(f"xw{L}i", (D, T), BF16)
    xwc = ctx.buf(f"xw{L}c", (D, T), BF16)
    xwo = ctx.buf(f"xw{L}o", (D, T), BF16)
    hband = ctx.buf(f"hb{L}", (D, T), F32)
    uf = ctx.inp(f"l{L}f_uf", (D, D), BF16)
    ui = ctx.inp(f"l{L}f_ui", (D, D), BF16)
    uc = ctx.inp(f"l{L}f_uc", (D, D), BF16)
    uo = ctx.inp(f"l{L}f_uo", (D, D), BF16)
    incap = ctx.inp(f"l{L}f_inc", (128, DC), F32)
    idap = ctx.inp("identb", (128, 128), BF16)

    S_ = ctx.sems
    blk = ctx.block
    TS = 512
    sU = {g: ctx.sbuf(f"ls{L}_u{g}", [128, DC * D], BF16) for g in "fico"}
    sxw = {g: ctx.sbuf(f"ls{L}_xw{g}", [128, DC * TS], BF16) for g in "fico"}
    sgt = {g: ctx.sbuf(f"ls{L}_g{g}", [128, DC * TS], F32) for g in "fic"}
    sc_ = ctx.sbuf(f"ls{L}_c", [128, DC * TS], F32)
    scb = ctx.sbuf(f"ls{L}_cb", [128, DC * TS], BF16)
    sh = ctx.sbuf(f"ls{L}_h", [128, DC * TS], F32)
    sinc = ctx.sbuf(f"ls{L}_in", [128, DC], F32)
    sid = ctx.sbuf(f"ls{L}_id", [128, 128], BF16)
    ps = [ctx.psum(f"ls{L}_p{i}", [128, TS]) for i in range(2)]

    @blk.sync
    def _(e):
        for g, ap in (("f", uf), ("i", ui), ("c", uc), ("o", uo)):
            ctx.dma(e, sU[g].rearrange("p (c n) -> p c n", c=DC),
                    ap.rearrange("(c p) n -> p c n", p=128), "qs")
        ctx.dma(e, sinc[:, :], incap[:, :], "qs")
        ctx.dma(e, sid[:], idap[:], "qs")
    u_rdy = ctx.cnt["qs"]

    @blk.vector
    def _(e):
        e.wait_ge(S_["qs"], u_rdy)
        ins = e.memset(sc_[:], 0.0)
        c1 = ctx.tag(ins, "ve")
        e.wait_ge(S_["ve"], c1)
        for ch in range(DC):
            ins = e.tensor_scalar(out=sc_[:, ch * TS:(ch + 1) * TS],
                                  in0=sc_[:, ch * TS:(ch + 1) * TS],
                                  scalar1=sinc[:, ch:ch + 1], scalar2=None,
                                  op0=ALU.add)
        c2 = ctx.tag(ins, "ve")
        e.wait_ge(S_["ve"], c2)
        ins = e.tensor_copy(out=scb[:], in_=sc_[:])
        ctx.tag(ins, "ve")
    cb_rdy = ctx.cnt["ve"]

    ACTF = {"f": AF.Sigmoid, "i": AF.Sigmoid, "c": AF.Tanh}
    prev_cb = cb_rdy
    for t in range(8):
        @blk.sync
        def _(e, t=t):
            if t >= 1:
                e.wait_ge(S_["mm"], ctx.cnt["mm"])
            ctx.fence(e, "qs")
            for g, ap in (("f", xwf), ("i", xwi), ("c", xwc)):
                ctx.dma(e, sb3(sxw[g], DC, TS), fm_view(ap, t * TS, TS), "qs")
        qs_t = ctx.cnt["qs"]

        s_ev = {}
        git = 0
        for g in "fic":
            for oc in range(DC):
                @blk.tensor
                def _(e, t=t, g=g, oc=oc, git=git, qs_t=qs_t, prev_cb=prev_cb):
                    if git == 0:
                        e.wait_ge(S_["qs"], qs_t)
                        e.wait_ge(S_["ve"], prev_cb)
                    if not (t == 0 and git <= 1):
                        e.wait_ge(S_["sc"], s_ev.get(git - 2, ctx.cnt["sc"]))
                    bank = ps[git % 2]
                    for c in range(DC):
                        e.matmul(bank[:], sU[g][:, (c * DC + oc) * 128:(c * DC + oc) * 128 + 128],
                                 scb[:, c * TS:(c + 1) * TS],
                                 start=(c == 0), stop=False)
                    ins = e.matmul(bank[:], sid[:], sxw[g][:, oc * TS:(oc + 1) * TS],
                                   start=False, stop=True)
                    ctx.tag(ins, "mm")
                t_g = ctx.cnt["mm"]

                @blk.scalar
                def _(e, g=g, oc=oc, t_g=t_g, git=git):
                    e.wait_ge(S_["mm"], t_g)
                    ins = e.activation(out=sgt[g][:, oc * TS:(oc + 1) * TS],
                                       in_=ps[git % 2][:], func=ACTF[g])
                    ctx.tag(ins, "sc")
                s_ev[git] = ctx.cnt["sc"]
                git += 1

        for oc in range(DC):
            @blk.vector
            def _(e, t=t, oc=oc, s_ev=dict(s_ev)):
                e.wait_ge(S_["sc"], s_ev[0 * DC + oc])
                e.wait_ge(S_["sc"], s_ev[1 * DC + oc])
                e.wait_ge(S_["sc"], s_ev[2 * DC + oc])
                sl = slice(oc * TS, (oc + 1) * TS)
                ins = e.tensor_tensor(out=sc_[:, sl], in0=sgt["f"][:, sl],
                                      in1=sc_[:, sl], op=ALU.mult)
                c1 = ctx.tag(ins, "ve")
                e.wait_ge(S_["ve"], c1)
                ins = e.tensor_tensor(out=sh[:, sl], in0=sgt["i"][:, sl],
                                      in1=sgt["c"][:, sl], op=ALU.mult)
                c2 = ctx.tag(ins, "ve")
                e.wait_ge(S_["ve"], c2)
                ins = e.tensor_tensor(out=sc_[:, sl], in0=sc_[:, sl],
                                      in1=sh[:, sl], op=ALU.add)
                ctx.tag(ins, "ve")

        @blk.vector
        def _(e, t=t, s_ev=dict(s_ev)):
            e.wait_ge(S_["sc"], s_ev[3 * DC - 1])
            e.wait_ge(S_["ve"], ctx.cnt["ve"])
            ins = e.tensor_copy(out=scb[:], in_=sc_[:])
            ctx.tag(ins, "ve")
        prev_cb = ctx.cnt["ve"]

    for t in range(8):
        @blk.sync
        def _(e, t=t):
            if t >= 1:
                e.wait_ge(S_["mm"], ctx.cnt["mm"])
            ctx.fence(e, "qs")
            ctx.dma(e, sb3(sxw["o"], DC, TS), fm_view(xwo, t * TS, TS), "qs")
        qs_t = ctx.cnt["qs"]

        s_hv = {}
        for oc in range(DC):
            @blk.tensor
            def _(e, t=t, oc=oc, qs_t=qs_t, prev_cb=prev_cb):
                if oc == 0:
                    e.wait_ge(S_["qs"], qs_t)
                    if t == 0:
                        e.wait_ge(S_["ve"], prev_cb)
                if not (t == 0 and oc <= 1):
                    e.wait_ge(S_["sc"], s_hv.get(oc - 2, ctx.cnt["sc"]))
                bank = ps[oc % 2]
                for c in range(DC):
                    e.matmul(bank[:], sU["o"][:, (c * DC + oc) * 128:(c * DC + oc) * 128 + 128],
                             scb[:, c * TS:(c + 1) * TS], start=(c == 0), stop=False)
                ins = e.matmul(bank[:], sid[:], sxw["o"][:, oc * TS:(oc + 1) * TS],
                               start=False, stop=True)
                ctx.tag(ins, "mm")
            t_h = ctx.cnt["mm"]

            @blk.scalar
            def _(e, oc=oc, t_h=t_h):
                e.wait_ge(S_["mm"], t_h)
                ins = e.activation(out=sh[:, oc * TS:(oc + 1) * TS],
                                   in_=ps[oc % 2][:], func=AF.Tanh)
                ctx.tag(ins, "sc")
            s_hv[oc] = ctx.cnt["sc"]

        @blk.gpsimd
        def _(e, t=t, s_last=None):
            e.wait_ge(S_["sc"], ctx.cnt["sc"])
            ctx.dma(e, fm_view(hband, t * TS, TS), sb3(sh, DC, TS), "qg")
            e.wait_ge(S_["qg"], ctx.cnt["qg"])
            ins = e.engine_nop()
            ctx.tag(ins, "gp")


def ph_lstm_return(ctx, L):
    hband = ctx.buf(f"hb{L}", (D, T), F32)
    hbb = ctx.buf(f"hb{L}b", (D, T), F32)
    ffap = ctx.buf(f"ffn{L}", (D, T), F32)
    agout2 = ctx.buf(f"ag{L}o2", (8, D, T), F32)
    ridx = ctx.inp("lstm_gidx", (128, 64), U32)
    S_ = ctx.sems
    blk = ctx.block
    sidx = ctx.sbuf(f"lr{L}_i", [128, 64], U32)
    sx = [ctx.sbuf(f"lr{L}_x{i}", [128, 8 * 512], F32) for i in range(2)]

    @blk.gpsimd
    def _(e):
        ctx.dma(e, sidx[:], ridx[:], "qg")
        ctx.dma(e, hbb[:, :], hband[:, :], "qg")
        e.wait_ge(S_["qg"], ctx.cnt["qg"])
        ins = e.collective_compute(
            "AllGather", ALU.bypass, replica_groups=[list(range(NCORES))],
            ins=[hbb.opt()], outs=[agout2.opt()])
        ctx.tag(ins, "qv", 1)
    ag_done = ctx.cnt["qv"]

    flat = agout2.rearrange("t d (b r) -> (t d b) r", r=512)
    g_st = {}
    for ch in range(DC):
        for t in range(8):
            @blk.gpsimd
            def _(e, ch=ch, t=t):
                if ch == 0 and t == 0:
                    e.wait_ge(S_["qv"], ag_done)
                if t == 0 and ch >= 2:
                    e.wait_ge(S_["gp"], g_st[ch - 2])
                from concourse.bass import IndirectOffsetOnAxis
                ins = e.indirect_dma_start(
                    out=sx[ch % 2][:, t * 512:(t + 1) * 512],
                    out_offset=None,
                    in_=flat,
                    in_offset=IndirectOffsetOnAxis(ap=sidx[:, t * 8 + ch:t * 8 + ch + 1],
                                                   axis=0))
                ctx.tag(ins, "qg", 16)

        @blk.gpsimd
        def _(e, ch=ch):
            e.wait_ge(S_["qg"], ctx.cnt["qg"])
            ctx.dma(e, ffap[ch * 128:(ch + 1) * 128, :], sx[ch % 2][:, :], "qg")
            e.wait_ge(S_["qg"], ctx.cnt["qg"])
            ins = e.engine_nop()
            ctx.tag(ins, "gp")
        g_st[ch] = ctx.cnt["gp"]


def lstm_phases(L):
    return [
        lambda c: ph_lstm_gather(c, L),
        lambda c: ph_mm(c, f"xl{L}f", f"xa{L}", 1024, f"l{L}f_wf", f"xw{L}f", 1024,
                        "fm", None, biascol=f"l{L}f_bf"),
        lambda c: ph_mm(c, f"xl{L}i", f"xa{L}", 1024, f"l{L}f_wi", f"xw{L}i", 1024,
                        "fm", None, biascol=f"l{L}f_bi"),
        lambda c: ph_mm(c, f"xl{L}c", f"xa{L}", 1024, f"l{L}f_wc", f"xw{L}c", 1024,
                        "fm", None, biascol=f"l{L}f_bc"),
        lambda c: ph_mm(c, f"xl{L}o", f"xa{L}", 1024, f"l{L}f_wo", f"xw{L}o", 1024,
                        "fm", None, biascol=f"l{L}f_bo"),
        lambda c: ph_lstm_scan(c, L),
        lambda c: ph_lstm_return(c, L),
    ]


def lstm_host_indices(core_id):
    idx = np.zeros((128, 64), np.uint32)
    for t in range(8):
        for ch in range(DC):
            p = np.arange(128)
            idx[:, t * 8 + ch] = ((t * 1024 + ch * 128 + p) * 8 + core_id).astype(np.uint32)
    return idx


def ph_l3_qsm(ctx, scale=0.25):
    qap = ctx.buf("l3q", (T, D), BF16)
    oap = ctx.buf("l3qsm", (T, D), BF16)
    S_ = ctx.sems
    blk = ctx.block
    sq = [ctx.sbuf(f"q3_{i}", [128, D], BF16) for i in range(2)]
    se = ctx.sbuf("q3_e", [128, D], F32)
    so = [ctx.sbuf(f"q3_o{i}", [128, D], BF16) for i in range(2)]
    scol = ctx.sbuf("q3_c", [128, 12], F32)

    for i in range(T // 128):
        @blk.sync
        def _(e, i=i):
            if i >= 2:
                e.wait_ge(S_["gp"], ctx.cnt["gp"] - 1)
            ctx.fence(e, "qs")
            ctx.dma(e, sq[i % 2][:], qap[i * 128:(i + 1) * 128, :], "qs")
        qs_i = ctx.cnt["qs"]

        @blk.vector
        def _(e, i=i, qs_i=qs_i):
            e.wait_ge(S_["qs"], qs_i)
            if i >= 1:
                e.wait_ge(S_["ve"], ctx.cnt["ve"])
                e.wait_ge(S_["sc"], ctx.cnt["sc"])
            ins = e.tensor_reduce(out=scol[:, 0:4],
                                  in_=sq[i % 2][:].rearrange("p (h n) -> p h n", h=4),
                                  axis=AX.X, op=ALU.max)
            c1 = ctx.tag(ins, "ve")
            e.wait_ge(S_["ve"], c1)
            ins = e.tensor_scalar(out=scol[:, 4:8], in0=scol[:, 0:4],
                                  scalar1=-scale, scalar2=None, op0=ALU.mult)
            ctx.tag(ins, "ve")
        v_mx = ctx.cnt["ve"]

        s_h = {}
        for h in range(4):
            @blk.scalar
            def _(e, i=i, h=h, v_mx=v_mx):
                e.wait_ge(S_["ve"], v_mx)
                ins = e.activation(out=se[:, h * 256:(h + 1) * 256],
                                   in_=sq[i % 2][:, h * 256:(h + 1) * 256],
                                   func=AF.Exp, scale=scale,
                                   bias=scol[:, 4 + h:5 + h],
                                   accum_out=scol[:, 8 + h:9 + h])
                ctx.tag(ins, "sc")
            s_h[h] = ctx.cnt["sc"]

        @blk.vector
        def _(e, i=i, s_h=dict(s_h)):
            e.wait_ge(S_["sc"], s_h[3])
            ins = e.reciprocal(out=scol[:, 8:12], in_=scol[:, 8:12])
            c1 = ctx.tag(ins, "ve")
            e.wait_ge(S_["ve"], c1)
            for h in range(4):
                ins = e.tensor_scalar(out=so[i % 2][:, h * 256:(h + 1) * 256],
                                      in0=se[:, h * 256:(h + 1) * 256],
                                      scalar1=scol[:, 8 + h:9 + h], scalar2=None,
                                      op0=ALU.mult)
            ctx.tag(ins, "ve")
        v_o = ctx.cnt["ve"]

        @blk.gpsimd
        def _(e, i=i, v_o=v_o):
            e.wait_ge(S_["ve"], v_o)
            ctx.dma(e, oap[i * 128:(i + 1) * 128, :], so[i % 2][:], "qg")
            e.wait_ge(S_["qg"], ctx.cnt["qg"])
            ins = e.engine_nop()
            ctx.tag(ins, "gp")


def ph_l3_ksm(ctx, scale=0.25):
    kap = ctx.buf("l3k", (D, T), BF16)
    eap = ctx.buf("l3ek", (D, T), BF16)
    sap = ctx.buf("l3eks", (128, DC), F32)
    S_ = ctx.sems
    blk = ctx.block
    sk = [ctx.sbuf(f"k3_{i}", [128, T], BF16) for i in range(2)]
    se = [ctx.sbuf(f"k3_e{i}", [128, T], BF16) for i in range(2)]
    scol = ctx.sbuf("k3_c", [128, 2 + DC], F32)

    for ch in range(DC):
        @blk.sync
        def _(e, ch=ch):
            if ch >= 2:
                e.wait_ge(S_["gp"], ctx.cnt["gp"] - 1)
            ctx.fence(e, "qs")
            ctx.dma(e, sk[ch % 2][:], kap[ch * 128:(ch + 1) * 128, :], "qs")
        qs_c = ctx.cnt["qs"]

        @blk.vector
        def _(e, ch=ch, qs_c=qs_c):
            e.wait_ge(S_["qs"], qs_c)
            if ch >= 1:
                e.wait_ge(S_["sc"], ctx.cnt["sc"])
            ins = e.tensor_reduce(out=scol[:, 0:1], in_=sk[ch % 2][:],
                                  axis=AX.X, op=ALU.max)
            c1 = ctx.tag(ins, "ve")
            e.wait_ge(S_["ve"], c1)
            ins = e.tensor_scalar(out=scol[:, 1:2], in0=scol[:, 0:1],
                                  scalar1=-scale, scalar2=None, op0=ALU.mult)
            ctx.tag(ins, "ve")
        v_mx = ctx.cnt["ve"]

        @blk.scalar
        def _(e, ch=ch, v_mx=v_mx):
            e.wait_ge(S_["ve"], v_mx)
            ins = e.activation(out=se[ch % 2][:], in_=sk[ch % 2][:],
                               func=AF.Exp, scale=scale, bias=scol[:, 1:2],
                               accum_out=scol[:, 2 + ch:3 + ch])
            ctx.tag(ins, "sc")
        s_e = ctx.cnt["sc"]

        @blk.gpsimd
        def _(e, ch=ch, s_e=s_e):
            e.wait_ge(S_["sc"], s_e)
            ctx.dma(e, eap[ch * 128:(ch + 1) * 128, :], se[ch % 2][:], "qg")
            if ch == DC - 1:
                ctx.dma(e, sap[:, :], scol[:, 2:2 + DC], "qg")
            e.wait_ge(S_["qg"], ctx.cnt["qg"])
            ins = e.engine_nop()
            ctx.tag(ins, "gp")


def ph_l3_ctx(ctx):
    eap = ctx.buf("l3ek", (D, T), BF16)
    sap = ctx.buf("l3eks", (128, DC), F32)
    vap = ctx.buf("l3v", (T, D), BF16)
    qap = ctx.buf("l3qsm", (T, D), BF16)
    aap = ctx.buf("l3at", (T, D), BF16)
    S_ = ctx.sems
    blk = ctx.block
    NTC = T // 128
    sek = [[ctx.sbuf(f"c3_ek{half}{i}", [128, 128], BF16) for i in range(2)]
           for half in range(2)]
    sv = [ctx.sbuf(f"c3_v{i}", [128, 256], BF16) for i in range(2)]
    sqh = [[ctx.sbuf(f"c3_q{half}{i}", [128, 128], BF16) for i in range(2)]
           for half in range(2)]
    sctx = ctx.sbuf("c3_cx", [128, 2 * 256], BF16)
    ssum = ctx.sbuf("c3_s", [128, DC], F32)
    sat = [ctx.sbuf(f"c3_a{i}", [128, 256], BF16) for i in range(2)]
    ps_c = [ctx.psum(f"c3_pc{i}", [128, 256]) for i in range(2)]
    ps_a = [ctx.psum(f"c3_pa{i}", [128, 256]) for i in range(2)]

    @blk.sync
    def _(e):
        ctx.dma(e, ssum[:, :], sap[:, :], "qs")
    sum_rdy = ctx.cnt["qs"]

    @blk.vector
    def _(e):
        e.wait_ge(S_["qs"], sum_rdy)
        ins = e.reciprocal(out=ssum[:], in_=ssum[:])
        ctx.tag(ins, "ve")
    rs_rdy = ctx.cnt["ve"]

    for h in range(4):
        t_cx = {}
        for tc in range(NTC):
            @blk.sync
            def _(e, h=h, tc=tc, t_cx=t_cx):
                if tc >= 2:
                    e.wait_ge(S_["mm"], t_cx[tc - 2])
                elif h > 0:
                    e.wait_ge(S_["ve"], ctx.cnt["ve"])
                ctx.fence(e, "qs")
                for half in range(2):
                    dkrow = h * 256 + half * 128
                    ctx.dma(e, sek[half][tc % 2][:],
                            eap[dkrow:dkrow + 128, tc * 128:(tc + 1) * 128],
                            "qs", transpose=True)
                ctx.dma(e, sv[tc % 2][:],
                        vap[tc * 128:(tc + 1) * 128, h * 256:(h + 1) * 256], "qs")
            qs_t = ctx.cnt["qs"]

            @blk.tensor
            def _(e, h=h, tc=tc, qs_t=qs_t):
                e.wait_ge(S_["qs"], qs_t)
                if tc == 0 and h > 0:
                    e.wait_ge(S_["ve"], ctx.cnt["ve"])
                for half in range(2):
                    ins = e.matmul(ps_c[half][:], sek[half][tc % 2][:],
                                   sv[tc % 2][:], start=(tc == 0),
                                   stop=(tc == NTC - 1))
                ctx.tag(ins, "mm")
            t_cx[tc] = ctx.cnt["mm"]

        @blk.vector
        def _(e, h=h, t_last=t_cx[NTC - 1], rs_rdy=rs_rdy):
            e.wait_ge(S_["mm"], t_last)
            e.wait_ge(S_["ve"], rs_rdy)
            for half in range(2):
                ins = e.tensor_scalar(out=sctx[:, half * 256:(half + 1) * 256],
                                      in0=ps_c[half][:],
                                      scalar1=ssum[:, 2 * h + half:2 * h + half + 1],
                                      scalar2=None, op0=ALU.mult)
            ctx.tag(ins, "ve")
        cx_rdy = ctx.cnt["ve"]

        t_at = {}
        s_at = {}
        for tc in range(NTC):
            @blk.sync
            def _(e, h=h, tc=tc, t_at=t_at):
                if tc >= 2:
                    e.wait_ge(S_["mm"], t_at[tc - 2])
                ctx.fence(e, "qs")
                for half in range(2):
                    ctx.dma(e, sqh[half][tc % 2][:],
                            qap[tc * 128:(tc + 1) * 128,
                                h * 256 + half * 128:h * 256 + half * 128 + 128],
                            "qs", transpose=True)
            qs_t = ctx.cnt["qs"]

            @blk.tensor
            def _(e, h=h, tc=tc, qs_t=qs_t, cx_rdy=cx_rdy, s_at=s_at):
                e.wait_ge(S_["qs"], qs_t)
                if tc == 0:
                    e.wait_ge(S_["ve"], cx_rdy)
                if tc >= 2:
                    e.wait_ge(S_["sc"], s_at[tc - 2])
                for half in range(2):
                    ins = e.matmul(ps_a[tc % 2][:], sqh[half][tc % 2][:],
                                   sctx[:, half * 256:(half + 1) * 256],
                                   start=(half == 0), stop=(half == 1))
                ctx.tag(ins, "mm")
            t_at[tc] = ctx.cnt["mm"]

            @blk.scalar
            def _(e, tc=tc, t_a=t_at[tc]):
                e.wait_ge(S_["mm"], t_a)
                ins = e.activation(out=sat[tc % 2][:], in_=ps_a[tc % 2][:],
                                   func=AF.Copy)
                ctx.tag(ins, "sc")
            s_at[tc] = ctx.cnt["sc"]

            @blk.gpsimd
            def _(e, h=h, tc=tc, s_a=s_at[tc]):
                e.wait_ge(S_["sc"], s_a)
                ctx.dma(e, aap[tc * 128:(tc + 1) * 128, h * 256:(h + 1) * 256],
                        sat[tc % 2][:], "qg")
                e.wait_ge(S_["qg"], ctx.cnt["qg"])
                ins = e.engine_nop()
                ctx.tag(ins, "gp")


def l3_attn_phases():
    return [
        lambda c: ph_ln_g(c, "l31", "xnf3", "l3h1", gain="l3_ln1g", bias="l3_ln1b"),
        lambda c: ph_mm(c, "l3q", "l3h1", 1024, "l3_wq", "l3q", 1024, "tok"),
        lambda c: ph_mm(c, "l3k", "l3h1", 1024, "l3_wk", "l3k", 1024, "fm"),
        lambda c: ph_mm(c, "l3v", "l3h1", 1024, "l3_wv", "l3v", 1024, "tok"),
        ph_l3_qsm,
        ph_l3_ksm,
        ph_l3_ctx,
        lambda c: ph_mm(c, "l3wo", "l3at", 1024, "l3_wo", "l3x2", 1024, "fm",
                        None, biascol="l3_bo", addsrc="xnf3", dst_dt=F32,
                        src_tok=True),
        lambda c: ph_ln_g(c, "l32", "l3x2", "l3h2", gain="l3_ln2g", bias="l3_ln2b"),
        lambda c: ph_mm(c, "l3f1", "l3h2", 1024, "l3_ff1", "l3f1", 4096, "fm",
                        AF.Gelu_apprx_tanh, biascol="l3_bff1"),
        lambda c: ph_mm(c, "l3f2", "l3f1", 4096, "l3_ff2", "l3x3", 1024, "fm",
                        None, biascol="l3_bff2", addsrc="l3x2", dst_dt=BF16),
        lambda c: ph_mm(c, "l3wd", "l3x3", 1024, "l3_wd", "attn3", 1024, "fm",
                        None, biascol="l3_bd", dst_dt=F32),
    ]


def ph_l2_ttt(ctx):
    xqap = ctx.buf("l2xq", (256, T), BF16)
    xkap = ctx.buf("l2xk", (256, T), BF16)
    xvap = ctx.buf("l2xv", (256, T), BF16)
    xqtp = ctx.buf("l2xqt", (T, 256), BF16)
    xktp = ctx.buf("l2xkt", (T, 256), BF16)
    xvtp = ctx.buf("l2xvt", (T, 256), BF16)
    lrap = ctx.buf("l2lr", (T, 2), F32)
    oap = ctx.buf("l2o", (T, 256), F32)
    w1ap = ctx.inp("l2_W1", (2, 128, 128), F32)
    b1ap = ctx.inp("l2_b1r", (1, 256), F32)
    lnwap = ctx.inp("l2_lnwr", (1, 256), F32)
    lnbap = ctx.inp("l2_lnbr", (1, 256), F32)
    tokap = ctx.inp("l2_tok", (1, 16), F32)
    invkap = ctx.inp("invk", (1, 16), F32)
    mkap = ctx.inp("t_inclT", (16, 16), F32)
    S_ = ctx.sems
    blk = ctx.block
    NS = 256

    sxq = ctx.sbuf("t2_xq", [128, 2 * T], BF16)
    sxk = ctx.sbuf("t2_xk", [128, 2 * T], BF16)
    sxv = ctx.sbuf("t2_xv", [128, 2 * T], BF16)
    sW1 = ctx.sbuf("t2_W1", [128, 256], F32)
    sW1b = ctx.sbuf("t2_W1b", [128, 256], BF16)
    sb1 = ctx.sbuf("t2_b1", [1, 256], F32)
    sb1b = ctx.sbuf("t2_b1b", [1, 256], BF16)
    sone1 = ctx.sbuf("t2_one", [1, 16], F32)
    sone1b = ctx.sbuf("t2_oneb", [1, 16], BF16)
    srow = ctx.sbuf("t2_row", [1, 4 * 256], F32)
    sgw = ctx.sbuf("t2_gw", [16, 256], F32)
    sgb = ctx.sbuf("t2_gb", [16, 256], F32)
    stir = ctx.sbuf("t2_tir", [16, 16], F32)
    stii = ctx.sbuf("t2_tii", [16, 16], F32)
    smk = ctx.sbuf("t2_mk", [16, 16], F32)
    slr = [ctx.sbuf(f"t2_lr{i}", [16, 2], F32) for i in range(2)]
    snlr = ctx.sbuf("t2_nlr", [16, 4], F32)
    snlrb = ctx.sbuf("t2_nlrb", [16, 4], BF16)
    sxkt = [ctx.sbuf(f"t2_kt{i}", [16, 256], BF16) for i in range(2)]
    sxvt = [ctx.sbuf(f"t2_vt{i}", [16, 256], BF16) for i in range(2)]
    sxqt = [ctx.sbuf(f"t2_qt{i}", [16, 256], BF16) for i in range(2)]
    sZ = ctx.sbuf("t2_Z", [16, 256], F32)
    sxc = ctx.sbuf("t2_xc", [16, 256], F32)
    sxh = ctx.sbuf("t2_xh", [16, 256], F32)
    st1 = ctx.sbuf("t2_t1", [16, 256], F32)
    sgy = ctx.sbuf("t2_gy", [16, 256], F32)
    sgr = ctx.sbuf("t2_gr", [16, 256], BF16)
    sam = ctx.sbuf("t2_am", [16, 64], BF16)
    slex = ctx.sbuf("t2_lex", [16, 256], BF16)
    ssq = ctx.sbuf("t2_sq", [16, 256], F32)
    scol = ctx.sbuf("t2_col", [16, 16], F32)
    sout = [ctx.sbuf(f"t2_o{i}", [16, 256], F32) for i in range(2)]
    psZ = ctx.psum("t2_pZ", [16, 256])
    psA = ctx.psum("t2_pA", [16, 32])
    psB = ctx.psum("t2_pB", [16, 256])
    psW = ctx.psum("t2_pW", [128, 256])
    psb = ctx.psum("t2_pb", [1, 256])
    psr = ctx.psum("t2_pr", [16, 256])

    MU, SS, S1, S2, RSTD, RDS, MAC, SAC = 0, 2, 4, 6, 8, 10, 12, 14
    dbg = ctx.buf("t2dbg", (8, 16, 256), F32) if TTT_DEBUG else None

    @blk.sync
    def _(e):
        ctx.dma(e, sW1.rearrange("a (h b) -> a h b", h=2),
                w1ap.rearrange("h a b -> a h b"), "qv")
        ctx.dma(e, sb1[:], b1ap[:], "qv")
        ctx.dma(e, srow[0:1, 0:256], lnwap[:], "qv")
        ctx.dma(e, srow[0:1, 256:512], lnbap[:], "qv")
        ctx.dma(e, srow[0:1, 512:528], invkap[:], "qv")
        ctx.dma(e, srow[0:1, 528:544], tokap[:], "qv")
        ctx.dma(e, smk[:], mkap[:], "qv")
        ctx.dma(e, sb3(sxq, 2, T), fm_view(xqap, 0, T, nch=2), "qv")
        ctx.dma(e, sb3(sxk, 2, T), fm_view(xkap, 0, T, nch=2), "qv")
        ctx.dma(e, sb3(sxv, 2, T), fm_view(xvap, 0, T, nch=2), "qv")
    ld_rdy = ctx.cnt["qv"]

    @blk.vector
    def _(e):
        e.wait_ge(S_["qv"], ld_rdy)
        ins = e.memset(sone1[:], 1.0)
        c = ctx.tag(ins, "ve"); e.wait_ge(S_["ve"], c)
        ins = e.memset(sone1b[:], 1.0)
        c = ctx.tag(ins, "ve"); e.wait_ge(S_["ve"], c)
        ins = e.tensor_copy(out=sb1b[:], in_=sb1[:])
        c = ctx.tag(ins, "ve"); e.wait_ge(S_["ve"], c)
        ins = e.tensor_tensor(out=srow[0:1, 512:528], in0=srow[0:1, 512:528],
                              in1=srow[0:1, 528:544], op=ALU.add)
        c = ctx.tag(ins, "ve"); e.wait_ge(S_["ve"], c)
        ins = e.tensor_scalar(out=srow[0:1, 512:528], in0=srow[0:1, 512:528],
                              scalar1=0.0, scalar2=None, op0=ALU.max)
        c = ctx.tag(ins, "ve"); e.wait_ge(S_["ve"], c)
        ins = e.tensor_copy(out=sW1b[:], in_=sW1[:])
        ctx.tag(ins, "ve")
    v_setup = ctx.cnt["ve"]

    reps = []
    for w0, w1_, dstt in ((0, 256, sgw), (256, 512, sgb), (512, 528, None)):
        @blk.tensor
        def _(e, w0=w0, w1_=w1_, first=(w0 == 0)):
            if first:
                e.wait_ge(S_["ve"], v_setup)
            else:
                e.wait_ge(S_["sc"], ctx.cnt["sc"])
            ins = e.matmul(psr[0:16, 0:w1_ - w0], sone1[:], srow[0:1, w0:w1_],
                           start=True, stop=True)
            ctx.tag(ins, "mm")
        t_r = ctx.cnt["mm"]

        @blk.scalar
        def _(e, w0=w0, w1_=w1_, t_r=t_r, dstt=dstt):
            e.wait_ge(S_["mm"], t_r)
            tgt = dstt[:] if dstt is not None else stir[:]
            ins = e.activation(out=tgt, in_=psr[0:16, 0:w1_ - w0], func=AF.Copy)
            ctx.tag(ins, "sc")
        reps.append(ctx.cnt["sc"])
    s_ti = ctx.cnt["sc"]

    @blk.vector
    def _(e, s_ti=s_ti):
        e.wait_ge(S_["sc"], s_ti)
        ins = e.tensor_tensor(out=stii[:], in0=stir[:], in1=smk[:], op=ALU.mult)
        ctx.tag(ins, "ve")
    v_tii = ctx.cnt["ve"]

    prev = dict(z_ev=0, a_done=v_tii, b_ev=0, state=v_tii, out=[0, 0], st=[0, 0])
    for n in range(NS):
        sl = n % 2
        t0 = n * 16

        @blk.sync
        def _(e, n=n, sl=sl, t0=t0, prev=dict(prev)):
            if n >= 2:
                e.wait_ge(S_["ve"], prev["out"][sl])
            ctx.fence(e, "qs")
            ctx.dma(e, slr[sl][:], lrap[t0:t0 + 16, :], "qs")
            ctx.dma(e, sxkt[sl][:], xktp[t0:t0 + 16, :], "qs")
            ctx.dma(e, sxvt[sl][:], xvtp[t0:t0 + 16, :], "qs")
            ctx.dma(e, sxqt[sl][:], xqtp[t0:t0 + 16, :], "qs")
        qs_n = ctx.cnt["qs"]

        @blk.tensor
        def _(e, n=n, t0=t0, prev=dict(prev)):
            if n == 0:
                e.wait_ge(S_["ve"], v_tii)
            else:
                e.wait_ge(S_["ve"], prev["state"])
                e.wait_ge(S_["sc"], prev["z_ev"])
                e.wait_ge(S_["ve"], prev["a_done"])
            for h in range(2):
                e.matmul(psZ[:, h * 128:(h + 1) * 128],
                         sxk[:, h * T + t0:h * T + t0 + 16],
                         sW1b[:, h * 128:(h + 1) * 128], start=(h == 0), stop=False)
            for h in range(2):
                e.matmul(psZ[:, h * 128:(h + 1) * 128], sone1b[:],
                         sb1b[0:1, h * 128:(h + 1) * 128], start=False, stop=(h == 1))
            for h in range(2):
                ins = e.matmul(psA[:, h * 16:(h + 1) * 16],
                               sxk[:, h * T + t0:h * T + t0 + 16],
                               sxq[:, h * T + t0:h * T + t0 + 16],
                               start=(h == 0), stop=(h == 1))
            ctx.tag(ins, "mm")
            if n > 0:
                e.wait_ge(S_["sc"], prev["b_ev"])
            for h in range(2):
                e.matmul(psB[:, h * 128:(h + 1) * 128],
                         sxq[:, h * T + t0:h * T + t0 + 16],
                         sW1b[:, h * 128:(h + 1) * 128], start=(h == 0), stop=False)
            for h in range(2):
                ins = e.matmul(psB[:, h * 128:(h + 1) * 128], sone1b[:],
                               sb1b[0:1, h * 128:(h + 1) * 128],
                               start=False, stop=False)
            ctx.tag(ins, "mm")
        t_z = ctx.cnt["mm"] - 1
        t_bp1 = ctx.cnt["mm"]

        @blk.scalar
        def _(e, n=n, t_z=t_z, prev=dict(prev)):
            e.wait_ge(S_["mm"], t_z)
            if n > 0:
                e.wait_ge(S_["ve"], prev["a_done"])
            e.activation(out=sZ[:, 0:128], in_=psZ[:, 0:128], func=AF.Copy,
                         accum_out=scol[:, MAC:MAC + 1])
            ins = e.activation(out=sZ[:, 128:256], in_=psZ[:, 128:256], func=AF.Copy,
                               accum_out=scol[:, MAC + 1:MAC + 2])
            ctx.tag(ins, "sc")
        s_z = ctx.cnt["sc"]

        if TTT_DEBUG and n == 0:
            @blk.gpsimd
            def _(e, s_z=s_z):
                e.wait_ge(S_["sc"], s_z)
                ctx.dma(e, dbg[3], sZ[:], "qg")
                ctx.dma(e, dbg[4][0:16, :], sgw[:], "qg")
                ctx.dma(e, dbg[5][0:16, :], sgb[:], "qg")
                e.wait_ge(S_["qg"], ctx.cnt["qg"])
                ins = e.engine_nop()
                ctx.tag(ins, "gp")

        @blk.vector
        def _(e, n=n, sl=sl, s_z=s_z, qs_n=qs_n):
            e.wait_ge(S_["sc"], s_z)
            e.wait_ge(S_["qs"], qs_n)
            if TTT_DEBUG and n == 0:
                e.wait_ge(S_["gp"], ctx.cnt["gp"])
            ins = e.tensor_scalar(out=scol[:, MU:MU + 2], in0=scol[:, MAC:MAC + 2],
                                  scalar1=1.0 / 128.0, scalar2=None, op0=ALU.mult)
            c = ctx.tag(ins, "ve"); e.wait_ge(S_["ve"], c)
            for h in range(2):
                ins = e.tensor_scalar(out=sxc[:, h * 128:(h + 1) * 128],
                                      in0=sZ[:, h * 128:(h + 1) * 128],
                                      scalar1=scol[:, MU + h:MU + h + 1],
                                      scalar2=None, op0=ALU.subtract)
            ctx.tag(ins, "ve")
        v_xc = ctx.cnt["ve"]

        @blk.scalar
        def _(e, v_xc=v_xc):
            e.wait_ge(S_["ve"], v_xc)
            e.activation(out=ssq[:, 0:128], in_=sxc[:, 0:128], func=AF.Square,
                         accum_out=scol[:, SAC:SAC + 1])
            ins = e.activation(out=ssq[:, 128:256], in_=sxc[:, 128:256], func=AF.Square,
                               accum_out=scol[:, SAC + 1:SAC + 2])
            ctx.tag(ins, "sc")
        s_sq = ctx.cnt["sc"]

        @blk.vector
        def _(e, s_sq=s_sq):
            e.wait_ge(S_["sc"], s_sq)
            ins = e.tensor_scalar(out=scol[:, SS:SS + 2], in0=scol[:, SAC:SAC + 2],
                                  scalar1=1.0 / 128.0, scalar2=1e-6,
                                  op0=ALU.mult, op1=ALU.add)
            ctx.tag(ins, "ve")
        v_var = ctx.cnt["ve"]

        @blk.scalar
        def _(e, v_var=v_var):
            e.wait_ge(S_["ve"], v_var)
            ins = e.activation(out=scol[:, RSTD:RSTD + 2], in_=scol[:, SS:SS + 2],
                               func=AF.Sqrt)
            ctx.tag(ins, "sc")
        s_std = ctx.cnt["sc"]

        @blk.vector
        def _(e, n=n, sl=sl, s_std=s_std):
            e.wait_ge(S_["sc"], s_std)
            ins = e.tensor_scalar(out=scol[:, RDS:RDS + 2], in0=scol[:, RSTD:RSTD + 2],
                                  scalar1=128.0, scalar2=None, op0=ALU.mult)
            c = ctx.tag(ins, "ve"); e.wait_ge(S_["ve"], c)
            ins = e.reciprocal(out=scol[:, RDS:RDS + 2], in_=scol[:, RDS:RDS + 2])
            c = ctx.tag(ins, "ve"); e.wait_ge(S_["ve"], c)
            ins = e.reciprocal(out=scol[:, RSTD:RSTD + 2], in_=scol[:, RSTD:RSTD + 2])
            c = ctx.tag(ins, "ve"); e.wait_ge(S_["ve"], c)
            for h in range(2):
                ins = e.tensor_scalar(out=sxh[:, h * 128:(h + 1) * 128],
                                      in0=sxc[:, h * 128:(h + 1) * 128],
                                      scalar1=scol[:, RSTD + h:RSTD + h + 1],
                                      scalar2=None, op0=ALU.mult)
            c = ctx.tag(ins, "ve"); e.wait_ge(S_["ve"], c)
            ins = e.tensor_tensor(out=st1[:], in0=sxh[:], in1=sgw[:], op=ALU.mult)
            c = ctx.tag(ins, "ve"); e.wait_ge(S_["ve"], c)
            ins = e.tensor_tensor(out=st1[:], in0=st1[:], in1=sgb[:], op=ALU.add)
            c = ctx.tag(ins, "ve"); e.wait_ge(S_["ve"], c)
            ins = e.tensor_tensor(out=st1[:], in0=st1[:], in1=sxvt[sl][:], op=ALU.subtract)
            c = ctx.tag(ins, "ve"); e.wait_ge(S_["ve"], c)
            ins = e.tensor_tensor(out=st1[:], in0=st1[:], in1=sxkt[sl][:], op=ALU.add)
            c = ctx.tag(ins, "ve"); e.wait_ge(S_["ve"], c)
            ins = e.tensor_tensor(out=sgy[:], in0=st1[:], in1=sgw[:], op=ALU.mult)
            c = ctx.tag(ins, "ve"); e.wait_ge(S_["ve"], c)
            ins = e.tensor_reduce(out=scol[:, S1:S1 + 2],
                                  in_=sgy[:].rearrange("p (h f) -> p h f", h=2),
                                  axis=AX.X, op=ALU.add)
            c = ctx.tag(ins, "ve"); e.wait_ge(S_["ve"], c)
            ins = e.tensor_tensor(out=st1[:], in0=sgy[:], in1=sxh[:], op=ALU.mult)
            c = ctx.tag(ins, "ve"); e.wait_ge(S_["ve"], c)
            ins = e.tensor_reduce(out=scol[:, S2:S2 + 2],
                                  in_=st1[:].rearrange("p (h f) -> p h f", h=2),
                                  axis=AX.X, op=ALU.add)
            c = ctx.tag(ins, "ve"); e.wait_ge(S_["ve"], c)
            for h in range(2):
                hs = slice(h * 128, (h + 1) * 128)
                ins = e.tensor_scalar(out=st1[:, hs], in0=sgy[:, hs], scalar1=128.0,
                                      scalar2=scol[:, S1 + h:S1 + h + 1],
                                      op0=ALU.mult, op1=ALU.subtract)
                c = ctx.tag(ins, "ve"); e.wait_ge(S_["ve"], c)
                ins = e.tensor_scalar(out=sxc[:, hs], in0=sxh[:, hs],
                                      scalar1=scol[:, S2 + h:S2 + h + 1],
                                      scalar2=None, op0=ALU.mult)
                c = ctx.tag(ins, "ve"); e.wait_ge(S_["ve"], c)
                ins = e.tensor_tensor(out=st1[:, hs], in0=st1[:, hs], in1=sxc[:, hs],
                                      op=ALU.subtract)
                c = ctx.tag(ins, "ve"); e.wait_ge(S_["ve"], c)
                ins = e.tensor_scalar(out=sgr[:, hs], in0=st1[:, hs],
                                      scalar1=scol[:, RDS + h:RDS + h + 1],
                                      scalar2=None, op0=ALU.mult)
                c = ctx.tag(ins, "ve"); e.wait_ge(S_["ve"], c)
            ins = e.tensor_scalar(out=snlr[:, 0:2], in0=slr[sl][:],
                                  scalar1=-1.0 / 128.0, scalar2=None, op0=ALU.mult)
            c = ctx.tag(ins, "ve"); e.wait_ge(S_["ve"], c)
            ins = e.tensor_scalar(out=snlr[:, 2:4], in0=snlr[:, 0:2],
                                  scalar1=stir[:, 15:16], scalar2=None, op0=ALU.mult)
            c = ctx.tag(ins, "ve"); e.wait_ge(S_["ve"], c)
            ins = e.tensor_copy(out=snlrb[:], in_=snlr[:])
            c = ctx.tag(ins, "ve"); e.wait_ge(S_["ve"], c)
            for h in range(2):
                ins = e.tensor_tensor(out=st1[:, 0:16], in0=psA[:, h * 16:(h + 1) * 16],
                                      in1=smk[:], op=ALU.mult)
                c = ctx.tag(ins, "ve"); e.wait_ge(S_["ve"], c)
                ins = e.tensor_tensor(out=st1[:, 16:32], in0=st1[:, 0:16],
                                      in1=stir[:], op=ALU.mult)
                c = ctx.tag(ins, "ve"); e.wait_ge(S_["ve"], c)
                ins = e.tensor_scalar(out=sam[:, h * 16:(h + 1) * 16],
                                      in0=st1[:, 16:32],
                                      scalar1=snlr[:, h:h + 1], scalar2=None,
                                      op0=ALU.mult)
                c = ctx.tag(ins, "ve"); e.wait_ge(S_["ve"], c)
                ins = e.tensor_scalar(out=sam[:, 32 + h * 16:32 + (h + 1) * 16],
                                      in0=stii[:], scalar1=snlr[:, h:h + 1],
                                      scalar2=None, op0=ALU.mult)
                c = ctx.tag(ins, "ve"); e.wait_ge(S_["ve"], c)
                ins = e.tensor_scalar(out=slex[:, h * 128:(h + 1) * 128],
                                      in0=sxkt[sl][:, h * 128:(h + 1) * 128],
                                      scalar1=snlr[:, 2 + h:3 + h], scalar2=None,
                                      op0=ALU.mult)
                c = ctx.tag(ins, "ve"); e.wait_ge(S_["ve"], c)
            ins = e.engine_nop()
            ctx.tag(ins, "ve")
        v_grad = ctx.cnt["ve"]

        @blk.tensor
        def _(e, n=n, v_grad=v_grad):
            e.wait_ge(S_["ve"], v_grad)
            for h in range(2):
                hs = slice(h * 128, (h + 1) * 128)
                e.matmul(psB[:, hs], sam[:, h * 16:(h + 1) * 16], sgr[:, hs],
                         start=False, stop=False)
                ins = e.matmul(psB[:, hs], sam[:, 32 + h * 16:32 + (h + 1) * 16],
                               sgr[:, hs], start=False, stop=(h == 1))
            ctx.tag(ins, "mm")
            for h in range(2):
                hs = slice(h * 128, (h + 1) * 128)
                e.matmul(psW[:, hs], slex[:, hs], sgr[:, hs], start=(h == 0),
                         stop=(h == 1))
                ins = e.matmul(psb[0:1, hs], snlrb[:, 2 + h:3 + h], sgr[:, hs],
                               start=(h == 0), stop=(h == 1))
            ctx.tag(ins, "mm")
        t_b2 = ctx.cnt["mm"] - 1
        t_st = ctx.cnt["mm"]

        @blk.scalar
        def _(e, t_b2=t_b2):
            e.wait_ge(S_["mm"], t_b2)
            e.activation(out=sZ[:, 0:128], in_=psB[:, 0:128], func=AF.Copy,
                         accum_out=scol[:, MAC:MAC + 1])
            ins = e.activation(out=sZ[:, 128:256], in_=psB[:, 128:256], func=AF.Copy,
                               accum_out=scol[:, MAC + 1:MAC + 2])
            ctx.tag(ins, "sc")
        s_b = ctx.cnt["sc"]
        prev["b_ev"] = s_b

        @blk.vector
        def _(e, s_b=s_b):
            e.wait_ge(S_["sc"], s_b)
            ins = e.tensor_scalar(out=scol[:, MU:MU + 2], in0=scol[:, MAC:MAC + 2],
                                  scalar1=1.0 / 128.0, scalar2=None, op0=ALU.mult)
            c = ctx.tag(ins, "ve"); e.wait_ge(S_["ve"], c)
            for h in range(2):
                ins = e.tensor_scalar(out=sxc[:, h * 128:(h + 1) * 128],
                                      in0=sZ[:, h * 128:(h + 1) * 128],
                                      scalar1=scol[:, MU + h:MU + h + 1],
                                      scalar2=None, op0=ALU.subtract)
            ctx.tag(ins, "ve")
        v_xc2 = ctx.cnt["ve"]

        @blk.scalar
        def _(e, v_xc2=v_xc2):
            e.wait_ge(S_["ve"], v_xc2)
            e.activation(out=ssq[:, 0:128], in_=sxc[:, 0:128], func=AF.Square,
                         accum_out=scol[:, SAC:SAC + 1])
            ins = e.activation(out=ssq[:, 128:256], in_=sxc[:, 128:256], func=AF.Square,
                               accum_out=scol[:, SAC + 1:SAC + 2])
            ctx.tag(ins, "sc")
        s_sq2 = ctx.cnt["sc"]

        @blk.vector
        def _(e, s_sq2=s_sq2):
            e.wait_ge(S_["sc"], s_sq2)
            ins = e.tensor_scalar(out=scol[:, SS:SS + 2], in0=scol[:, SAC:SAC + 2],
                                  scalar1=1.0 / 128.0, scalar2=1e-6,
                                  op0=ALU.mult, op1=ALU.add)
            ctx.tag(ins, "ve")
        v_var2 = ctx.cnt["ve"]

        @blk.scalar
        def _(e, v_var2=v_var2):
            e.wait_ge(S_["ve"], v_var2)
            ins = e.activation(out=scol[:, RSTD:RSTD + 2], in_=scol[:, SS:SS + 2],
                               func=AF.Sqrt)
            ctx.tag(ins, "sc")
        s_std2 = ctx.cnt["sc"]

        @blk.vector
        def _(e, n=n, sl=sl, s_std2=s_std2, t_st=t_st, pst=prev["st"][sl]):
            e.wait_ge(S_["sc"], s_std2)
            if n >= 2:
                e.wait_ge(S_["gp"], pst)
            ins = e.reciprocal(out=scol[:, RSTD:RSTD + 2], in_=scol[:, RSTD:RSTD + 2])
            c = ctx.tag(ins, "ve"); e.wait_ge(S_["ve"], c)
            for h in range(2):
                hs = slice(h * 128, (h + 1) * 128)
                ins = e.tensor_scalar(out=sxh[:, hs], in0=sxc[:, hs],
                                      scalar1=scol[:, RSTD + h:RSTD + h + 1],
                                      scalar2=None, op0=ALU.mult)
            c = ctx.tag(ins, "ve"); e.wait_ge(S_["ve"], c)
            ins = e.tensor_tensor(out=st1[:], in0=sxh[:], in1=sgw[:], op=ALU.mult)
            c = ctx.tag(ins, "ve"); e.wait_ge(S_["ve"], c)
            ins = e.tensor_tensor(out=st1[:], in0=st1[:], in1=sgb[:], op=ALU.add)
            c = ctx.tag(ins, "ve"); e.wait_ge(S_["ve"], c)
            ins = e.tensor_tensor(out=sout[sl][:], in0=st1[:], in1=sxqt[sl][:],
                                  op=ALU.add)
            c = ctx.tag(ins, "ve"); e.wait_ge(S_["ve"], c)
            e.wait_ge(S_["mm"], t_st)
            ins = e.tensor_tensor(out=sW1[:], in0=sW1[:], in1=psW[:], op=ALU.add)
            c = ctx.tag(ins, "ve"); e.wait_ge(S_["ve"], c)
            ins = e.tensor_copy(out=sW1b[:], in_=sW1[:])
            c = ctx.tag(ins, "ve"); e.wait_ge(S_["ve"], c)
            ins = e.tensor_tensor(out=sb1[:], in0=sb1[:], in1=psb[:], op=ALU.add)
            c = ctx.tag(ins, "ve"); e.wait_ge(S_["ve"], c)
            ins = e.tensor_copy(out=sb1b[:], in_=sb1[:])
            ctx.tag(ins, "ve")
        v_fin = ctx.cnt["ve"]
        prev["state"] = v_fin
        prev["a_done"] = v_grad
        prev["z_ev"] = s_z

        @blk.gpsimd
        def _(e, n=n, sl=sl, t0=t0, v_fin=v_fin):
            e.wait_ge(S_["ve"], v_fin)
            if TTT_DEBUG and n == 0:
                ctx.dma(e, dbg[0], sZ[:], "qg")
                ctx.dma(e, dbg[1], sgy[:], "qg")
                ctx.dma(e, dbg[2], sgr[:].rearrange("p c -> p c") if False else sgr[:], "qg")
                ctx.dma(e, dbg[6], sxh[:], "qg")
                ctx.dma(e, dbg[7][0:16, 0:4], snlr[:], "qg")
            ctx.dma(e, oap[t0:t0 + 16, :], sout[sl][:], "qg")
            e.wait_ge(S_["qg"], ctx.cnt["qg"])
            ins = e.engine_nop()
            ctx.tag(ins, "gp")
        prev["out"][sl] = v_fin
        prev["st"][sl] = ctx.cnt["gp"]


def ph_lnt(ctx, tagp, src, dst, C, gamma, beta, eps, src_dt=F32):
    sap = ctx.buf(src, (T, C), src_dt)
    dap = ctx.buf(dst, (T, C), BF16)
    gap = ctx.inp(gamma, (1, C), F32)
    bap = ctx.inp(beta, (1, C), F32)
    S_ = ctx.sems
    blk = ctx.block
    sx = [ctx.sbuf(f"{tagp}_x{i}", [128, C], src_dt) for i in range(2)]
    sxc = ctx.sbuf(f"{tagp}_xc", [128, C], F32)
    ssq = ctx.sbuf(f"{tagp}_sq", [128, C], F32)
    so = [ctx.sbuf(f"{tagp}_o{i}", [128, C], BF16) for i in range(2)]
    sg = ctx.sbuf(f"{tagp}_g", [128, C], F32)
    sb_ = ctx.sbuf(f"{tagp}_b", [128, C], F32)
    srow = ctx.sbuf(f"{tagp}_r", [1, 2 * C], F32)
    sone = ctx.sbuf(f"{tagp}_1", [1, 128], F32)
    scol = ctx.sbuf(f"{tagp}_c", [128, 8], F32)
    psr = ctx.psum(f"{tagp}_p", [128, C])
    MAC, SAC, MU, SS, RSTD = 0, 1, 2, 3, 4

    @blk.sync
    def _(e):
        ctx.dma(e, srow[0:1, 0:C], gap[:], "qs")
        ctx.dma(e, srow[0:1, C:2 * C], bap[:], "qs")
    r_rdy = ctx.cnt["qs"]

    @blk.vector
    def _(e):
        ins = e.memset(sone[:], 1.0)
        ctx.tag(ins, "ve")
    one_rdy = ctx.cnt["ve"]

    gb_rep = {}
    for w, dstt in ((0, sg), (1, sb_)):
        @blk.tensor
        def _(e, w=w):
            e.wait_ge(S_["qs"], r_rdy)
            e.wait_ge(S_["ve"], one_rdy)
            if w == 1:
                e.wait_ge(S_["sc"], gb_rep[0])
            ins = e.matmul(psr[:], sone[:], srow[0:1, w * C:(w + 1) * C],
                           start=True, stop=True)
            ctx.tag(ins, "mm")
        t_r = ctx.cnt["mm"]

        @blk.scalar
        def _(e, w=w, t_r=t_r, dstt=dstt):
            e.wait_ge(S_["mm"], t_r)
            ins = e.activation(out=dstt[:], in_=psr[:], func=AF.Copy)
            ctx.tag(ins, "sc")
        gb_rep[w] = ctx.cnt["sc"]
    rep_rdy = ctx.cnt["sc"]

    for i in range(T // 128):
        @blk.sync
        def _(e, i=i):
            if i >= 2:
                e.wait_ge(S_["gp"], ctx.cnt["gp"] - 1)
            ctx.fence(e, "qs")
            ctx.dma(e, sx[i % 2][:], sap[i * 128:(i + 1) * 128, :], "qs")
        qs_i = ctx.cnt["qs"]

        @blk.vector
        def _(e, i=i, qs_i=qs_i):
            e.wait_ge(S_["qs"], qs_i)
            if i >= 1:
                e.wait_ge(S_["sc"], ctx.cnt["sc"])
            ins = e.tensor_reduce(out=scol[:, MAC:MAC + 1], in_=sx[i % 2][:],
                                  axis=AX.X, op=ALU.add)
            c = ctx.tag(ins, "ve"); e.wait_ge(S_["ve"], c)
            ins = e.tensor_scalar(out=scol[:, MU:MU + 1], in0=scol[:, MAC:MAC + 1],
                                  scalar1=1.0 / C, scalar2=None, op0=ALU.mult)
            c = ctx.tag(ins, "ve"); e.wait_ge(S_["ve"], c)
            ins = e.tensor_scalar(out=sxc[:], in0=sx[i % 2][:],
                                  scalar1=scol[:, MU:MU + 1], scalar2=None,
                                  op0=ALU.subtract)
            ctx.tag(ins, "ve")
        v_xc = ctx.cnt["ve"]

        @blk.scalar
        def _(e, v_xc=v_xc):
            e.wait_ge(S_["ve"], v_xc)
            ins = e.activation(out=ssq[:], in_=sxc[:], func=AF.Square,
                               accum_out=scol[:, SAC:SAC + 1])
            ctx.tag(ins, "sc")
        s_sq = ctx.cnt["sc"]

        @blk.vector
        def _(e, s_sq=s_sq, eps=eps):
            e.wait_ge(S_["sc"], s_sq)
            ins = e.tensor_scalar(out=scol[:, SS:SS + 1], in0=scol[:, SAC:SAC + 1],
                                  scalar1=1.0 / C, scalar2=float(eps),
                                  op0=ALU.mult, op1=ALU.add)
            ctx.tag(ins, "ve")
        v_var = ctx.cnt["ve"]

        @blk.scalar
        def _(e, v_var=v_var):
            e.wait_ge(S_["ve"], v_var)
            ins = e.activation(out=scol[:, RSTD:RSTD + 1], in_=scol[:, SS:SS + 1],
                               func=AF.Sqrt)
            ctx.tag(ins, "sc")
        s_std = ctx.cnt["sc"]

        @blk.vector
        def _(e, i=i, s_std=s_std):
            e.wait_ge(S_["sc"], s_std)
            ins = e.reciprocal(out=scol[:, RSTD:RSTD + 1], in_=scol[:, RSTD:RSTD + 1])
            c = ctx.tag(ins, "ve"); e.wait_ge(S_["ve"], c)
            ins = e.tensor_scalar(out=sxc[:], in0=sxc[:],
                                  scalar1=scol[:, RSTD:RSTD + 1], scalar2=None,
                                  op0=ALU.mult)
            c = ctx.tag(ins, "ve"); e.wait_ge(S_["ve"], c)
            ins = e.tensor_tensor(out=sxc[:], in0=sxc[:], in1=sg[:], op=ALU.mult)
            c = ctx.tag(ins, "ve"); e.wait_ge(S_["ve"], c)
            ins = e.tensor_tensor(out=so[i % 2][:], in0=sxc[:], in1=sb_[:], op=ALU.add)
            ctx.tag(ins, "ve")
        v_o = ctx.cnt["ve"]

        @blk.gpsimd
        def _(e, i=i, v_o=v_o):
            e.wait_ge(S_["ve"], v_o)
            ctx.dma(e, dap[i * 128:(i + 1) * 128, :], so[i % 2][:], "qg")
            e.wait_ge(S_["qg"], ctx.cnt["qg"])
            ins = e.engine_nop()
            ctx.tag(ins, "gp")


def ph_t2tok(ctx, tagp, src, C, dst):
    sap = ctx.buf(src, (C, T), BF16)
    dap = ctx.buf(dst, (T, C), BF16)
    S_ = ctx.sems
    blk = ctx.block
    CC = C // 128
    sx = [ctx.sbuf(f"{tagp}_x{i}", [128, C], BF16) for i in range(2)]

    for i in range(T // 128):
        @blk.sync
        def _(e, i=i):
            if i >= 2:
                e.wait_ge(S_["gp"], ctx.cnt["gp"] - 1)
            ctx.fence(e, "qs")
            for cc in range(CC):
                ctx.dma(e, sx[i % 2][:, cc * 128:(cc + 1) * 128],
                        sap[cc * 128:(cc + 1) * 128, i * 128:(i + 1) * 128],
                        "qs", transpose=True)
        qs_i = ctx.cnt["qs"]

        @blk.gpsimd
        def _(e, i=i, qs_i=qs_i):
            e.wait_ge(S_["qs"], qs_i)
            ctx.dma(e, dap[i * 128:(i + 1) * 128, :], sx[i % 2][:], "qg")
            e.wait_ge(S_["qg"], ctx.cnt["qg"])
            ins = e.engine_nop()
            ctx.tag(ins, "gp")


def l2_attn_phases():
    return [
        lambda c: ph_mm(c, "l2h", "xn2", 1024, "l2_down", "l2h", 256, "fm",
                        AF.Silu, biascol="l2_downb"),
        lambda c: ph_mm(c, "l2xq", "l2h", 256, "l2_wq", "l2xq", 256, "fm"),
        lambda c: ph_mm(c, "l2xk", "l2h", 256, "l2_wk", "l2xk", 256, "fm"),
        lambda c: ph_mm(c, "l2xv", "l2h", 256, "l2_wv", "l2xv", 256, "fm"),
        lambda c: ph_mm(c, "l2lr", "l2h", 256, "l2_lrw", "l2lr", 2, "tok",
                        AF.Sigmoid, biasrow="l2_lrbr", dst_dt=F32),
        lambda c: ph_t2tok(c, "l2tq", "l2xq", 256, "l2xqt"),
        lambda c: ph_t2tok(c, "l2tk", "l2xk", 256, "l2xkt"),
        lambda c: ph_t2tok(c, "l2tv", "l2xv", 256, "l2xvt"),
        ph_l2_ttt,
        lambda c: ph_lnt(c, "l2p1", "l2o", "l2on", 256, "l2_postw", "l2_postb", 1e-5),
        lambda c: ph_mm(c, "l2wo", "l2on", 256, "l2_wo", "l2wo", 256, "tok",
                        None, dst_dt=F32, src_tok=True),
        lambda c: ph_lnt(c, "l2p2", "l2wo", "l2on2", 256, "l2_lnw2", "l2_lnb2", 1e-5),
        lambda c: ph_mm(c, "l2up", "l2on2", 256, "l2_up", "attn2", 1024, "fm",
                        None, biascol="l2_upb", dst_dt=F32, src_tok=True),
    ]




def ph_l1_conv(ctx):
    qp = ctx.buf("l1qp", (D, T), BF16)
    kp = ctx.buf("l1kp", (D, T), BF16)
    vp = ctx.buf("l1vp", (D, T), BF16)
    oq = ctx.buf("l1q", (D, T), BF16)
    ok = ctx.buf("l1k", (D, T), BF16)
    ov = ctx.buf("l1v", (D, T), BF16)
    fir_out = {1: ctx.buf("l1f1", (D, T), BF16), 3: ctx.buf("l1f3", (D, T), BF16),
               7: ctx.buf("l1f7", (D, T), BF16), 31: ctx.buf("l1f31", (D, T), BF16)}
    wq = ctx.inp("l1_qc", (D, 4), F32)
    wk = ctx.inp("l1_kc", (D, 4), F32)
    wv = ctx.inp("l1_vc", (D, 4), F32)
    wf = {1: ctx.inp("l1_fir1", (D, 1), F32), 3: ctx.inp("l1_fir3", (D, 3), F32),
          7: ctx.inp("l1_fir7", (D, 7), F32), 31: ctx.inp("l1_fir31", (D, 31), F32)}
    S_ = ctx.sems
    blk = ctx.block
    PAD = 32
    sx = ctx.sbuf("c1_x", [128, PAD + T], BF16)
    sacc = ctx.sbuf("c1_a", [128, T], F32)
    sv = ctx.sbuf("c1_v", [128, PAD + T], BF16)
    so = ctx.sbuf("c1_o", [128, T], BF16)
    swt = ctx.sbuf("c1_w", [128, 64], F32)

    for ch in range(DC):
        @blk.sync
        def _(e, ch=ch):
            if ch > 0:
                e.wait_ge(S_["ve"], ctx.cnt["ve"])
            ctx.fence(e, "qv")
            ctx.dma(e, swt[:, 0:4], wq[ch * 128:(ch + 1) * 128, :], "qv")
            ctx.dma(e, swt[:, 4:8], wk[ch * 128:(ch + 1) * 128, :], "qv")
            ctx.dma(e, swt[:, 8:12], wv[ch * 128:(ch + 1) * 128, :], "qv")
            ctx.dma(e, swt[:, 12:13], wf[1][ch * 128:(ch + 1) * 128, :], "qv")
            ctx.dma(e, swt[:, 16:19], wf[3][ch * 128:(ch + 1) * 128, :], "qv")
            ctx.dma(e, swt[:, 24:31], wf[7][ch * 128:(ch + 1) * 128, :], "qv")
            ctx.dma(e, swt[:, 32:63], wf[31][ch * 128:(ch + 1) * 128, :], "qv")
        w_rdy = ctx.cnt["qv"]

        for which, pre, wof, dst in (("q", qp, 0, oq), ("k", kp, 4, ok), ("v", vp, 8, ov)):
            @blk.sync
            def _(e, ch=ch, pre=pre):
                e.wait_ge(S_["sc"], ctx.cnt["sc"])
                ctx.fence(e, "qs")
                ctx.dma(e, sx[:, PAD:PAD + T], fm_view(pre, 0, T, c0=ch, nch=1)[:, 0, :], "qs")
            x_rdy = ctx.cnt["qs"]

            @blk.vector
            def _(e, ch=ch, wof=wof, x_rdy=x_rdy, w_rdy=w_rdy):
                e.wait_ge(S_["qs"], x_rdy)
                e.wait_ge(S_["qv"], w_rdy)
                e.wait_ge(S_["sc"], ctx.cnt["sc"])
                ins = e.memset(sx[:, 0:PAD], 0.0)
                c = ctx.tag(ins, "ve"); e.wait_ge(S_["ve"], c)
                ins = e.tensor_scalar(out=sacc[:], in0=sx[:, PAD - 3:PAD - 3 + T],
                                      scalar1=swt[:, wof:wof + 1], scalar2=None,
                                      op0=ALU.mult)
                c = ctx.tag(ins, "ve"); e.wait_ge(S_["ve"], c)
                for j in range(1, 4):
                    ins = e.scalar_tensor_tensor(
                        out=sacc[:], in0=sx[:, PAD - 3 + j:PAD - 3 + j + T],
                        scalar=swt[:, wof + j:wof + j + 1], in1=sacc[:],
                        op0=ALU.mult, op1=ALU.add)
                    c = ctx.tag(ins, "ve"); e.wait_ge(S_["ve"], c)
                ctx.tag(e.engine_nop(), "ve")
            v_acc = ctx.cnt["ve"]

            @blk.scalar
            def _(e, ch=ch, v_acc=v_acc, which=which):
                e.wait_ge(S_["ve"], v_acc)
                if which == "v":
                    e.wait_ge(S_["gp"], ctx.cnt["gp"])
                    ins = e.activation(out=sv[:, PAD:PAD + T], in_=sacc[:], func=AF.Silu)
                else:
                    e.wait_ge(S_["gp"], ctx.cnt["gp"])
                    ins = e.activation(out=so[:], in_=sacc[:], func=AF.Silu)
                ctx.tag(ins, "sc")
            s_si = ctx.cnt["sc"]

            @blk.gpsimd
            def _(e, ch=ch, s_si=s_si, dst=dst, which=which):
                e.wait_ge(S_["sc"], s_si)
                srcv = sv[:, PAD:PAD + T] if which == "v" else so[:]
                ctx.dma(e, fm_view(dst, 0, T, c0=ch, nch=1)[:, 0, :], srcv, "qg")
                e.wait_ge(S_["qg"], ctx.cnt["qg"])
                ctx.tag(e.engine_nop(), "gp")

        @blk.vector
        def _(e, ch=ch):
            e.wait_ge(S_["sc"], ctx.cnt["sc"])
            ins = e.memset(sv[:, 0:PAD], 0.0)
            ctx.tag(ins, "ve")
        v_pad = ctx.cnt["ve"]

        for kw, wof in ((1, 12), (3, 16), (7, 24), (31, 32)):
            @blk.vector
            def _(e, ch=ch, kw=kw, wof=wof, v_pad=v_pad):
                e.wait_ge(S_["ve"], v_pad)
                e.wait_ge(S_["gp"], ctx.cnt["gp"])
                ins = e.tensor_scalar(out=sacc[:], in0=sv[:, PAD - (kw - 1):PAD - (kw - 1) + T],
                                      scalar1=swt[:, wof:wof + 1], scalar2=None,
                                      op0=ALU.mult)
                c = ctx.tag(ins, "ve"); e.wait_ge(S_["ve"], c)
                for j in range(1, kw):
                    ins = e.scalar_tensor_tensor(
                        out=sacc[:], in0=sv[:, PAD - (kw - 1) + j:PAD - (kw - 1) + j + T],
                        scalar=swt[:, wof + j:wof + j + 1], in1=sacc[:],
                        op0=ALU.mult, op1=ALU.add)
                    c = ctx.tag(ins, "ve"); e.wait_ge(S_["ve"], c)
                ins = e.tensor_copy(out=so[:], in_=sacc[:])
                ctx.tag(ins, "ve")
            v_f = ctx.cnt["ve"]

            @blk.gpsimd
            def _(e, ch=ch, kw=kw, v_f=v_f):
                e.wait_ge(S_["ve"], v_f)
                ctx.dma(e, fm_view(fir_out[kw], 0, T, c0=ch, nch=1)[:, 0, :], so[:], "qg")
                e.wait_ge(S_["qg"], ctx.cnt["qg"])
                ctx.tag(e.engine_nop(), "gp")


def ph_l1_beta(ctx):
    xn = ctx.buf("xn1", (D, T), BF16)
    bap = ctx.buf("l1beta", (4, T), F32)
    wap = ctx.inp("l1_bw", (D, 4), BF16)
    S_ = ctx.sems
    blk = ctx.block
    TS = 512
    sx = [ctx.sbuf(f"b1_x{i}", [128, DC * TS], BF16) for i in range(2)]
    sw = ctx.sbuf("b1_w", [128, DC * 4], BF16)
    so = [ctx.sbuf(f"b1_o{i}", [4, TS], F32) for i in range(2)]
    ps = ctx.psum("b1_p", [4, TS])

    @blk.sync
    def _(e):
        ctx.dma(e, sw.rearrange("p (c n) -> p c n", c=DC),
                wap.rearrange("(c p) n -> p c n", p=128), "qv")
    w_rdy = ctx.cnt["qv"]

    for j in range(T // TS):
        @blk.sync
        def _(e, j=j):
            if j >= 2:
                e.wait_ge(S_["gp"], ctx.cnt["gp"] - 1)
            ctx.fence(e, "qs")
            ctx.dma(e, sb3(sx[j % 2], DC, TS), fm_view(xn, j * TS, TS), "qs")
        qs_j = ctx.cnt["qs"]

        @blk.tensor
        def _(e, j=j, qs_j=qs_j, w_rdy=w_rdy):
            e.wait_ge(S_["qs"], qs_j)
            if j == 0:
                e.wait_ge(S_["qv"], w_rdy)
            else:
                e.wait_ge(S_["sc"], ctx.cnt["sc"])
            for c in range(DC):
                ins = e.matmul(ps[:], sw[:, c * 4:(c + 1) * 4],
                               sx[j % 2][:, c * TS:(c + 1) * TS],
                               start=(c == 0), stop=(c == DC - 1))
            ctx.tag(ins, "mm")
        t_j = ctx.cnt["mm"]

        @blk.scalar
        def _(e, j=j, t_j=t_j):
            e.wait_ge(S_["mm"], t_j)
            ins = e.activation(out=so[j % 2][:], in_=ps[:], func=AF.Sigmoid)
            ctx.tag(ins, "sc")
        s_j = ctx.cnt["sc"]

        @blk.gpsimd
        def _(e, j=j, s_j=s_j):
            e.wait_ge(S_["sc"], s_j)
            ctx.dma(e, bap[:, j * TS:(j + 1) * TS], so[j % 2][:], "qg")
            e.wait_ge(S_["qg"], ctx.cnt["qg"])
            ctx.tag(e.engine_nop(), "gp")


def ph_l1_norm(ctx):
    q = ctx.buf("l1q", (D, T), BF16)
    k = ctx.buf("l1k", (D, T), BF16)
    v = ctx.buf("l1v", (D, T), BF16)
    bt = ctx.buf("l1beta", (4, T), F32)
    qn = ctx.buf("l1qn", (D, T), BF16)
    kn = ctx.buf("l1kn", (D, T), BF16)
    kb = ctx.buf("l1kb", (D, T), BF16)
    vb = ctx.buf("l1vb", (D, T), BF16)
    S_ = ctx.sems
    blk = ctx.block
    TS = 512
    sq = ctx.sbuf("n1_q", [128, 2 * TS], BF16)
    sk = ctx.sbuf("n1_k", [128, 2 * TS], BF16)
    sv = ctx.sbuf("n1_v", [128, 2 * TS], BF16)
    ssqq = ctx.sbuf("n1_sqq", [128, 2 * TS], F32)
    ssqk = ctx.sbuf("n1_sqk", [128, 2 * TS], F32)
    sone = ctx.sbuf("n1_1", [128, 1], F32)
    sonr = ctx.sbuf("n1_1r", [1, 128], F32)
    srow = ctx.sbuf("n1_r", [1, 2 * TS], F32)
    sbr = ctx.sbuf("n1_b", [1, TS], F32)
    sout = ctx.sbuf("n1_o", [128, 8 * TS], BF16)
    ps_sq_ = ctx.psum("n1_psq", [1, TS])
    ps_sk_ = ctx.psum("n1_psk", [1, TS])
    ps_rq = ctx.psum("n1_prq", [128, TS])
    ps_rk = ctx.psum("n1_prk", [128, TS])
    ps_bb = ctx.psum("n1_pbb", [128, TS])
    QS, KS = 0, TS

    @blk.vector
    def _(e):
        e.memset(sone[:], 1.0)
        ins = e.memset(sonr[:], 1.0)
        ctx.tag(ins, "x1")
    one_rdy = ctx.cnt["x1"]

    for h in range(4):
        for j in range(T // TS):
            @blk.sync
            def _(e, h=h, j=j):
                e.wait_ge(S_["gp"], ctx.cnt["gp"])
                ctx.fence(e, "qs")
                ctx.dma(e, sb3(sq, 2, TS), fm_view(q, j * TS, TS, c0=2 * h, nch=2), "qs")
                ctx.dma(e, sb3(sk, 2, TS), fm_view(k, j * TS, TS, c0=2 * h, nch=2), "qs")
                ctx.dma(e, sb3(sv, 2, TS), fm_view(v, j * TS, TS, c0=2 * h, nch=2), "qs")
                ctx.dma(e, sbr[:], bt[h:h + 1, j * TS:(j + 1) * TS], "qs")
            qs_i = ctx.cnt["qs"]

            @blk.scalar
            def _(e, qs_i=qs_i):
                e.wait_ge(S_["qs"], qs_i)
                e.wait_ge(S_["mm"], ctx.cnt["mm"])
                for cc in range(2):
                    e.activation(out=ssqq[:, cc * TS:(cc + 1) * TS],
                                 in_=sq[:, cc * TS:(cc + 1) * TS], func=AF.Square)
                for cc in range(2):
                    ins = e.activation(out=ssqk[:, cc * TS:(cc + 1) * TS],
                                       in_=sk[:, cc * TS:(cc + 1) * TS], func=AF.Square)
                ctx.tag(ins, "sc")
            s_sq = ctx.cnt["sc"]

            @blk.tensor
            def _(e, s_sq=s_sq, qs_i=qs_i):
                e.wait_ge(S_["sc"], s_sq)
                e.wait_ge(S_["x1"], one_rdy)
                e.wait_ge(S_["ve"], ctx.cnt["ve"])
                for cc in range(2):
                    e.matmul(ps_sq_[:], sone[:], ssqq[:, cc * TS:(cc + 1) * TS],
                             start=(cc == 0), stop=(cc == 1))
                for cc in range(2):
                    ins = e.matmul(ps_sk_[:], sone[:], ssqk[:, cc * TS:(cc + 1) * TS],
                                   start=(cc == 0), stop=(cc == 1))
                ctx.tag(ins, "mm")
            t_s = ctx.cnt["mm"]

            @blk.vector
            def _(e, t_s=t_s):
                e.wait_ge(S_["mm"], t_s)
                ins = e.tensor_scalar(out=srow[0:1, QS:QS + TS], in0=ps_sq_[:],
                                      scalar1=1e-6, scalar2=None, op0=ALU.add)
                c1 = ctx.tag(ins, "ve"); e.wait_ge(S_["ve"], c1)
                ins = e.tensor_scalar(out=srow[0:1, KS:KS + TS], in0=ps_sk_[:],
                                      scalar1=1e-6, scalar2=None, op0=ALU.add)
                ctx.tag(ins, "ve")
            v_r1 = ctx.cnt["ve"]

            @blk.scalar
            def _(e, v_r1=v_r1):
                e.wait_ge(S_["ve"], v_r1)
                ins = e.activation(out=srow[:], in_=srow[:], func=AF.Sqrt)
                ctx.tag(ins, "sc")
            s_r = ctx.cnt["sc"]

            @blk.vector
            def _(e, s_r=s_r):
                e.wait_ge(S_["sc"], s_r)
                ins = e.reciprocal(out=srow[:], in_=srow[:])
                ctx.tag(ins, "ve")
            v_r2 = ctx.cnt["ve"]

            @blk.tensor
            def _(e, v_r2=v_r2, qs_i=qs_i):
                e.wait_ge(S_["ve"], v_r2)
                e.matmul(ps_rq[:], sonr[:], srow[0:1, QS:QS + TS], start=True, stop=True)
                e.matmul(ps_rk[:], sonr[:], srow[0:1, KS:KS + TS], start=True, stop=True)
                ins = e.matmul(ps_bb[:], sonr[:], sbr[0:1, :], start=True, stop=True)
                ctx.tag(ins, "mm")
            t_b = ctx.cnt["mm"]

            @blk.vector
            def _(e, h=h, j=j, t_b=t_b):
                e.wait_ge(S_["mm"], t_b)
                for cc in range(2):
                    ins = e.tensor_tensor(out=sout[:, (0 * 2 + cc) * TS:(0 * 2 + cc + 1) * TS],
                                          in0=sq[:, cc * TS:(cc + 1) * TS],
                                          in1=ps_rq[:], op=ALU.mult)
                    c = ctx.tag(ins, "ve"); e.wait_ge(S_["ve"], c)
                    ins = e.tensor_tensor(out=sout[:, (1 * 2 + cc) * TS:(1 * 2 + cc + 1) * TS],
                                          in0=sk[:, cc * TS:(cc + 1) * TS],
                                          in1=ps_rk[:], op=ALU.mult)
                    c = ctx.tag(ins, "ve"); e.wait_ge(S_["ve"], c)
                    ins = e.tensor_tensor(out=sout[:, (2 * 2 + cc) * TS:(2 * 2 + cc + 1) * TS],
                                          in0=sout[:, (1 * 2 + cc) * TS:(1 * 2 + cc + 1) * TS],
                                          in1=ps_bb[:], op=ALU.mult)
                    c = ctx.tag(ins, "ve"); e.wait_ge(S_["ve"], c)
                    ins = e.tensor_tensor(out=sout[:, (3 * 2 + cc) * TS:(3 * 2 + cc + 1) * TS],
                                          in0=sv[:, cc * TS:(cc + 1) * TS],
                                          in1=ps_bb[:], op=ALU.mult)
                    c = ctx.tag(ins, "ve"); e.wait_ge(S_["ve"], c)
                ctx.tag(e.engine_nop(), "ve")
            v_o = ctx.cnt["ve"]

            @blk.gpsimd
            def _(e, h=h, j=j, v_o=v_o):
                e.wait_ge(S_["ve"], v_o)
                for w, dst in ((0, qn), (1, kn), (2, kb), (3, vb)):
                    ctx.dma(e, fm_view(dst, j * TS, TS, c0=2 * h, nch=2),
                            sb3(sout[:, w * 2 * TS:(w * 2 + 2) * TS], 2, TS), "qg")
                e.wait_ge(S_["qg"], ctx.cnt["qg"])
                ctx.tag(e.engine_nop(), "gp")


def ph_l1_delta(ctx):
    qn = ctx.buf("l1qn", (D, T), BF16)
    kn = ctx.buf("l1kn", (D, T), BF16)
    kb = ctx.buf("l1kb", (D, T), BF16)
    knt = ctx.buf("l1knt", (T, D), BF16)
    vbt = ctx.buf("l1vbt", (T, D), BF16)
    ot = ctx.buf("l1dt", (T, D), BF16)
    apow = ctx.buf("l1ap", (128, 32, 320), BF16)
    nsT = ctx.inp("d_nstrictT", (32, 32), F32)
    nsL = ctx.inp("d_nstrictL", (32, 32), F32)
    incl = ctx.inp("d_inclT", (32, 32), F32)
    eye = ctx.inp("eye32", (32, 32), BF16)
    S_ = ctx.sems
    blk = ctx.block
    NCH = 128
    C = 32

    sqn = ctx.sbuf("d1_q", [128, 4 * T], BF16)
    skn = ctx.sbuf("d1_k", [128, 4 * T], BF16)
    skb = ctx.sbuf("d1_kb", [128, 4 * T], BF16)
    G = 4
    smAa = ctx.sbuf("d1_mAa", [32, G * 2 * 32], BF16)
    smAp = ctx.sbuf("d1_mAp", [32, G * 2 * 32], BF16)
    smsk = ctx.sbuf("d1_msk", [32, 3 * 32], F32)
    smskT = ctx.sbuf("d1_mskT", [32, G * 2 * 32], F32)
    smskL = ctx.sbuf("d1_mskL", [32, G * 2 * 32], F32)
    seye = ctx.sbuf("d1_eye", [32, 32], BF16)
    spwg = [ctx.sbuf(f"d1_pw{i}", [32, G * 320], BF16) for i in range(2)]
    spw = ctx.sbuf("d1_pw", [32, 320], BF16)
    sS = ctx.sbuf("d1_S", [128, 2 * 512], F32)
    sSb = ctx.sbuf("d1_Sb", [128, 2 * 512], BF16)
    skt = [ctx.sbuf(f"d1_kt{i}", [32, 4 * 2 * 256], BF16) for i in range(2)]
    svt = [ctx.sbuf(f"d1_vt{i}", [32, 4 * 2 * 256], BF16) for i in range(2)]
    sr = ctx.sbuf("d1_r", [32, 256], BF16)
    sU = ctx.sbuf("d1_U", [32, 256], BF16)
    sat = ctx.sbuf("d1_at", [32, 32], BF16)
    so = ctx.sbuf("d1_o", [32, 2 * 256], F32)
    psA = ctx.psum("d1_pA", [32, G * 2 * 32])
    psP = ctx.psum("d1_pP", [32, G * 2 * 32])
    psR = ctx.psum("d1_pR", [32, 256])
    psU = ctx.psum("d1_pU", [32, 256])
    psO = ctx.psum("d1_pO", [32, 256])
    psAt = ctx.psum("d1_pAt", [32, 32])
    psS = ctx.psum("d1_pS", [128, 512])

    @blk.sync
    def _(e):
        ctx.dma(e, smsk[:, 0:32], nsT[:], "qv")
        ctx.dma(e, smsk[:, 32:64], nsL[:], "qv")
        ctx.dma(e, smsk[:, 64:96], incl[:], "qv")
        ctx.dma(e, seye[:], eye[:], "qv")
    msk_rdy = ctx.cnt["qv"]

    @blk.vector
    def _(e):
        e.wait_ge(S_["qv"], msk_rdy)
        for b in range(2 * G):
            e.tensor_copy(out=smskT[:, b * 32:(b + 1) * 32], in_=smsk[:, 0:32])
            ins = e.tensor_copy(out=smskL[:, b * 32:(b + 1) * 32],
                                in_=smsk[:, 32:64])
        ctx.tag(ins, "ve")
    mskrep_rdy = ctx.cnt["ve"]

    for hg in range(2):
        h0 = 2 * hg

        @blk.sync
        def _(e, hg=hg, h0=h0):
            if hg > 0:
                e.wait_ge(S_["mm"], ctx.cnt["mm"])
            ctx.dma(e, sb3(sqn, 4, T), fm_view(qn, 0, T, c0=2 * h0, nch=4), "qv")
            ctx.dma(e, sb3(skn, 4, T), fm_view(kn, 0, T, c0=2 * h0, nch=4), "qv")
            ctx.dma(e, sb3(skb, 4, T), fm_view(kb, 0, T, c0=2 * h0, nch=4), "qv")
        ld_rdy = ctx.cnt["qv"]

        def pw_out(sl, hl, lvl):
            return spwg[sl].rearrange("p (g n) -> p g n", g=G)[
                :, :, (hl * 5 + lvl) * 32:(hl * 5 + lvl + 1) * 32]

        def ab_blk(sb, hl):
            return sb.rearrange("p (g h n) -> p g h n", g=G, h=2)[:, :, hl, :]

        g_pw = {}
        for cg in range(NCH // G):
            c0 = cg * G
            sl = cg % 2

            @blk.tensor
            def _(e, cg=cg, c0=c0, ld_rdy=ld_rdy):
                if cg == 0:
                    e.wait_ge(S_["qv"], max(ld_rdy, msk_rdy))
                e.wait_ge(S_["ve"], ctx.cnt["ve"])
                n = 0
                for g in range(G):
                    for hl in range(2):
                        for cc in range(2):
                            off = (2 * hl + cc) * T + (c0 + g) * C
                            ins = e.matmul(psA[:, (g * 2 + hl) * 32:(g * 2 + hl + 1) * 32],
                                           skn[:, off:off + C], skb[:, off:off + C],
                                           start=(n == 0), stop=(n == 4 * G - 1))
                            n += 1
                ctx.tag(ins, "mm")
                n = 0
                for g in range(G):
                    for hl in range(2):
                        for cc in range(2):
                            off = (2 * hl + cc) * T + (c0 + g) * C
                            ins = e.matmul(psP[:, (g * 2 + hl) * 32:(g * 2 + hl + 1) * 32],
                                           skb[:, off:off + C], skn[:, off:off + C],
                                           start=(n == 0), stop=(n == 4 * G - 1))
                            n += 1
                ctx.tag(ins, "mm")
            t_A = ctx.cnt["mm"]

            @blk.vector
            def _(e, cg=cg, sl=sl, t_A=t_A, g_pw=dict(g_pw)):
                e.wait_ge(S_["mm"], t_A)
                if cg >= 2:
                    e.wait_ge(S_["gp"], g_pw[cg - 2])
                ins = e.tensor_tensor(out=smAa[:], in0=psA[:], in1=smskT[:],
                                      op=ALU.mult)
                c1 = ctx.tag(ins, "ve"); e.wait_ge(S_["ve"], c1)
                ins = e.tensor_tensor(out=smAp[:], in0=psP[:], in1=smskL[:],
                                      op=ALU.mult)
                c1 = ctx.tag(ins, "ve"); e.wait_ge(S_["ve"], c1)
                for hl in range(2):
                    ins = e.tensor_copy(out=pw_out(sl, hl, 0), in_=ab_blk(smAa, hl))
                    c1 = ctx.tag(ins, "ve"); e.wait_ge(S_["ve"], c1)
                ctx.tag(e.engine_nop(), "ve")
            prev_v = ctx.cnt["ve"]

            for lvl in range(1, 5):
                @blk.tensor
                def _(e, lvl=lvl, prev_v=prev_v):
                    e.wait_ge(S_["ve"], prev_v)
                    n = 0
                    for g in range(G):
                        for hl in range(2):
                            blkc = slice((g * 2 + hl) * 32, (g * 2 + hl + 1) * 32)
                            ins = e.matmul(psA[:, blkc], smAp[:, blkc], smAa[:, blkc],
                                           start=(n == 0), stop=(n == 2 * G - 1))
                            n += 1
                    ctx.tag(ins, "mm")
                    n = 0
                    for g in range(G):
                        for hl in range(2):
                            blkc = slice((g * 2 + hl) * 32, (g * 2 + hl + 1) * 32)
                            ins = e.matmul(psP[:, blkc], smAa[:, blkc], smAp[:, blkc],
                                           start=(n == 0), stop=(n == 2 * G - 1))
                            n += 1
                    ctx.tag(ins, "mm")
                t_sq = ctx.cnt["mm"]

                @blk.vector
                def _(e, lvl=lvl, sl=sl, t_sq=t_sq):
                    e.wait_ge(S_["mm"], t_sq)
                    ins = e.tensor_copy(out=smAa[:], in_=psA[:])
                    c1 = ctx.tag(ins, "ve"); e.wait_ge(S_["ve"], c1)
                    ins = e.tensor_copy(out=smAp[:], in_=psP[:])
                    c1 = ctx.tag(ins, "ve"); e.wait_ge(S_["ve"], c1)
                    for hl in range(2):
                        ins = e.tensor_copy(out=pw_out(sl, hl, lvl),
                                            in_=ab_blk(smAa, hl))
                        c1 = ctx.tag(ins, "ve"); e.wait_ge(S_["ve"], c1)
                    ctx.tag(e.engine_nop(), "ve")
                prev_v = ctx.cnt["ve"]

            @blk.gpsimd
            def _(e, c0=c0, sl=sl, prev_v=prev_v):
                e.wait_ge(S_["ve"], prev_v)
                ctx.dma(e, apow[c0:c0 + G].rearrange("g p n -> p g n"),
                        spwg[sl].rearrange("p (g n) -> p g n", g=G), "qg")
                e.wait_ge(S_["qg"], ctx.cnt["qg"])
                ctx.tag(e.engine_nop(), "gp")
            g_pw[cg] = ctx.cnt["gp"]
        p1_done = ctx.cnt["gp"]

        @blk.vector
        def _(e):
            e.memset(sS[:], 0.0)
            ins = e.memset(sSb[:], 0.0)
            ctx.tag(ins, "ve")
        prev_state = ctx.cnt["ve"]

        g_last = {0: 0, 1: 0}
        for c in range(NCH):
            t0 = c * C
            qslot = (c // 4) % 2

            if c % 4 == 0:
                @blk.sync
                def _(e, c=c, hg=hg, h0=h0, qslot=qslot, p1_done=p1_done,
                      g_last=dict(g_last)):
                    if c == 0:
                        e.wait_ge(S_["gp"], p1_done)
                    if (c // 4) >= 2:
                        e.wait_ge(S_["gp"], g_last[qslot])
                    ctx.fence(e, "qs")
                    ctx.dma(e, skt[qslot].rearrange("p (cc n) -> p cc n", cc=4),
                            knt[:, h0 * 256:(h0 + 2) * 256]
                            .rearrange("(cc p) n -> p cc n", p=32)[:, c:c + 4, :], "qs")
                    ctx.dma(e, svt[qslot].rearrange("p (cc n) -> p cc n", cc=4),
                            vbt[:, h0 * 256:(h0 + 2) * 256]
                            .rearrange("(cc p) n -> p cc n", p=32)[:, c:c + 4, :], "qs")
            qs_c = ctx.cnt["qs"]

            @blk.sync
            def _(e, c=c, p1_done=p1_done):
                if c == 0:
                    e.wait_ge(S_["gp"], p1_done)
                e.wait_ge(S_["mm"], ctx.cnt["mm"])
                ctx.fence(e, "qs")
                ctx.dma(e, spw[:], apow[c], "qs")
            pw_c = ctx.cnt["qs"]

            ccol = (c % 4) * 512

            for hl in range(2):
                @blk.tensor
                def _(e, c=c, hl=hl, t0=t0, pw_c=pw_c, qs_c=qs_c,
                      prev_state=prev_state):
                    if hl == 0:
                        e.wait_ge(S_["qs"], max(pw_c, qs_c))
                        e.wait_ge(S_["ve"], prev_state)
                    for cc in range(2):
                        off = (2 * hl + cc) * T + t0
                        ins = e.matmul(psR[:], skb[:, off:off + C],
                                       sSb[:, (hl * 2 + cc) * 256:(hl * 2 + cc + 1) * 256],
                                       start=(cc == 0), stop=(cc == 1))
                    ctx.tag(ins, "mm")
                t_r = ctx.cnt["mm"]

                @blk.vector
                def _(e, hl=hl, t_r=t_r, qslot=qslot, ccol=ccol):
                    e.wait_ge(S_["mm"], t_r)
                    ins = e.tensor_tensor(out=sr[:],
                                          in0=svt[qslot][:, ccol + hl * 256:ccol + (hl + 1) * 256],
                                          in1=psR[:], op=ALU.subtract)
                    ctx.tag(ins, "ve")
                v_r = ctx.cnt["ve"]

                @blk.tensor
                def _(e, hl=hl, v_r=v_r):
                    e.wait_ge(S_["ve"], v_r)
                    ins = e.matmul(psU[:], seye[:], sr[:], start=True, stop=False)
                    ctx.tag(ins, "mm")
                prev_t = ctx.cnt["mm"]

                for lvl in range(5):
                    @blk.vector
                    def _(e, lvl=lvl, prev_t=prev_t):
                        e.wait_ge(S_["mm"], prev_t)
                        ins = e.tensor_copy(out=sU[:], in_=psU[:])
                        ctx.tag(ins, "ve")
                    v_u = ctx.cnt["ve"]

                    @blk.tensor
                    def _(e, hl=hl, lvl=lvl, v_u=v_u):
                        e.wait_ge(S_["ve"], v_u)
                        ins = e.matmul(psU[:], spw[:, (hl * 5 + lvl) * 32:(hl * 5 + lvl + 1) * 32],
                                       sU[:], start=False, stop=(lvl == 4))
                        ctx.tag(ins, "mm")
                    prev_t = ctx.cnt["mm"]

                @blk.vector
                def _(e, prev_t=prev_t):
                    e.wait_ge(S_["mm"], prev_t)
                    ins = e.tensor_copy(out=sU[:], in_=psU[:])
                    ctx.tag(ins, "ve")
                v_uh = ctx.cnt["ve"]

                @blk.tensor
                def _(e, hl=hl, t0=t0, v_uh=v_uh):
                    e.wait_ge(S_["ve"], v_uh)
                    for cc in range(2):
                        off = (2 * hl + cc) * T + t0
                        ins = e.matmul(psAt[:], skn[:, off:off + C], sqn[:, off:off + C],
                                       start=(cc == 0), stop=(cc == 1))
                    ctx.tag(ins, "mm")
                t_at = ctx.cnt["mm"]

                @blk.vector
                def _(e, t_at=t_at):
                    e.wait_ge(S_["mm"], t_at)
                    ins = e.tensor_tensor(out=sat[:], in0=psAt[:], in1=smsk[:, 64:96],
                                          op=ALU.mult)
                    ctx.tag(ins, "ve")
                v_at = ctx.cnt["ve"]

                @blk.tensor
                def _(e, hl=hl, t0=t0, v_at=v_at, qslot=qslot, ccol=ccol):
                    e.wait_ge(S_["ve"], v_at)
                    for cc in range(2):
                        off = (2 * hl + cc) * T + t0
                        e.matmul(psO[:], sqn[:, off:off + C],
                                 sSb[:, (hl * 2 + cc) * 256:(hl * 2 + cc + 1) * 256],
                                 start=(cc == 0), stop=False)
                    ins = e.matmul(psO[:], sat[:], sU[:], start=False, stop=True)
                    ctx.tag(ins, "mm")
                    for cc in range(2):
                        ins = e.matmul(psS[:, cc * 256:(cc + 1) * 256],
                                       skt[qslot][:, ccol + hl * 256 + cc * 128:ccol + hl * 256 + cc * 128 + 128],
                                       sU[:], start=(cc == 0), stop=(cc == 1))
                    ctx.tag(ins, "mm")
                t_su = ctx.cnt["mm"]
                t_o = t_su - 1

                @blk.vector
                def _(e, hl=hl, t_o=t_o, t_su=t_su):
                    e.wait_ge(S_["mm"], t_o)
                    ins = e.tensor_copy(out=so[:, hl * 256:(hl + 1) * 256], in_=psO[:])
                    c1 = ctx.tag(ins, "ve"); e.wait_ge(S_["ve"], c1)
                    e.wait_ge(S_["mm"], t_su)
                    for cc in range(2):
                        ins = e.tensor_tensor(out=sS[:, (hl * 2 + cc) * 256:(hl * 2 + cc + 1) * 256],
                                              in0=sS[:, (hl * 2 + cc) * 256:(hl * 2 + cc + 1) * 256],
                                              in1=psS[:, cc * 256:(cc + 1) * 256], op=ALU.add)
                        c1 = ctx.tag(ins, "ve"); e.wait_ge(S_["ve"], c1)
                    ins = e.tensor_copy(out=sSb[:, hl * 512:(hl + 1) * 512],
                                        in_=sS[:, hl * 512:(hl + 1) * 512])
                    ctx.tag(ins, "ve")
            prev_state = ctx.cnt["ve"]

            @blk.gpsimd
            def _(e, c=c, h0=h0, t0=t0, prev_state=prev_state):
                e.wait_ge(S_["ve"], prev_state)
                ctx.dma(e, ot[t0:t0 + C, h0 * 256:(h0 + 2) * 256]
                        .rearrange("p (hh n) -> p hh n", hh=2),
                        so.rearrange("p (hh n) -> p hh n", hh=2), "qg")
                e.wait_ge(S_["qg"], ctx.cnt["qg"])
                ctx.tag(e.engine_nop(), "gp")
            g_last[qslot] = ctx.cnt["gp"]


def ph_l1_stats(ctx):
    branches = [("l1f1", False), ("l1f3", False), ("l1f7", False), ("l1f31", False),
                ("l1dt", True), ("l1v", False)]
    stap = ctx.buf("l1st", (128, T), BF16)
    hsap = ctx.inp("headselb", (8 * 128, 4), BF16)
    S_ = ctx.sems
    blk = ctx.block
    TS = 512
    sx = ctx.sbuf("s1_x", [128, DC * TS], BF16)
    sq2 = ctx.sbuf("s1_q", [128, TS], BF16)
    sab = ctx.sbuf("s1_a", [128, TS], BF16)
    shs = ctx.sbuf("s1_hs", [128, DC * 4], BF16)
    srw = ctx.sbuf("s1_r", [4, 4 * TS], F32)
    sst = ctx.sbuf("s1_st", [4, 4 * TS], BF16)
    ps_x = ctx.psum("s1_px", [4, TS])
    ps_q = ctx.psum("s1_pq", [4, TS])
    ps_a = ctx.psum("s1_pa", [4, TS])

    @blk.sync
    def _(e):
        ctx.dma(e, shs.rearrange("p (c n) -> p c n", c=DC),
                hsap.rearrange("(c p) n -> p c n", p=128), "qv")
    hs_rdy = ctx.cnt["qv"]

    @blk.vector
    def _(e):
        ins = e.memset(sq2[:], 0.0)
        ctx.tag(ins, "x1")
    z_rdy = ctx.cnt["x1"]

    @blk.gpsimd
    def _(e, z_rdy=z_rdy):
        e.wait_ge(S_["x1"], z_rdy)
        for j in range(T // TS):
            ctx.dma(e, stap[:, j * TS:(j + 1) * TS], sq2[:], "qg")
        e.wait_ge(S_["qg"], ctx.cnt["qg"])
        ctx.tag(e.engine_nop(), "gp")

    for bi, (bname, is_tok) in enumerate(branches):
        bap = ctx.buf(bname, (T, D) if is_tok else (D, T), BF16)
        for j in range(T // TS):
            @blk.sync
            def _(e, bi=bi, j=j, bap=bap, is_tok=is_tok):
                e.wait_ge(S_["mm"], ctx.cnt["mm"])
                e.wait_ge(S_["sc"], ctx.cnt["sc"])
                ctx.fence(e, "qs")
                if is_tok:
                    for cc in range(DC):
                        ctx.dma(e, sx[:, cc * TS:(cc + 1) * TS],
                                bap[j * TS:(j + 1) * TS, cc * 128:(cc + 1) * 128],
                                "qs", transpose=True)
                else:
                    ctx.dma(e, sb3(sx, DC, TS), fm_view(bap, j * TS, TS), "qs")
            qs_j = ctx.cnt["qs"]

            for cc in range(DC):
                @blk.scalar
                def _(e, cc=cc, qs_j=qs_j):
                    if cc == 0:
                        e.wait_ge(S_["qs"], qs_j)
                        e.wait_ge(S_["ve"], ctx.cnt["ve"])
                        e.wait_ge(S_["gp"], ctx.cnt["gp"])
                    if cc > 0:
                        e.wait_ge(S_["mm"], ctx.cnt["mm"] - 1)
                    e.activation(out=sq2[:], in_=sx[:, cc * TS:(cc + 1) * TS],
                                 func=AF.Square)
                    ins = e.activation(out=sab[:], in_=sx[:, cc * TS:(cc + 1) * TS],
                                       func=AF.Abs)
                    ctx.tag(ins, "sc")
                s_c = ctx.cnt["sc"]

                @blk.tensor
                def _(e, cc=cc, s_c=s_c):
                    e.wait_ge(S_["sc"], s_c)
                    if cc == 0:
                        e.wait_ge(S_["qv"], hs_rdy)
                        e.wait_ge(S_["ve"], ctx.cnt["ve"])
                    e.matmul(ps_x[:], shs[:, cc * 4:(cc + 1) * 4],
                             sx[:, cc * TS:(cc + 1) * TS],
                             start=(cc == 0), stop=(cc == DC - 1))
                    e.matmul(ps_q[:], shs[:, cc * 4:(cc + 1) * 4], sq2[:],
                             start=(cc == 0), stop=(cc == DC - 1))
                    ins = e.matmul(ps_a[:], shs[:, cc * 4:(cc + 1) * 4], sab[:],
                                   start=(cc == 0), stop=(cc == DC - 1))
                    ctx.tag(ins, "mm")
            t_sums = ctx.cnt["mm"]

            @blk.vector
            def _(e, t_sums=t_sums):
                e.wait_ge(S_["mm"], t_sums)
                ins = e.tensor_scalar(out=srw[:, 0:TS], in0=ps_x[:], scalar1=1.0 / 256,
                                      scalar2=None, op0=ALU.mult)
                c1 = ctx.tag(ins, "ve"); e.wait_ge(S_["ve"], c1)
                ins = e.tensor_scalar(out=srw[:, TS:2 * TS], in0=ps_a[:], scalar1=1.0 / 256,
                                      scalar2=None, op0=ALU.mult)
                c1 = ctx.tag(ins, "ve"); e.wait_ge(S_["ve"], c1)
                ins = e.tensor_copy(out=srw[:, 2 * TS:3 * TS], in_=ps_q[:])
                c1 = ctx.tag(ins, "ve"); e.wait_ge(S_["ve"], c1)
                ins = e.tensor_tensor(out=srw[:, 3 * TS:4 * TS], in0=ps_x[:],
                                      in1=srw[:, 0:TS], op=ALU.mult)
                c1 = ctx.tag(ins, "ve"); e.wait_ge(S_["ve"], c1)
                ins = e.tensor_tensor(out=srw[:, 3 * TS:4 * TS], in0=srw[:, 2 * TS:3 * TS],
                                      in1=srw[:, 3 * TS:4 * TS], op=ALU.subtract)
                c1 = ctx.tag(ins, "ve"); e.wait_ge(S_["ve"], c1)
                ins = e.tensor_scalar(out=srw[:, 3 * TS:4 * TS], in0=srw[:, 3 * TS:4 * TS],
                                      scalar1=1.0 / 255, scalar2=None, op0=ALU.mult)
                ctx.tag(ins, "ve")
            v_pre = ctx.cnt["ve"]

            @blk.scalar
            def _(e, v_pre=v_pre):
                e.wait_ge(S_["ve"], v_pre)
                e.activation(out=sst[:, 3 * TS:4 * TS].rearrange("p n -> p n")
                             if False else srw[:, 2 * TS:3 * TS],
                             in_=srw[:, 2 * TS:3 * TS], func=AF.Sqrt)
                ins = e.activation(out=srw[:, 3 * TS:4 * TS], in_=srw[:, 3 * TS:4 * TS],
                                   func=AF.Sqrt)
                ctx.tag(ins, "sc")
            s_sqr = ctx.cnt["sc"]

            @blk.vector
            def _(e, s_sqr=s_sqr):
                e.wait_ge(S_["sc"], s_sqr)
                e.tensor_copy(out=sst[:, 0:TS], in_=srw[:, 0:TS])
                e.tensor_copy(out=sst[:, TS:2 * TS], in_=srw[:, 3 * TS:4 * TS])
                e.tensor_copy(out=sst[:, 2 * TS:3 * TS], in_=srw[:, TS:2 * TS])
                ins = e.tensor_copy(out=sst[:, 3 * TS:4 * TS], in_=srw[:, 2 * TS:3 * TS])
                ctx.tag(ins, "ve")
            v_pk = ctx.cnt["ve"]

            @blk.gpsimd
            def _(e, bi=bi, j=j, v_pk=v_pk):
                e.wait_ge(S_["ve"], v_pk)
                for s in range(4):
                    ctx.dma(e, stap.rearrange("(hh r) t -> r hh t", hh=4)
                            [bi * 4 + s, 0:4, j * TS:(j + 1) * TS]
                            if False else
                            stap[0:128, j * TS:(j + 1) * TS]
                            .rearrange("(hh r) t -> r hh t", r=32)[bi * 4 + s, 0:4, :],
                            sst[:, s * TS:(s + 1) * TS], "qg")
                e.wait_ge(S_["qg"], ctx.cnt["qg"])
                ctx.tag(e.engine_nop(), "gp")


def ph_l1_logits(ctx):
    hg = ctx.buf("l1hg", (D, T), BF16)
    w2 = ctx.inp("l1_mw2", (D, 24), BF16)
    b2 = ctx.inp("l1_mb2", (1, 24), F32)
    glt = ctx.inp("l1_glt", (4, 1), F32)
    m424 = ctx.inp("map424", (4, 24), F32)
    idap = ctx.inp("identf", (128, 128), F32)
    wfap = ctx.buf("l1wf", (24, T), F32)
    S_ = ctx.sems
    blk = ctx.block
    sw2 = ctx.sbuf("g1_w2", [128, DC * 24], BF16)
    sb2 = ctx.sbuf("g1_b2", [1, 24], F32)
    sb2b = ctx.sbuf("g1_b2b", [1, 24], BF16)
    soneb = ctx.sbuf("g1_1b", [1, 128], BF16)
    sg = ctx.sbuf("g1_g", [4, 8], F32)
    sm = ctx.sbuf("g1_m", [4, 24], F32)
    sone = ctx.sbuf("g1_1", [1, 128], F32)
    srow = ctx.sbuf("g1_r", [1, 24], F32)
    sitr = ctx.sbuf("g1_it", [128, 24], F32)
    sidf = ctx.sbuf("g1_id", [128, 128], F32)
    sx = [ctx.sbuf(f"g1_x{i}", [128, DC * 128], BF16) for i in range(2)]
    sl = ctx.sbuf("g1_l", [128, 24], F32)
    scol = ctx.sbuf("g1_c", [128, 12], F32)
    swf = [ctx.sbuf(f"g1_o{i}", [24, 128], F32) for i in range(2)]
    ps = ctx.psum("g1_p", [128, 24])
    ps2 = ctx.psum("g1_p2", [1, 24])
    ps3 = ctx.psum("g1_p3", [128, 24])
    ps4 = ctx.psum("g1_p4", [24, 128])

    @blk.sync
    def _(e):
        ctx.dma(e, sw2.rearrange("p (c n) -> p c n", c=DC),
                w2.rearrange("(c p) n -> p c n", p=128), "qv")
        ctx.dma(e, sb2[:], b2[:], "qv")
        ctx.dma(e, sg[:, 0:1], glt[:], "qv")
        ctx.dma(e, sm[:], m424[:], "qv")
        ctx.dma(e, sidf[:], idap[:], "qv")
    w_rdy = ctx.cnt["qv"]

    @blk.vector
    def _(e):
        e.wait_ge(S_["qv"], w_rdy)
        ins = e.memset(sone[:], 1.0)
        c = ctx.tag(ins, "ve"); e.wait_ge(S_["ve"], c)
        ins = e.memset(soneb[:], 1.0)
        c = ctx.tag(ins, "ve"); e.wait_ge(S_["ve"], c)
        ins = e.tensor_copy(out=sb2b[:], in_=sb2[:])
        c = ctx.tag(ins, "ve"); e.wait_ge(S_["ve"], c)
        ins = e.memset(sg[:, 4:5], 1.0)
        c = ctx.tag(ins, "ve"); e.wait_ge(S_["ve"], c)
        ins = e.tensor_scalar(out=sg[:, 1:2], in0=sg[:, 0:1], scalar1=0.0,
                              scalar2=None, op0=ALU.max)
        ctx.tag(ins, "ve")
    v_t1 = ctx.cnt["ve"]

    @blk.scalar
    def _(e, v_t1=v_t1):
        e.wait_ge(S_["ve"], v_t1)
        e.activation(out=sg[:, 2:3], in_=sg[:, 0:1], func=AF.Abs)
        ins = e.activation(out=sg[:, 3:4], in_=sg[:, 2:3], func=AF.Exp, scale=-1.0)
        ctx.tag(ins, "sc")
    s_t1 = ctx.cnt["sc"]

    @blk.scalar
    def _(e, s_t1=s_t1):
        e.wait_ge(S_["sc"], s_t1)
        ins = e.activation(out=sg[:, 5:6], in_=sg[:, 3:4], func=AF.Ln,
                           bias=sg[:, 4:5])
        ctx.tag(ins, "sc")
    s_t2 = ctx.cnt["sc"]

    @blk.vector
    def _(e, s_t2=s_t2):
        e.wait_ge(S_["sc"], s_t2)
        ins = e.tensor_tensor(out=sg[:, 6:7], in0=sg[:, 1:2], in1=sg[:, 5:6], op=ALU.add)
        c = ctx.tag(ins, "ve"); e.wait_ge(S_["ve"], c)
        ins = e.tensor_scalar(out=sg[:, 6:7], in0=sg[:, 6:7], scalar1=0.5,
                              scalar2=None, op0=ALU.add)
        c = ctx.tag(ins, "ve"); e.wait_ge(S_["ve"], c)
        ins = e.reciprocal(out=sg[:, 7:8], in_=sg[:, 6:7])
        ctx.tag(ins, "ve")
    v_tau = ctx.cnt["ve"]

    @blk.tensor
    def _(e, v_tau=v_tau):
        e.wait_ge(S_["ve"], v_tau)
        ins = e.matmul(ps2[:], sg[:, 7:8], sm[:], start=True, stop=True)
        ctx.tag(ins, "mm")
    t_row = ctx.cnt["mm"]

    @blk.scalar
    def _(e, t_row=t_row):
        e.wait_ge(S_["mm"], t_row)
        ins = e.activation(out=srow[:], in_=ps2[:], func=AF.Copy)
        ctx.tag(ins, "sc")
    s_row = ctx.cnt["sc"]

    @blk.tensor
    def _(e, s_row=s_row):
        e.wait_ge(S_["sc"], s_row)
        ins = e.matmul(ps3[:], sone[:], srow[:], start=True, stop=True)
        ctx.tag(ins, "mm")
    t_itr = ctx.cnt["mm"]

    @blk.scalar
    def _(e, t_itr=t_itr):
        e.wait_ge(S_["mm"], t_itr)
        ins = e.activation(out=sitr[:], in_=ps3[:], func=AF.Copy)
        ctx.tag(ins, "sc")
    s_itr = ctx.cnt["sc"]

    for i in range(T // 128):
        @blk.sync
        def _(e, i=i):
            if i >= 2:
                e.wait_ge(S_["gp"], ctx.cnt["gp"] - 1)
            ctx.fence(e, "qs")
            ctx.dma(e, sb3(sx[i % 2], DC, 128), fm_view(hg, i * 128, 128), "qs")
        qs_i = ctx.cnt["qs"]

        @blk.tensor
        def _(e, i=i, qs_i=qs_i, s_itr=s_itr):
            e.wait_ge(S_["qs"], qs_i)
            if i == 0:
                e.wait_ge(S_["sc"], s_itr)
            else:
                e.wait_ge(S_["ve"], ctx.cnt["ve"])
            for c in range(DC):
                e.matmul(ps[:], sx[i % 2][:, c * 128:(c + 1) * 128],
                         sw2[:, c * 24:(c + 1) * 24],
                         start=(c == 0), stop=False)
            ins = e.matmul(ps[:], soneb[0:1, 0:128], sb2b[:], start=False, stop=True)
            ctx.tag(ins, "mm")
        t_l = ctx.cnt["mm"]

        @blk.vector
        def _(e, i=i, t_l=t_l):
            e.wait_ge(S_["mm"], t_l)
            ins = e.tensor_tensor(out=sl[:], in0=ps[:], in1=sitr[:], op=ALU.mult)
            ctx.tag(ins, "ve")
        v_l = ctx.cnt["ve"]

        @blk.vector
        def _(e, i=i, v_l=v_l):
            e.wait_ge(S_["ve"], v_l)
            ins = e.tensor_reduce(out=scol[:, 0:4],
                                  in_=sl[:].rearrange("p (h s) -> p h s", h=4),
                                  axis=AX.X, op=ALU.max)
            c = ctx.tag(ins, "ve"); e.wait_ge(S_["ve"], c)
            ins = e.tensor_scalar(out=scol[:, 4:8], in0=scol[:, 0:4], scalar1=-1.0,
                                  scalar2=None, op0=ALU.mult)
            ctx.tag(ins, "ve")
        v_mx = ctx.cnt["ve"]

        s_e = {}
        for h in range(4):
            @blk.scalar
            def _(e, i=i, h=h, v_mx=v_mx):
                e.wait_ge(S_["ve"], v_mx)
                ins = e.activation(out=sl[:, h * 6:(h + 1) * 6], in_=sl[:, h * 6:(h + 1) * 6],
                                   func=AF.Exp, bias=scol[:, 4 + h:5 + h],
                                   accum_out=scol[:, 8 + h:9 + h])
                ctx.tag(ins, "sc")
            s_e[h] = ctx.cnt["sc"]

        @blk.vector
        def _(e, i=i, s_e=dict(s_e)):
            e.wait_ge(S_["sc"], s_e[3])
            ins = e.reciprocal(out=scol[:, 8:12], in_=scol[:, 8:12])
            c = ctx.tag(ins, "ve"); e.wait_ge(S_["ve"], c)
            for h in range(4):
                ins = e.tensor_scalar(out=sl[:, h * 6:(h + 1) * 6],
                                      in0=sl[:, h * 6:(h + 1) * 6],
                                      scalar1=scol[:, 8 + h:9 + h], scalar2=None,
                                      op0=ALU.mult)
            ctx.tag(ins, "ve")
        v_w = ctx.cnt["ve"]

        @blk.tensor
        def _(e, i=i, v_w=v_w):
            e.wait_ge(S_["ve"], v_w)
            if i >= 1:
                e.wait_ge(S_["sc"], ctx.cnt["sc"])
            ins = e.transpose(ps4[:, :], sl[:, 0:24], sidf[:])
            ctx.tag(ins, "mm")
        t_tr = ctx.cnt["mm"]

        @blk.scalar
        def _(e, i=i, t_tr=t_tr):
            e.wait_ge(S_["mm"], t_tr)
            ins = e.activation(out=swf[i % 2][:], in_=ps4[0:24, :], func=AF.Copy)
            ctx.tag(ins, "sc")
        s_tr = ctx.cnt["sc"]

        @blk.gpsimd
        def _(e, i=i, s_tr=s_tr):
            e.wait_ge(S_["sc"], s_tr)
            ctx.dma(e, wfap[:, i * 128:(i + 1) * 128], swf[i % 2][:], "qg")
            e.wait_ge(S_["qg"], ctx.cnt["qg"])
            ctx.tag(e.engine_nop(), "gp")


def ph_l1_fuse(ctx):
    wfap = ctx.buf("l1wf", (24, T), F32)
    o24ap = ctx.inp("onehot24", (24, 24 * 128), F32)
    onw = ctx.inp("l1_onorm", (128, DC), F32)
    fu = ctx.buf("l1fu", (D, T), BF16)
    branches = [("l1f1", False), ("l1f3", False), ("l1f7", False), ("l1f31", False),
                ("l1dt", True), ("l1v", False)]
    S_ = ctx.sems
    blk = ctx.block
    TS = 512
    sw = ctx.sbuf("u1_w", [24, TS], F32)
    so24 = ctx.sbuf("u1_o24", [24, 24 * 128], F32)
    sone = ctx.sbuf("u1_1", [1, 128], F32)
    sonc = ctx.sbuf("u1_1c", [128, 1], F32)
    sonw = ctx.sbuf("u1_on", [128, DC], F32)
    sx = ctx.sbuf("u1_x", [128, 2 * TS], BF16)
    swr = ctx.sbuf("u1_wr", [128, TS], F32)
    sac = ctx.sbuf("u1_ac", [128, DC * TS], F32)
    ssq = ctx.sbuf("u1_sq", [128, TS], F32)
    srow = ctx.sbuf("u1_r", [1, TS], F32)
    so = ctx.sbuf("u1_o", [128, DC * TS], BF16)
    ps_w = ctx.psum("u1_pw", [128, TS])
    ps_s = ctx.psum("u1_ps", [1, TS])
    ps_r = ctx.psum("u1_pr", [128, TS])

    @blk.sync
    def _(e):
        ctx.dma(e, sonw[:], onw[:], "qv")
        ctx.dma(e, so24[:], o24ap[:], "qv")
    on_rdy = ctx.cnt["qv"]

    @blk.vector
    def _(e):
        e.memset(sone[:], 1.0)
        ins = e.memset(sonc[:], 1.0)
        ctx.tag(ins, "x1")
    one_rdy = ctx.cnt["x1"]

    for j in range(T // TS):
        @blk.sync
        def _(e, j=j):
            e.wait_ge(S_["mm"], ctx.cnt["mm"])
            ctx.fence(e, "qs")
            ctx.dma(e, sw[:], wfap[:, j * TS:(j + 1) * TS], "qs")
        w_rdy = ctx.cnt["qs"]

        for h in range(4):
            for bi, (bname, is_tok) in enumerate(branches):
                bap = ctx.buf(bname, (T, D) if is_tok else (D, T), BF16)

                @blk.sync
                def _(e, j=j, h=h, bi=bi, bap=bap, is_tok=is_tok):
                    e.wait_ge(S_["ve"], ctx.cnt["ve"])
                    ctx.fence(e, "qs")
                    if is_tok:
                        for cc in range(2):
                            ctx.dma(e, sx[:, cc * TS:(cc + 1) * TS],
                                    bap[j * TS:(j + 1) * TS,
                                        (2 * h + cc) * 128:(2 * h + cc + 1) * 128],
                                    "qs", transpose=True)
                    else:
                        ctx.dma(e, sb3(sx, 2, TS), fm_view(bap, j * TS, TS, c0=2 * h, nch=2), "qs")
                qs_b = ctx.cnt["qs"]

                @blk.tensor
                def _(e, j=j, h=h, bi=bi, w_rdy=w_rdy, qs_b=qs_b):
                    if bi == 0:
                        e.wait_ge(S_["qs"], w_rdy)
                        e.wait_ge(S_["x1"], one_rdy)
                        if h == 0 and j == 0:
                            e.wait_ge(S_["qv"], on_rdy)
                    e.wait_ge(S_["sc"], ctx.cnt["sc"])
                    e.wait_ge(S_["ve"], ctx.cnt["ve"])
                    r = h * 6 + bi
                    ins = e.matmul(ps_w[:], so24[:, r * 128:(r + 1) * 128],
                                   sw[0:24, :], start=True, stop=True)
                    ctx.tag(ins, "mm")
                t_w = ctx.cnt["mm"]

                @blk.scalar
                def _(e, t_w=t_w):
                    e.wait_ge(S_["mm"], t_w)
                    ins = e.activation(out=swr[:], in_=ps_w[:], func=AF.Copy)
                    ctx.tag(ins, "sc")
                s_w = ctx.cnt["sc"]

                @blk.vector
                def _(e, j=j, h=h, bi=bi, s_w=s_w, qs_b=qs_b):
                    e.wait_ge(S_["sc"], s_w)
                    e.wait_ge(S_["qs"], qs_b)
                    for cc in range(2):
                        dst = sac[:, (2 * h + cc) * TS:(2 * h + cc + 1) * TS]
                        if bi == 0:
                            ins = e.tensor_tensor(out=dst, in0=sx[:, cc * TS:(cc + 1) * TS],
                                                  in1=swr[:], op=ALU.mult)
                            c1 = ctx.tag(ins, "ve"); e.wait_ge(S_["ve"], c1)
                        else:
                            ins = e.tensor_tensor(out=ssq[:], in0=sx[:, cc * TS:(cc + 1) * TS],
                                                  in1=swr[:], op=ALU.mult)
                            c1 = ctx.tag(ins, "ve"); e.wait_ge(S_["ve"], c1)
                            ins = e.tensor_tensor(out=dst, in0=dst, in1=ssq[:], op=ALU.add)
                            c1 = ctx.tag(ins, "ve"); e.wait_ge(S_["ve"], c1)
                    ctx.tag(e.engine_nop(), "ve")

        for h in range(4):
            @blk.scalar
            def _(e, j=j, h=h):
                e.wait_ge(S_["ve"], ctx.cnt["ve"])
                e.wait_ge(S_["mm"], ctx.cnt["mm"])
                e.activation(out=ssq[:], in_=sac[:, (2 * h) * TS:(2 * h + 1) * TS],
                             func=AF.Square)
                ins = e.activation(out=swr[:], in_=sac[:, (2 * h + 1) * TS:(2 * h + 2) * TS],
                                   func=AF.Square)
                ctx.tag(ins, "sc")
            s_q = ctx.cnt["sc"]

            @blk.tensor
            def _(e, h=h, s_q=s_q):
                e.wait_ge(S_["sc"], s_q)
                e.wait_ge(S_["ve"], ctx.cnt["ve"])
                e.matmul(ps_s[:], sonc[:], ssq[:], start=True, stop=False)
                ins = e.matmul(ps_s[:], sonc[:], swr[:], start=False, stop=True)
                ctx.tag(ins, "mm")
            t_ss = ctx.cnt["mm"]

            @blk.vector
            def _(e, t_ss=t_ss):
                e.wait_ge(S_["mm"], t_ss)
                ins = e.tensor_scalar(out=srow[:], in0=ps_s[:], scalar1=1.0 / 256,
                                      scalar2=1e-5, op0=ALU.mult, op1=ALU.add)
                ctx.tag(ins, "ve")
            v_ms = ctx.cnt["ve"]

            @blk.scalar
            def _(e, v_ms=v_ms):
                e.wait_ge(S_["ve"], v_ms)
                ins = e.activation(out=srow[:], in_=srow[:], func=AF.Sqrt)
                ctx.tag(ins, "sc")
            s_sd = ctx.cnt["sc"]

            @blk.vector
            def _(e, s_sd=s_sd):
                e.wait_ge(S_["sc"], s_sd)
                ins = e.reciprocal(out=srow[:], in_=srow[:])
                ctx.tag(ins, "ve")
            v_rs = ctx.cnt["ve"]

            @blk.tensor
            def _(e, v_rs=v_rs):
                e.wait_ge(S_["ve"], v_rs)
                ins = e.matmul(ps_r[:], sone[:], srow[:], start=True, stop=True)
                ctx.tag(ins, "mm")
            t_rb = ctx.cnt["mm"]

            @blk.vector
            def _(e, j=j, h=h, t_rb=t_rb, on_rdy=on_rdy):
                e.wait_ge(S_["mm"], t_rb)
                if j == 0 and h == 0:
                    e.wait_ge(S_["qv"], on_rdy)
                for cc in range(2):
                    sl_ = slice((2 * h + cc) * TS, (2 * h + cc + 1) * TS)
                    ins = e.tensor_tensor(out=sac[:, sl_], in0=sac[:, sl_],
                                          in1=ps_r[:], op=ALU.mult)
                    c1 = ctx.tag(ins, "ve"); e.wait_ge(S_["ve"], c1)
                    ins = e.tensor_scalar(out=so[:, sl_], in0=sac[:, sl_],
                                          scalar1=sonw[:, 2 * h + cc:2 * h + cc + 1],
                                          scalar2=None, op0=ALU.mult)
                    c1 = ctx.tag(ins, "ve"); e.wait_ge(S_["ve"], c1)
                ctx.tag(e.engine_nop(), "ve")
        v_done = ctx.cnt["ve"]

        @blk.gpsimd
        def _(e, j=j, v_done=v_done):
            e.wait_ge(S_["ve"], v_done)
            ctx.dma(e, fm_view(fu, j * TS, TS), sb3(so, DC, TS), "qg")
            e.wait_ge(S_["qg"], ctx.cnt["qg"])
            ctx.tag(e.engine_nop(), "gp")


def l1_attn_phases():
    return [
        lambda c: ph_mm(c, "l1qp", "xn1", 1024, "l1_qw", "l1qp", 1024, "fm"),
        lambda c: ph_mm(c, "l1kp", "xn1", 1024, "l1_kw", "l1kp", 1024, "fm"),
        lambda c: ph_mm(c, "l1vp", "xn1", 1024, "l1_vw", "l1vp", 1024, "fm"),
        ph_l1_conv,
        ph_l1_beta,
        ph_l1_norm,
        lambda c: ph_t2tok(c, "l1tk", "l1kn", 1024, "l1knt"),
        lambda c: ph_t2tok(c, "l1tv", "l1vb", 1024, "l1vbt"),
        ph_l1_delta,
        ph_l1_stats,
        lambda c: ph_mm(c, "l1hg", "xn1", 1024, "l1_mw1", "l1hg", 1024, "fm",
                        AF.Gelu, biascol="l1_mb1", src_extra=("l1st", 128)),
        ph_l1_logits,
        ph_l1_fuse,
        lambda c: ph_mm(c, "l1ow", "l1fu", 1024, "l1_ow", "attn1", 1024, "fm",
                        None, dst_dt=F32),
    ]
IFACE = {}
for _L in range(4):
    IFACE[f"xres{_L + 1}"] = ((D, T), np.float32)
    IFACE[f"xn{_L}"] = ((D, T), BF16NP)
    IFACE[f"attn{_L}"] = ((D, T), np.float32)
    IFACE[f"ffn{_L}"] = ((D, T), np.float32)
IFACE["xnf3"] = ((D, T), np.float32)
del IFACE["xres4"]


def l0_phase_list():
    return [
        lambda c: ph_ln(c, 0),
        lambda c: ph_mm(c, "l0q", "xn0", 1024, "l0_wq", "l0q", 1024, "fm",
                        AF.Silu, alt=True),
        lambda c: ph_mm(c, "l0k", "xn0", 1024, "l0_wk", "l0kb", 1024, "fm",
                        AF.Sigmoid, alt=True, dst2="l0k"),
        lambda c: ph_mm(c, "l0v", "xn0", 1024, "l0_wv", "l0v", 1024, "tok"),
        lambda c: ph_mm(c, "l0g1", "xn0", 1024, "l0_wg1", "l0g1", 128, "fm"),
        lambda c: ph_mm(c, "l0gate", "l0g1", 128, "l0_wg2", "l0gate", 1024,
                        "tok", AF.Sigmoid),
        ph_l0_scan,
        ph_l0_out,
        lambda c: ph_mm(c, "l0wo", "l0on", 1024, "l0_wo", "attn0", 1024, "fm",
                        None, dst_dt=F32, src_tok=True),
        lambda c: ph_mm(c, "l0sc", "xnf0", 1024, "l0_wgatef", "l0sc", 8, "tok",
                        None, dst_dt=F32, io_dt=F32),
        ph_l0_moe,
        lambda c: ph_residual(c, 0),
    ]


def layer_groups():
    g0 = (l0_phase_list(), set(), {"xres1"})
    g1 = ([lambda c: ph_ln(c, 1)] + l1_attn_phases() + [ph_l1ffn,
          lambda c: ph_residual(c, 1)], {"xres1"}, {"xres2"})
    g2 = ([lambda c: ph_ln(c, 2)] + l2_attn_phases() + lstm_phases(2) +
          [lambda c: ph_residual(c, 2)], {"xres2"}, {"xres3"})
    g3 = ([lambda c: ph_ln(c, 3)] + l3_attn_phases() + lstm_phases(3) +
          [lambda c: ph_residual(c, 3)], {"xres3"}, set())
    return [g0, g1, g2, g3]


FUSED = True


def _build(phases, fused, reads, writes):
    ctx = Ctx(fused, reads=reads, writes=writes)
    for ph in phases:
        ph(ctx)
        ctx.barrier()
    return ctx.close()


def kernel(**inputs):
    W = prep_weights(inputs)
    Cc = prep_consts()
    x = np.ascontiguousarray(np.asarray(inputs["L0_attn_x"], dtype=np.float32))
    percore_x = [np.ascontiguousarray(x[c].T) for c in range(NCORES)]
    gidx = [lstm_host_indices(c) for c in range(NCORES)]
    store = [dict() for _ in range(NCORES)]

    if FUSED:
        groups = [(sum([g[0] for g in layer_groups()], []), set(), set())]
    else:
        groups = layer_groups()

    for phases, reads, writes in groups:
        nc = _build(phases, FUSED, reads, writes)
        pid = nc.partition_id_tensor.name if nc.partition_id_tensor else None
        need = {}
        for alloc in nc.m.functions[0].allocations:
            if getattr(alloc, "kind", None) == "ExternalInput":
                nm = alloc.memorylocations[0].name
                need[nm] = mybir.dt.np(alloc.dtype)
        in_maps = []
        for c in range(NCORES):
            m = {}
            for nm, dt in need.items():
                if nm == pid:
                    continue
                if nm == "x_in":
                    arr = percore_x[c]
                elif nm == "lstm_gidx":
                    arr = gidx[c]
                elif nm in store[c]:
                    arr = store[c][nm]
                elif nm in W:
                    arr = W[nm]
                elif nm in Cc:
                    arr = Cc[nm]
                else:
                    raise KeyError(f"missing input {nm}")
                m[nm] = np.ascontiguousarray(arr).astype(dt, copy=False)
            in_maps.append(m)
        res = run_bass_kernel_spmd(nc, in_maps, core_ids=list(range(NCORES)))
        for c in range(NCORES):
            store[c].update(res.results[c])

    out = np.stack([store[c]["out"].T for c in range(NCORES)], axis=0)
    return np.ascontiguousarray(out, dtype=np.float32)
